# version 03
# baseline (speedup 1.0000x reference)
import numpy as np
import ml_dtypes
import concourse.bass as bass
import concourse.mybir as mybir
from concourse.bass_utils import run_bass_kernel_spmd
from contextlib import ExitStack

F32 = mybir.dt.float32
BF16 = mybir.dt.bfloat16
I32 = mybir.dt.int32
AF = mybir.ActivationFunctionType
ALU = mybir.AluOpType
AX = mybir.AxisListType

D = 4096
DC = 32
T = 2048
NT = 4
TW = 512
NCORES = 8
EPS = 1e-5
NEG = -30000.0


class Prog:
    def __init__(self):
        self.nc = bass.Bass("TRN2", target_bir_lowering=False, debug=False)
        self.ctx = ExitStack()
        self.sems = {}
        self.semv = {}

    def sem(self, name):
        if name not in self.sems:
            self.sems[name] = self.ctx.enter_context(self.nc.semaphore(name))
            self.semv[name] = 0
        return self.sems[name]

    def sem2(self, name):
        return [self.sem(name + "0"), self.sem(name + "1")]

    def _uniq(self, name):
        if not hasattr(self, "_names"):
            self._names = {}
        k = self._names.get(name, 0)
        self._names[name] = k + 1
        return name if k == 0 else f"{name}__{k}"

    def sb(self, name, shape, dt=F32, stack=None):
        return (stack or self.ctx).enter_context(
            self.nc.sbuf_tensor(self._uniq(name), shape, dt))

    def psum(self, name, shape, dt=F32, stack=None):
        return (stack or self.ctx).enter_context(
            self.nc.psum_tensor(self._uniq(name), shape, dt))

    def din(self, name, shape, dt=F32):
        return self.nc.dram_tensor(name, shape, dt, kind="ExternalInput").ap()

    def dout(self, name, shape, dt=F32):
        return self.nc.dram_tensor(name, shape, dt, kind="ExternalOutput").ap()

    def dint(self, name, shape, dt=F32):
        return self.nc.dram_tensor(name, shape, dt, kind="Internal").ap()


def ap3(t, parts, dims):
    h = t[parts[0]:parts[0] + parts[1], 0:1]
    return bass.AP(h.tensor, h.offset, [h.ap[0]] + [list(d) for d in dims])



def emit_consts(p: Prog, cst):
    nc = p.nc
    with nc.Block() as block:
        @block.gpsimd
        def _(g):
            g.iota(cst["io_free"][:, :], [[1, 128]], channel_multiplier=0)
            g.iota(cst["io_part"][:, :], [[0, 128]], channel_multiplier=1)
            g.iota(cst["io_g32"][:, :], [[1, 4], [0, 32]], channel_multiplier=0)

        @block.vector
        def _(v):
            v.memset(cst["ones32"][:, :], 1.0)
            v.memset(cst["ones1"][:, :], 1.0)
            v.memset(cst["eps"][:, :], EPS)
            v.memset(cst["neg8"][:, :], -8.0)

    with nc.Block() as block:
        @block.vector
        def _(v):
            v.tensor_tensor(cst["ident_bf"][:, :], cst["io_free"][:, :],
                            cst["io_part"][:, :], ALU.is_equal)
            v.tensor_tensor(cst["ident_f"][:, :], cst["io_free"][:, :],
                            cst["io_part"][:, :], ALU.is_equal)
            for j in range(NT):
                v.tensor_scalar(cst["sel"][j][:, :], cst["io_part"][:, :],
                                32 * j, None, ALU.is_equal)


def emit_stream_stats(p: Prog, tag, src_ap, ps_s1, ps_s2, cst, xbufs, sqbuf):
    nc = p.nc
    ld = p.sem2(tag + "_ld"); b_ld = [p.semv[tag + "_ld0"], p.semv[tag + "_ld1"]]
    sq = p.sem(tag + "_sq"); b_sq = p.semv[tag + "_sq"]
    mm = p.sem(tag + "_mm"); b_mm = p.semv[tag + "_mm"]
    ldt = lambda c: b_ld[c % 2] + 16 * (c // 2 + 1)

    with nc.Block() as block:
        @block.sync
        def _(sync):
            for c in range(DC):
                if c >= 2:
                    sync.wait_ge(sq, b_sq + c - 1)
                    sync.wait_ge(mm, b_mm + c - 1)
                sync.dma_start(out=xbufs[c % 2][:, :],
                               in_=src_ap[c * 128:(c + 1) * 128, :]).then_inc(ld[c % 2], 16)

        @block.scalar
        def _(s):
            for c in range(DC):
                s.wait_ge(ld[c % 2], ldt(c))
                if c >= 2:
                    s.wait_ge(mm, b_mm + c - 1)
                s.activation(sqbuf[c % 2][:, :], xbufs[c % 2][:, :],
                             AF.Square).then_inc(sq, 1)

        @block.tensor
        def _(t):
            for c in range(DC):
                t.wait_ge(ld[c % 2], ldt(c))
                for j in range(NT):
                    t.matmul(ps_s1[32 * j:32 * j + 32, :], cst["ones32"][:, :],
                             xbufs[c % 2][:, j * TW:(j + 1) * TW],
                             start=(c == 0), stop=(c == DC - 1),
                             tile_position=(0, 32 * j), skip_group_check=True)
                t.wait_ge(sq, b_sq + c + 1)
                for j in range(NT):
                    mmi = t.matmul(ps_s2[32 * j:32 * j + 32, :], cst["ones32"][:, :],
                                   sqbuf[c % 2][:, j * TW:(j + 1) * TW],
                                   start=(c == 0), stop=(c == DC - 1),
                                   tile_position=(0, 32 * j), skip_group_check=True)
                mmi.then_inc(mm, 1)

    p.semv[tag + "_ld0"] = b_ld[0] + 16 * (DC // 2)
    p.semv[tag + "_ld1"] = b_ld[1] + 16 * (DC // 2)
    p.semv[tag + "_sq"] = b_sq + DC
    p.semv[tag + "_mm"] = b_mm + DC


def emit_rows(p: Prog, tag, ps_s1, ps_s2, ps_aux, cst, rowbufs, RB, NB):
    nc = p.nc
    s1g, s2g, varg, rstdg, negmg = (rowbufs[k] for k in
                                    ("s1", "s2", "var", "rstd", "negm"))
    bc = p.sem(tag + "_bc"); b_bc = p.semv[tag + "_bc"]
    ev = p.sem(tag + "_ev"); b_ev = p.semv[tag + "_ev"]

    with nc.Block() as block:
        @block.vector
        def _(v):
            v.tensor_copy(s1g[:, :], ps_s1[:, :])
            v.tensor_copy(s2g[:, :], ps_s2[:, :])
            v.tensor_scalar(s1g[:, :], s1g[:, :], 1.0 / D, None, ALU.mult)
            v.tensor_scalar(s2g[:, :], s2g[:, :], 1.0 / D, None, ALU.mult)
            v.tensor_tensor(varg[:, :], s1g[:, :], s1g[:, :], ALU.mult)
            v.tensor_tensor(varg[:, :], s2g[:, :], varg[:, :], ALU.subtract).then_inc(ev, 1)

        @block.scalar
        def _(s):
            s.wait_ge(ev, b_ev + 1)
            s.activation(rstdg[:, :], varg[:, :], AF.Ln, bias=cst["eps"][:, :], scale=1.0)
            s.activation(rstdg[:, :], rstdg[:, :], AF.Exp, scale=-0.5).then_inc(ev, 1)

        @block.vector
        def _(v):
            v.wait_ge(ev, b_ev + 2)
            v.scalar_tensor_tensor(negmg[:, :], s1g[:, :], -1.0, rstdg[:, :],
                                   ALU.mult, ALU.mult).then_inc(ev, 1)

        @block.tensor
        def _(t):
            t.wait_ge(ev, b_ev + 3)
            for j in range(NT):
                t.matmul(ps_aux[0][:, :], cst["sel"][j][:, :], rstdg[:, :],
                         start=True, stop=True).then_inc(bc, 1)
                t.wait_ge(ev, b_ev + 3 + 2 * j + 1)
                t.matmul(ps_aux[1][:, :], cst["sel"][j][:, :], negmg[:, :],
                         start=True, stop=True).then_inc(bc, 1)
                t.wait_ge(ev, b_ev + 3 + 2 * j + 2)

        @block.vector
        def _(v):
            for j in range(NT):
                v.wait_ge(bc, b_bc + 2 * j + 1)
                v.tensor_copy(RB[:, j * TW:(j + 1) * TW], ps_aux[0][:, :]).then_inc(ev, 1)
                v.wait_ge(bc, b_bc + 2 * j + 2)
                v.tensor_copy(NB[:, j * TW:(j + 1) * TW], ps_aux[1][:, :]).then_inc(ev, 1)

    p.semv[tag + "_bc"] = b_bc + 2 * NT
    p.semv[tag + "_ev"] = b_ev + 3 + 2 * NT


def emit_hpass(p: Prog, tag, src_ap, RB, NB, h_sb, xbufs, tscr):
    nc = p.nc
    ld = p.sem2(tag + "_ld"); b_ld = [p.semv[tag + "_ld0"], p.semv[tag + "_ld1"]]
    hd = p.sem(tag + "_hd"); b_hd = p.semv[tag + "_hd"]

    with nc.Block() as block:
        @block.sync
        def _(sync):
            for c in range(DC):
                if c >= 2:
                    sync.wait_ge(hd, b_hd + c - 1)
                sync.dma_start(out=xbufs[c % 2][:, :],
                               in_=src_ap[c * 128:(c + 1) * 128, :]).then_inc(ld[c % 2], 16)

        @block.vector
        def _(v):
            for c in range(DC):
                v.wait_ge(ld[c % 2], b_ld[c % 2] + 16 * (c // 2 + 1))
                v.tensor_tensor(tscr[:, :], xbufs[c % 2][:, :], RB[:, :], ALU.mult)
                v.tensor_tensor(h_sb[:, c * T:(c + 1) * T], tscr[:, :], NB[:, :],
                                ALU.add).then_inc(hd, 1)

    p.semv[tag + "_ld0"] = b_ld[0] + 16 * (DC // 2)
    p.semv[tag + "_ld1"] = b_ld[1] + 16 * (DC // 2)
    p.semv[tag + "_hd"] = b_hd + DC


def emit_update(p: Prog, tag, x_ap, a_ap, RB2, NB2, out_ap, ps_s1, ps_s2, cst,
                bufs, do_stats=True):
    nc = p.nc
    HW = T // 2
    NI = DC * 2
    lx = p.sem2(tag + "_lx"); b_lx = [p.semv[tag + "_lx0"], p.semv[tag + "_lx1"]]
    la = p.sem2(tag + "_la"); b_la = [p.semv[tag + "_la0"], p.semv[tag + "_la1"]]
    st = p.sem2(tag + "_st"); b_st = [p.semv[tag + "_st0"], p.semv[tag + "_st1"]]
    up = p.sem(tag + "_up"); b_up = p.semv[tag + "_up"]
    sq = p.sem(tag + "_sq"); b_sq = p.semv[tag + "_sq"]
    mm = p.sem(tag + "_mm"); b_mm = p.semv[tag + "_mm"]
    xb, ab, xn, t1, sqb = bufs["xb"], bufs["ab"], bufs["xn"], bufs["t1"], bufs["sq"]
    lxt = lambda i: b_lx[i % 2] + 16 * (i // 2 + 1)
    lat = lambda i: b_la[i % 2] + 16 * (i // 2 + 1)
    stt = lambda i: b_st[i % 2] + 16 * (i // 2 + 1)

    def dsl(i):
        c, u = i // 2, i % 2
        return (slice(c * 128, (c + 1) * 128), slice(u * HW, (u + 1) * HW))

    with nc.Block() as block:
        @block.sync
        def _(sync):
            for i in range(NI):
                if i >= 2:
                    sync.wait_ge(up, b_up + i - 1)
                r, tsl = dsl(i)
                sync.dma_start(out=xb[i % 2][:, :],
                               in_=x_ap[r, tsl]).then_inc(lx[i % 2], 16)

        @block.gpsimd
        def _(g):
            for i in range(NI):
                if i >= 2:
                    g.wait_ge(up, b_up + i - 1)
                r, tsl = dsl(i)
                g.dma_start(out=ab[i % 2][:, :],
                            in_=a_ap[r, tsl]).then_inc(la[i % 2], 16)
                if i >= 1:
                    g.wait_ge(up, b_up + i)
                    rp, tp = dsl(i - 1)
                    g.dma_start(out=out_ap[rp, tp],
                                in_=xn[(i - 1) % 2][:, :]).then_inc(st[(i - 1) % 2], 16)
            g.wait_ge(up, b_up + NI)
            rp, tp = dsl(NI - 1)
            g.dma_start(out=out_ap[rp, tp],
                        in_=xn[(NI - 1) % 2][:, :]).then_inc(st[(NI - 1) % 2], 16)
            g.wait_ge(st[0], b_st[0] + 16 * (NI // 2))
            g.wait_ge(st[1], b_st[1] + 16 * (NI // 2))

        @block.vector
        def _(v):
            for i in range(NI):
                u = i % 2
                v.wait_ge(la[i % 2], lat(i))
                v.tensor_tensor(t1[:, :], ab[i % 2][:, :],
                                RB2[:, u * HW:(u + 1) * HW], ALU.mult)
                v.tensor_tensor(t1[:, :], t1[:, :],
                                NB2[:, u * HW:(u + 1) * HW], ALU.add)
                v.wait_ge(lx[i % 2], lxt(i))
                if i >= 2:
                    v.wait_ge(st[i % 2], stt(i - 2))
                    if do_stats:
                        v.wait_ge(sq, b_sq + i - 1)
                        v.wait_ge(mm, b_mm + i - 1)
                v.tensor_tensor(xn[i % 2][:, :], t1[:, :], xb[i % 2][:, :],
                                ALU.add).then_inc(up, 1)

        if do_stats:
            @block.scalar
            def _(s):
                for i in range(NI):
                    s.wait_ge(up, b_up + i + 1)
                    if i >= 2:
                        s.wait_ge(mm, b_mm + i - 1)
                    s.activation(sqb[i % 2][:, :], xn[i % 2][:, :],
                                 AF.Square).then_inc(sq, 1)

            @block.tensor
            def _(t):
                for i in range(NI):
                    c, u = i // 2, i % 2
                    t.wait_ge(up, b_up + i + 1)
                    for jj in range(2):
                        j = 2 * u + jj
                        t.matmul(ps_s1[32 * j:32 * j + 32, :], cst["ones32"][:, :],
                                 xn[i % 2][:, jj * TW:(jj + 1) * TW],
                                 start=(c == 0), stop=(c == DC - 1),
                                 tile_position=(0, 32 * j), skip_group_check=True)
                    t.wait_ge(sq, b_sq + i + 1)
                    for jj in range(2):
                        j = 2 * u + jj
                        mmi = t.matmul(ps_s2[32 * j:32 * j + 32, :], cst["ones32"][:, :],
                                       sqb[i % 2][:, jj * TW:(jj + 1) * TW],
                                       start=(c == 0), stop=(c == DC - 1),
                                       tile_position=(0, 32 * j), skip_group_check=True)
                    mmi.then_inc(mm, 1)

    for k in range(2):
        p.semv[tag + f"_lx{k}"] = b_lx[k] + 16 * (NI // 2)
        p.semv[tag + f"_la{k}"] = b_la[k] + 16 * (NI // 2)
        p.semv[tag + f"_st{k}"] = b_st[k] + 16 * (NI // 2)
    p.semv[tag + "_up"] = b_up + NI
    if do_stats:
        p.semv[tag + "_sq"] = b_sq + NI
        p.semv[tag + "_mm"] = b_mm + NI


def emit_maskprep(p: Prog, tag, maskT_ap, cst, bmP, stack):
    nc = p.nc
    mbuf = [p.sb(f"{tag}_mb{i}", [128, 4096], I32, stack) for i in range(2)]
    s1t = p.sb(f"{tag}_s1", [128, 128], F32, stack)
    bmTnb = p.sb(f"{tag}_bmTnb", [128, 128], F32, stack)
    bm01 = p.sb(f"{tag}_bm01", [128, 128], F32, stack)
    selc_a = p.sb(f"{tag}_sca", [128, 128], I32, stack)
    ps_tr = p.psum(f"{tag}_pstr", [128, 128], F32, stack)
    ps_b = p.psum(f"{tag}_psb", [128, 128], F32, stack)
    ld = p.sem2(tag + "_ld"); b_ld = [p.semv[tag + "_ld0"], p.semv[tag + "_ld1"]]
    rd = p.sem(tag + "_rd"); b_rd = p.semv[tag + "_rd"]
    tr = p.sem(tag + "_tr"); b_tr = p.semv[tag + "_tr"]

    with nc.Block() as block:
        @block.sync
        def _(sync):
            for i in range(DC):
                if i >= 2:
                    sync.wait_ge(rd, b_rd + 2 * (i - 1))
                sync.dma_start(out=mbuf[i % 2][:, :],
                               in_=maskT_ap[i * 128:(i + 1) * 128, :]).then_inc(ld[i % 2], 16)

        @block.vector
        def _(v):
            for i in range(DC):
                v.wait_ge(ld[i % 2], b_ld[i % 2] + 16 * (i // 2 + 1))
                v.reduce_sum(s1t[:, :],
                             mbuf[i % 2][:, :].rearrange("p (n r) -> p n r", r=32),
                             axis=AX.X).then_inc(rd, 1)
                v.wait_ge(tr, b_tr + i + 1)
                v.reduce_sum(bmTnb[:, 4 * i:4 * i + 4],
                             ps_tr[:, :].rearrange("p (n r) -> p n r", r=32),
                             axis=AX.X).then_inc(rd, 1)

        @block.tensor
        def _(t):
            for i in range(DC):
                t.wait_ge(rd, b_rd + 2 * i + 1)
                t.transpose(ps_tr[:, :], s1t[:, :], cst["ident_f"][:, :]).then_inc(tr, 1)
                t.wait_ge(rd, b_rd + 2 * i + 2)
    p.semv[tag + "_ld0"] = b_ld[0] + 16 * (DC // 2)
    p.semv[tag + "_ld1"] = b_ld[1] + 16 * (DC // 2)
    p.semv[tag + "_rd"] = b_rd + 2 * DC
    p.semv[tag + "_tr"] = b_tr + DC

    with nc.Block() as block:
        @block.vector
        def _(v):
            v.tensor_scalar(bmTnb[:, :], bmTnb[:, :], 0.5, None, ALU.is_gt)

    with nc.Block() as block:
        @block.tensor
        def _(t):
            t.transpose(ps_tr[:, :], bmTnb[:, :], cst["ident_f"][:, :])

    with nc.Block() as block:
        @block.vector
        def _(v):
            v.tensor_copy(bm01[:, :], ps_tr[:, :])

    b_tr = p.semv[tag + "_tr"]; b_rd = p.semv[tag + "_rd"]
    with nc.Block() as block:
        @block.vector
        def _(v):
            for c in range(DC):
                if c >= 1:
                    v.wait_ge(tr, b_tr + c)
                v.tensor_scalar(selc_a[:, :], cst["io_g32"][:, :], 4 * c, None, ALU.add)
                v.tensor_tensor(s1t[:, :], cst["io_part"][:, :], selc_a[:, :],
                                ALU.is_equal).then_inc(rd, 1)
                v.wait_ge(tr, b_tr + c + 1)
                v.tensor_copy(bmP[:, c * 128:(c + 1) * 128], ps_b[:, :]).then_inc(rd, 1)

        @block.tensor
        def _(t):
            for c in range(DC):
                t.wait_ge(rd, b_rd + 2 * c + 1)
                t.matmul(ps_b[:, :], s1t[:, :], bm01[:, :],
                         start=True, stop=True).then_inc(tr, 1)
    p.semv[tag + "_rd"] = b_rd + 2 * DC
    p.semv[tag + "_tr"] = b_tr + DC


def emit_mm_l0a(p: Prog, tag, w_ap, bias_sb, h_sb, a_out_ap, bmP, cst,
                ps_mm, ps_s1, ps_s2, wraw, atmp, sqb):
    nc = p.nc
    wl = p.sem2(tag + "_wl"); b_wl = [p.semv[tag + "_wl0"], p.semv[tag + "_wl1"]]
    spd = p.sem2(tag + "_spd"); b_spd = [p.semv[tag + "_spd0"], p.semv[tag + "_spd1"]]
    wp = p.sem(tag + "_wp"); b_wp = p.semv[tag + "_wp"]
    mmd = p.sem(tag + "_mmd"); b_mmd = p.semv[tag + "_mmd"]
    evd = p.sem(tag + "_evd"); b_evd = p.semv[tag + "_evd"]
    sqd = p.sem(tag + "_sqd"); b_sqd = p.semv[tag + "_sqd"]
    std = p.sem(tag + "_std"); b_std = p.semv[tag + "_std"]
    wlt = lambda o: b_wl[o % 2] + 16 * (o // 2 + 1)
    spt = lambda ti: b_spd[ti % 2] + 16 * (ti // 2 + 1)

    NTILE = DC * NT

    with nc.Block() as block:
        @block.gpsimd
        def _(g):
            for o in range(DC):
                if o >= 2:
                    g.wait_ge(mmd, b_mmd + (o - 1) * NT)
                g.dma_start(out=wraw[o % 2][:, :],
                            in_=w_ap[o * 128:(o + 1) * 128, :]).then_inc(wl[o % 2], 16)

        @block.vector
        def _(v):
            for o in range(DC):
                v.wait_ge(wl[o % 2], wlt(o))
                bh = bmP[:, 4 * o:4 * o + 1]
                bview = bass.AP(bh.tensor, bh.offset,
                                [bh.ap[0], [128, DC], [1, 4], [0, 32]])
                v.tensor_tensor(
                    wraw[o % 2][:, :].rearrange("p (c f r) -> p c f r", c=DC, f=4),
                    wraw[o % 2][:, :].rearrange("p (c f r) -> p c f r", c=DC, f=4),
                    bview, ALU.mult).then_inc(wp, 1)

        @block.tensor
        def _(t):
            for o in range(DC):
                t.wait_ge(wp, b_wp + o + 1)
                for j in range(NT):
                    ti = o * NT + j
                    bank = ti % 2
                    if ti >= 2:
                        t.wait_ge(evd, b_evd + ti - 1)
                    for c in range(DC):
                        mi = t.matmul(ps_mm[bank][:, :],
                                      wraw[o % 2][:, c * 128:(c + 1) * 128],
                                      h_sb[:, c * T + j * TW:c * T + (j + 1) * TW],
                                      start=(c == 0), stop=(c == DC - 1))
                    mi.then_inc(mmd, 1)
                    if ti >= 1:
                        op, jp = (ti - 1) // NT, (ti - 1) % NT
                        t.wait_ge(sqd, b_sqd + ti)
                        t.matmul(ps_s1[32 * jp:32 * jp + 32, :], cst["ones32"][:, :],
                                 atmp[(ti - 1) % 2][:, :],
                                 start=(op == 0), stop=(op == DC - 1),
                                 tile_position=(0, 32 * jp), skip_group_check=True)
                        t.matmul(ps_s2[32 * jp:32 * jp + 32, :], cst["ones32"][:, :],
                                 sqb[(ti - 1) % 2][:, :],
                                 start=(op == 0), stop=(op == DC - 1),
                                 tile_position=(0, 32 * jp), skip_group_check=True).then_inc(std, 1)
            ti = NTILE
            op, jp = (ti - 1) // NT, (ti - 1) % NT
            t.wait_ge(sqd, b_sqd + ti)
            t.matmul(ps_s1[32 * jp:32 * jp + 32, :], cst["ones32"][:, :],
                     atmp[(ti - 1) % 2][:, :], start=(op == 0), stop=True,
                     tile_position=(0, 32 * jp), skip_group_check=True)
            t.matmul(ps_s2[32 * jp:32 * jp + 32, :], cst["ones32"][:, :],
                     sqb[(ti - 1) % 2][:, :], start=(op == 0), stop=True,
                     tile_position=(0, 32 * jp), skip_group_check=True).then_inc(std, 1)

        @block.scalar
        def _(s):
            for o in range(DC):
                for j in range(NT):
                    ti = o * NT + j
                    s.wait_ge(mmd, b_mmd + ti + 1)
                    if ti >= 2:
                        s.wait_ge(spd[ti % 2], spt(ti - 2))
                        s.wait_ge(std, b_std + ti - 1)
                    s.activation(atmp[ti % 2][:, :], ps_mm[ti % 2][:, :],
                                 AF.Identity, bias=bias_sb[:, o:o + 1],
                                 scale=1.0).then_inc(evd, 1)
                    s.activation(sqb[ti % 2][:, :], atmp[ti % 2][:, :],
                                 AF.Square).then_inc(sqd, 1)

        @block.sync
        def _(sync):
            for o in range(DC):
                for j in range(NT):
                    ti = o * NT + j
                    sync.wait_ge(evd, b_evd + ti + 1)
                    sync.dma_start(out=a_out_ap[o * 128:(o + 1) * 128,
                                                j * TW:(j + 1) * TW],
                                   in_=atmp[ti % 2][:, :]).then_inc(spd[ti % 2], 16)
            sync.wait_ge(spd[0], b_spd[0] + 16 * (NTILE // 2))
            sync.wait_ge(spd[1], b_spd[1] + 16 * (NTILE // 2))

    for k in range(2):
        p.semv[tag + f"_wl{k}"] = b_wl[k] + 16 * (DC // 2)
        p.semv[tag + f"_spd{k}"] = b_spd[k] + 16 * (NTILE // 2)
    p.semv[tag + "_wp"] = b_wp + DC
    p.semv[tag + "_mmd"] = b_mmd + NTILE
    p.semv[tag + "_evd"] = b_evd + NTILE
    p.semv[tag + "_sqd"] = b_sqd + NTILE
    p.semv[tag + "_std"] = b_std + NTILE


def emit_gates(p: Prog, tag, gw_sb, h_sb, ps_aux, cst, lg_fm, lg_tm, scr_tm,
               r_tm, r_fmT):
    nc = p.nc
    gd = p.sem(tag + "_gd"); b_gd = p.semv[tag + "_gd"]
    ev = p.sem(tag + "_ev"); b_ev = p.semv[tag + "_ev"]

    with nc.Block() as block:
        @block.tensor
        def _(t):
            for j in range(NT):
                if j >= 1:
                    t.wait_ge(ev, b_ev + j)
                for c in range(DC):
                    mi = t.matmul(ps_aux[0][0:8, :], gw_sb[:, c * 8:(c + 1) * 8],
                                  h_sb[:, c * T + j * TW:c * T + (j + 1) * TW],
                                  start=(c == 0), stop=(c == DC - 1))
                mi.then_inc(gd, 1)

        @block.vector
        def _(v):
            for j in range(NT):
                v.wait_ge(gd, b_gd + j + 1)
                v.tensor_copy(lg_fm[0:8, j * TW:(j + 1) * TW],
                              ps_aux[0][0:8, :]).then_inc(ev, 1)
    p.semv[tag + "_gd"] = b_gd + NT
    p.semv[tag + "_ev"] = b_ev + NT

    b_gd = p.semv[tag + "_gd"]; b_ev = p.semv[tag + "_ev"]
    with nc.Block() as block:
        @block.tensor
        def _(t):
            for k in range(16):
                mi = t.transpose(ps_aux[1][:, k * 8:(k + 1) * 8],
                                 lg_fm[0:8, k * 128:(k + 1) * 128],
                                 cst["ident_f"][0:8, 0:8])
            mi.then_inc(gd, 1)

        @block.vector
        def _(v):
            v.wait_ge(gd, b_gd + 1)
            v.tensor_copy(lg_tm[:, :], ps_aux[1][:, 0:128])
    p.semv[tag + "_gd"] = b_gd + 1

    sm = p.sem(tag + "_sm"); b_sm = p.semv[tag + "_sm"]
    with nc.Block() as block:
        @block.vector
        def _(v):
            v.reduce_max(scr_tm[:, 0:16],
                         lg_tm[:, :].rearrange("p (g e) -> p g e", e=8), axis=AX.X)
            mh = scr_tm[:, 0:1]
            mview = bass.AP(mh.tensor, mh.offset, [mh.ap[0], [1, 16], [0, 8]])
            v.tensor_tensor(lg_tm[:, :].rearrange("p (g e) -> p g e", e=8),
                            lg_tm[:, :].rearrange("p (g e) -> p g e", e=8),
                            mview, ALU.subtract).then_inc(sm, 1)

        @block.scalar
        def _(s):
            s.wait_ge(sm, b_sm + 1)
            s.activation(lg_tm[:, :], lg_tm[:, :], AF.Exp).then_inc(sm, 1)

        @block.vector
        def _(v):
            v.wait_ge(sm, b_sm + 2)
            v.reduce_sum(scr_tm[:, 0:16],
                         lg_tm[:, :].rearrange("p (g e) -> p g e", e=8), axis=AX.X)
            v.drain()
            v.reciprocal(scr_tm[:, 16:32], scr_tm[:, 0:16])
            v.drain()
            dh = scr_tm[:, 16:17]
            dview = bass.AP(dh.tensor, dh.offset, [dh.ap[0], [1, 16], [0, 8]])
            v.tensor_tensor(r_tm[:, :].rearrange("p (g e) -> p g e", e=8),
                            lg_tm[:, :].rearrange("p (g e) -> p g e", e=8),
                            dview, ALU.mult).then_inc(sm, 1)
    p.semv[tag + "_sm"] = b_sm + 3

    b_gd = p.semv[tag + "_gd"]; b_ev = p.semv[tag + "_ev"]
    with nc.Block() as block:
        @block.tensor
        def _(t):
            for g4 in range(4):
                if g4 >= 1:
                    t.wait_ge(ev, b_ev + g4)
                for k4 in range(4):
                    k = g4 * 4 + k4
                    mi = t.transpose(ps_aux[0][0:8, k4 * 128:(k4 + 1) * 128],
                                     r_tm[:, k * 8:(k + 1) * 8],
                                     cst["ident_f"][:, :])
                mi.then_inc(gd, 1)

        @block.vector
        def _(v):
            for g4 in range(4):
                v.wait_ge(gd, b_gd + g4 + 1)
                v.tensor_copy(r_fmT[0:8, g4 * TW:(g4 + 1) * TW],
                              ps_aux[0][0:8, 0:TW]).then_inc(ev, 1)
    p.semv[tag + "_gd"] = b_gd + 4
    p.semv[tag + "_ev"] = b_ev + 4


def emit_upass(p: Prog, tag, ics_sb, r_fmT, h_sb, ps_aux, cst):
    nc = p.nc
    ism = p.sem(tag + "_ism"); b_ism = p.semv[tag + "_ism"]
    ut = p.sem(tag + "_ut"); b_ut = p.semv[tag + "_ut"]

    with nc.Block() as block:
        @block.tensor
        def _(t):
            for c in range(DC):
                for j in range(NT):
                    idx = c * NT + j
                    if idx >= 2:
                        t.wait_ge(ut, b_ut + idx - 1)
                    t.matmul(ps_aux[idx % 2][:, :],
                             ics_sb[0:8, c * 128:(c + 1) * 128],
                             r_fmT[0:8, j * TW:(j + 1) * TW],
                             start=True, stop=True).then_inc(ism, 1)

        @block.vector
        def _(v):
            for c in range(DC):
                for j in range(NT):
                    idx = c * NT + j
                    v.wait_ge(ism, b_ism + idx + 1)
                    sl = slice(c * T + j * TW, c * T + (j + 1) * TW)
                    v.tensor_tensor(h_sb[:, sl], h_sb[:, sl],
                                    ps_aux[idx % 2][:, :], ALU.mult).then_inc(ut, 1)
    p.semv[tag + "_ism"] = b_ism + DC * NT
    p.semv[tag + "_ut"] = b_ut + DC * NT


def emit_mm_l0f(p: Prog, tag, w_ap, biasf_sb, ocs_sb, r_fmT, h_sb, a_out_ap,
                cst, ps_mm, ps_os, ps_s1, ps_s2, wraw, os_sb, tev, atmp, sqb):
    nc = p.nc
    wl = p.sem2(tag + "_wl"); b_wl = [p.semv[tag + "_wl0"], p.semv[tag + "_wl1"]]
    spd = p.sem2(tag + "_spd"); b_spd = [p.semv[tag + "_spd0"], p.semv[tag + "_spd1"]]
    wp = p.sem(tag + "_wp"); b_wp = p.semv[tag + "_wp"]
    mmd = p.sem(tag + "_mmd"); b_mmd = p.semv[tag + "_mmd"]
    osd = p.sem(tag + "_osd"); b_osd = p.semv[tag + "_osd"]
    osc = p.sem(tag + "_osc"); b_osc = p.semv[tag + "_osc"]
    tvd = p.sem(tag + "_tvd"); b_tvd = p.semv[tag + "_tvd"]
    acd = p.sem(tag + "_acd"); b_acd = p.semv[tag + "_acd"]
    sqd = p.sem(tag + "_sqd"); b_sqd = p.semv[tag + "_sqd"]
    std = p.sem(tag + "_std"); b_std = p.semv[tag + "_std"]
    wlt = lambda o: b_wl[o % 2] + 16 * (o // 2 + 1)
    spt = lambda ti: b_spd[ti % 2] + 16 * (ti // 2 + 1)
    NTILE = DC * NT

    with nc.Block() as block:
        @block.gpsimd
        def _(g):
            for o in range(DC):
                if o >= 2:
                    g.wait_ge(mmd, b_mmd + (o - 1) * NT)
                g.dma_start(out=wraw[o % 2][:, :],
                            in_=w_ap[o * 128:(o + 1) * 128, :]).then_inc(wl[o % 2], 16)

        @block.scalar
        def _(s):
            def evac(ti):
                o = ti // NT
                s.wait_ge(tvd, b_tvd + ti + 1)
                if ti >= 2:
                    s.wait_ge(spd[ti % 2], spt(ti - 2))
                    s.wait_ge(std, b_std + ti - 1)
                s.activation(atmp[ti % 2][:, :], tev[ti % 2][:, :],
                             AF.Identity, bias=biasf_sb[:, o:o + 1],
                             scale=1.0).then_inc(acd, 1)
                s.activation(sqb[ti % 2][:, :], atmp[ti % 2][:, :],
                             AF.Square).then_inc(sqd, 1)

            for o in range(DC):
                if o >= 2:
                    for j in range(NT):
                        evac((o - 2) * NT + j)
                s.wait_ge(wl[o % 2], wlt(o))
                s.activation(wraw[o % 2][:, :], wraw[o % 2][:, :],
                             AF.Sign).then_inc(wp, 1)
            for ti in range((DC - 2) * NT, DC * NT):
                evac(ti)

        @block.tensor
        def _(t):
            for o in range(DC):
                t.wait_ge(wp, b_wp + o + 1)
                for j in range(NT):
                    ti = o * NT + j
                    t.wait_ge(osc, b_osc + ti)
                    t.matmul(ps_os[:, :],
                             ocs_sb[0:8, o * 128:(o + 1) * 128],
                             r_fmT[0:8, j * TW:(j + 1) * TW],
                             start=True, stop=True).then_inc(osd, 1)
                    if ti >= 2:
                        t.wait_ge(tvd, b_tvd + ti - 1)
                    for c in range(DC):
                        mi = t.matmul(ps_mm[ti % 2][:, :],
                                      wraw[o % 2][:, c * 128:(c + 1) * 128],
                                      h_sb[:, c * T + j * TW:c * T + (j + 1) * TW],
                                      start=(c == 0), stop=(c == DC - 1))
                    mi.then_inc(mmd, 1)
                    if ti >= 1:
                        op, jp = (ti - 1) // NT, (ti - 1) % NT
                        t.wait_ge(sqd, b_sqd + ti)
                        t.matmul(ps_s1[32 * jp:32 * jp + 32, :], cst["ones32"][:, :],
                                 atmp[(ti - 1) % 2][:, :],
                                 start=(op == 0), stop=(op == DC - 1),
                                 tile_position=(0, 32 * jp), skip_group_check=True)
                        t.matmul(ps_s2[32 * jp:32 * jp + 32, :], cst["ones32"][:, :],
                                 sqb[(ti - 1) % 2][:, :],
                                 start=(op == 0), stop=(op == DC - 1),
                                 tile_position=(0, 32 * jp),
                                 skip_group_check=True).then_inc(std, 1)
            ti = NTILE
            op, jp = (ti - 1) // NT, (ti - 1) % NT
            t.wait_ge(sqd, b_sqd + ti)
            t.matmul(ps_s1[32 * jp:32 * jp + 32, :], cst["ones32"][:, :],
                     atmp[(ti - 1) % 2][:, :], start=(op == 0), stop=True,
                     tile_position=(0, 32 * jp), skip_group_check=True)
            t.matmul(ps_s2[32 * jp:32 * jp + 32, :], cst["ones32"][:, :],
                     sqb[(ti - 1) % 2][:, :], start=(op == 0), stop=True,
                     tile_position=(0, 32 * jp),
                     skip_group_check=True).then_inc(std, 1)

        @block.vector
        def _(v):
            for o in range(DC):
                for j in range(NT):
                    ti = o * NT + j
                    v.wait_ge(osd, b_osd + ti + 1)
                    if ti >= 2:
                        v.wait_ge(tvd, b_tvd + ti - 1)
                    v.tensor_copy(os_sb[ti % 2][:, :], ps_os[:, :]).then_inc(osc, 1)
                    v.wait_ge(mmd, b_mmd + ti + 1)
                    if ti >= 2:
                        v.wait_ge(acd, b_acd + ti - 1)
                    v.tensor_tensor(tev[ti % 2][:, :], ps_mm[ti % 2][:, :],
                                    os_sb[ti % 2][:, :], ALU.mult).then_inc(tvd, 1)

        @block.sync
        def _(sync):
            for o in range(DC):
                for j in range(NT):
                    ti = o * NT + j
                    sync.wait_ge(acd, b_acd + ti + 1)
                    sync.dma_start(out=a_out_ap[o * 128:(o + 1) * 128,
                                                j * TW:(j + 1) * TW],
                                   in_=atmp[ti % 2][:, :]).then_inc(spd[ti % 2], 16)
            sync.wait_ge(spd[0], b_spd[0] + 16 * (NTILE // 2))
            sync.wait_ge(spd[1], b_spd[1] + 16 * (NTILE // 2))

    for k in range(2):
        p.semv[tag + f"_wl{k}"] = b_wl[k] + 16 * (DC // 2)
        p.semv[tag + f"_spd{k}"] = b_spd[k] + 16 * (NTILE // 2)
    p.semv[tag + "_wp"] = b_wp + DC
    p.semv[tag + "_mmd"] = b_mmd + NTILE
    p.semv[tag + "_osd"] = b_osd + NTILE
    p.semv[tag + "_osc"] = b_osc + NTILE
    p.semv[tag + "_tvd"] = b_tvd + NTILE
    p.semv[tag + "_acd"] = b_acd + NTILE
    p.semv[tag + "_sqd"] = b_sqd + NTILE
    p.semv[tag + "_std"] = b_std + NTILE


def emit_deqproj(p: Prog, tag, codes_ap, sc_ap, NOUT, mode, h_sb, cst, bufs,
                 out_ap, ps_mm, ps_aux, psbt=None, dst_fn=None):
    nc = p.nc
    cl = p.sem2(tag + "_cl"); b_cl = [p.semv[tag + "_cl0"], p.semv[tag + "_cl1"]]
    sl = p.sem(tag + "_sl"); b_sl = p.semv[tag + "_sl"]
    spd = p.sem2(tag + "_spd"); b_spd = [p.semv[tag + "_spd0"], p.semv[tag + "_spd1"]]
    acp = p.sem(tag + "_acp"); b_acp = p.semv[tag + "_acp"]
    bcm = p.sem(tag + "_bcm"); b_bcm = p.semv[tag + "_bcm"]
    dqd = p.sem(tag + "_dqd"); b_dqd = p.semv[tag + "_dqd"]
    mmd = p.sem(tag + "_mmd"); b_mmd = p.semv[tag + "_mmd"]
    acd = p.sem(tag + "_acd"); b_acd = p.semv[tag + "_acd"]
    rpd = p.sem(tag + "_rpd"); b_rpd = p.semv[tag + "_rpd"]
    trd = p.sem(tag + "_trd"); b_trd = p.semv[tag + "_trd"]
    wbf, scrow = bufs["wbf"], bufs["scrow"]
    et, esw, e1, e2, er = bufs["et"], bufs["esw"], bufs["e1"], bufs["e2"], bufs["er"]
    COS, SSIN = bufs.get("COS"), bufs.get("SSIN")
    clt = lambda o: b_cl[o % 2] + 16 * (o // 2 + 1)
    spt = lambda ti: b_spd[ti % 2] + 16 * (ti // 2 + 1)
    NTILE = NOUT * NT

    with nc.Block() as block:
        @block.gpsimd
        def _(g):
            for o in range(NOUT):
                if o >= 2:
                    g.wait_ge(mmd, b_mmd + (o - 1) * NT)
                g.dma_start(out=wbf[o % 2][:, :],
                            in_=codes_ap[o * 128:(o + 1) * 128, :]).then_inc(cl[o % 2], 16)

        @block.sync
        def _(sync):
            def store(ti):
                o, j = ti // NT, ti % NT
                sync.wait_ge(rpd, b_rpd + ti + 1)
                if dst_fn is not None:
                    dst = dst_fn(o, j)
                    srcv = (er[ti % 2][:, :].rearrange("q (k f) -> q k f", k=4)
                            if mode == "vtrans" else er[ti % 2][:, :])
                    sync.dma_start(out=dst, in_=srcv).then_inc(spd[ti % 2], 16)
                elif mode == "rope":
                    sync.dma_start(out=out_ap[o * 128:(o + 1) * 128,
                                              j * TW:(j + 1) * TW],
                                   in_=er[ti % 2][:, :]).then_inc(spd[ti % 2], 16)
                else:
                    dst = out_ap[j * TW:(j + 1) * TW,
                                 o * 128:(o + 1) * 128].rearrange(
                                     "(k q) f -> q k f", q=128)
                    sync.dma_start(out=dst, in_=er[ti % 2][:, :].rearrange(
                        "q (k f) -> q k f", k=4)).then_inc(spd[ti % 2], 16)

            for o in range(NOUT):
                if o >= 2:
                    for j in range(NT):
                        store((o - 2) * NT + j)
                if o >= 1:
                    sync.wait_ge(bcm, b_bcm + 8 * o)
                sync.dma_start(out=scrow[:, :],
                               in_=sc_ap[o:o + 1, :]).then_inc(sl, 16)
            for ti in range((NOUT - 2) * NT, NOUT * NT):
                store(ti)
            sync.wait_ge(spd[0], b_spd[0] + 16 * ((NTILE + 1) // 2))
            sync.wait_ge(spd[1], b_spd[1] + 16 * (NTILE // 2))

        @block.scalar
        def _(s):
            def evac(ti):
                s.wait_ge(mmd, b_mmd + ti + 1)
                if ti >= 2:
                    s.wait_ge(rpd, b_rpd + ti - 1)
                s.activation(et[ti % 2][:, :], ps_mm[ti % 2][:, :],
                             AF.Identity).then_inc(acd, 1)

            for o in range(NOUT):
                if o >= 2:
                    for j in range(NT):
                        evac((o - 2) * NT + j)
                s.wait_ge(cl[o % 2], clt(o))
                s.activation(wbf[o % 2][:, :], wbf[o % 2][:, :],
                             AF.Identity, bias=cst["neg8"][:, :],
                             scale=1.0).then_inc(acp, 1)
            for ti in range((NOUT - 2) * NT, NOUT * NT):
                evac(ti)

        @block.tensor
        def _(t):
            for o in range(NOUT):
                t.wait_ge(sl, b_sl + 16 * (o + 1))
                for sidx in range(8):
                    if o * 8 + sidx >= 2:
                        t.wait_ge(dqd, b_dqd + o * 8 + sidx - 1)
                    t.matmul(ps_aux[(o * 8 + sidx) % 2][:, :], cst["ones1"][:, :],
                             scrow[0:1, sidx * TW:(sidx + 1) * TW],
                             start=True, stop=True).then_inc(bcm, 1)
                for j in range(NT):
                    ti = o * NT + j
                    t.wait_ge(dqd, b_dqd + 8 * (o + 1))
                    if ti >= 2:
                        t.wait_ge(acd, b_acd + ti - 1)
                    for c in range(DC):
                        mi = t.matmul(ps_mm[ti % 2][:, :],
                                      wbf[o % 2][:, c * 128:(c + 1) * 128],
                                      h_sb[:, c * T + j * TW:c * T + (j + 1) * TW],
                                      start=(c == 0), stop=(c == DC - 1))
                    mi.then_inc(mmd, 1)
                    if mode == "vtrans":
                        t.wait_ge(acd, b_acd + ti + 1)
                        t.wait_ge(rpd, b_rpd + max(ti - 1, 0))
                        for k4 in range(4):
                            mi2 = t.transpose(psbt[ti % 2][:, k4 * 128:(k4 + 1) * 128],
                                              et[ti % 2][:, k4 * 128:(k4 + 1) * 128],
                                              cst["ident_bf"][:, :])
                        mi2.then_inc(trd, 1)

        @block.vector
        def _(v):
            for o in range(NOUT):
                for sidx in range(8):
                    si = o * 8 + sidx
                    v.wait_ge(bcm, b_bcm + si + 1)
                    v.wait_ge(acp, b_acp + o + 1)
                    sslice = slice(sidx * TW, (sidx + 1) * TW)
                    v.tensor_tensor(wbf[o % 2][:, sslice], wbf[o % 2][:, sslice],
                                    ps_aux[si % 2][:, :], ALU.mult).then_inc(dqd, 1)
                for j in range(NT):
                    ti = o * NT + j
                    if mode == "rope":
                        v.wait_ge(acd, b_acd + ti + 1)
                        if ti >= 2:
                            v.wait_ge(spd[ti % 2], spt(ti - 2))
                        v.stream_shuffle(esw[:, :], et[ti % 2][:, :],
                                         [(k + 1 if k % 2 == 0 else k - 1)
                                          for k in range(32)])
                        v.tensor_tensor(e1[:, :], et[ti % 2][:, :],
                                        COS[:, j * TW:(j + 1) * TW], ALU.mult)
                        v.tensor_tensor(e2[:, :], esw[:, :],
                                        SSIN[:, j * TW:(j + 1) * TW], ALU.mult)
                        v.tensor_tensor(er[ti % 2][:, :], e1[:, :], e2[:, :],
                                        ALU.add).then_inc(rpd, 1)
                    else:
                        v.wait_ge(trd, b_trd + ti + 1)
                        if ti >= 2:
                            v.wait_ge(spd[ti % 2], spt(ti - 2))
                        v.tensor_copy(er[ti % 2][:, :],
                                      psbt[ti % 2][:, :]).then_inc(rpd, 1)

    for k in range(2):
        p.semv[tag + f"_cl{k}"] = b_cl[k] + 16 * ((NOUT + 1 - k) // 2)
        p.semv[tag + f"_spd{k}"] = b_spd[k] + 16 * ((NTILE + 1 - k) // 2)
    p.semv[tag + "_sl"] = b_sl + 16 * NOUT
    p.semv[tag + "_acp"] = b_acp + NOUT
    p.semv[tag + "_bcm"] = b_bcm + 8 * NOUT
    p.semv[tag + "_dqd"] = b_dqd + 8 * NOUT
    p.semv[tag + "_mmd"] = b_mmd + NTILE
    p.semv[tag + "_acd"] = b_acd + NTILE
    p.semv[tag + "_rpd"] = b_rpd + NTILE
    if mode == "vtrans":
        p.semv[tag + "_trd"] = b_trd + NTILE


SCALE = 128 ** -0.5


def attn_schedule():
    tiles = []
    ch_c = ev_c = gr_c = 0
    for gl in range(4):
        for h4 in range(4):
            for qt in range(32):
                L = qt * 128 + 128
                chunks = []
                k = 0
                while k < L:
                    size = min(512, L - k)
                    chunks.append((k, size))
                    k += size
                nsub = L // 128
                ngr = (nsub + 7) // 8
                tiles.append(dict(gl=gl, hloc=gl * 4 + h4, qt=qt, L=L,
                                  chunks=chunks, nsub=nsub, ngr=ngr,
                                  ch_base=ch_c, ev_base=ev_c, gr_base=gr_c))
                ch_c += len(chunks)
                ev_c += len(chunks) + 1
                gr_c += ngr
    return tiles, dict(chunks=ch_c, evacs=ev_c, groups=gr_c, tiles=len(tiles))


def emit_attention(p: Prog, tag, q_ap, kf_ap, vf_ap, att_sb, cst, bufs,
                   ps_s, ps_av, psbt):
    nc = p.nc
    sched, tot = attn_schedule()
    NTL = tot["tiles"]
    kvl = p.sem(tag + "_kvl"); b_kvl = p.semv[tag + "_kvl"]
    qld = p.sem(tag + "_qld"); b_qld = p.semv[tag + "_qld"]
    smd = p.sem(tag + "_smd"); b_smd = p.semv[tag + "_smd"]
    sed = p.sem(tag + "_sed"); b_sed = p.semv[tag + "_sed"]
    nmx = p.sem(tag + "_nmx"); b_nmx = p.semv[tag + "_nmx"]
    exd = p.sem(tag + "_exd"); b_exd = p.semv[tag + "_exd"]
    pnd = p.sem(tag + "_pnd"); b_pnd = p.semv[tag + "_pnd"]
    trd = p.sem(tag + "_trd"); b_trd = p.semv[tag + "_trd"]
    ptd = p.sem(tag + "_ptd"); b_ptd = p.semv[tag + "_ptd"]
    avg = p.sem(tag + "_avg"); b_avg = p.semv[tag + "_avg"]
    aed = p.sem(tag + "_aed"); b_aed = p.semv[tag + "_aed"]
    kg, vg, qh = bufs["kg"], bufs["vg"], bufs["qh"]
    Srow, Prow, PT = bufs["Srow"], bufs["Prow"], bufs["PT"]
    TRI, cmax = bufs["TRI"], bufs["cmax"]
    negmax, rowsum, dinv = bufs["negmax"], bufs["rowsum"], bufs["dinv"]

    ch_evdone = []
    for t in sched:
        nch = len(t["chunks"])
        for ci in range(nch):
            if ci == nch - 1:
                ch_evdone.append(t["ev_base"] + nch + 1)
            else:
                ch_evdone.append(t["ev_base"] + ci + 1)

    def gr_done(ti):
        return sched[ti]["gr_base"] + sched[ti]["ngr"]

    with nc.Block() as block:
        @block.sync
        def _(sync):
            for ti, t in enumerate(sched):
                gl, hloc, qt = t["gl"], t["hloc"], t["qt"]
                if qt == 0 and t["hloc"] % 4 == 0:
                    if gl >= 1:
                        sync.wait_ge(avg, b_avg + gr_done(128 * gl - 1))
                    sync.dma_start(out=kg[:, :],
                                   in_=kf_ap[gl * 128:(gl + 1) * 128, :]
                                   ).then_inc(kvl, 16)
                    sync.dma_start(out=vg[:, :],
                                   in_=vf_ap[:, gl * 128:(gl + 1) * 128].rearrange(
                                       "(i q) f -> q i f", q=128)
                                   ).then_inc(kvl, 16)
                if qt == 0:
                    if hloc >= 1:
                        sync.wait_ge(avg, b_avg + gr_done(32 * hloc - 1))
                    sync.dma_start(out=qh[:, :],
                                   in_=q_ap[hloc * 128:(hloc + 1) * 128, :]
                                   ).then_inc(qld, 16)

        @block.tensor
        def _(te):
            def trav(ti):
                t = sched[ti]
                te.wait_ge(pnd, b_pnd + ti + 1)
                if ti >= 2:
                    te.wait_ge(aed, b_aed + ti - 1)
                for G in range(t["ngr"]):
                    gg = t["gr_base"] + G
                    if gg >= 2:
                        te.wait_ge(ptd, b_ptd + gg - 1)
                    subs = range(G * 8, min(G * 8 + 8, t["nsub"]))
                    for si, sub in enumerate(subs):
                        mi = te.transpose(psbt[gg % 2][:, si * 128:(si + 1) * 128],
                                          Prow[:, sub * 128:(sub + 1) * 128],
                                          cst["ident_bf"][:, :])
                    mi.then_inc(trd, 1)
                    te.wait_ge(ptd, b_ptd + gg + 1)
                    for si, sub in enumerate(subs):
                        mi = te.matmul(ps_av[ti % 2][:, 0:128],
                                       vg[:, sub * 128:(sub + 1) * 128],
                                       PT[gg % 2][:, si * 128:(si + 1) * 128],
                                       start=(sub == 0), stop=(sub == t["nsub"] - 1))
                    mi.then_inc(avg, 1)

            for ti, t in enumerate(sched):
                gl, hloc, qt = t["gl"], t["hloc"], t["qt"]
                if ti >= 1:
                    trav(ti - 1)
                if qt == 0 and hloc % 4 == 0:
                    te.wait_ge(kvl, b_kvl + 32 * (gl + 1))
                if qt == 0:
                    te.wait_ge(qld, b_qld + 16 * (hloc + 1))
                for ci, (k0, sz) in enumerate(t["chunks"]):
                    gc = t["ch_base"] + ci
                    if gc >= 2:
                        te.wait_ge(sed, b_sed + ch_evdone[gc - 2])
                    te.matmul(ps_s[gc % 2][:, 0:sz],
                              qh[:, qt * 128:(qt + 1) * 128],
                              kg[:, k0:k0 + sz],
                              start=True, stop=True).then_inc(smd, 1)
            trav(NTL - 1)

        @block.vector
        def _(v):
            def pt_copies(ti):
                t = sched[ti]
                for G in range(t["ngr"]):
                    gg = t["gr_base"] + G
                    v.wait_ge(trd, b_trd + gg + 1)
                    if gg >= 2:
                        v.wait_ge(avg, b_avg + gg - 1)
                    nsubs = min(8, t["nsub"] - G * 8)
                    v.tensor_copy(PT[gg % 2][:, 0:nsubs * 128],
                                  psbt[gg % 2][:, 0:nsubs * 128]).then_inc(ptd, 1)

            for ti, t in enumerate(sched):
                nch = len(t["chunks"])
                if ti >= 1:
                    pt_copies(ti - 1)
                for ci, (k0, sz) in enumerate(t["chunks"]):
                    gc = t["ch_base"] + ci
                    v.wait_ge(smd, b_smd + gc + 1)
                    if ti >= 2 and ci == 0:
                        v.wait_ge(exd, b_exd + ti - 1)
                    v.tensor_scalar(Srow[ti % 2][:, k0:k0 + sz],
                                    ps_s[gc % 2][:, 0:sz], SCALE, None,
                                    ALU.mult, ALU.max,
                                    accum_out=cmax[:, ci:ci + 1]).then_inc(sed, 1)
                    if ci == nch - 1:
                        v.scalar_tensor_tensor(
                            Srow[ti % 2][:, t["L"] - 128:t["L"]],
                            ps_s[gc % 2][:, sz - 128:sz], SCALE,
                            TRI[:, :], ALU.mult, ALU.add).then_inc(sed, 1)
                if ti >= 2:
                    v.wait_ge(exd, b_exd + ti - 1)
                v.reduce_max(negmax[ti % 2][:, :], cmax[:, 0:nch], axis=AX.X,
                             negate=True).then_inc(nmx, 1)
                v.wait_ge(exd, b_exd + ti + 1)
                v.reciprocal(dinv[ti % 2][:, :], rowsum[ti % 2][:, :])
                v.drain()
                v.tensor_scalar(Prow[:, 0:t["L"]], Prow[:, 0:t["L"]],
                                dinv[ti % 2][:, :], None, ALU.mult).then_inc(pnd, 1)
            pt_copies(NTL - 1)

        @block.scalar
        def _(s):
            def av_evac(ti):
                tp = sched[ti]
                s.wait_ge(avg, b_avg + gr_done(ti))
                s.activation(att_sb[:, tp["hloc"] * 4096 + tp["qt"] * 128:
                                    tp["hloc"] * 4096 + (tp["qt"] + 1) * 128],
                             ps_av[ti % 2][:, 0:128],
                             AF.Identity).then_inc(aed, 1)

            for ti, t in enumerate(sched):
                s.wait_ge(nmx, b_nmx + ti + 1)
                s.wait_ge(sed, b_sed + t["ev_base"] + len(t["chunks"]) + 1)
                if ti >= 1:
                    s.wait_ge(trd, b_trd + gr_done(ti - 1))
                if ti >= 2:
                    s.wait_ge(pnd, b_pnd + ti - 1)
                s.activation(Prow[:, 0:t["L"]], Srow[ti % 2][:, 0:t["L"]],
                             AF.Exp, bias=negmax[ti % 2][:, :], scale=1.0,
                             accum_out=rowsum[ti % 2][:, :]).then_inc(exd, 1)
                if ti >= 1:
                    av_evac(ti - 1)
            av_evac(NTL - 1)

    p.semv[tag + "_kvl"] = b_kvl + 32 * 4
    p.semv[tag + "_qld"] = b_qld + 16 * 16
    p.semv[tag + "_smd"] = b_smd + tot["chunks"]
    p.semv[tag + "_sed"] = b_sed + tot["evacs"]
    p.semv[tag + "_nmx"] = b_nmx + NTL
    p.semv[tag + "_exd"] = b_exd + NTL
    p.semv[tag + "_pnd"] = b_pnd + NTL
    p.semv[tag + "_trd"] = b_trd + tot["groups"]
    p.semv[tag + "_ptd"] = b_ptd + tot["groups"]
    p.semv[tag + "_avg"] = b_avg + tot["groups"]
    p.semv[tag + "_aed"] = b_aed + NTL


def emit_oproj(p: Prog, tag, codes_ap, sc_ap, h_sb, a_out_ap, cst,
               ps_mm, ps_aux, ps_s1, ps_s2, bufs):
    nc = p.nc
    cl = p.sem2(tag + "_cl"); b_cl = [p.semv[tag + "_cl0"], p.semv[tag + "_cl1"]]
    sl = p.sem(tag + "_sl"); b_sl = p.semv[tag + "_sl"]
    spd = p.sem2(tag + "_spd"); b_spd = [p.semv[tag + "_spd0"], p.semv[tag + "_spd1"]]
    acp = p.sem(tag + "_acp"); b_acp = p.semv[tag + "_acp"]
    bcm = p.sem(tag + "_bcm"); b_bcm = p.semv[tag + "_bcm"]
    dqd = p.sem(tag + "_dqd"); b_dqd = p.semv[tag + "_dqd"]
    mmd = p.sem(tag + "_mmd"); b_mmd = p.semv[tag + "_mmd"]
    acd = p.sem(tag + "_acd"); b_acd = p.semv[tag + "_acd"]
    sqd = p.sem(tag + "_sqd"); b_sqd = p.semv[tag + "_sqd"]
    std = p.sem(tag + "_std"); b_std = p.semv[tag + "_std"]
    wbf, scrow = bufs["wbf"], bufs["scrow"]
    atmp, sqb = bufs["atmp"], bufs["sqb"]
    clt = lambda o: b_cl[o % 2] + 16 * (o // 2 + 1)
    spt = lambda ti: b_spd[ti % 2] + 16 * (ti // 2 + 1)
    NTILE = DC * NT

    with nc.Block() as block:
        @block.gpsimd
        def _(g):
            for o in range(DC):
                if o >= 2:
                    g.wait_ge(mmd, b_mmd + (o - 1) * NT)
                g.dma_start(out=wbf[o % 2][:, :],
                            in_=codes_ap[o * 128:(o + 1) * 128, :]).then_inc(cl[o % 2], 16)

        @block.sync
        def _(sync):
            def spill(ti):
                o, j = ti // NT, ti % NT
                sync.wait_ge(acd, b_acd + ti + 1)
                sync.dma_start(out=a_out_ap[o * 128:(o + 1) * 128,
                                            j * TW:(j + 1) * TW],
                               in_=atmp[ti % 2][:, :]).then_inc(spd[ti % 2], 16)

            for o in range(DC):
                if o >= 2:
                    for j in range(NT):
                        spill((o - 2) * NT + j)
                if o >= 1:
                    sync.wait_ge(bcm, b_bcm + 8 * o)
                sync.dma_start(out=scrow[:, :],
                               in_=sc_ap[o:o + 1, :]).then_inc(sl, 16)
            for ti in range((DC - 2) * NT, DC * NT):
                spill(ti)
            sync.wait_ge(spd[0], b_spd[0] + 16 * (NTILE // 2))
            sync.wait_ge(spd[1], b_spd[1] + 16 * (NTILE // 2))

        @block.scalar
        def _(s):
            def evac(ti):
                s.wait_ge(mmd, b_mmd + ti + 1)
                if ti >= 2:
                    s.wait_ge(spd[ti % 2], spt(ti - 2))
                    s.wait_ge(std, b_std + ti - 1)
                s.activation(atmp[ti % 2][:, :], ps_mm[ti % 2][:, :],
                             AF.Identity).then_inc(acd, 1)
                s.activation(sqb[ti % 2][:, :], atmp[ti % 2][:, :],
                             AF.Square).then_inc(sqd, 1)

            for o in range(DC):
                if o >= 2:
                    for j in range(NT):
                        evac((o - 2) * NT + j)
                s.wait_ge(cl[o % 2], clt(o))
                s.activation(wbf[o % 2][:, :], wbf[o % 2][:, :],
                             AF.Identity, bias=cst["neg8"][:, :],
                             scale=1.0).then_inc(acp, 1)
            for ti in range((DC - 2) * NT, DC * NT):
                evac(ti)

        @block.tensor
        def _(t):
            for o in range(DC):
                t.wait_ge(sl, b_sl + 16 * (o + 1))
                for sidx in range(8):
                    if o * 8 + sidx >= 2:
                        t.wait_ge(dqd, b_dqd + o * 8 + sidx - 1)
                    t.matmul(ps_aux[(o * 8 + sidx) % 2][:, :], cst["ones1"][:, :],
                             scrow[0:1, sidx * TW:(sidx + 1) * TW],
                             start=True, stop=True).then_inc(bcm, 1)
                for j in range(NT):
                    ti = o * NT + j
                    t.wait_ge(dqd, b_dqd + 8 * (o + 1))
                    if ti >= 2:
                        t.wait_ge(acd, b_acd + ti - 1)
                    for c in range(DC):
                        mi = t.matmul(ps_mm[ti % 2][:, :],
                                      wbf[o % 2][:, c * 128:(c + 1) * 128],
                                      h_sb[:, c * T + j * TW:c * T + (j + 1) * TW],
                                      start=(c == 0), stop=(c == DC - 1))
                    mi.then_inc(mmd, 1)
                    if ti >= 1:
                        op, jp = (ti - 1) // NT, (ti - 1) % NT
                        t.wait_ge(sqd, b_sqd + ti)
                        t.matmul(ps_s1[32 * jp:32 * jp + 32, :], cst["ones32"][:, :],
                                 atmp[(ti - 1) % 2][:, :],
                                 start=(op == 0), stop=(op == DC - 1),
                                 tile_position=(0, 32 * jp), skip_group_check=True)
                        t.matmul(ps_s2[32 * jp:32 * jp + 32, :], cst["ones32"][:, :],
                                 sqb[(ti - 1) % 2][:, :],
                                 start=(op == 0), stop=(op == DC - 1),
                                 tile_position=(0, 32 * jp),
                                 skip_group_check=True).then_inc(std, 1)
            ti = NTILE
            op, jp = (ti - 1) // NT, (ti - 1) % NT
            t.wait_ge(sqd, b_sqd + ti)
            t.matmul(ps_s1[32 * jp:32 * jp + 32, :], cst["ones32"][:, :],
                     atmp[(ti - 1) % 2][:, :], start=(op == 0), stop=True,
                     tile_position=(0, 32 * jp), skip_group_check=True)
            t.matmul(ps_s2[32 * jp:32 * jp + 32, :], cst["ones32"][:, :],
                     sqb[(ti - 1) % 2][:, :], start=(op == 0), stop=True,
                     tile_position=(0, 32 * jp),
                     skip_group_check=True).then_inc(std, 1)

        @block.vector
        def _(v):
            for o in range(DC):
                for sidx in range(8):
                    si = o * 8 + sidx
                    v.wait_ge(bcm, b_bcm + si + 1)
                    v.wait_ge(acp, b_acp + o + 1)
                    sslice = slice(sidx * TW, (sidx + 1) * TW)
                    v.tensor_tensor(wbf[o % 2][:, sslice], wbf[o % 2][:, sslice],
                                    ps_aux[si % 2][:, :], ALU.mult).then_inc(dqd, 1)

    for k in range(2):
        p.semv[tag + f"_cl{k}"] = b_cl[k] + 16 * (DC // 2)
        p.semv[tag + f"_spd{k}"] = b_spd[k] + 16 * (NTILE // 2)
    p.semv[tag + "_sl"] = b_sl + 16 * DC
    p.semv[tag + "_acp"] = b_acp + DC
    p.semv[tag + "_bcm"] = b_bcm + 8 * DC
    p.semv[tag + "_dqd"] = b_dqd + 8 * DC
    p.semv[tag + "_mmd"] = b_mmd + NTILE
    p.semv[tag + "_acd"] = b_acd + NTILE
    p.semv[tag + "_sqd"] = b_sqd + NTILE
    p.semv[tag + "_std"] = b_std + NTILE


def build_attn():
    p = Prog()
    q_att = p.din("q_att", [2048, 2 * T], BF16)
    k_att = p.din("k_att", [512, 2 * T], BF16)
    v_att = p.din("v_att", [2 * T, 512], BF16)
    tri = p.din("tri", [128, 128], F32)
    att_out = p.dout("att_out", [2048, 2 * T], BF16)

    com = alloc_common(p)
    cst = com["cst"]
    nc = p.nc
    with ExitStack() as stk:
        bufs = {
            "kg": p.sb("kg", [128, 2 * T], BF16, stk),
            "vg": p.sb("vg", [128, 2 * T], BF16, stk),
            "qh": p.sb("qh", [128, 2 * T], BF16, stk),
            "Srow": [p.sb(f"Srow{i}", [128, 2 * T], F32, stk) for i in range(2)],
            "Prow": p.sb("Prow", [128, 2 * T], BF16, stk),
            "PT": [p.sb(f"PTb{i}", [128, 1024], BF16, stk) for i in range(2)],
            "TRI": p.sb("TRI", [128, 128], F32, stk),
            "cmax": p.sb("cmax", [128, 8], F32, stk),
            "negmax": [p.sb(f"negmax{i}", [128, 1], F32, stk) for i in range(2)],
            "rowsum": [p.sb(f"rowsum{i}", [128, 1], F32, stk) for i in range(2)],
            "dinv": [p.sb(f"dinv{i}", [128, 1], F32, stk) for i in range(2)],
        }
        psbt = [p.psum(f"psbt{i}", [128, TW], BF16, stk) for i in range(2)]
        ldt = p.sem("ldt")
        with nc.Block() as block:
            @block.sync
            def _(sync):
                sync.dma_start(out=bufs["TRI"][:, :], in_=tri).then_inc(ldt, 16)
                sync.wait_ge(ldt, p.semv["ldt"] + 16)
        p.semv["ldt"] += 16
        emit_attention(p, "at", q_att, k_att, v_att, com["h_sb"], cst, bufs,
                       com["ps_mm"], com["ps_aux"], psbt)
        std = p.sem("stda")
        with nc.Block() as block:
            @block.sync
            def _(sync):
                for hl in range(16):
                    sync.dma_start(out=att_out[hl * 128:(hl + 1) * 128, :],
                                   in_=com["h_sb"][:, hl * 4096:(hl + 1) * 4096]
                                   ).then_inc(std, 16)
                sync.wait_ge(std, p.semv["stda"] + 16 * 16)
        p.semv["stda"] += 16 * 16
    p.ctx.close()
    return p


def _tri_mask():
    r = np.arange(128)
    return np.where(r[:, None] >= r[None, :], 0.0, NEG).astype(np.float32)


def build_oproj():
    p = Prog()
    nc = p.nc
    att = p.din("att", [D, T], BF16)
    oc = p.din("oc", [D, D], I32)
    osd = p.din("osd", [DC, D], F32)
    x2 = p.din("x2", [D, T], F32)
    x3 = p.dout("x3", [D, T], F32)
    a_dram = p.dint("a_scr", [D, T], F32)

    com = alloc_common(p)
    cst = com["cst"]
    lda = p.sem("lda")
    with nc.Block() as block:
        @block.sync
        def _(sync):
            for c in range(DC):
                sync.dma_start(out=com["h_sb"][:, c * T:(c + 1) * T],
                               in_=att[c * 128:(c + 1) * 128, :]).then_inc(lda, 16)
            sync.wait_ge(lda, p.semv["lda"] + 16 * DC)
    p.semv["lda"] += 16 * DC

    with ExitStack() as stk:
        bufs = {
            "wbf": [p.sb(f"owbf{i}", [128, D], BF16, stk) for i in range(2)],
            "scrow": p.sb("oscrow", [1, D], F32, stk),
            "atmp": [p.sb(f"oatmp{i}", [128, TW], F32, stk) for i in range(2)],
            "sqb": [p.sb(f"osqb{i}", [128, TW], F32, stk) for i in range(2)],
        }
        emit_oproj(p, "mm", oc, osd, com["h_sb"], a_dram, cst,
                   com["ps_mm"], com["ps_aux"], com["ps_s1"], com["ps_s2"], bufs)
    with ExitStack() as stkln:
        alloc_ln(p, com, stkln)
        with ExitStack() as stk:
            phase_rows2_update(p, "u2", com, x2, a_dram, x3, stk, do_stats=False)
    p.ctx.close()
    return p


def emit_router(p: Prog, tag, rw_sb, h_sb, ps_aux, cst, lg_fm, lg_tm, scr_tm,
                r_tm, w_fmT):
    nc = p.nc
    gd = p.sem(tag + "_gd"); b_gd = p.semv[tag + "_gd"]
    ev = p.sem(tag + "_ev"); b_ev = p.semv[tag + "_ev"]

    with nc.Block() as block:
        @block.tensor
        def _(t):
            for j in range(NT):
                if j >= 1:
                    t.wait_ge(ev, b_ev + j)
                for c in range(DC):
                    mi = t.matmul(ps_aux[0][0:8, :], rw_sb[:, c * 8:(c + 1) * 8],
                                  h_sb[:, c * T + j * TW:c * T + (j + 1) * TW],
                                  start=(c == 0), stop=(c == DC - 1))
                mi.then_inc(gd, 1)

        @block.vector
        def _(v):
            for j in range(NT):
                v.wait_ge(gd, b_gd + j + 1)
                v.tensor_copy(lg_fm[0:8, j * TW:(j + 1) * TW],
                              ps_aux[0][0:8, :]).then_inc(ev, 1)
    p.semv[tag + "_gd"] = b_gd + NT
    p.semv[tag + "_ev"] = b_ev + NT

    b_gd = p.semv[tag + "_gd"]
    with nc.Block() as block:
        @block.tensor
        def _(t):
            for k in range(16):
                mi = t.transpose(ps_aux[1][:, k * 8:(k + 1) * 8],
                                 lg_fm[0:8, k * 128:(k + 1) * 128],
                                 cst["ident_f"][0:8, 0:8])
            mi.then_inc(gd, 1)

        @block.vector
        def _(v):
            v.wait_ge(gd, b_gd + 1)
            v.tensor_copy(lg_tm[:, :], ps_aux[1][:, 0:128])
    p.semv[tag + "_gd"] = b_gd + 1

    sm = p.sem(tag + "_sm"); b_sm = p.semv[tag + "_sm"]
    with nc.Block() as block:
        @block.vector
        def _(v):
            g3 = lambda x: x[:, :].rearrange("p (g e) -> p g e", e=8)
            mx1 = scr_tm[:, 0:16]
            mx2 = scr_tm[:, 16:32]
            mh = scr_tm[:, 0:1]
            m1view = bass.AP(mh.tensor, mh.offset, [mh.ap[0], [1, 16], [0, 8]])
            m2h = scr_tm[:, 16:17]
            m2view = bass.AP(m2h.tensor, m2h.offset, [m2h.ap[0], [1, 16], [0, 8]])
            v.reduce_max(mx1, g3(lg_tm), axis=AX.X)
            v.tensor_tensor(g3(r_tm), g3(lg_tm), m1view, ALU.is_ge)
            v.scalar_tensor_tensor(g3(r_tm), r_tm[:, :].rearrange(
                "p (g e) -> p g e", e=8), NEG, g3(lg_tm), ALU.mult, ALU.add)
            v.reduce_max(mx2, g3(r_tm), axis=AX.X)
            v.tensor_tensor(g3(lg_tm), g3(lg_tm), m1view,
                            ALU.subtract).then_inc(sm, 1)

        @block.scalar
        def _(s):
            s.wait_ge(sm, b_sm + 1)
            s.activation(lg_tm[:, :], lg_tm[:, :], AF.Exp).then_inc(sm, 1)

        @block.vector
        def _(v):
            v.wait_ge(sm, b_sm + 2)
            mx1 = scr_tm[:, 0:16]
            mx2 = scr_tm[:, 16:32]
            v.tensor_tensor(mx2, mx2, mx1, ALU.subtract).then_inc(sm, 1)

        @block.scalar
        def _(s):
            s.wait_ge(sm, b_sm + 3)
            s.activation(scr_tm[:, 16:32], scr_tm[:, 16:32],
                         AF.Exp).then_inc(sm, 1)

        @block.vector
        def _(v):
            v.wait_ge(sm, b_sm + 4)
            g3 = lambda x: x[:, :].rearrange("p (g e) -> p g e", e=8)
            e2h = scr_tm[:, 16:17]
            e2view = bass.AP(e2h.tensor, e2h.offset, [e2h.ap[0], [1, 16], [0, 8]])
            v.tensor_tensor(g3(r_tm), g3(lg_tm), e2view, ALU.is_ge)
            v.tensor_tensor(lg_tm[:, :], lg_tm[:, :], r_tm[:, :], ALU.mult)
            v.tensor_scalar(scr_tm[:, 32:48], scr_tm[:, 16:32], 1.0, None, ALU.add)
            v.drain()
            v.reciprocal(scr_tm[:, 48:64], scr_tm[:, 32:48])
            v.drain()
            dh = scr_tm[:, 48:49]
            dview = bass.AP(dh.tensor, dh.offset, [dh.ap[0], [1, 16], [0, 8]])
            v.tensor_tensor(g3(r_tm), g3(lg_tm), dview, ALU.mult).then_inc(sm, 1)
    p.semv[tag + "_sm"] = b_sm + 5

    b_gd = p.semv[tag + "_gd"]; b_ev = p.semv[tag + "_ev"]
    with nc.Block() as block:
        @block.tensor
        def _(t):
            for g4 in range(4):
                if g4 >= 1:
                    t.wait_ge(ev, b_ev + g4)
                for k4 in range(4):
                    k = g4 * 4 + k4
                    mi = t.transpose(ps_aux[0][0:8, k4 * 128:(k4 + 1) * 128],
                                     r_tm[:, k * 8:(k + 1) * 8],
                                     cst["ident_f"][:, :])
                mi.then_inc(gd, 1)

        @block.vector
        def _(v):
            for g4 in range(4):
                v.wait_ge(gd, b_gd + g4 + 1)
                v.tensor_copy(w_fmT[0:8, g4 * TW:(g4 + 1) * TW],
                              ps_aux[0][0:8, 0:TW]).then_inc(ev, 1)
    p.semv[tag + "_gd"] = b_gd + 4
    p.semv[tag + "_ev"] = b_ev + 4


def emit_mid(p: Prog, tag, lA_sb, P2_sb, w_fmT, h_sb, mid_sb, wx_sb, ps_mm,
             ps_aux, cst):
    nc = p.nc
    wxd = p.sem(tag + "_wxd"); b_wxd = p.semv[tag + "_wxd"]
    wxc = p.sem(tag + "_wxc"); b_wxc = p.semv[tag + "_wxc"]
    mmd = p.sem(tag + "_mmd"); b_mmd = p.semv[tag + "_mmd"]
    mtd = p.sem(tag + "_mtd"); b_mtd = p.semv[tag + "_mtd"]

    with nc.Block() as block:
        @block.tensor
        def _(t):
            for j in range(NT):
                if j >= 2:
                    t.wait_ge(wxc, b_wxc + j - 1)
                t.matmul(ps_aux[j % 2][:, :], P2_sb[0:8, :],
                         w_fmT[0:8, j * TW:(j + 1) * TW],
                         start=True, stop=True).then_inc(wxd, 1)
            for j in range(NT):
                if j >= 2:
                    t.wait_ge(mtd, b_mtd + j - 1)
                for c in range(DC):
                    mi = t.matmul(ps_mm[j % 2][:, :],
                                  lA_sb[:, c * 128:(c + 1) * 128],
                                  h_sb[:, c * T + j * TW:c * T + (j + 1) * TW],
                                  start=(c == 0), stop=(c == DC - 1))
                mi.then_inc(mmd, 1)

        @block.vector
        def _(v):
            for j in range(NT):
                v.wait_ge(wxd, b_wxd + j + 1)
                v.tensor_copy(wx_sb[:, j * TW:(j + 1) * TW],
                              ps_aux[j % 2][:, :]).then_inc(wxc, 1)
            for j in range(NT):
                v.wait_ge(mmd, b_mmd + j + 1)
                v.tensor_tensor(mid_sb[:, j * TW:(j + 1) * TW],
                                ps_mm[j % 2][:, :],
                                wx_sb[:, j * TW:(j + 1) * TW],
                                ALU.mult).then_inc(mtd, 1)
    p.semv[tag + "_wxd"] = b_wxd + NT
    p.semv[tag + "_wxc"] = b_wxc + NT
    p.semv[tag + "_mmd"] = b_mmd + NT
    p.semv[tag + "_mtd"] = b_mtd + NT


def emit_mm_l1f(p: Prog, tag, w_ap, lB_sb, mid_sb, h_sb, a_out_ap, cst,
                ps_mm, ps_s1, ps_s2, wraw, atmp, sqb):
    nc = p.nc
    wl = p.sem2(tag + "_wl"); b_wl = [p.semv[tag + "_wl0"], p.semv[tag + "_wl1"]]
    spd = p.sem2(tag + "_spd"); b_spd = [p.semv[tag + "_spd0"], p.semv[tag + "_spd1"]]
    mmd = p.sem(tag + "_mmd"); b_mmd = p.semv[tag + "_mmd"]
    evd = p.sem(tag + "_evd"); b_evd = p.semv[tag + "_evd"]
    sqd = p.sem(tag + "_sqd"); b_sqd = p.semv[tag + "_sqd"]
    std = p.sem(tag + "_std"); b_std = p.semv[tag + "_std"]
    wlt = lambda o: b_wl[o % 2] + 16 * (o // 2 + 1)
    spt = lambda ti: b_spd[ti % 2] + 16 * (ti // 2 + 1)
    NTILE = DC * NT

    with nc.Block() as block:
        @block.gpsimd
        def _(g):
            for o in range(DC):
                if o >= 2:
                    g.wait_ge(mmd, b_mmd + (o - 1) * NT)
                g.dma_start(out=wraw[o % 2][:, :],
                            in_=w_ap[o * 128:(o + 1) * 128, :]).then_inc(wl[o % 2], 16)

        @block.tensor
        def _(t):
            for o in range(DC):
                t.wait_ge(wl[o % 2], wlt(o))
                for j in range(NT):
                    ti = o * NT + j
                    if ti >= 2:
                        t.wait_ge(evd, b_evd + ti - 1)
                    for c in range(DC):
                        t.matmul(ps_mm[ti % 2][:, :],
                                 wraw[o % 2][:, c * 128:(c + 1) * 128],
                                 h_sb[:, c * T + j * TW:c * T + (j + 1) * TW],
                                 start=(c == 0), stop=False)
                    mi = t.matmul(ps_mm[ti % 2][:, :],
                                  lB_sb[:, o * 128:(o + 1) * 128],
                                  mid_sb[:, j * TW:(j + 1) * TW],
                                  start=False, stop=True)
                    mi.then_inc(mmd, 1)
                    if ti >= 1:
                        op, jp = (ti - 1) // NT, (ti - 1) % NT
                        t.wait_ge(sqd, b_sqd + ti)
                        t.matmul(ps_s1[32 * jp:32 * jp + 32, :], cst["ones32"][:, :],
                                 atmp[(ti - 1) % 2][:, :],
                                 start=(op == 0), stop=(op == DC - 1),
                                 tile_position=(0, 32 * jp), skip_group_check=True)
                        t.matmul(ps_s2[32 * jp:32 * jp + 32, :], cst["ones32"][:, :],
                                 sqb[(ti - 1) % 2][:, :],
                                 start=(op == 0), stop=(op == DC - 1),
                                 tile_position=(0, 32 * jp),
                                 skip_group_check=True).then_inc(std, 1)
            ti = NTILE
            op, jp = (ti - 1) // NT, (ti - 1) % NT
            t.wait_ge(sqd, b_sqd + ti)
            t.matmul(ps_s1[32 * jp:32 * jp + 32, :], cst["ones32"][:, :],
                     atmp[(ti - 1) % 2][:, :], start=(op == 0), stop=True,
                     tile_position=(0, 32 * jp), skip_group_check=True)
            t.matmul(ps_s2[32 * jp:32 * jp + 32, :], cst["ones32"][:, :],
                     sqb[(ti - 1) % 2][:, :], start=(op == 0), stop=True,
                     tile_position=(0, 32 * jp),
                     skip_group_check=True).then_inc(std, 1)

        @block.scalar
        def _(s):
            for o in range(DC):
                for j in range(NT):
                    ti = o * NT + j
                    s.wait_ge(mmd, b_mmd + ti + 1)
                    if ti >= 2:
                        s.wait_ge(spd[ti % 2], spt(ti - 2))
                        s.wait_ge(std, b_std + ti - 1)
                    s.activation(atmp[ti % 2][:, :], ps_mm[ti % 2][:, :],
                                 AF.Identity).then_inc(evd, 1)
                    s.activation(sqb[ti % 2][:, :], atmp[ti % 2][:, :],
                                 AF.Square).then_inc(sqd, 1)

        @block.sync
        def _(sync):
            for o in range(DC):
                for j in range(NT):
                    ti = o * NT + j
                    sync.wait_ge(evd, b_evd + ti + 1)
                    sync.dma_start(out=a_out_ap[o * 128:(o + 1) * 128,
                                                j * TW:(j + 1) * TW],
                                   in_=atmp[ti % 2][:, :]).then_inc(spd[ti % 2], 16)
            sync.wait_ge(spd[0], b_spd[0] + 16 * (NTILE // 2))
            sync.wait_ge(spd[1], b_spd[1] + 16 * (NTILE // 2))

    for k in range(2):
        p.semv[tag + f"_wl{k}"] = b_wl[k] + 16 * (DC // 2)
        p.semv[tag + f"_spd{k}"] = b_spd[k] + 16 * (NTILE // 2)
    p.semv[tag + "_mmd"] = b_mmd + NTILE
    p.semv[tag + "_evd"] = b_evd + NTILE
    p.semv[tag + "_sqd"] = b_sqd + NTILE
    p.semv[tag + "_std"] = b_std + NTILE


def emit_ffn1(p: Prog, com, aps):
    nc = p.nc
    cst = com["cst"]
    with ExitStack() as stkln:
        alloc_ln(p, com, stkln)
        with ExitStack() as stk:
            phase_stats_rows(p, "st", com, aps["x3"], stk)
        with ExitStack() as stk:
            phase_h(p, "h", com, aps["x3"], stk)

    with ExitStack() as stkf:
        mid_sb = p.sb("mid_sb", [128, T], BF16, stkf)
        lB_sb = p.sb("lB_sb", [128, D], BF16, stkf)
        with ExitStack() as stk:
            rw_sb = p.sb("rw_sb", [128, DC * 8], BF16, stk)
            lA_sb = p.sb("lA_sb", [128, DC * 128], BF16, stk)
            P2_sb = p.sb("P2_sb", [8, 128], F32, stk)
            lg_fm = p.sb("lg_fm", [8, T], F32, stk)
            lg_tm = p.sb("lg_tm", [128, 128], F32, stk)
            scr_tm = p.sb("scr_tm", [128, 64], F32, stk)
            r_tm = p.sb("r_tm", [128, 128], F32, stk)
            w_fmT = p.sb("w_fmT", [8, T], F32, stk)
            wx_sb = p.sb("wx_sb", [128, T], F32, stk)
            ldr = p.sem("ldr")
            ldr2 = p.sem("ldr2")
            with nc.Block() as block:
                @block.gpsimd
                def _(g):
                    g.dma_start(out=rw_sb[:, :], in_=aps["rw"]).then_inc(ldr, 16)
                    g.dma_start(out=lA_sb[:, :].rearrange("p (c e) -> p c e", c=DC),
                                in_=aps["lA"].rearrange("(c p) e -> p c e", p=128)
                                ).then_inc(ldr, 16)
                    g.dma_start(out=lB_sb[:, :], in_=aps["lB"]).then_inc(ldr, 16)
                    g.wait_ge(ldr, p.semv["ldr"] + 48)

                @block.sync
                def _(sync):
                    sync.dma_start(out=P2_sb[:, :], in_=aps["P2"]).then_inc(ldr2, 16)
                    sync.wait_ge(ldr2, p.semv["ldr2"] + 16)
            p.semv["ldr"] += 48
            p.semv["ldr2"] += 16
            emit_router(p, "gt", rw_sb, com["h_sb"], com["ps_aux"], cst,
                        lg_fm, lg_tm, scr_tm, r_tm, w_fmT)
            emit_mid(p, "md", lA_sb, P2_sb, w_fmT, com["h_sb"], mid_sb, wx_sb,
                     com["ps_mm"], com["ps_aux"], cst)
        with ExitStack() as stk:
            wraw = [p.sb(f"b_wraw{i}", [128, D], BF16, stk) for i in range(2)]
            atmp = [p.sb(f"b_atmp{i}", [128, TW], F32, stk) for i in range(2)]
            sqb = [p.sb(f"b_sqtb{i}", [128, TW], F32, stk) for i in range(2)]
            emit_mm_l1f(p, "mm", aps["bW"], lB_sb, mid_sb, com["h_sb"],
                        aps["a_scr"], cst, com["ps_mm"], com["ps_s1"],
                        com["ps_s2"], wraw, atmp, sqb)
    with ExitStack() as stkln:
        alloc_ln(p, com, stkln, tag="u3")
        with ExitStack() as stk:
            phase_rows2_update(p, "u", com, aps["x3"], aps["a_scr"],
                               aps["out"], stk, do_stats=False)


def build_ffn1():
    p = Prog()
    aps = {
        "x3": p.din("x3", [D, T], F32),
        "bW": p.din("bW", [D, D], F32),
        "rw": p.din("rw", [128, DC * 8], F32),
        "lA": p.din("lA", [D, 128], F32),
        "lB": p.din("lB", [128, D], F32),
        "P2": p.din("P2", [8, 128], F32),
        "out": p.dout("out", [D, T], F32),
        "a_scr": p.dint("a_scr", [D, T], F32),
    }
    com = alloc_common(p)
    emit_ffn1(p, com, aps)
    p.ctx.close()
    return p


def _prep_ffn1(inputs):
    P2 = np.zeros((8, 128), np.float32)
    for e in range(8):
        P2[e, e * 16:(e + 1) * 16] = 2.0
    return {
        "bW": _tile_w(np.ascontiguousarray(inputs["L1_ffn_base_W"].T)).reshape(D, D),
        "rw": np.ascontiguousarray(
            inputs["L1_ffn_router_W"].T.reshape(DC, 128, 8).transpose(1, 0, 2)
            .reshape(128, DC * 8)),
        "lA": np.ascontiguousarray(
            inputs["L1_ffn_lora_A"].transpose(1, 0, 2).reshape(D, 128)),
        "lB": np.ascontiguousarray(inputs["L1_ffn_lora_B"].reshape(128, D)),
        "P2": P2,
    }


def alloc_common(p: Prog):
    cst = {
        "io_free": p.sb("io_free", [128, 128], I32),
        "io_part": p.sb("io_part", [128, 128], I32),
        "io_g32": p.sb("io_g32", [128, 128], I32),
        "ones32": p.sb("ones32", [128, 32], F32),
        "ones1": p.sb("ones1", [1, 128], F32),
        "eps": p.sb("eps", [128, 1], F32),
        "neg8": p.sb("neg8", [128, 1], F32),
        "ident_bf": p.sb("ident_bf", [128, 128], BF16),
        "ident_f": p.sb("ident_f", [128, 128], F32),
        "sel": [p.sb(f"sel{j}", [128, 128], F32) for j in range(NT)],
    }
    com = {
        "cst": cst,
        "ps_s1": p.psum("ps_s1", [128, TW], F32),
        "ps_s2": p.psum("ps_s2", [128, TW], F32),
        "ps_aux": [p.psum(f"ps_aux{i}", [128, TW], F32) for i in range(2)],
        "ps_mm": [p.psum(f"ps_mm{i}", [128, TW], F32) for i in range(2)],
        "h_sb": p.sb("h_sb", [128, DC * T], BF16),
    }
    emit_consts(p, cst)
    return com


def alloc_ln(p: Prog, com, stack, tag=""):
    com["RB"] = p.sb(tag + "RB", [128, T], F32, stack)
    com["NB"] = p.sb(tag + "NB", [128, T], F32, stack)
    com["rowbufs"] = {k: p.sb(f"{tag}row_{k}", [128, TW], F32, stack)
                      for k in ("s1", "s2", "var", "rstd", "negm")}


def phase_stats_rows(p: Prog, tag, com, src_ap, stack):
    xb = [p.sb(f"{tag}_xb{i}", [128, T], F32, stack) for i in range(2)]
    sq = [p.sb(f"{tag}_sq{i}", [128, T], F32, stack) for i in range(2)]
    emit_stream_stats(p, tag + "s", src_ap, com["ps_s1"], com["ps_s2"],
                      com["cst"], xb, sq)
    emit_rows(p, tag + "r", com["ps_s1"], com["ps_s2"], com["ps_aux"],
              com["cst"], com["rowbufs"], com["RB"], com["NB"])


def phase_h(p: Prog, tag, com, src_ap, stack):
    xb = [p.sb(f"{tag}_xb{i}", [128, T], F32, stack) for i in range(2)]
    ts = p.sb(f"{tag}_ts", [128, T], F32, stack)
    emit_hpass(p, tag, src_ap, com["RB"], com["NB"], com["h_sb"], xb, ts)


def phase_rows2_update(p: Prog, tag, com, x_ap, a_ap, out_ap, stack,
                       do_stats=True):
    emit_rows(p, tag + "r", com["ps_s1"], com["ps_s2"], com["ps_aux"],
              com["cst"], com["rowbufs"], com["RB"], com["NB"])
    HW = T // 2
    bufs = {
        "xb": [p.sb(f"{tag}_xb{i}", [128, HW], F32, stack) for i in range(2)],
        "ab": [p.sb(f"{tag}_ab{i}", [128, HW], F32, stack) for i in range(2)],
        "xn": [p.sb(f"{tag}_xn{i}", [128, HW], F32, stack) for i in range(2)],
        "t1": p.sb(f"{tag}_t1", [128, HW], F32, stack),
        "sq": [p.sb(f"{tag}_sqh{i}", [128, HW], F32, stack) for i in range(2)],
    }
    emit_update(p, tag + "u", x_ap, a_ap, com["RB"], com["NB"], out_ap,
                com["ps_s1"], com["ps_s2"], com["cst"], bufs, do_stats=do_stats)


def emit_block0(p: Prog, com, aps, dbg=None, stack=None):
    nc = p.nc
    cst = com["cst"]
    bias_sb = p.sb("bias_sb", [128, DC], F32, stack)
    biasf_sb = p.sb("biasf_sb", [128, DC], F32, stack)

    ld0 = p.sem("ld0")
    with nc.Block() as block:
        @block.sync
        def _(sync):
            sync.dma_start(out=bias_sb[:, :], in_=aps["bias0"]).then_inc(ld0, 16)
            sync.dma_start(out=biasf_sb[:, :], in_=aps["biasf"]).then_inc(ld0, 16)
            sync.wait_ge(ld0, p.semv["ld0"] + 32)
    p.semv["ld0"] += 32

    with ExitStack() as stk_w:
        bmP = p.sb("bmP", [128, D], BF16, stk_w)
        with ExitStack() as stk:
            emit_maskprep(p, "mk", aps["maskT"], cst, bmP, stk)
        with ExitStack() as stk:
            phase_stats_rows(p, "st", com, aps["x0"], stk)
        with ExitStack() as stk:
            phase_h(p, "h", com, aps["x0"], stk)
        with ExitStack() as stk:
            wraw = [p.sb(f"a_wraw{i}", [128, D], BF16, stk) for i in range(2)]
            atmp = [p.sb(f"a_atmp{i}", [128, TW], F32, stk) for i in range(2)]
            sqb = [p.sb(f"a_sqtb{i}", [128, TW], F32, stk) for i in range(2)]
            emit_mm_l0a(p, "mm", aps["w0"], bias_sb, com["h_sb"], aps["a_scr"],
                        bmP, cst, com["ps_mm"], com["ps_s1"], com["ps_s2"],
                        wraw, atmp, sqb)
    with ExitStack() as stk:
        phase_rows2_update(p, "u", com, aps["x0"], aps["a_scr"], aps["x1"],
                           stk, do_stats=True)

    with ExitStack() as stk_f:
        r_fmT = p.sb("r_fmT", [8, T], BF16, stk_f)
        ocs_sb = p.sb("ocs_sb", [8, D], BF16, stk_f)
        emit_rows(p, "str", com["ps_s1"], com["ps_s2"], com["ps_aux"],
                  cst, com["rowbufs"], com["RB"], com["NB"])
        with ExitStack() as stk:
            phase_h(p, "h", com, aps["x1"], stk)
        with ExitStack() as stk:
            gw_sb = p.sb("gw_sb", [128, DC * 8], BF16, stk)
            ics_sb = p.sb("ics_sb", [8, D], BF16, stk)
            lg_fm = p.sb("lg_fm", [8, T], F32, stk)
            lg_tm = p.sb("lg_tm", [128, 128], F32, stk)
            scr_tm = p.sb("scr_tm", [128, 32], F32, stk)
            r_tm = p.sb("r_tm", [128, 128], F32, stk)
            ldg = p.sem("ldg")
            with nc.Block() as block:
                @block.gpsimd
                def _(g):
                    g.dma_start(out=gw_sb[:, :], in_=aps["gw"]).then_inc(ldg, 16)
                    g.dma_start(out=ics_sb[:, :], in_=aps["ics"]).then_inc(ldg, 16)
                    g.dma_start(out=ocs_sb[:, :], in_=aps["ocs"]).then_inc(ldg, 16)
                    g.wait_ge(ldg, p.semv["ldg"] + 48)
            p.semv["ldg"] += 48
            emit_gates(p, "gt", gw_sb, com["h_sb"], com["ps_aux"], cst,
                       lg_fm, lg_tm, scr_tm, r_tm, r_fmT)
            if dbg is not None:
                dsem = p.sem("dbgg")
                with nc.Block() as block:
                    @block.sync
                    def _(sync):
                        sync.dma_start(out=dbg["r_fmT"], in_=r_fmT[:, :]).then_inc(dsem, 16)
                        sync.dma_start(out=dbg["lg_fm"], in_=lg_fm[:, :]).then_inc(dsem, 16)
                        sync.dma_start(out=dbg["lg_tm"], in_=lg_tm[:, :]).then_inc(dsem, 16)
                        sync.dma_start(out=dbg["r_tm"], in_=r_tm[:, :]).then_inc(dsem, 16)
                        sync.wait_ge(dsem, p.semv["dbgg"] + 64)
                p.semv["dbgg"] += 64
            emit_upass(p, "up", ics_sb, r_fmT, com["h_sb"], com["ps_aux"], cst)
        with ExitStack() as stk:
            wraw = [p.sb(f"f_wraw{i}", [128, D], BF16, stk) for i in range(2)]
            os_sb = [p.sb(f"f_os{i}", [128, TW], F32, stk) for i in range(2)]
            tev = [p.sb(f"f_tev{i}", [128, TW], F32, stk) for i in range(2)]
            atmp = [p.sb(f"f_atmp{i}", [128, TW], F32, stk) for i in range(2)]
            sqb = [p.sb(f"f_sqtb{i}", [128, TW], F32, stk) for i in range(2)]
            emit_mm_l0f(p, "mm", aps["wf"], biasf_sb, ocs_sb, r_fmT,
                        com["h_sb"], aps["a_scr"], cst, com["ps_mm"],
                        com["ps_aux"][0], com["ps_s1"], com["ps_s2"],
                        wraw, os_sb, tev, atmp, sqb)
    with ExitStack() as stk:
        phase_rows2_update(p, "u", com, aps["x1"], aps["a_scr"], aps["x2"],
                           stk, do_stats=False)


def build_l0():
    p = Prog()
    aps = {
        "x0": p.din("x0", [D, T], F32),
        "w0": p.din("w0", [D, D], F32),
        "maskT": p.din("maskT", [D, D], I32),
        "bias0": p.din("bias0", [128, DC], F32),
        "gw": p.din("gw", [128, DC * 8], F32),
        "ics": p.din("ics", [8, D], F32),
        "ocs": p.din("ocs", [8, D], F32),
        "wf": p.din("wf", [D, D], F32),
        "biasf": p.din("biasf", [128, DC], F32),
        "x2": p.dout("x2", [D, T], F32),
        "a_scr": p.dint("a_scr", [D, T], F32),
        "x1": p.dint("x1_scr", [D, T], F32),
    }
    com = alloc_common(p)
    alloc_ln(p, com, p.ctx)
    emit_block0(p, com, aps)
    p.ctx.close()
    return p


def _tile_w(wT):
    K, N = wT.shape
    return np.ascontiguousarray(
        wT.reshape(K // 128, 128, N // 128, 128).transpose(2, 1, 0, 3))


def _shard_x(x):
    out = []
    for core in range(NCORES):
        b, half = core // 2, core % 2
        sl = x[b, half * T:(half + 1) * T, :]
        out.append(np.ascontiguousarray(sl.T))
    return out


def build_qkv():
    p = Prog()
    nc = p.nc
    x2 = p.din("x2", [D, T], F32)
    qc = p.din("qc", [D, D], I32)
    qs = p.din("qs", [DC, D], F32)
    kc = p.din("kc", [1024, D], I32)
    ks = p.din("ks", [8, D], F32)
    vc = p.din("vc", [1024, D], I32)
    vs = p.din("vs", [8, D], F32)
    cosd = p.din("cosd", [128, T], BF16)
    ssind = p.din("ssind", [128, T], BF16)
    q_out = p.dout("q_rot", [D, T], BF16)
    k_out = p.dout("k_rot", [1024, T], BF16)
    v_out = p.dout("v_tm", [T, 1024], BF16)

    com = alloc_common(p)
    cst = com["cst"]
    with ExitStack() as stkln:
        alloc_ln(p, com, stkln)
        with ExitStack() as stk:
            phase_stats_rows(p, "st2", com, x2, stk)
        with ExitStack() as stk:
            phase_h(p, "h2", com, x2, stk)
    with ExitStack() as stk:
        bufs = {
            "wbf": [p.sb(f"dwbf{i}", [128, D], BF16, stk) for i in range(2)],
            "scrow": p.sb("scrow", [1, D], F32, stk),
            "et": [p.sb(f"et{i}", [128, TW], BF16, stk) for i in range(2)],
            "esw": p.sb("esw", [128, TW], BF16, stk),
            "e1": p.sb("e1", [128, TW], BF16, stk),
            "e2": p.sb("e2", [128, TW], BF16, stk),
            "er": [p.sb(f"er{i}", [128, TW], BF16, stk) for i in range(2)],
            "COS": p.sb("COS", [128, T], BF16, stk),
            "SSIN": p.sb("SSIN", [128, T], BF16, stk),
        }
        psbt = [p.psum(f"psbt{i}", [128, TW], BF16, stk) for i in range(2)]
        ldc = p.sem("ldc")
        with nc.Block() as block:
            @block.sync
            def _(sync):
                sync.dma_start(out=bufs["COS"][:, :], in_=cosd).then_inc(ldc, 16)
                sync.dma_start(out=bufs["SSIN"][:, :], in_=ssind).then_inc(ldc, 16)
                sync.wait_ge(ldc, p.semv["ldc"] + 32)
        p.semv["ldc"] += 32
        emit_deqproj(p, "dq", qc, qs, DC, "rope", com["h_sb"], cst, bufs,
                     q_out, com["ps_mm"], com["ps_aux"], psbt)
        emit_deqproj(p, "dq", kc, ks, 8, "rope", com["h_sb"], cst, bufs,
                     k_out, com["ps_mm"], com["ps_aux"], psbt)
        emit_deqproj(p, "dq", vc, vs, 8, "vtrans", com["h_sb"], cst, bufs,
                     v_out, com["ps_mm"], com["ps_aux"], psbt)
    p.ctx.close()
    return p


def _rope_tables(half):
    pos = (half * T + np.arange(T, dtype=np.float64))
    jf = np.arange(64, dtype=np.float64)
    inv = 1.0 / (10000.0 ** (2.0 * jf / 128.0))
    f = pos[None, :] * inv[:, None]
    cos = np.repeat(np.cos(f), 2, axis=0)
    sin = np.sin(f)
    ssin = np.empty((128, T), np.float64)
    ssin[0::2] = -sin
    ssin[1::2] = sin
    bf = ml_dtypes.bfloat16
    return (np.ascontiguousarray(cos.astype(bf)),
            np.ascontiguousarray(ssin.astype(bf)))


def _prep_qkv(inputs):
    def sc_dev(s, n_chunks):
        return np.ascontiguousarray(
            s.reshape(32, n_chunks, 128).transpose(1, 0, 2).reshape(n_chunks, D))
    return {
        "qc": _tile_w(np.ascontiguousarray(inputs["L1_attn_q_codes"])).reshape(D, D),
        "qs": sc_dev(inputs["L1_attn_q_scales"], DC),
        "kc": _tile_w(np.ascontiguousarray(inputs["L1_attn_k_codes"])).reshape(1024, D),
        "ks": sc_dev(inputs["L1_attn_k_scales"], 8),
        "vc": _tile_w(np.ascontiguousarray(inputs["L1_attn_v_codes"])).reshape(1024, D),
        "vs": sc_dev(inputs["L1_attn_v_scales"], 8),
    }


def _prep_l0(inputs):
    maps = {
        "w0": _tile_w(np.ascontiguousarray(inputs["L0_attn_weight"].T)).reshape(D, D),
        "maskT": np.ascontiguousarray(inputs["L0_attn_mask"].T),
        "bias0": np.ascontiguousarray(inputs["L0_attn_bias"].reshape(DC, 128).T),
        "gw": np.ascontiguousarray(
            inputs["L0_ffn_gate_w"].T.reshape(DC, 128, 8).transpose(1, 0, 2)
            .reshape(128, DC * 8)),
        "ics": np.ascontiguousarray(inputs["L0_ffn_in_channel_scale"]),
        "ocs": np.ascontiguousarray(inputs["L0_ffn_out_channel_scale"]),
        "wf": _tile_w(np.ascontiguousarray(inputs["L0_ffn_weight"].T)).reshape(D, D),
        "biasf": np.ascontiguousarray(inputs["L0_ffn_bias"].reshape(DC, 128).T),
    }
    return maps


def build_full():
    p = Prog()
    nc = p.nc
    aps = {
        "x0": p.din("x0", [D, T], F32),
        "w0": p.din("w0", [D, D], F32),
        "maskT": p.din("maskT", [D, D], I32),
        "bias0": p.din("bias0", [128, DC], F32),
        "gw": p.din("gw", [128, DC * 8], F32),
        "ics": p.din("ics", [8, D], F32),
        "ocs": p.din("ocs", [8, D], F32),
        "wf": p.din("wf", [D, D], F32),
        "biasf": p.din("biasf", [128, DC], F32),
        "x1": p.dint("x1_scr", [D, T], F32),
        "x2": p.dint("x2_scr", [D, T], F32),
        "a_scr": p.dint("a_scr", [D, T], F32),
    }
    qc = p.din("qc", [D, D], I32)
    qs = p.din("qs", [DC, D], F32)
    kc = p.din("kc", [1024, D], I32)
    ks = p.din("ks", [8, D], F32)
    vc = p.din("vc", [1024, D], I32)
    vs = p.din("vs", [8, D], F32)
    cosd = p.din("cosd", [128, T], BF16)
    ssind = p.din("ssind", [128, T], BF16)
    tri = p.din("tri", [128, 128], F32)
    oc = p.din("oc", [D, D], I32)
    osd = p.din("osd", [DC, D], F32)
    aps_f1 = {
        "x3": p.dint("x3_scr", [D, T], F32),
        "bW": p.din("bW", [D, D], F32),
        "rw": p.din("rw", [128, DC * 8], F32),
        "lA": p.din("lA", [D, 128], F32),
        "lB": p.din("lB", [128, D], F32),
        "P2": p.din("P2", [8, 128], F32),
        "out": p.dout("out", [D, T], F32),
        "a_scr": aps["a_scr"],
    }
    agqk_in = p.dint("agqk_in", [5120, T], BF16)
    agqk_out = p.dint("agqk_out", [2 * 5120, T], BF16)
    agv_in = p.dint("agv_in", [T, 1024], BF16)
    agv_out = p.dint("agv_out", [2 * T, 1024], BF16)
    agat_in = p.dint("agat_in", [2048, 2 * T], BF16)
    agat_out = p.dint("agat_out", [4096, 2 * T], BF16)
    qfull = p.dint("qfull", [2048, 2 * T], BF16)
    kfull = p.dint("kfull", [512, 2 * T], BF16)
    vfull = p.dint("vfull", [2 * T, 512], BF16)
    attfull = p.dint("attfull", [D, T], BF16)

    com = alloc_common(p)
    cst = com["cst"]

    with ExitStack() as stkln:
        alloc_ln(p, com, stkln, tag="b0")
        emit_block0(p, com, aps, stack=stkln)
        with ExitStack() as stk:
            phase_stats_rows(p, "st", com, aps["x2"], stk)
        with ExitStack() as stk:
            phase_h(p, "h", com, aps["x2"], stk)

    qk_h = agqk_in.tensor
    v_h = agv_in.tensor

    def q_dst(o, j):
        return bass.AP(qk_h, o * 128 * T + j * TW, [[T, 128], [1, TW]])

    def k_dst(o, j):
        return bass.AP(qk_h, (4096 + o * 128) * T + j * TW, [[T, 128], [1, TW]])

    def v_dst(o, j):
        return bass.AP(v_h, j * TW * 1024 + o * 128,
                       [[1024, 128], [128 * 1024, 4], [1, 128]])

    with ExitStack() as stk:
        bufs = {
            "wbf": [p.sb(f"dwbf{i}", [128, D], BF16, stk) for i in range(2)],
            "scrow": p.sb("scrow", [1, D], F32, stk),
            "et": [p.sb(f"et{i}", [128, TW], BF16, stk) for i in range(2)],
            "esw": p.sb("esw", [128, TW], BF16, stk),
            "e1": p.sb("e1", [128, TW], BF16, stk),
            "e2": p.sb("e2", [128, TW], BF16, stk),
            "er": [p.sb(f"er{i}", [128, TW], BF16, stk) for i in range(2)],
            "COS": p.sb("COS", [128, T], BF16, stk),
            "SSIN": p.sb("SSIN", [128, T], BF16, stk),
        }
        psbt = [p.psum(f"psbt{i}", [128, TW], BF16, stk) for i in range(2)]
        ldc = p.sem("ldc")
        with nc.Block() as block:
            @block.sync
            def _(sync):
                sync.dma_start(out=bufs["COS"][:, :], in_=cosd).then_inc(ldc, 16)
                sync.dma_start(out=bufs["SSIN"][:, :], in_=ssind).then_inc(ldc, 16)
                sync.wait_ge(ldc, p.semv["ldc"] + 32)
        p.semv["ldc"] += 32
        emit_deqproj(p, "dq", qc, qs, DC, "rope", com["h_sb"], cst, bufs,
                     None, com["ps_mm"], com["ps_aux"], psbt, dst_fn=q_dst)
        emit_deqproj(p, "dq", kc, ks, 8, "rope", com["h_sb"], cst, bufs,
                     None, com["ps_mm"], com["ps_aux"], psbt, dst_fn=k_dst)
        emit_deqproj(p, "dq", vc, vs, 8, "vtrans", com["h_sb"], cst, bufs,
                     None, com["ps_mm"], com["ps_aux"], psbt, dst_fn=v_dst)

    cc = p.sem("cc")
    with nc.Block() as block:
        @block.gpsimd
        def _(g):
            g.collective_compute(
                "AllGather", ALU.bypass,
                replica_groups=[[0, 1], [2, 3], [4, 5], [6, 7]],
                ins=[agqk_in.opt()], outs=[agqk_out.opt()],
            ).then_inc(cc)
            g.collective_compute(
                "AllGather", ALU.bypass,
                replica_groups=[[0, 1], [2, 3], [4, 5], [6, 7]],
                ins=[agv_in.opt()], outs=[agv_out.opt()],
            ).then_inc(cc)
            g.wait_ge(cc, p.semv["cc"] + 2)
    p.semv["cc"] += 2

    pid = nc.partition_id()
    roq = (pid % 2) * 2048
    rok = (pid % 2) * 512
    rpk = p.sem("rpk")
    with nc.Block() as block:
        @block.gpsimd
        def _(g):
            n = 0
            for s2 in range(2):
                g.dma_start(out=qfull[:, s2 * T:(s2 + 1) * T],
                            in_=agqk_out[bass.ds(roq + s2 * 5120, 2048), :]
                            ).then_inc(rpk, 16)
                g.dma_start(out=kfull[:, s2 * T:(s2 + 1) * T],
                            in_=agqk_out[bass.ds(rok + (s2 * 5120 + 4096), 512), :]
                            ).then_inc(rpk, 16)
                g.dma_start(out=vfull[s2 * T:(s2 + 1) * T, :],
                            in_=agv_out[s2 * T:(s2 + 1) * T, bass.ds(rok, 512)]
                            ).then_inc(rpk, 16)
                n += 3
            g.wait_ge(rpk, p.semv["rpk"] + 16 * n)
    p.semv["rpk"] += 16 * 6

    with ExitStack() as stk:
        abufs = {
            "kg": p.sb("kg", [128, 2 * T], BF16, stk),
            "vg": p.sb("vg", [128, 2 * T], BF16, stk),
            "qh": p.sb("qh", [128, 2 * T], BF16, stk),
            "Srow": [p.sb(f"Srow{i}", [128, 2 * T], F32, stk) for i in range(2)],
            "Prow": p.sb("Prow", [128, 2 * T], BF16, stk),
            "PT": [p.sb(f"PTb{i}", [128, 1024], BF16, stk) for i in range(2)],
            "TRI": p.sb("TRI", [128, 128], F32, stk),
            "cmax": p.sb("cmax", [128, 8], F32, stk),
            "negmax": [p.sb(f"negmax{i}", [128, 1], F32, stk) for i in range(2)],
            "rowsum": [p.sb(f"rowsum{i}", [128, 1], F32, stk) for i in range(2)],
            "dinv": [p.sb(f"dinv{i}", [128, 1], F32, stk) for i in range(2)],
        }
        psbt = [p.psum(f"psbta{i}", [128, 1024], BF16, stk) for i in range(2)]
        ldt = p.sem("ldt")
        with nc.Block() as block:
            @block.sync
            def _(sync):
                sync.dma_start(out=abufs["TRI"][:, :], in_=tri).then_inc(ldt, 16)
                sync.wait_ge(ldt, p.semv["ldt"] + 16)
        p.semv["ldt"] += 16
        emit_attention(p, "at", qfull, kfull, vfull, com["h_sb"], cst, abufs,
                       com["ps_mm"], com["ps_aux"], psbt)

    sta = p.sem("sta")
    with nc.Block() as block:
        @block.sync
        def _(sync):
            for hl in range(16):
                sync.dma_start(out=agat_in[hl * 128:(hl + 1) * 128, :],
                               in_=com["h_sb"][:, hl * 4096:(hl + 1) * 4096]
                               ).then_inc(sta, 16)
            sync.wait_ge(sta, p.semv["sta"] + 16 * 16)
    p.semv["sta"] += 16 * 16
    with nc.Block() as block:
        @block.gpsimd
        def _(g):
            g.collective_compute(
                "AllGather", ALU.bypass,
                replica_groups=[[0, 1], [2, 3], [4, 5], [6, 7]],
                ins=[agat_in.opt()], outs=[agat_out.opt()],
            ).then_inc(cc)
            g.wait_ge(cc, p.semv["cc"] + 1)
    p.semv["cc"] += 1

    lda = p.sem("lda")
    ldb = p.sem("ldb")
    with nc.Block() as block:
        @block.gpsimd
        def _(g):
            coff = (nc.partition_id() % 2) * T
            g.dma_start(out=attfull[:, :],
                        in_=agat_out[:, bass.ds(coff, T)]).then_inc(lda, 16)
            g.wait_ge(lda, p.semv["lda"] + 16)
    p.semv["lda"] += 16
    with nc.Block() as block:
        @block.sync
        def _(sync):
            for c in range(DC):
                sync.dma_start(out=com["h_sb"][:, c * T:(c + 1) * T],
                               in_=attfull[c * 128:(c + 1) * 128, :]
                               ).then_inc(ldb, 16)
            sync.wait_ge(ldb, p.semv["ldb"] + 16 * DC)
    p.semv["ldb"] += 16 * DC

    with ExitStack() as stk:
        bufs = {
            "wbf": [p.sb(f"owbf{i}", [128, D], BF16, stk) for i in range(2)],
            "scrow": p.sb("oscrow", [1, D], F32, stk),
            "atmp": [p.sb(f"oatmp{i}", [128, TW], F32, stk) for i in range(2)],
            "sqb": [p.sb(f"osqb{i}", [128, TW], F32, stk) for i in range(2)],
        }
        emit_oproj(p, "mm", oc, osd, com["h_sb"], aps["a_scr"], cst,
                   com["ps_mm"], com["ps_aux"], com["ps_s1"], com["ps_s2"], bufs)
    with ExitStack() as stkln:
        alloc_ln(p, com, stkln, tag="u2")
        with ExitStack() as stk:
            phase_rows2_update(p, "u", com, aps["x2"], aps["a_scr"],
                               aps_f1["x3"], stk, do_stats=False)

    emit_ffn1(p, com, aps_f1)
    p.ctx.close()
    return p


def _full_in_maps(inputs):
    x_sh = _shard_x(inputs["L0_attn_data"].astype(np.float32, copy=False))
    w_l0 = _prep_l0(inputs)
    w_qkv = _prep_qkv(inputs)
    w_f1 = _prep_ffn1(inputs)

    def sc_dev(s, n_chunks):
        return np.ascontiguousarray(
            s.reshape(32, n_chunks, 128).transpose(1, 0, 2).reshape(n_chunks, D))

    oc = _tile_w(np.ascontiguousarray(inputs["L1_attn_o_codes"])).reshape(D, D)
    osd = sc_dev(inputs["L1_attn_o_scales"], DC)
    tri = _tri_mask()
    in_maps = []
    for c in range(NCORES):
        cos, ssin = _rope_tables(c % 2)
        m = dict(w_l0)
        m.update(w_qkv)
        m.update(w_f1)
        m.update(x0=x_sh[c], cosd=cos, ssind=ssin, tri=tri, oc=oc, osd=osd)
        in_maps.append(m)
    return in_maps


def kernel(**inputs):
    in_maps = _full_in_maps(inputs)
    p = build_full()
    res = run_bass_kernel_spmd(p.nc, in_maps, core_ids=list(range(NCORES)))
    out = np.empty((4, 4096, 4096), np.float32)
    for c in range(NCORES):
        b, half = c // 2, c % 2
        out[b, half * T:(half + 1) * T, :] = res.results[c]["out"].T
    return out


# version 05
# speedup vs baseline: 1.1064x; 1.1064x over previous
import numpy as np
import ml_dtypes
import concourse.bass as bass
import concourse.mybir as mybir
from concourse.bass_utils import run_bass_kernel_spmd
from contextlib import ExitStack

F32 = mybir.dt.float32
BF16 = mybir.dt.bfloat16
I32 = mybir.dt.int32
AF = mybir.ActivationFunctionType
ALU = mybir.AluOpType
AX = mybir.AxisListType

D = 4096
DC = 32
T = 2048
NT = 4
TW = 512
NCORES = 8
EPS = 1e-5
NEG = -30000.0


class Prog:
    def __init__(self):
        self.nc = bass.Bass("TRN2", target_bir_lowering=False, debug=False)
        self.ctx = ExitStack()
        self.sems = {}
        self.semv = {}

    def sem(self, name):
        if name not in self.sems:
            self.sems[name] = self.ctx.enter_context(self.nc.semaphore(name))
            self.semv[name] = 0
        return self.sems[name]

    def sem2(self, name):
        return [self.sem(name + "0"), self.sem(name + "1")]

    def _uniq(self, name):
        if not hasattr(self, "_names"):
            self._names = {}
        k = self._names.get(name, 0)
        self._names[name] = k + 1
        return name if k == 0 else f"{name}__{k}"

    def sb(self, name, shape, dt=F32, stack=None):
        return (stack or self.ctx).enter_context(
            self.nc.sbuf_tensor(self._uniq(name), shape, dt))

    def psum(self, name, shape, dt=F32, stack=None):
        return (stack or self.ctx).enter_context(
            self.nc.psum_tensor(self._uniq(name), shape, dt))

    def din(self, name, shape, dt=F32):
        return self.nc.dram_tensor(name, shape, dt, kind="ExternalInput").ap()

    def dout(self, name, shape, dt=F32):
        return self.nc.dram_tensor(name, shape, dt, kind="ExternalOutput").ap()

    def dint(self, name, shape, dt=F32):
        return self.nc.dram_tensor(name, shape, dt, kind="Internal").ap()


def ap3(t, parts, dims):
    h = t[parts[0]:parts[0] + parts[1], 0:1]
    return bass.AP(h.tensor, h.offset, [h.ap[0]] + [list(d) for d in dims])



def emit_consts(p: Prog, cst):
    nc = p.nc
    with nc.Block() as block:
        @block.gpsimd
        def _(g):
            g.iota(cst["io_free"][:, :], [[1, 128]], channel_multiplier=0)
            g.iota(cst["io_part"][:, :], [[0, 128]], channel_multiplier=1)
            g.iota(cst["io_g32"][:, :], [[1, 4], [0, 32]], channel_multiplier=0)

        @block.vector
        def _(v):
            v.memset(cst["ones32"][:, :], 1.0)
            v.memset(cst["ones1"][:, :], 1.0)
            v.memset(cst["eps"][:, :], EPS)
            v.memset(cst["neg8"][:, :], -8.0)

    with nc.Block() as block:
        @block.vector
        def _(v):
            v.tensor_tensor(cst["ident_bf"][:, :], cst["io_free"][:, :],
                            cst["io_part"][:, :], ALU.is_equal)
            v.tensor_tensor(cst["ident_f"][:, :], cst["io_free"][:, :],
                            cst["io_part"][:, :], ALU.is_equal)


def emit_stream_stats(p: Prog, tag, src_ap, ps_s1, ps_s2, cst, xbufs, sqbuf):
    nc = p.nc
    ld = p.sem2(tag + "_ld"); b_ld = [p.semv[tag + "_ld0"], p.semv[tag + "_ld1"]]
    sq = p.sem(tag + "_sq"); b_sq = p.semv[tag + "_sq"]
    mm = p.sem(tag + "_mm"); b_mm = p.semv[tag + "_mm"]
    ldt = lambda c: b_ld[c % 2] + 16 * (c // 2 + 1)

    with nc.Block() as block:
        @block.sync
        def _(sync):
            for c in range(DC):
                if c >= 2:
                    sync.wait_ge(sq, b_sq + c - 1)
                    sync.wait_ge(mm, b_mm + c - 1)
                sync.dma_start(out=xbufs[c % 2][:, :],
                               in_=src_ap[c * 128:(c + 1) * 128, :]).then_inc(ld[c % 2], 16)

        @block.scalar
        def _(s):
            for c in range(DC):
                s.wait_ge(ld[c % 2], ldt(c))
                if c >= 2:
                    s.wait_ge(mm, b_mm + c - 1)
                s.activation(sqbuf[c % 2][:, :], xbufs[c % 2][:, :],
                             AF.Square).then_inc(sq, 1)

        @block.tensor
        def _(t):
            for c in range(DC):
                t.wait_ge(ld[c % 2], ldt(c))
                for j in range(NT):
                    t.matmul(ps_s1[32 * j:32 * j + 32, :], cst["ones32"][:, :],
                             xbufs[c % 2][:, j * TW:(j + 1) * TW],
                             start=(c == 0), stop=(c == DC - 1),
                             tile_position=(0, 32 * j), skip_group_check=True)
                t.wait_ge(sq, b_sq + c + 1)
                for j in range(NT):
                    mmi = t.matmul(ps_s2[32 * j:32 * j + 32, :], cst["ones32"][:, :],
                                   sqbuf[c % 2][:, j * TW:(j + 1) * TW],
                                   start=(c == 0), stop=(c == DC - 1),
                                   tile_position=(0, 32 * j), skip_group_check=True)
                mmi.then_inc(mm, 1)

    p.semv[tag + "_ld0"] = b_ld[0] + 16 * (DC // 2)
    p.semv[tag + "_ld1"] = b_ld[1] + 16 * (DC // 2)
    p.semv[tag + "_sq"] = b_sq + DC
    p.semv[tag + "_mm"] = b_mm + DC


def emit_rows(p: Prog, tag, ps_s1, ps_s2, ps_aux, cst, rowbufs, RB, NB):
    nc = p.nc
    s1g, s2g, varg, rstdg, negmg = (rowbufs[k] for k in
                                    ("s1", "s2", "var", "rstd", "negm"))
    bc = p.sem(tag + "_bc"); b_bc = p.semv[tag + "_bc"]
    ev = p.sem(tag + "_ev"); b_ev = p.semv[tag + "_ev"]

    with nc.Block() as block:
        @block.vector
        def _(v):
            v.tensor_copy(s1g[:, :], ps_s1[:, :])
            v.tensor_copy(s2g[:, :], ps_s2[:, :])
            v.tensor_scalar(s1g[:, :], s1g[:, :], 1.0 / D, None, ALU.mult)
            v.tensor_scalar(s2g[:, :], s2g[:, :], 1.0 / D, None, ALU.mult)
            v.tensor_tensor(varg[:, :], s1g[:, :], s1g[:, :], ALU.mult)
            v.tensor_tensor(varg[:, :], s2g[:, :], varg[:, :], ALU.subtract).then_inc(ev, 1)

        @block.scalar
        def _(s):
            s.wait_ge(ev, b_ev + 1)
            s.activation(rstdg[:, :], varg[:, :], AF.Ln, bias=cst["eps"][:, :], scale=1.0)
            s.activation(rstdg[:, :], rstdg[:, :], AF.Exp, scale=-0.5).then_inc(ev, 1)

        @block.vector
        def _(v):
            v.wait_ge(ev, b_ev + 2)
            v.scalar_tensor_tensor(negmg[:, :], s1g[:, :], -1.0, rstdg[:, :],
                                   ALU.mult, ALU.mult).then_inc(ev, 1)

        @block.tensor
        def _(t):
            t.wait_ge(ev, b_ev + 3)
            for j in range(NT):
                t.matmul(ps_aux[0][:, :], cst["sel"][j][:, :], rstdg[:, :],
                         start=True, stop=True).then_inc(bc, 1)
                t.wait_ge(ev, b_ev + 3 + 2 * j + 1)
                t.matmul(ps_aux[1][:, :], cst["sel"][j][:, :], negmg[:, :],
                         start=True, stop=True).then_inc(bc, 1)
                t.wait_ge(ev, b_ev + 3 + 2 * j + 2)

        @block.vector
        def _(v):
            for j in range(NT):
                v.wait_ge(bc, b_bc + 2 * j + 1)
                v.tensor_copy(RB[:, j * TW:(j + 1) * TW], ps_aux[0][:, :]).then_inc(ev, 1)
                v.wait_ge(bc, b_bc + 2 * j + 2)
                v.tensor_copy(NB[:, j * TW:(j + 1) * TW], ps_aux[1][:, :]).then_inc(ev, 1)

    p.semv[tag + "_bc"] = b_bc + 2 * NT
    p.semv[tag + "_ev"] = b_ev + 3 + 2 * NT


def emit_hpass(p: Prog, tag, src_ap, RB, NB, h_sb, xbufs, tscr):
    nc = p.nc
    ld = p.sem2(tag + "_ld"); b_ld = [p.semv[tag + "_ld0"], p.semv[tag + "_ld1"]]
    hd = p.sem(tag + "_hd"); b_hd = p.semv[tag + "_hd"]

    with nc.Block() as block:
        @block.sync
        def _(sync):
            for c in range(DC):
                if c >= 2:
                    sync.wait_ge(hd, b_hd + c - 1)
                sync.dma_start(out=xbufs[c % 2][:, :],
                               in_=src_ap[c * 128:(c + 1) * 128, :]).then_inc(ld[c % 2], 16)

        @block.vector
        def _(v):
            for c in range(DC):
                v.wait_ge(ld[c % 2], b_ld[c % 2] + 16 * (c // 2 + 1))
                v.tensor_tensor(tscr[:, :], xbufs[c % 2][:, :], RB[:, :], ALU.mult)
                v.tensor_tensor(h_sb[:, c * T:(c + 1) * T], tscr[:, :], NB[:, :],
                                ALU.add).then_inc(hd, 1)

    p.semv[tag + "_ld0"] = b_ld[0] + 16 * (DC // 2)
    p.semv[tag + "_ld1"] = b_ld[1] + 16 * (DC // 2)
    p.semv[tag + "_hd"] = b_hd + DC


def emit_update(p: Prog, tag, x_ap, a_ap, RB2, NB2, out_ap, ps_s1, ps_s2, cst,
                bufs, do_stats=True):
    nc = p.nc
    HW = T // 2
    NI = DC * 2
    lx = p.sem2(tag + "_lx"); b_lx = [p.semv[tag + "_lx0"], p.semv[tag + "_lx1"]]
    la = p.sem2(tag + "_la"); b_la = [p.semv[tag + "_la0"], p.semv[tag + "_la1"]]
    st = p.sem2(tag + "_st"); b_st = [p.semv[tag + "_st0"], p.semv[tag + "_st1"]]
    up = p.sem(tag + "_up"); b_up = p.semv[tag + "_up"]
    sq = p.sem(tag + "_sq"); b_sq = p.semv[tag + "_sq"]
    mm = p.sem(tag + "_mm"); b_mm = p.semv[tag + "_mm"]
    xb, ab, xn, t1, sqb = bufs["xb"], bufs["ab"], bufs["xn"], bufs["t1"], bufs["sq"]
    lxt = lambda i: b_lx[i % 2] + 16 * (i // 2 + 1)
    lat = lambda i: b_la[i % 2] + 16 * (i // 2 + 1)
    stt = lambda i: b_st[i % 2] + 16 * (i // 2 + 1)

    def dsl(i):
        c, u = i // 2, i % 2
        return (slice(c * 128, (c + 1) * 128), slice(u * HW, (u + 1) * HW))

    with nc.Block() as block:
        @block.sync
        def _(sync):
            for i in range(NI):
                if i >= 2:
                    sync.wait_ge(up, b_up + i - 1)
                r, tsl = dsl(i)
                sync.dma_start(out=xb[i % 2][:, :],
                               in_=x_ap[r, tsl]).then_inc(lx[i % 2], 16)

        @block.gpsimd
        def _(g):
            for i in range(NI):
                if i >= 2:
                    g.wait_ge(up, b_up + i - 1)
                r, tsl = dsl(i)
                g.dma_start(out=ab[i % 2][:, :],
                            in_=a_ap[r, tsl]).then_inc(la[i % 2], 16)
                if i >= 1:
                    g.wait_ge(up, b_up + i)
                    rp, tp = dsl(i - 1)
                    g.dma_start(out=out_ap[rp, tp],
                                in_=xn[(i - 1) % 2][:, :]).then_inc(st[(i - 1) % 2], 16)
            g.wait_ge(up, b_up + NI)
            rp, tp = dsl(NI - 1)
            g.dma_start(out=out_ap[rp, tp],
                        in_=xn[(NI - 1) % 2][:, :]).then_inc(st[(NI - 1) % 2], 16)
            g.wait_ge(st[0], b_st[0] + 16 * (NI // 2))
            g.wait_ge(st[1], b_st[1] + 16 * (NI // 2))

        @block.vector
        def _(v):
            for i in range(NI):
                u = i % 2
                v.wait_ge(la[i % 2], lat(i))
                v.tensor_tensor(t1[:, :], ab[i % 2][:, :],
                                RB2[:, u * HW:(u + 1) * HW], ALU.mult)
                v.tensor_tensor(t1[:, :], t1[:, :],
                                NB2[:, u * HW:(u + 1) * HW], ALU.add)
                v.wait_ge(lx[i % 2], lxt(i))
                if i >= 2:
                    v.wait_ge(st[i % 2], stt(i - 2))
                    if do_stats:
                        v.wait_ge(sq, b_sq + i - 1)
                        v.wait_ge(mm, b_mm + i - 1)
                v.tensor_tensor(xn[i % 2][:, :], t1[:, :], xb[i % 2][:, :],
                                ALU.add).then_inc(up, 1)

        if do_stats:
            @block.scalar
            def _(s):
                for i in range(NI):
                    s.wait_ge(up, b_up + i + 1)
                    if i >= 2:
                        s.wait_ge(mm, b_mm + i - 1)
                    s.activation(sqb[i % 2][:, :], xn[i % 2][:, :],
                                 AF.Square).then_inc(sq, 1)

            @block.tensor
            def _(t):
                for i in range(NI):
                    c, u = i // 2, i % 2
                    t.wait_ge(up, b_up + i + 1)
                    for jj in range(2):
                        j = 2 * u + jj
                        t.matmul(ps_s1[32 * j:32 * j + 32, :], cst["ones32"][:, :],
                                 xn[i % 2][:, jj * TW:(jj + 1) * TW],
                                 start=(c == 0), stop=(c == DC - 1),
                                 tile_position=(0, 32 * j), skip_group_check=True)
                    t.wait_ge(sq, b_sq + i + 1)
                    for jj in range(2):
                        j = 2 * u + jj
                        mmi = t.matmul(ps_s2[32 * j:32 * j + 32, :], cst["ones32"][:, :],
                                       sqb[i % 2][:, jj * TW:(jj + 1) * TW],
                                       start=(c == 0), stop=(c == DC - 1),
                                       tile_position=(0, 32 * j), skip_group_check=True)
                    mmi.then_inc(mm, 1)

    for k in range(2):
        p.semv[tag + f"_lx{k}"] = b_lx[k] + 16 * (NI // 2)
        p.semv[tag + f"_la{k}"] = b_la[k] + 16 * (NI // 2)
        p.semv[tag + f"_st{k}"] = b_st[k] + 16 * (NI // 2)
    p.semv[tag + "_up"] = b_up + NI
    if do_stats:
        p.semv[tag + "_sq"] = b_sq + NI
        p.semv[tag + "_mm"] = b_mm + NI


def emit_maskprep(p: Prog, tag, maskT_ap, cst, bmP, stack):
    nc = p.nc
    mbuf = [p.sb(f"{tag}_mb{i}", [128, 4096], I32, stack) for i in range(2)]
    s1t = p.sb(f"{tag}_s1", [128, 128], F32, stack)
    bmTnb = p.sb(f"{tag}_bmTnb", [128, 128], F32, stack)
    bm01 = p.sb(f"{tag}_bm01", [128, 128], F32, stack)
    selc_a = p.sb(f"{tag}_sca", [128, 128], I32, stack)
    ps_tr = p.psum(f"{tag}_pstr", [128, 128], F32, stack)
    ps_b = p.psum(f"{tag}_psb", [128, 128], F32, stack)
    ld = p.sem2(tag + "_ld"); b_ld = [p.semv[tag + "_ld0"], p.semv[tag + "_ld1"]]
    rd = p.sem(tag + "_rd"); b_rd = p.semv[tag + "_rd"]
    tr = p.sem(tag + "_tr"); b_tr = p.semv[tag + "_tr"]

    with nc.Block() as block:
        @block.sync
        def _(sync):
            for i in range(DC):
                if i >= 2:
                    sync.wait_ge(rd, b_rd + 2 * (i - 1))
                sync.dma_start(out=mbuf[i % 2][:, :],
                               in_=maskT_ap[i * 128:(i + 1) * 128, :]).then_inc(ld[i % 2], 16)

        @block.vector
        def _(v):
            for i in range(DC):
                v.wait_ge(ld[i % 2], b_ld[i % 2] + 16 * (i // 2 + 1))
                v.reduce_sum(s1t[:, :],
                             mbuf[i % 2][:, :].rearrange("p (n r) -> p n r", r=32),
                             axis=AX.X).then_inc(rd, 1)
                v.wait_ge(tr, b_tr + i + 1)
                v.reduce_sum(bmTnb[:, 4 * i:4 * i + 4],
                             ps_tr[:, :].rearrange("p (n r) -> p n r", r=32),
                             axis=AX.X).then_inc(rd, 1)

        @block.tensor
        def _(t):
            for i in range(DC):
                t.wait_ge(rd, b_rd + 2 * i + 1)
                t.transpose(ps_tr[:, :], s1t[:, :], cst["ident_f"][:, :]).then_inc(tr, 1)
                t.wait_ge(rd, b_rd + 2 * i + 2)
    p.semv[tag + "_ld0"] = b_ld[0] + 16 * (DC // 2)
    p.semv[tag + "_ld1"] = b_ld[1] + 16 * (DC // 2)
    p.semv[tag + "_rd"] = b_rd + 2 * DC
    p.semv[tag + "_tr"] = b_tr + DC

    with nc.Block() as block:
        @block.vector
        def _(v):
            v.tensor_scalar(bmTnb[:, :], bmTnb[:, :], 0.5, None, ALU.is_gt)

    with nc.Block() as block:
        @block.tensor
        def _(t):
            t.transpose(ps_tr[:, :], bmTnb[:, :], cst["ident_f"][:, :])

    with nc.Block() as block:
        @block.vector
        def _(v):
            v.tensor_copy(bm01[:, :], ps_tr[:, :])

    b_tr = p.semv[tag + "_tr"]; b_rd = p.semv[tag + "_rd"]
    with nc.Block() as block:
        @block.vector
        def _(v):
            for c in range(DC):
                if c >= 1:
                    v.wait_ge(tr, b_tr + c)
                v.tensor_scalar(selc_a[:, :], cst["io_g32"][:, :], 4 * c, None, ALU.add)
                v.tensor_tensor(s1t[:, :], cst["io_part"][:, :], selc_a[:, :],
                                ALU.is_equal).then_inc(rd, 1)
                v.wait_ge(tr, b_tr + c + 1)
                v.tensor_copy(bmP[:, c * 128:(c + 1) * 128], ps_b[:, :]).then_inc(rd, 1)

        @block.tensor
        def _(t):
            for c in range(DC):
                t.wait_ge(rd, b_rd + 2 * c + 1)
                t.matmul(ps_b[:, :], s1t[:, :], bm01[:, :],
                         start=True, stop=True).then_inc(tr, 1)
    p.semv[tag + "_rd"] = b_rd + 2 * DC
    p.semv[tag + "_tr"] = b_tr + DC


def emit_mm_l0a(p: Prog, tag, w_ap, bias_sb, h_sb, a_out_ap, bmP, cst,
                ps_mm, ps_s1, ps_s2, wraw, atmp, sqb):
    nc = p.nc
    wl = p.sem2(tag + "_wl"); b_wl = [p.semv[tag + "_wl0"], p.semv[tag + "_wl1"]]
    spd = p.sem2(tag + "_spd"); b_spd = [p.semv[tag + "_spd0"], p.semv[tag + "_spd1"]]
    wp = p.sem(tag + "_wp"); b_wp = p.semv[tag + "_wp"]
    mmd = p.sem(tag + "_mmd"); b_mmd = p.semv[tag + "_mmd"]
    evd = p.sem(tag + "_evd"); b_evd = p.semv[tag + "_evd"]
    sqd = p.sem(tag + "_sqd"); b_sqd = p.semv[tag + "_sqd"]
    std = p.sem(tag + "_std"); b_std = p.semv[tag + "_std"]
    wlt = lambda o: b_wl[o % 2] + 16 * (o // 2 + 1)
    spt = lambda ti: b_spd[ti % 2] + 16 * (ti // 2 + 1)

    NTILE = DC * NT

    with nc.Block() as block:
        @block.gpsimd
        def _(g):
            for o in range(DC):
                if o >= 2:
                    g.wait_ge(mmd, b_mmd + (o - 1) * NT)
                g.dma_start(out=wraw[o % 2][:, :],
                            in_=w_ap[o * 128:(o + 1) * 128, :]).then_inc(wl[o % 2], 16)

        @block.vector
        def _(v):
            for o in range(DC):
                v.wait_ge(wl[o % 2], wlt(o))
                bh = bmP[:, 4 * o:4 * o + 1]
                bview = bass.AP(bh.tensor, bh.offset,
                                [bh.ap[0], [128, DC], [1, 4], [0, 32]])
                v.tensor_tensor(
                    wraw[o % 2][:, :].rearrange("p (c f r) -> p c f r", c=DC, f=4),
                    wraw[o % 2][:, :].rearrange("p (c f r) -> p c f r", c=DC, f=4),
                    bview, ALU.mult).then_inc(wp, 1)

        @block.tensor
        def _(t):
            for o in range(DC):
                t.wait_ge(wp, b_wp + o + 1)
                for j in range(NT):
                    ti = o * NT + j
                    bank = ti % 2
                    if ti >= 2:
                        t.wait_ge(evd, b_evd + ti - 1)
                    for c in range(DC):
                        mi = t.matmul(ps_mm[bank][:, :],
                                      wraw[o % 2][:, c * 128:(c + 1) * 128],
                                      h_sb[:, c * T + j * TW:c * T + (j + 1) * TW],
                                      start=(c == 0), stop=(c == DC - 1))
                    mi.then_inc(mmd, 1)
                    if ti >= 1:
                        op, jp = (ti - 1) // NT, (ti - 1) % NT
                        t.wait_ge(sqd, b_sqd + ti)
                        t.matmul(ps_s1[32 * jp:32 * jp + 32, :], cst["ones32"][:, :],
                                 atmp[(ti - 1) % 2][:, :],
                                 start=(op == 0), stop=(op == DC - 1),
                                 tile_position=(0, 32 * jp), skip_group_check=True)
                        t.matmul(ps_s2[32 * jp:32 * jp + 32, :], cst["ones32"][:, :],
                                 sqb[(ti - 1) % 2][:, :],
                                 start=(op == 0), stop=(op == DC - 1),
                                 tile_position=(0, 32 * jp), skip_group_check=True).then_inc(std, 1)
            ti = NTILE
            op, jp = (ti - 1) // NT, (ti - 1) % NT
            t.wait_ge(sqd, b_sqd + ti)
            t.matmul(ps_s1[32 * jp:32 * jp + 32, :], cst["ones32"][:, :],
                     atmp[(ti - 1) % 2][:, :], start=(op == 0), stop=True,
                     tile_position=(0, 32 * jp), skip_group_check=True)
            t.matmul(ps_s2[32 * jp:32 * jp + 32, :], cst["ones32"][:, :],
                     sqb[(ti - 1) % 2][:, :], start=(op == 0), stop=True,
                     tile_position=(0, 32 * jp), skip_group_check=True).then_inc(std, 1)

        @block.scalar
        def _(s):
            for o in range(DC):
                for j in range(NT):
                    ti = o * NT + j
                    s.wait_ge(mmd, b_mmd + ti + 1)
                    if ti >= 2:
                        s.wait_ge(spd[ti % 2], spt(ti - 2))
                        s.wait_ge(std, b_std + ti - 1)
                    s.activation(atmp[ti % 2][:, :], ps_mm[ti % 2][:, :],
                                 AF.Identity, bias=bias_sb[:, o:o + 1],
                                 scale=1.0).then_inc(evd, 1)
                    s.activation(sqb[ti % 2][:, :], atmp[ti % 2][:, :],
                                 AF.Square).then_inc(sqd, 1)

        @block.sync
        def _(sync):
            for o in range(DC):
                for j in range(NT):
                    ti = o * NT + j
                    sync.wait_ge(evd, b_evd + ti + 1)
                    sync.dma_start(out=a_out_ap[o * 128:(o + 1) * 128,
                                                j * TW:(j + 1) * TW],
                                   in_=atmp[ti % 2][:, :]).then_inc(spd[ti % 2], 16)
            sync.wait_ge(spd[0], b_spd[0] + 16 * (NTILE // 2))
            sync.wait_ge(spd[1], b_spd[1] + 16 * (NTILE // 2))

    for k in range(2):
        p.semv[tag + f"_wl{k}"] = b_wl[k] + 16 * (DC // 2)
        p.semv[tag + f"_spd{k}"] = b_spd[k] + 16 * (NTILE // 2)
    p.semv[tag + "_wp"] = b_wp + DC
    p.semv[tag + "_mmd"] = b_mmd + NTILE
    p.semv[tag + "_evd"] = b_evd + NTILE
    p.semv[tag + "_sqd"] = b_sqd + NTILE
    p.semv[tag + "_std"] = b_std + NTILE


def emit_gates(p: Prog, tag, gw_sb, h_sb, ps_aux, cst, lg_fm, lg_tm, scr_tm,
               r_tm, r_fmT):
    nc = p.nc
    gd = p.sem(tag + "_gd"); b_gd = p.semv[tag + "_gd"]
    ev = p.sem(tag + "_ev"); b_ev = p.semv[tag + "_ev"]

    with nc.Block() as block:
        @block.tensor
        def _(t):
            for j in range(NT):
                if j >= 1:
                    t.wait_ge(ev, b_ev + j)
                for c in range(DC):
                    mi = t.matmul(ps_aux[0][0:8, :], gw_sb[:, c * 8:(c + 1) * 8],
                                  h_sb[:, c * T + j * TW:c * T + (j + 1) * TW],
                                  start=(c == 0), stop=(c == DC - 1))
                mi.then_inc(gd, 1)

        @block.vector
        def _(v):
            for j in range(NT):
                v.wait_ge(gd, b_gd + j + 1)
                v.tensor_copy(lg_fm[0:8, j * TW:(j + 1) * TW],
                              ps_aux[0][0:8, :]).then_inc(ev, 1)
    p.semv[tag + "_gd"] = b_gd + NT
    p.semv[tag + "_ev"] = b_ev + NT

    b_gd = p.semv[tag + "_gd"]; b_ev = p.semv[tag + "_ev"]
    with nc.Block() as block:
        @block.tensor
        def _(t):
            for k in range(16):
                mi = t.transpose(ps_aux[1][:, k * 8:(k + 1) * 8],
                                 lg_fm[0:8, k * 128:(k + 1) * 128],
                                 cst["ident_f"][0:8, 0:8])
            mi.then_inc(gd, 1)

        @block.vector
        def _(v):
            v.wait_ge(gd, b_gd + 1)
            v.tensor_copy(lg_tm[:, :], ps_aux[1][:, 0:128])
    p.semv[tag + "_gd"] = b_gd + 1

    sm = p.sem(tag + "_sm"); b_sm = p.semv[tag + "_sm"]
    with nc.Block() as block:
        @block.vector
        def _(v):
            v.reduce_max(scr_tm[:, 0:16],
                         lg_tm[:, :].rearrange("p (g e) -> p g e", e=8), axis=AX.X)
            mh = scr_tm[:, 0:1]
            mview = bass.AP(mh.tensor, mh.offset, [mh.ap[0], [1, 16], [0, 8]])
            v.tensor_tensor(lg_tm[:, :].rearrange("p (g e) -> p g e", e=8),
                            lg_tm[:, :].rearrange("p (g e) -> p g e", e=8),
                            mview, ALU.subtract).then_inc(sm, 1)

        @block.scalar
        def _(s):
            s.wait_ge(sm, b_sm + 1)
            s.activation(lg_tm[:, :], lg_tm[:, :], AF.Exp).then_inc(sm, 1)

        @block.vector
        def _(v):
            v.wait_ge(sm, b_sm + 2)
            v.reduce_sum(scr_tm[:, 0:16],
                         lg_tm[:, :].rearrange("p (g e) -> p g e", e=8), axis=AX.X)
            v.drain()
            v.reciprocal(scr_tm[:, 16:32], scr_tm[:, 0:16])
            v.drain()
            dh = scr_tm[:, 16:17]
            dview = bass.AP(dh.tensor, dh.offset, [dh.ap[0], [1, 16], [0, 8]])
            v.tensor_tensor(r_tm[:, :].rearrange("p (g e) -> p g e", e=8),
                            lg_tm[:, :].rearrange("p (g e) -> p g e", e=8),
                            dview, ALU.mult).then_inc(sm, 1)
    p.semv[tag + "_sm"] = b_sm + 3

    b_gd = p.semv[tag + "_gd"]; b_ev = p.semv[tag + "_ev"]
    with nc.Block() as block:
        @block.tensor
        def _(t):
            for g4 in range(4):
                if g4 >= 1:
                    t.wait_ge(ev, b_ev + g4)
                for k4 in range(4):
                    k = g4 * 4 + k4
                    mi = t.transpose(ps_aux[0][0:8, k4 * 128:(k4 + 1) * 128],
                                     r_tm[:, k * 8:(k + 1) * 8],
                                     cst["ident_f"][:, :])
                mi.then_inc(gd, 1)

        @block.vector
        def _(v):
            for g4 in range(4):
                v.wait_ge(gd, b_gd + g4 + 1)
                v.tensor_copy(r_fmT[0:8, g4 * TW:(g4 + 1) * TW],
                              ps_aux[0][0:8, 0:TW]).then_inc(ev, 1)
    p.semv[tag + "_gd"] = b_gd + 4
    p.semv[tag + "_ev"] = b_ev + 4


def emit_upass(p: Prog, tag, ics_sb, r_fmT, h_sb, ps_aux, cst):
    nc = p.nc
    ism = p.sem(tag + "_ism"); b_ism = p.semv[tag + "_ism"]
    ut = p.sem(tag + "_ut"); b_ut = p.semv[tag + "_ut"]

    with nc.Block() as block:
        @block.tensor
        def _(t):
            for c in range(DC):
                for j in range(NT):
                    idx = c * NT + j
                    if idx >= 2:
                        t.wait_ge(ut, b_ut + idx - 1)
                    t.matmul(ps_aux[idx % 2][:, :],
                             ics_sb[0:8, c * 128:(c + 1) * 128],
                             r_fmT[0:8, j * TW:(j + 1) * TW],
                             start=True, stop=True).then_inc(ism, 1)

        @block.vector
        def _(v):
            for c in range(DC):
                for j in range(NT):
                    idx = c * NT + j
                    v.wait_ge(ism, b_ism + idx + 1)
                    sl = slice(c * T + j * TW, c * T + (j + 1) * TW)
                    v.tensor_tensor(h_sb[:, sl], h_sb[:, sl],
                                    ps_aux[idx % 2][:, :], ALU.mult).then_inc(ut, 1)
    p.semv[tag + "_ism"] = b_ism + DC * NT
    p.semv[tag + "_ut"] = b_ut + DC * NT


def emit_mm_l0f(p: Prog, tag, w_ap, biasf_sb, ocs_sb, r_fmT, h_sb, a_out_ap,
                cst, ps_mm, ps_os, ps_s1, ps_s2, wraw, os_sb, tev, atmp, sqb):
    nc = p.nc
    wl = p.sem2(tag + "_wl"); b_wl = [p.semv[tag + "_wl0"], p.semv[tag + "_wl1"]]
    spd = p.sem2(tag + "_spd"); b_spd = [p.semv[tag + "_spd0"], p.semv[tag + "_spd1"]]
    wp = p.sem(tag + "_wp"); b_wp = p.semv[tag + "_wp"]
    mmd = p.sem(tag + "_mmd"); b_mmd = p.semv[tag + "_mmd"]
    osd = p.sem(tag + "_osd"); b_osd = p.semv[tag + "_osd"]
    osc = p.sem(tag + "_osc"); b_osc = p.semv[tag + "_osc"]
    tvd = p.sem(tag + "_tvd"); b_tvd = p.semv[tag + "_tvd"]
    acd = p.sem(tag + "_acd"); b_acd = p.semv[tag + "_acd"]
    sqd = p.sem(tag + "_sqd"); b_sqd = p.semv[tag + "_sqd"]
    std = p.sem(tag + "_std"); b_std = p.semv[tag + "_std"]
    wlt = lambda o: b_wl[o % 2] + 16 * (o // 2 + 1)
    spt = lambda ti: b_spd[ti % 2] + 16 * (ti // 2 + 1)
    NTILE = DC * NT

    with nc.Block() as block:
        @block.gpsimd
        def _(g):
            for o in range(DC):
                if o >= 2:
                    g.wait_ge(mmd, b_mmd + (o - 1) * NT)
                g.dma_start(out=wraw[o % 2][:, :],
                            in_=w_ap[o * 128:(o + 1) * 128, :]).then_inc(wl[o % 2], 16)

        @block.scalar
        def _(s):
            def evac(ti):
                o = ti // NT
                s.wait_ge(tvd, b_tvd + ti + 1)
                if ti >= 2:
                    s.wait_ge(spd[ti % 2], spt(ti - 2))
                    s.wait_ge(std, b_std + ti - 1)
                s.activation(atmp[ti % 2][:, :], tev[ti % 2][:, :],
                             AF.Identity, bias=biasf_sb[:, o:o + 1],
                             scale=1.0).then_inc(acd, 1)
                s.activation(sqb[ti % 2][:, :], atmp[ti % 2][:, :],
                             AF.Square).then_inc(sqd, 1)

            for o in range(DC):
                if o >= 2:
                    for j in range(NT):
                        evac((o - 2) * NT + j)
                s.wait_ge(wl[o % 2], wlt(o))
                s.activation(wraw[o % 2][:, :], wraw[o % 2][:, :],
                             AF.Sign).then_inc(wp, 1)
            for ti in range((DC - 2) * NT, DC * NT):
                evac(ti)

        @block.tensor
        def _(t):
            for o in range(DC):
                t.wait_ge(wp, b_wp + o + 1)
                for j in range(NT):
                    ti = o * NT + j
                    t.wait_ge(osc, b_osc + ti)
                    t.matmul(ps_os[:, :],
                             ocs_sb[0:8, o * 128:(o + 1) * 128],
                             r_fmT[0:8, j * TW:(j + 1) * TW],
                             start=True, stop=True).then_inc(osd, 1)
                    if ti >= 2:
                        t.wait_ge(tvd, b_tvd + ti - 1)
                    for c in range(DC):
                        mi = t.matmul(ps_mm[ti % 2][:, :],
                                      wraw[o % 2][:, c * 128:(c + 1) * 128],
                                      h_sb[:, c * T + j * TW:c * T + (j + 1) * TW],
                                      start=(c == 0), stop=(c == DC - 1))
                    mi.then_inc(mmd, 1)
                    if ti >= 1:
                        op, jp = (ti - 1) // NT, (ti - 1) % NT
                        t.wait_ge(sqd, b_sqd + ti)
                        t.matmul(ps_s1[32 * jp:32 * jp + 32, :], cst["ones32"][:, :],
                                 atmp[(ti - 1) % 2][:, :],
                                 start=(op == 0), stop=(op == DC - 1),
                                 tile_position=(0, 32 * jp), skip_group_check=True)
                        t.matmul(ps_s2[32 * jp:32 * jp + 32, :], cst["ones32"][:, :],
                                 sqb[(ti - 1) % 2][:, :],
                                 start=(op == 0), stop=(op == DC - 1),
                                 tile_position=(0, 32 * jp),
                                 skip_group_check=True).then_inc(std, 1)
            ti = NTILE
            op, jp = (ti - 1) // NT, (ti - 1) % NT
            t.wait_ge(sqd, b_sqd + ti)
            t.matmul(ps_s1[32 * jp:32 * jp + 32, :], cst["ones32"][:, :],
                     atmp[(ti - 1) % 2][:, :], start=(op == 0), stop=True,
                     tile_position=(0, 32 * jp), skip_group_check=True)
            t.matmul(ps_s2[32 * jp:32 * jp + 32, :], cst["ones32"][:, :],
                     sqb[(ti - 1) % 2][:, :], start=(op == 0), stop=True,
                     tile_position=(0, 32 * jp),
                     skip_group_check=True).then_inc(std, 1)

        @block.vector
        def _(v):
            for o in range(DC):
                for j in range(NT):
                    ti = o * NT + j
                    v.wait_ge(osd, b_osd + ti + 1)
                    if ti >= 2:
                        v.wait_ge(tvd, b_tvd + ti - 1)
                    v.tensor_copy(os_sb[ti % 2][:, :], ps_os[:, :]).then_inc(osc, 1)
                    v.wait_ge(mmd, b_mmd + ti + 1)
                    if ti >= 2:
                        v.wait_ge(acd, b_acd + ti - 1)
                    v.tensor_tensor(tev[ti % 2][:, :], ps_mm[ti % 2][:, :],
                                    os_sb[ti % 2][:, :], ALU.mult).then_inc(tvd, 1)

        @block.sync
        def _(sync):
            for o in range(DC):
                for j in range(NT):
                    ti = o * NT + j
                    sync.wait_ge(acd, b_acd + ti + 1)
                    sync.dma_start(out=a_out_ap[o * 128:(o + 1) * 128,
                                                j * TW:(j + 1) * TW],
                                   in_=atmp[ti % 2][:, :]).then_inc(spd[ti % 2], 16)
            sync.wait_ge(spd[0], b_spd[0] + 16 * (NTILE // 2))
            sync.wait_ge(spd[1], b_spd[1] + 16 * (NTILE // 2))

    for k in range(2):
        p.semv[tag + f"_wl{k}"] = b_wl[k] + 16 * (DC // 2)
        p.semv[tag + f"_spd{k}"] = b_spd[k] + 16 * (NTILE // 2)
    p.semv[tag + "_wp"] = b_wp + DC
    p.semv[tag + "_mmd"] = b_mmd + NTILE
    p.semv[tag + "_osd"] = b_osd + NTILE
    p.semv[tag + "_osc"] = b_osc + NTILE
    p.semv[tag + "_tvd"] = b_tvd + NTILE
    p.semv[tag + "_acd"] = b_acd + NTILE
    p.semv[tag + "_sqd"] = b_sqd + NTILE
    p.semv[tag + "_std"] = b_std + NTILE


def emit_deqproj(p: Prog, tag, codes_ap, sc_ap, NOUT, mode, h_sb, cst, bufs,
                 out_ap, ps_mm, ps_aux, psbt=None, dst_fn=None):
    nc = p.nc
    cl = p.sem2(tag + "_cl"); b_cl = [p.semv[tag + "_cl0"], p.semv[tag + "_cl1"]]
    sl = p.sem(tag + "_sl"); b_sl = p.semv[tag + "_sl"]
    spd = p.sem2(tag + "_spd"); b_spd = [p.semv[tag + "_spd0"], p.semv[tag + "_spd1"]]
    acp = p.sem(tag + "_acp"); b_acp = p.semv[tag + "_acp"]
    bcm = p.sem(tag + "_bcm"); b_bcm = p.semv[tag + "_bcm"]
    dqd = p.sem(tag + "_dqd"); b_dqd = p.semv[tag + "_dqd"]
    mmd = p.sem(tag + "_mmd"); b_mmd = p.semv[tag + "_mmd"]
    acd = p.sem(tag + "_acd"); b_acd = p.semv[tag + "_acd"]
    rpd = p.sem(tag + "_rpd"); b_rpd = p.semv[tag + "_rpd"]
    trd = p.sem(tag + "_trd"); b_trd = p.semv[tag + "_trd"]
    wbf, scrow = bufs["wbf"], bufs["scrow"]
    et, esw, e1, e2, er = bufs["et"], bufs["esw"], bufs["e1"], bufs["e2"], bufs["er"]
    COS, SSIN = bufs.get("COS"), bufs.get("SSIN")
    clt = lambda o: b_cl[o % 2] + 16 * (o // 2 + 1)
    spt = lambda ti: b_spd[ti % 2] + 16 * (ti // 2 + 1)
    NTILE = NOUT * NT

    with nc.Block() as block:
        @block.gpsimd
        def _(g):
            for o in range(NOUT):
                if o >= 2:
                    g.wait_ge(mmd, b_mmd + (o - 1) * NT)
                g.dma_start(out=wbf[o % 2][:, :],
                            in_=codes_ap[o * 128:(o + 1) * 128, :]).then_inc(cl[o % 2], 16)

        @block.sync
        def _(sync):
            def store(ti):
                o, j = ti // NT, ti % NT
                sync.wait_ge(rpd, b_rpd + ti + 1)
                if dst_fn is not None:
                    dst = dst_fn(o, j)
                    srcv = (er[ti % 2][:, :].rearrange("q (k f) -> q k f", k=4)
                            if mode == "vtrans" else er[ti % 2][:, :])
                    sync.dma_start(out=dst, in_=srcv).then_inc(spd[ti % 2], 16)
                elif mode == "rope":
                    sync.dma_start(out=out_ap[o * 128:(o + 1) * 128,
                                              j * TW:(j + 1) * TW],
                                   in_=er[ti % 2][:, :]).then_inc(spd[ti % 2], 16)
                else:
                    dst = out_ap[j * TW:(j + 1) * TW,
                                 o * 128:(o + 1) * 128].rearrange(
                                     "(k q) f -> q k f", q=128)
                    sync.dma_start(out=dst, in_=er[ti % 2][:, :].rearrange(
                        "q (k f) -> q k f", k=4)).then_inc(spd[ti % 2], 16)

            for o in range(NOUT):
                if o >= 2:
                    for j in range(NT):
                        store((o - 2) * NT + j)
                if o >= 1:
                    sync.wait_ge(bcm, b_bcm + 8 * o)
                sync.dma_start(out=scrow[:, :],
                               in_=sc_ap[o:o + 1, :]).then_inc(sl, 16)
            for ti in range((NOUT - 2) * NT, NOUT * NT):
                store(ti)
            sync.wait_ge(spd[0], b_spd[0] + 16 * ((NTILE + 1) // 2))
            sync.wait_ge(spd[1], b_spd[1] + 16 * (NTILE // 2))

        @block.scalar
        def _(s):
            def evac(ti):
                s.wait_ge(mmd, b_mmd + ti + 1)
                if ti >= 2:
                    s.wait_ge(rpd, b_rpd + ti - 1)
                s.activation(et[ti % 2][:, :], ps_mm[ti % 2][:, :],
                             AF.Identity).then_inc(acd, 1)

            for o in range(NOUT):
                if o >= 2:
                    for j in range(NT):
                        evac((o - 2) * NT + j)
                s.wait_ge(cl[o % 2], clt(o))
                s.activation(wbf[o % 2][:, :], wbf[o % 2][:, :],
                             AF.Identity, bias=cst["neg8"][:, :],
                             scale=1.0).then_inc(acp, 1)
            for ti in range((NOUT - 2) * NT, NOUT * NT):
                evac(ti)

        @block.tensor
        def _(t):
            for o in range(NOUT):
                t.wait_ge(sl, b_sl + 16 * (o + 1))
                for sidx in range(8):
                    if o * 8 + sidx >= 2:
                        t.wait_ge(dqd, b_dqd + o * 8 + sidx - 1)
                    t.matmul(ps_aux[(o * 8 + sidx) % 2][:, :], cst["ones1"][:, :],
                             scrow[0:1, sidx * TW:(sidx + 1) * TW],
                             start=True, stop=True).then_inc(bcm, 1)
                for j in range(NT):
                    ti = o * NT + j
                    t.wait_ge(dqd, b_dqd + 8 * (o + 1))
                    if ti >= 2:
                        t.wait_ge(acd, b_acd + ti - 1)
                    for c in range(DC):
                        mi = t.matmul(ps_mm[ti % 2][:, :],
                                      wbf[o % 2][:, c * 128:(c + 1) * 128],
                                      h_sb[:, c * T + j * TW:c * T + (j + 1) * TW],
                                      start=(c == 0), stop=(c == DC - 1))
                    mi.then_inc(mmd, 1)
                    if mode == "vtrans":
                        t.wait_ge(acd, b_acd + ti + 1)
                        t.wait_ge(rpd, b_rpd + max(ti - 1, 0))
                        for k4 in range(4):
                            mi2 = t.transpose(psbt[ti % 2][:, k4 * 128:(k4 + 1) * 128],
                                              et[ti % 2][:, k4 * 128:(k4 + 1) * 128],
                                              cst["ident_bf"][:, :])
                        mi2.then_inc(trd, 1)

        @block.vector
        def _(v):
            for o in range(NOUT):
                for sidx in range(8):
                    si = o * 8 + sidx
                    v.wait_ge(bcm, b_bcm + si + 1)
                    v.wait_ge(acp, b_acp + o + 1)
                    sslice = slice(sidx * TW, (sidx + 1) * TW)
                    v.tensor_tensor(wbf[o % 2][:, sslice], wbf[o % 2][:, sslice],
                                    ps_aux[si % 2][:, :], ALU.mult).then_inc(dqd, 1)
                for j in range(NT):
                    ti = o * NT + j
                    if mode == "rope":
                        v.wait_ge(acd, b_acd + ti + 1)
                        if ti >= 2:
                            v.wait_ge(spd[ti % 2], spt(ti - 2))
                        v.stream_shuffle(esw[:, :], et[ti % 2][:, :],
                                         [(k + 1 if k % 2 == 0 else k - 1)
                                          for k in range(32)])
                        v.tensor_tensor(e1[:, :], et[ti % 2][:, :],
                                        COS[:, j * TW:(j + 1) * TW], ALU.mult)
                        v.tensor_tensor(e2[:, :], esw[:, :],
                                        SSIN[:, j * TW:(j + 1) * TW], ALU.mult)
                        v.tensor_tensor(er[ti % 2][:, :], e1[:, :], e2[:, :],
                                        ALU.add).then_inc(rpd, 1)
                    else:
                        v.wait_ge(trd, b_trd + ti + 1)
                        if ti >= 2:
                            v.wait_ge(spd[ti % 2], spt(ti - 2))
                        v.tensor_copy(er[ti % 2][:, :],
                                      psbt[ti % 2][:, :]).then_inc(rpd, 1)

    for k in range(2):
        p.semv[tag + f"_cl{k}"] = b_cl[k] + 16 * ((NOUT + 1 - k) // 2)
        p.semv[tag + f"_spd{k}"] = b_spd[k] + 16 * ((NTILE + 1 - k) // 2)
    p.semv[tag + "_sl"] = b_sl + 16 * NOUT
    p.semv[tag + "_acp"] = b_acp + NOUT
    p.semv[tag + "_bcm"] = b_bcm + 8 * NOUT
    p.semv[tag + "_dqd"] = b_dqd + 8 * NOUT
    p.semv[tag + "_mmd"] = b_mmd + NTILE
    p.semv[tag + "_acd"] = b_acd + NTILE
    p.semv[tag + "_rpd"] = b_rpd + NTILE
    if mode == "vtrans":
        p.semv[tag + "_trd"] = b_trd + NTILE


SCALE = 128 ** -0.5


def attn_schedule():
    tiles = []
    ch_c = ev_c = gr_c = 0
    for gl in range(4):
        for h4 in range(4):
            for qt in range(32):
                L = qt * 128 + 128
                chunks = []
                k = 0
                while k < L:
                    size = min(512, L - k)
                    chunks.append((k, size))
                    k += size
                nsub = L // 128
                ngr = (nsub + 7) // 8
                tiles.append(dict(gl=gl, hloc=gl * 4 + h4, qt=qt, L=L,
                                  chunks=chunks, nsub=nsub, ngr=ngr,
                                  ch_base=ch_c, ev_base=ev_c, gr_base=gr_c))
                ch_c += len(chunks)
                ev_c += len(chunks) + 1
                gr_c += ngr
    return tiles, dict(chunks=ch_c, evacs=ev_c, groups=gr_c, tiles=len(tiles))


def emit_attention(p: Prog, tag, q_ap, kf_ap, vf_ap, att_sb, cst, bufs,
                   ps_s, ps_av, psbt):
    nc = p.nc
    sched, tot = attn_schedule()
    NTL = tot["tiles"]
    kgl = p.sem(tag + "_kgl"); b_kgl = p.semv[tag + "_kgl"]
    vgl = p.sem(tag + "_vgl"); b_vgl = p.semv[tag + "_vgl"]
    qld = p.sem(tag + "_qld"); b_qld = p.semv[tag + "_qld"]
    smd = p.sem(tag + "_smd"); b_smd = p.semv[tag + "_smd"]
    sed = p.sem(tag + "_sed"); b_sed = p.semv[tag + "_sed"]
    nmx = p.sem(tag + "_nmx"); b_nmx = p.semv[tag + "_nmx"]
    exd = p.sem(tag + "_exd"); b_exd = p.semv[tag + "_exd"]
    pnd = p.sem(tag + "_pnd"); b_pnd = p.semv[tag + "_pnd"]
    trd = p.sem(tag + "_trd"); b_trd = p.semv[tag + "_trd"]
    ptd = p.sem(tag + "_ptd"); b_ptd = p.semv[tag + "_ptd"]
    avg = p.sem(tag + "_avg"); b_avg = p.semv[tag + "_avg"]
    aed = p.sem(tag + "_aed"); b_aed = p.semv[tag + "_aed"]
    kg, vg, qh = bufs["kg"], bufs["vg"], bufs["qh"]
    Srow, Prow, PT = bufs["Srow"], bufs["Prow"], bufs["PT"]
    TRI, cmax = bufs["TRI"], bufs["cmax"]
    negmax, rowsum, dinv = bufs["negmax"], bufs["rowsum"], bufs["dinv"]

    ch_evdone = []
    for t in sched:
        nch = len(t["chunks"])
        for ci in range(nch):
            if ci == nch - 1:
                ch_evdone.append(t["ev_base"] + nch + 1)
            else:
                ch_evdone.append(t["ev_base"] + ci + 1)

    def gr_done(ti):
        return sched[ti]["gr_base"] + sched[ti]["ngr"]

    with nc.Block() as block:
        @block.sync
        def _(sync):
            for ti, t in enumerate(sched):
                gl, hloc, qt = t["gl"], t["hloc"], t["qt"]
                if qt == 0:
                    if hloc >= 1:
                        sync.wait_ge(smd, b_smd + sched[32 * hloc]["ch_base"])
                    sync.dma_start(out=qh[:, :],
                                   in_=q_ap[hloc * 128:(hloc + 1) * 128, :]
                                   ).then_inc(qld, 16)
                if qt == 0 and hloc % 4 == 0:
                    sync.dma_start(out=kg[:, :],
                                   in_=kf_ap[gl * 128:(gl + 1) * 128, :]
                                   ).then_inc(kgl, 16)
                    if gl >= 1:
                        sync.wait_ge(avg, b_avg + gr_done(128 * gl - 1))
                    sync.dma_start(out=vg[:, :],
                                   in_=vf_ap[:, gl * 128:(gl + 1) * 128].rearrange(
                                       "(i q) f -> q i f", q=128)
                                   ).then_inc(vgl, 16)

        @block.tensor
        def _(te):
            def avgrp(tj, G):
                t = sched[tj]
                gg = t["gr_base"] + G
                te.wait_ge(ptd, b_ptd + gg + 1)
                subs = range(G * 8, min(G * 8 + 8, t["nsub"]))
                for si, sub in enumerate(subs):
                    mi = te.matmul(ps_av[tj % 2][:, 0:128],
                                   vg[:, sub * 128:(sub + 1) * 128],
                                   PT[gg % 2][:, si * 128:(si + 1) * 128],
                                   start=(sub == 0), stop=(sub == t["nsub"] - 1))
                mi.then_inc(avg, 1)

            def trav(tj):
                t = sched[tj]
                te.wait_ge(pnd, b_pnd + tj + 1)
                if tj >= 2:
                    te.wait_ge(aed, b_aed + tj - 1)
                if t["qt"] == 0 and t["hloc"] % 4 == 0:
                    te.wait_ge(vgl, b_vgl + 16 * (t["gl"] + 1))
                for G in range(t["ngr"]):
                    gg = t["gr_base"] + G
                    if gg >= 2:
                        te.wait_ge(ptd, b_ptd + gg - 1)
                    subs = range(G * 8, min(G * 8 + 8, t["nsub"]))
                    for si, sub in enumerate(subs):
                        mi = te.transpose(psbt[gg % 2][:, si * 128:(si + 1) * 128],
                                          Prow[tj % 2][:, sub * 128:(sub + 1) * 128],
                                          cst["ident_bf"][:, :])
                    mi.then_inc(trd, 1)
                    if G >= 1:
                        avgrp(tj, G - 1)
                avgrp(tj, t["ngr"] - 1)

            for ti, t in enumerate(sched):
                gl, hloc, qt = t["gl"], t["hloc"], t["qt"]
                if qt == 0 and hloc % 4 == 0:
                    te.wait_ge(kgl, b_kgl + 16 * (gl + 1))
                if qt == 0:
                    te.wait_ge(qld, b_qld + 16 * (hloc + 1))
                for ci, (k0, sz) in enumerate(t["chunks"]):
                    gc = t["ch_base"] + ci
                    if gc >= 2:
                        te.wait_ge(sed, b_sed + ch_evdone[gc - 2])
                    te.matmul(ps_s[gc % 2][:, 0:sz],
                              qh[:, qt * 128:(qt + 1) * 128],
                              kg[:, k0:k0 + sz],
                              start=True, stop=True).then_inc(smd, 1)
                if ti >= 1:
                    trav(ti - 1)
            trav(NTL - 1)

        @block.vector
        def _(v):
            def norm(tj):
                v.wait_ge(exd, b_exd + tj + 1)
                v.tensor_scalar(Prow[tj % 2][:, 0:sched[tj]["L"]],
                                Prow[tj % 2][:, 0:sched[tj]["L"]],
                                dinv[tj % 2][:, :], None,
                                ALU.mult).then_inc(pnd, 1)

            for ti, t in enumerate(sched):
                nch = len(t["chunks"])
                if ti >= 1:
                    norm(ti - 1)
                for ci, (k0, sz) in enumerate(t["chunks"]):
                    gc = t["ch_base"] + ci
                    v.wait_ge(smd, b_smd + gc + 1)
                    if ti >= 2 and ci == 0:
                        v.wait_ge(exd, b_exd + ti - 1)
                    v.tensor_scalar(Srow[ti % 2][:, k0:k0 + sz],
                                    ps_s[gc % 2][:, 0:sz], SCALE, None,
                                    ALU.mult, ALU.max,
                                    accum_out=cmax[:, ci:ci + 1]).then_inc(sed, 1)
                    if ci == nch - 1:
                        v.scalar_tensor_tensor(
                            Srow[ti % 2][:, t["L"] - 128:t["L"]],
                            ps_s[gc % 2][:, sz - 128:sz], SCALE,
                            TRI[:, :], ALU.mult, ALU.add).then_inc(sed, 1)
                if ti >= 2:
                    v.wait_ge(exd, b_exd + ti - 1)
                v.reduce_max(negmax[ti % 2][:, :], cmax[:, 0:nch], axis=AX.X,
                             negate=True).then_inc(nmx, 1)
            norm(NTL - 1)

        @block.scalar
        def _(s):
            def pt_copies(tj):
                t = sched[tj]
                for G in range(t["ngr"]):
                    gg = t["gr_base"] + G
                    s.wait_ge(trd, b_trd + gg + 1)
                    if gg >= 2:
                        s.wait_ge(avg, b_avg + gg - 1)
                    nsubs = min(8, t["nsub"] - G * 8)
                    s.activation(PT[gg % 2][:, 0:nsubs * 128],
                                 psbt[gg % 2][:, 0:nsubs * 128],
                                 AF.Identity).then_inc(ptd, 1)

            def av_evac(tj):
                tp = sched[tj]
                s.wait_ge(avg, b_avg + gr_done(tj))
                s.activation(att_sb[:, tp["hloc"] * 4096 + tp["qt"] * 128:
                                    tp["hloc"] * 4096 + (tp["qt"] + 1) * 128],
                             ps_av[tj % 2][:, 0:128],
                             AF.Identity).then_inc(aed, 1)

            for ti, t in enumerate(sched):
                if ti >= 1:
                    pt_copies(ti - 1)
                    av_evac(ti - 1)
                s.wait_ge(nmx, b_nmx + ti + 1)
                s.wait_ge(sed, b_sed + t["ev_base"] + len(t["chunks"]) + 1)
                if ti >= 2:
                    s.wait_ge(trd, b_trd + gr_done(ti - 2))
                    s.wait_ge(pnd, b_pnd + ti - 1)
                s.activation(Prow[ti % 2][:, 0:t["L"]], Srow[ti % 2][:, 0:t["L"]],
                             AF.Exp, bias=negmax[ti % 2][:, :], scale=1.0,
                             accum_out=rowsum[ti % 2][:, :])
                s.activation(negmax[ti % 2][:, :], rowsum[ti % 2][:, :], AF.Ln)
                s.activation(dinv[ti % 2][:, :], negmax[ti % 2][:, :],
                             AF.Exp, scale=-1.0).then_inc(exd, 1)
            pt_copies(NTL - 1)
            av_evac(NTL - 1)

    p.semv[tag + "_kgl"] = b_kgl + 16 * 4
    p.semv[tag + "_vgl"] = b_vgl + 16 * 4
    p.semv[tag + "_qld"] = b_qld + 16 * 16
    p.semv[tag + "_smd"] = b_smd + tot["chunks"]
    p.semv[tag + "_sed"] = b_sed + tot["evacs"]
    p.semv[tag + "_nmx"] = b_nmx + NTL
    p.semv[tag + "_exd"] = b_exd + NTL
    p.semv[tag + "_pnd"] = b_pnd + NTL
    p.semv[tag + "_trd"] = b_trd + tot["groups"]
    p.semv[tag + "_ptd"] = b_ptd + tot["groups"]
    p.semv[tag + "_avg"] = b_avg + tot["groups"]
    p.semv[tag + "_aed"] = b_aed + NTL


def emit_oproj(p: Prog, tag, codes_ap, sc_ap, h_sb, a_out_ap, cst,
               ps_mm, ps_aux, ps_s1, ps_s2, bufs):
    nc = p.nc
    cl = p.sem2(tag + "_cl"); b_cl = [p.semv[tag + "_cl0"], p.semv[tag + "_cl1"]]
    sl = p.sem(tag + "_sl"); b_sl = p.semv[tag + "_sl"]
    spd = p.sem2(tag + "_spd"); b_spd = [p.semv[tag + "_spd0"], p.semv[tag + "_spd1"]]
    acp = p.sem(tag + "_acp"); b_acp = p.semv[tag + "_acp"]
    bcm = p.sem(tag + "_bcm"); b_bcm = p.semv[tag + "_bcm"]
    dqd = p.sem(tag + "_dqd"); b_dqd = p.semv[tag + "_dqd"]
    mmd = p.sem(tag + "_mmd"); b_mmd = p.semv[tag + "_mmd"]
    acd = p.sem(tag + "_acd"); b_acd = p.semv[tag + "_acd"]
    sqd = p.sem(tag + "_sqd"); b_sqd = p.semv[tag + "_sqd"]
    std = p.sem(tag + "_std"); b_std = p.semv[tag + "_std"]
    wbf, scrow = bufs["wbf"], bufs["scrow"]
    atmp, sqb = bufs["atmp"], bufs["sqb"]
    clt = lambda o: b_cl[o % 2] + 16 * (o // 2 + 1)
    spt = lambda ti: b_spd[ti % 2] + 16 * (ti // 2 + 1)
    NTILE = DC * NT

    with nc.Block() as block:
        @block.gpsimd
        def _(g):
            for o in range(DC):
                if o >= 2:
                    g.wait_ge(mmd, b_mmd + (o - 1) * NT)
                g.dma_start(out=wbf[o % 2][:, :],
                            in_=codes_ap[o * 128:(o + 1) * 128, :]).then_inc(cl[o % 2], 16)

        @block.sync
        def _(sync):
            def spill(ti):
                o, j = ti // NT, ti % NT
                sync.wait_ge(acd, b_acd + ti + 1)
                sync.dma_start(out=a_out_ap[o * 128:(o + 1) * 128,
                                            j * TW:(j + 1) * TW],
                               in_=atmp[ti % 2][:, :]).then_inc(spd[ti % 2], 16)

            for o in range(DC):
                if o >= 2:
                    for j in range(NT):
                        spill((o - 2) * NT + j)
                if o >= 1:
                    sync.wait_ge(bcm, b_bcm + 8 * o)
                sync.dma_start(out=scrow[:, :],
                               in_=sc_ap[o:o + 1, :]).then_inc(sl, 16)
            for ti in range((DC - 2) * NT, DC * NT):
                spill(ti)
            sync.wait_ge(spd[0], b_spd[0] + 16 * (NTILE // 2))
            sync.wait_ge(spd[1], b_spd[1] + 16 * (NTILE // 2))

        @block.scalar
        def _(s):
            def evac(ti):
                s.wait_ge(mmd, b_mmd + ti + 1)
                if ti >= 2:
                    s.wait_ge(spd[ti % 2], spt(ti - 2))
                    s.wait_ge(std, b_std + ti - 1)
                s.activation(atmp[ti % 2][:, :], ps_mm[ti % 2][:, :],
                             AF.Identity).then_inc(acd, 1)
                s.activation(sqb[ti % 2][:, :], atmp[ti % 2][:, :],
                             AF.Square).then_inc(sqd, 1)

            for o in range(DC):
                if o >= 2:
                    for j in range(NT):
                        evac((o - 2) * NT + j)
                s.wait_ge(cl[o % 2], clt(o))
                s.activation(wbf[o % 2][:, :], wbf[o % 2][:, :],
                             AF.Identity, bias=cst["neg8"][:, :],
                             scale=1.0).then_inc(acp, 1)
            for ti in range((DC - 2) * NT, DC * NT):
                evac(ti)

        @block.tensor
        def _(t):
            for o in range(DC):
                t.wait_ge(sl, b_sl + 16 * (o + 1))
                for sidx in range(8):
                    if o * 8 + sidx >= 2:
                        t.wait_ge(dqd, b_dqd + o * 8 + sidx - 1)
                    t.matmul(ps_aux[(o * 8 + sidx) % 2][:, :], cst["ones1"][:, :],
                             scrow[0:1, sidx * TW:(sidx + 1) * TW],
                             start=True, stop=True).then_inc(bcm, 1)
                for j in range(NT):
                    ti = o * NT + j
                    t.wait_ge(dqd, b_dqd + 8 * (o + 1))
                    if ti >= 2:
                        t.wait_ge(acd, b_acd + ti - 1)
                    for c in range(DC):
                        mi = t.matmul(ps_mm[ti % 2][:, :],
                                      wbf[o % 2][:, c * 128:(c + 1) * 128],
                                      h_sb[:, c * T + j * TW:c * T + (j + 1) * TW],
                                      start=(c == 0), stop=(c == DC - 1))
                    mi.then_inc(mmd, 1)
                    if ti >= 1:
                        op, jp = (ti - 1) // NT, (ti - 1) % NT
                        t.wait_ge(sqd, b_sqd + ti)
                        t.matmul(ps_s1[32 * jp:32 * jp + 32, :], cst["ones32"][:, :],
                                 atmp[(ti - 1) % 2][:, :],
                                 start=(op == 0), stop=(op == DC - 1),
                                 tile_position=(0, 32 * jp), skip_group_check=True)
                        t.matmul(ps_s2[32 * jp:32 * jp + 32, :], cst["ones32"][:, :],
                                 sqb[(ti - 1) % 2][:, :],
                                 start=(op == 0), stop=(op == DC - 1),
                                 tile_position=(0, 32 * jp),
                                 skip_group_check=True).then_inc(std, 1)
            ti = NTILE
            op, jp = (ti - 1) // NT, (ti - 1) % NT
            t.wait_ge(sqd, b_sqd + ti)
            t.matmul(ps_s1[32 * jp:32 * jp + 32, :], cst["ones32"][:, :],
                     atmp[(ti - 1) % 2][:, :], start=(op == 0), stop=True,
                     tile_position=(0, 32 * jp), skip_group_check=True)
            t.matmul(ps_s2[32 * jp:32 * jp + 32, :], cst["ones32"][:, :],
                     sqb[(ti - 1) % 2][:, :], start=(op == 0), stop=True,
                     tile_position=(0, 32 * jp),
                     skip_group_check=True).then_inc(std, 1)

        @block.vector
        def _(v):
            for o in range(DC):
                for sidx in range(8):
                    si = o * 8 + sidx
                    v.wait_ge(bcm, b_bcm + si + 1)
                    v.wait_ge(acp, b_acp + o + 1)
                    sslice = slice(sidx * TW, (sidx + 1) * TW)
                    v.tensor_tensor(wbf[o % 2][:, sslice], wbf[o % 2][:, sslice],
                                    ps_aux[si % 2][:, :], ALU.mult).then_inc(dqd, 1)

    for k in range(2):
        p.semv[tag + f"_cl{k}"] = b_cl[k] + 16 * (DC // 2)
        p.semv[tag + f"_spd{k}"] = b_spd[k] + 16 * (NTILE // 2)
    p.semv[tag + "_sl"] = b_sl + 16 * DC
    p.semv[tag + "_acp"] = b_acp + DC
    p.semv[tag + "_bcm"] = b_bcm + 8 * DC
    p.semv[tag + "_dqd"] = b_dqd + 8 * DC
    p.semv[tag + "_mmd"] = b_mmd + NTILE
    p.semv[tag + "_acd"] = b_acd + NTILE
    p.semv[tag + "_sqd"] = b_sqd + NTILE
    p.semv[tag + "_std"] = b_std + NTILE


def build_attn():
    p = Prog()
    q_att = p.din("q_att", [2048, 2 * T], BF16)
    k_att = p.din("k_att", [512, 2 * T], BF16)
    v_att = p.din("v_att", [2 * T, 512], BF16)
    tri = p.din("tri", [128, 128], F32)
    att_out = p.dout("att_out", [2048, 2 * T], BF16)

    com = alloc_common(p)
    cst = com["cst"]
    nc = p.nc
    with ExitStack() as stk:
        bufs = {
            "kg": p.sb("kg", [128, 2 * T], BF16, stk),
            "vg": p.sb("vg", [128, 2 * T], BF16, stk),
            "qh": p.sb("qh", [128, 2 * T], BF16, stk),
            "Srow": [p.sb(f"Srow{i}", [128, 2 * T], F32, stk) for i in range(2)],
            "Prow": [p.sb(f"Prow{i}", [128, 2 * T], BF16, stk) for i in range(2)],
            "PT": [p.sb(f"PTb{i}", [128, 1024], BF16, stk) for i in range(2)],
            "TRI": p.sb("TRI", [128, 128], F32, stk),
            "cmax": p.sb("cmax", [128, 8], F32, stk),
            "negmax": [p.sb(f"negmax{i}", [128, 1], F32, stk) for i in range(2)],
            "rowsum": [p.sb(f"rowsum{i}", [128, 1], F32, stk) for i in range(2)],
            "dinv": [p.sb(f"dinv{i}", [128, 1], F32, stk) for i in range(2)],
        }
        psbt = [p.psum(f"psbt{i}", [128, 1024], BF16, stk) for i in range(2)]
        ldt = p.sem("ldt")
        with nc.Block() as block:
            @block.sync
            def _(sync):
                sync.dma_start(out=bufs["TRI"][:, :], in_=tri).then_inc(ldt, 16)
                sync.wait_ge(ldt, p.semv["ldt"] + 16)
        p.semv["ldt"] += 16
        emit_attention(p, "at", q_att, k_att, v_att, com["h_sb"], cst, bufs,
                       com["ps_mm"], com["ps_aux"], psbt)
        std = p.sem("stda")
        with nc.Block() as block:
            @block.sync
            def _(sync):
                for hl in range(16):
                    sync.dma_start(out=att_out[hl * 128:(hl + 1) * 128, :],
                                   in_=com["h_sb"][:, hl * 4096:(hl + 1) * 4096]
                                   ).then_inc(std, 16)
                sync.wait_ge(std, p.semv["stda"] + 16 * 16)
        p.semv["stda"] += 16 * 16
    p.ctx.close()
    return p


def _tri_mask():
    r = np.arange(128)
    return np.where(r[:, None] >= r[None, :], 0.0, NEG).astype(np.float32)


def build_oproj():
    p = Prog()
    nc = p.nc
    att = p.din("att", [D, T], BF16)
    oc = p.din("oc", [D, D], I32)
    osd = p.din("osd", [DC, D], F32)
    x2 = p.din("x2", [D, T], F32)
    x3 = p.dout("x3", [D, T], F32)
    a_dram = p.dint("a_scr", [D, T], F32)

    com = alloc_common(p)
    cst = com["cst"]
    lda = p.sem("lda")
    with nc.Block() as block:
        @block.sync
        def _(sync):
            for c in range(DC):
                sync.dma_start(out=com["h_sb"][:, c * T:(c + 1) * T],
                               in_=att[c * 128:(c + 1) * 128, :]).then_inc(lda, 16)
            sync.wait_ge(lda, p.semv["lda"] + 16 * DC)
    p.semv["lda"] += 16 * DC

    with ExitStack() as stk:
        bufs = {
            "wbf": [p.sb(f"owbf{i}", [128, D], BF16, stk) for i in range(2)],
            "scrow": p.sb("oscrow", [1, D], F32, stk),
            "atmp": [p.sb(f"oatmp{i}", [128, TW], F32, stk) for i in range(2)],
            "sqb": [p.sb(f"osqb{i}", [128, TW], F32, stk) for i in range(2)],
        }
        emit_oproj(p, "mm", oc, osd, com["h_sb"], a_dram, cst,
                   com["ps_mm"], com["ps_aux"], com["ps_s1"], com["ps_s2"], bufs)
    with ExitStack() as stkln:
        alloc_ln(p, com, stkln)
        with ExitStack() as stk:
            phase_rows2_update(p, "u2", com, x2, a_dram, x3, stk, do_stats=False)
    p.ctx.close()
    return p


def emit_router(p: Prog, tag, rw_sb, h_sb, ps_aux, cst, lg_fm, lg_tm, scr_tm,
                r_tm, w_fmT):
    nc = p.nc
    gd = p.sem(tag + "_gd"); b_gd = p.semv[tag + "_gd"]
    ev = p.sem(tag + "_ev"); b_ev = p.semv[tag + "_ev"]

    with nc.Block() as block:
        @block.tensor
        def _(t):
            for j in range(NT):
                if j >= 1:
                    t.wait_ge(ev, b_ev + j)
                for c in range(DC):
                    mi = t.matmul(ps_aux[0][0:8, :], rw_sb[:, c * 8:(c + 1) * 8],
                                  h_sb[:, c * T + j * TW:c * T + (j + 1) * TW],
                                  start=(c == 0), stop=(c == DC - 1))
                mi.then_inc(gd, 1)

        @block.vector
        def _(v):
            for j in range(NT):
                v.wait_ge(gd, b_gd + j + 1)
                v.tensor_copy(lg_fm[0:8, j * TW:(j + 1) * TW],
                              ps_aux[0][0:8, :]).then_inc(ev, 1)
    p.semv[tag + "_gd"] = b_gd + NT
    p.semv[tag + "_ev"] = b_ev + NT

    b_gd = p.semv[tag + "_gd"]
    with nc.Block() as block:
        @block.tensor
        def _(t):
            for k in range(16):
                mi = t.transpose(ps_aux[1][:, k * 8:(k + 1) * 8],
                                 lg_fm[0:8, k * 128:(k + 1) * 128],
                                 cst["ident_f"][0:8, 0:8])
            mi.then_inc(gd, 1)

        @block.vector
        def _(v):
            v.wait_ge(gd, b_gd + 1)
            v.tensor_copy(lg_tm[:, :], ps_aux[1][:, 0:128])
    p.semv[tag + "_gd"] = b_gd + 1

    sm = p.sem(tag + "_sm"); b_sm = p.semv[tag + "_sm"]
    with nc.Block() as block:
        @block.vector
        def _(v):
            g3 = lambda x: x[:, :].rearrange("p (g e) -> p g e", e=8)
            mx1 = scr_tm[:, 0:16]
            mx2 = scr_tm[:, 16:32]
            mh = scr_tm[:, 0:1]
            m1view = bass.AP(mh.tensor, mh.offset, [mh.ap[0], [1, 16], [0, 8]])
            m2h = scr_tm[:, 16:17]
            m2view = bass.AP(m2h.tensor, m2h.offset, [m2h.ap[0], [1, 16], [0, 8]])
            v.reduce_max(mx1, g3(lg_tm), axis=AX.X)
            v.tensor_tensor(g3(r_tm), g3(lg_tm), m1view, ALU.is_ge)
            v.scalar_tensor_tensor(g3(r_tm), r_tm[:, :].rearrange(
                "p (g e) -> p g e", e=8), NEG, g3(lg_tm), ALU.mult, ALU.add)
            v.reduce_max(mx2, g3(r_tm), axis=AX.X)
            v.tensor_tensor(g3(lg_tm), g3(lg_tm), m1view,
                            ALU.subtract).then_inc(sm, 1)

        @block.scalar
        def _(s):
            s.wait_ge(sm, b_sm + 1)
            s.activation(lg_tm[:, :], lg_tm[:, :], AF.Exp).then_inc(sm, 1)

        @block.vector
        def _(v):
            v.wait_ge(sm, b_sm + 2)
            mx1 = scr_tm[:, 0:16]
            mx2 = scr_tm[:, 16:32]
            v.tensor_tensor(mx2, mx2, mx1, ALU.subtract).then_inc(sm, 1)

        @block.scalar
        def _(s):
            s.wait_ge(sm, b_sm + 3)
            s.activation(scr_tm[:, 16:32], scr_tm[:, 16:32],
                         AF.Exp).then_inc(sm, 1)

        @block.vector
        def _(v):
            v.wait_ge(sm, b_sm + 4)
            g3 = lambda x: x[:, :].rearrange("p (g e) -> p g e", e=8)
            e2h = scr_tm[:, 16:17]
            e2view = bass.AP(e2h.tensor, e2h.offset, [e2h.ap[0], [1, 16], [0, 8]])
            v.tensor_tensor(g3(r_tm), g3(lg_tm), e2view, ALU.is_ge)
            v.tensor_tensor(lg_tm[:, :], lg_tm[:, :], r_tm[:, :], ALU.mult)
            v.tensor_scalar(scr_tm[:, 32:48], scr_tm[:, 16:32], 1.0, None, ALU.add)
            v.drain()
            v.reciprocal(scr_tm[:, 48:64], scr_tm[:, 32:48])
            v.drain()
            dh = scr_tm[:, 48:49]
            dview = bass.AP(dh.tensor, dh.offset, [dh.ap[0], [1, 16], [0, 8]])
            v.tensor_tensor(g3(r_tm), g3(lg_tm), dview, ALU.mult).then_inc(sm, 1)
    p.semv[tag + "_sm"] = b_sm + 5

    b_gd = p.semv[tag + "_gd"]; b_ev = p.semv[tag + "_ev"]
    with nc.Block() as block:
        @block.tensor
        def _(t):
            for g4 in range(4):
                if g4 >= 1:
                    t.wait_ge(ev, b_ev + g4)
                for k4 in range(4):
                    k = g4 * 4 + k4
                    mi = t.transpose(ps_aux[0][0:8, k4 * 128:(k4 + 1) * 128],
                                     r_tm[:, k * 8:(k + 1) * 8],
                                     cst["ident_f"][:, :])
                mi.then_inc(gd, 1)

        @block.vector
        def _(v):
            for g4 in range(4):
                v.wait_ge(gd, b_gd + g4 + 1)
                v.tensor_copy(w_fmT[0:8, g4 * TW:(g4 + 1) * TW],
                              ps_aux[0][0:8, 0:TW]).then_inc(ev, 1)
    p.semv[tag + "_gd"] = b_gd + 4
    p.semv[tag + "_ev"] = b_ev + 4


def emit_mid(p: Prog, tag, lA_sb, P2_sb, w_fmT, h_sb, mid_sb, wx_sb, ps_mm,
             ps_aux, cst):
    nc = p.nc
    wxd = p.sem(tag + "_wxd"); b_wxd = p.semv[tag + "_wxd"]
    wxc = p.sem(tag + "_wxc"); b_wxc = p.semv[tag + "_wxc"]
    mmd = p.sem(tag + "_mmd"); b_mmd = p.semv[tag + "_mmd"]
    mtd = p.sem(tag + "_mtd"); b_mtd = p.semv[tag + "_mtd"]

    with nc.Block() as block:
        @block.tensor
        def _(t):
            for j in range(NT):
                if j >= 2:
                    t.wait_ge(wxc, b_wxc + j - 1)
                t.matmul(ps_aux[j % 2][:, :], P2_sb[0:8, :],
                         w_fmT[0:8, j * TW:(j + 1) * TW],
                         start=True, stop=True).then_inc(wxd, 1)
            for j in range(NT):
                if j >= 2:
                    t.wait_ge(mtd, b_mtd + j - 1)
                for c in range(DC):
                    mi = t.matmul(ps_mm[j % 2][:, :],
                                  lA_sb[:, c * 128:(c + 1) * 128],
                                  h_sb[:, c * T + j * TW:c * T + (j + 1) * TW],
                                  start=(c == 0), stop=(c == DC - 1))
                mi.then_inc(mmd, 1)

        @block.vector
        def _(v):
            for j in range(NT):
                v.wait_ge(wxd, b_wxd + j + 1)
                v.tensor_copy(wx_sb[:, j * TW:(j + 1) * TW],
                              ps_aux[j % 2][:, :]).then_inc(wxc, 1)
            for j in range(NT):
                v.wait_ge(mmd, b_mmd + j + 1)
                v.tensor_tensor(mid_sb[:, j * TW:(j + 1) * TW],
                                ps_mm[j % 2][:, :],
                                wx_sb[:, j * TW:(j + 1) * TW],
                                ALU.mult).then_inc(mtd, 1)
    p.semv[tag + "_wxd"] = b_wxd + NT
    p.semv[tag + "_wxc"] = b_wxc + NT
    p.semv[tag + "_mmd"] = b_mmd + NT
    p.semv[tag + "_mtd"] = b_mtd + NT


def emit_mm_l1f(p: Prog, tag, w_ap, lB_sb, mid_sb, h_sb, a_out_ap, cst,
                ps_mm, ps_s1, ps_s2, wraw, atmp, sqb):
    nc = p.nc
    wl = p.sem2(tag + "_wl"); b_wl = [p.semv[tag + "_wl0"], p.semv[tag + "_wl1"]]
    spd = p.sem2(tag + "_spd"); b_spd = [p.semv[tag + "_spd0"], p.semv[tag + "_spd1"]]
    mmd = p.sem(tag + "_mmd"); b_mmd = p.semv[tag + "_mmd"]
    evd = p.sem(tag + "_evd"); b_evd = p.semv[tag + "_evd"]
    sqd = p.sem(tag + "_sqd"); b_sqd = p.semv[tag + "_sqd"]
    std = p.sem(tag + "_std"); b_std = p.semv[tag + "_std"]
    wlt = lambda o: b_wl[o % 2] + 16 * (o // 2 + 1)
    spt = lambda ti: b_spd[ti % 2] + 16 * (ti // 2 + 1)
    NTILE = DC * NT

    with nc.Block() as block:
        @block.gpsimd
        def _(g):
            for o in range(DC):
                if o >= 2:
                    g.wait_ge(mmd, b_mmd + (o - 1) * NT)
                g.dma_start(out=wraw[o % 2][:, :],
                            in_=w_ap[o * 128:(o + 1) * 128, :]).then_inc(wl[o % 2], 16)

        @block.tensor
        def _(t):
            for o in range(DC):
                t.wait_ge(wl[o % 2], wlt(o))
                for j in range(NT):
                    ti = o * NT + j
                    if ti >= 2:
                        t.wait_ge(evd, b_evd + ti - 1)
                    for c in range(DC):
                        t.matmul(ps_mm[ti % 2][:, :],
                                 wraw[o % 2][:, c * 128:(c + 1) * 128],
                                 h_sb[:, c * T + j * TW:c * T + (j + 1) * TW],
                                 start=(c == 0), stop=False)
                    mi = t.matmul(ps_mm[ti % 2][:, :],
                                  lB_sb[:, o * 128:(o + 1) * 128],
                                  mid_sb[:, j * TW:(j + 1) * TW],
                                  start=False, stop=True)
                    mi.then_inc(mmd, 1)
                    if ti >= 1:
                        op, jp = (ti - 1) // NT, (ti - 1) % NT
                        t.wait_ge(sqd, b_sqd + ti)
                        t.matmul(ps_s1[32 * jp:32 * jp + 32, :], cst["ones32"][:, :],
                                 atmp[(ti - 1) % 2][:, :],
                                 start=(op == 0), stop=(op == DC - 1),
                                 tile_position=(0, 32 * jp), skip_group_check=True)
                        t.matmul(ps_s2[32 * jp:32 * jp + 32, :], cst["ones32"][:, :],
                                 sqb[(ti - 1) % 2][:, :],
                                 start=(op == 0), stop=(op == DC - 1),
                                 tile_position=(0, 32 * jp),
                                 skip_group_check=True).then_inc(std, 1)
            ti = NTILE
            op, jp = (ti - 1) // NT, (ti - 1) % NT
            t.wait_ge(sqd, b_sqd + ti)
            t.matmul(ps_s1[32 * jp:32 * jp + 32, :], cst["ones32"][:, :],
                     atmp[(ti - 1) % 2][:, :], start=(op == 0), stop=True,
                     tile_position=(0, 32 * jp), skip_group_check=True)
            t.matmul(ps_s2[32 * jp:32 * jp + 32, :], cst["ones32"][:, :],
                     sqb[(ti - 1) % 2][:, :], start=(op == 0), stop=True,
                     tile_position=(0, 32 * jp),
                     skip_group_check=True).then_inc(std, 1)

        @block.scalar
        def _(s):
            for o in range(DC):
                for j in range(NT):
                    ti = o * NT + j
                    s.wait_ge(mmd, b_mmd + ti + 1)
                    if ti >= 2:
                        s.wait_ge(spd[ti % 2], spt(ti - 2))
                        s.wait_ge(std, b_std + ti - 1)
                    s.activation(atmp[ti % 2][:, :], ps_mm[ti % 2][:, :],
                                 AF.Identity).then_inc(evd, 1)
                    s.activation(sqb[ti % 2][:, :], atmp[ti % 2][:, :],
                                 AF.Square).then_inc(sqd, 1)

        @block.sync
        def _(sync):
            for o in range(DC):
                for j in range(NT):
                    ti = o * NT + j
                    sync.wait_ge(evd, b_evd + ti + 1)
                    sync.dma_start(out=a_out_ap[o * 128:(o + 1) * 128,
                                                j * TW:(j + 1) * TW],
                                   in_=atmp[ti % 2][:, :]).then_inc(spd[ti % 2], 16)
            sync.wait_ge(spd[0], b_spd[0] + 16 * (NTILE // 2))
            sync.wait_ge(spd[1], b_spd[1] + 16 * (NTILE // 2))

    for k in range(2):
        p.semv[tag + f"_wl{k}"] = b_wl[k] + 16 * (DC // 2)
        p.semv[tag + f"_spd{k}"] = b_spd[k] + 16 * (NTILE // 2)
    p.semv[tag + "_mmd"] = b_mmd + NTILE
    p.semv[tag + "_evd"] = b_evd + NTILE
    p.semv[tag + "_sqd"] = b_sqd + NTILE
    p.semv[tag + "_std"] = b_std + NTILE


def emit_ffn1(p: Prog, com, aps):
    nc = p.nc
    cst = com["cst"]
    with ExitStack() as stkln:
        alloc_ln(p, com, stkln)
        with ExitStack() as stk:
            phase_stats_rows(p, "st", com, aps["x3"], stk)
        with ExitStack() as stk:
            phase_h(p, "h", com, aps["x3"], stk)

    with ExitStack() as stkf:
        mid_sb = p.sb("mid_sb", [128, T], BF16, stkf)
        lB_sb = p.sb("lB_sb", [128, D], BF16, stkf)
        with ExitStack() as stk:
            rw_sb = p.sb("rw_sb", [128, DC * 8], BF16, stk)
            lA_sb = p.sb("lA_sb", [128, DC * 128], BF16, stk)
            P2_sb = p.sb("P2_sb", [8, 128], F32, stk)
            lg_fm = p.sb("lg_fm", [8, T], F32, stk)
            lg_tm = p.sb("lg_tm", [128, 128], F32, stk)
            scr_tm = p.sb("scr_tm", [128, 64], F32, stk)
            r_tm = p.sb("r_tm", [128, 128], F32, stk)
            w_fmT = p.sb("w_fmT", [8, T], F32, stk)
            wx_sb = p.sb("wx_sb", [128, T], F32, stk)
            ldr = p.sem("ldr")
            ldr2 = p.sem("ldr2")
            with nc.Block() as block:
                @block.gpsimd
                def _(g):
                    g.dma_start(out=rw_sb[:, :], in_=aps["rw"]).then_inc(ldr, 16)
                    g.dma_start(out=lA_sb[:, :].rearrange("p (c e) -> p c e", c=DC),
                                in_=aps["lA"].rearrange("(c p) e -> p c e", p=128)
                                ).then_inc(ldr, 16)
                    g.dma_start(out=lB_sb[:, :], in_=aps["lB"]).then_inc(ldr, 16)
                    g.wait_ge(ldr, p.semv["ldr"] + 48)

                @block.sync
                def _(sync):
                    sync.dma_start(out=P2_sb[:, :], in_=aps["P2"]).then_inc(ldr2, 16)
                    sync.wait_ge(ldr2, p.semv["ldr2"] + 16)
            p.semv["ldr"] += 48
            p.semv["ldr2"] += 16
            emit_router(p, "gt", rw_sb, com["h_sb"], com["ps_aux"], cst,
                        lg_fm, lg_tm, scr_tm, r_tm, w_fmT)
            emit_mid(p, "md", lA_sb, P2_sb, w_fmT, com["h_sb"], mid_sb, wx_sb,
                     com["ps_mm"], com["ps_aux"], cst)
        with ExitStack() as stk:
            wraw = [p.sb(f"b_wraw{i}", [128, D], BF16, stk) for i in range(2)]
            atmp = [p.sb(f"b_atmp{i}", [128, TW], F32, stk) for i in range(2)]
            sqb = [p.sb(f"b_sqtb{i}", [128, TW], F32, stk) for i in range(2)]
            emit_mm_l1f(p, "mm", aps["bW"], lB_sb, mid_sb, com["h_sb"],
                        aps["a_scr"], cst, com["ps_mm"], com["ps_s1"],
                        com["ps_s2"], wraw, atmp, sqb)
    with ExitStack() as stkln:
        alloc_ln(p, com, stkln, tag="u3")
        with ExitStack() as stk:
            phase_rows2_update(p, "u", com, aps["x3"], aps["a_scr"],
                               aps["out"], stk, do_stats=False)


def build_ffn1():
    p = Prog()
    aps = {
        "x3": p.din("x3", [D, T], F32),
        "bW": p.din("bW", [D, D], F32),
        "rw": p.din("rw", [128, DC * 8], F32),
        "lA": p.din("lA", [D, 128], F32),
        "lB": p.din("lB", [128, D], F32),
        "P2": p.din("P2", [8, 128], F32),
        "out": p.dout("out", [D, T], F32),
        "a_scr": p.dint("a_scr", [D, T], F32),
    }
    com = alloc_common(p)
    emit_ffn1(p, com, aps)
    p.ctx.close()
    return p


def _prep_ffn1(inputs):
    P2 = np.zeros((8, 128), np.float32)
    for e in range(8):
        P2[e, e * 16:(e + 1) * 16] = 2.0
    return {
        "bW": _tile_w(np.ascontiguousarray(inputs["L1_ffn_base_W"].T)).reshape(D, D),
        "rw": np.ascontiguousarray(
            inputs["L1_ffn_router_W"].T.reshape(DC, 128, 8).transpose(1, 0, 2)
            .reshape(128, DC * 8)),
        "lA": np.ascontiguousarray(
            inputs["L1_ffn_lora_A"].transpose(1, 0, 2).reshape(D, 128)),
        "lB": np.ascontiguousarray(inputs["L1_ffn_lora_B"].reshape(128, D)),
        "P2": P2,
    }


def alloc_common(p: Prog):
    cst = {
        "io_free": p.sb("io_free", [128, 128], I32),
        "io_part": p.sb("io_part", [128, 128], I32),
        "io_g32": p.sb("io_g32", [128, 128], I32),
        "ones32": p.sb("ones32", [128, 32], F32),
        "ones1": p.sb("ones1", [1, 128], F32),
        "eps": p.sb("eps", [128, 1], F32),
        "neg8": p.sb("neg8", [128, 1], F32),
        "ident_bf": p.sb("ident_bf", [128, 128], BF16),
        "ident_f": p.sb("ident_f", [128, 128], F32),
    }
    com = {
        "cst": cst,
        "ps_s1": p.psum("ps_s1", [128, TW], F32),
        "ps_s2": p.psum("ps_s2", [128, TW], F32),
        "ps_aux": [p.psum(f"ps_aux{i}", [128, TW], F32) for i in range(2)],
        "ps_mm": [p.psum(f"ps_mm{i}", [128, TW], F32) for i in range(2)],
        "h_sb": p.sb("h_sb", [128, DC * T], BF16),
    }
    emit_consts(p, cst)
    return com


def alloc_ln(p: Prog, com, stack, tag=""):
    com["RB"] = p.sb(tag + "RB", [128, T], F32, stack)
    com["NB"] = p.sb(tag + "NB", [128, T], F32, stack)
    com["rowbufs"] = {k: p.sb(f"{tag}row_{k}", [128, TW], F32, stack)
                      for k in ("s1", "s2", "var", "rstd", "negm")}
    cst = com["cst"]
    cst["sel"] = [p.sb(f"{tag}sel{j}", [128, 128], F32, stack)
                  for j in range(NT)]
    with p.nc.Block() as block:
        @block.vector
        def _(v):
            for j in range(NT):
                v.tensor_scalar(cst["sel"][j][:, :], cst["io_part"][:, :],
                                32 * j, None, ALU.is_equal)


def phase_stats_rows(p: Prog, tag, com, src_ap, stack):
    xb = [p.sb(f"{tag}_xb{i}", [128, T], F32, stack) for i in range(2)]
    sq = [p.sb(f"{tag}_sq{i}", [128, T], F32, stack) for i in range(2)]
    emit_stream_stats(p, tag + "s", src_ap, com["ps_s1"], com["ps_s2"],
                      com["cst"], xb, sq)
    emit_rows(p, tag + "r", com["ps_s1"], com["ps_s2"], com["ps_aux"],
              com["cst"], com["rowbufs"], com["RB"], com["NB"])


def phase_h(p: Prog, tag, com, src_ap, stack):
    xb = [p.sb(f"{tag}_xb{i}", [128, T], F32, stack) for i in range(2)]
    ts = p.sb(f"{tag}_ts", [128, T], F32, stack)
    emit_hpass(p, tag, src_ap, com["RB"], com["NB"], com["h_sb"], xb, ts)


def phase_rows2_update(p: Prog, tag, com, x_ap, a_ap, out_ap, stack,
                       do_stats=True):
    emit_rows(p, tag + "r", com["ps_s1"], com["ps_s2"], com["ps_aux"],
              com["cst"], com["rowbufs"], com["RB"], com["NB"])
    HW = T // 2
    bufs = {
        "xb": [p.sb(f"{tag}_xb{i}", [128, HW], F32, stack) for i in range(2)],
        "ab": [p.sb(f"{tag}_ab{i}", [128, HW], F32, stack) for i in range(2)],
        "xn": [p.sb(f"{tag}_xn{i}", [128, HW], F32, stack) for i in range(2)],
        "t1": p.sb(f"{tag}_t1", [128, HW], F32, stack),
        "sq": [p.sb(f"{tag}_sqh{i}", [128, HW], F32, stack) for i in range(2)],
    }
    emit_update(p, tag + "u", x_ap, a_ap, com["RB"], com["NB"], out_ap,
                com["ps_s1"], com["ps_s2"], com["cst"], bufs, do_stats=do_stats)


def emit_block0(p: Prog, com, aps, dbg=None, stack=None):
    nc = p.nc
    cst = com["cst"]
    bias_sb = p.sb("bias_sb", [128, DC], F32, stack)
    biasf_sb = p.sb("biasf_sb", [128, DC], F32, stack)

    ld0 = p.sem("ld0")
    with nc.Block() as block:
        @block.sync
        def _(sync):
            sync.dma_start(out=bias_sb[:, :], in_=aps["bias0"]).then_inc(ld0, 16)
            sync.dma_start(out=biasf_sb[:, :], in_=aps["biasf"]).then_inc(ld0, 16)
            sync.wait_ge(ld0, p.semv["ld0"] + 32)
    p.semv["ld0"] += 32

    with ExitStack() as stk_w:
        bmP = p.sb("bmP", [128, D], BF16, stk_w)
        with ExitStack() as stk:
            emit_maskprep(p, "mk", aps["maskT"], cst, bmP, stk)
        with ExitStack() as stk:
            phase_stats_rows(p, "st", com, aps["x0"], stk)
        with ExitStack() as stk:
            phase_h(p, "h", com, aps["x0"], stk)
        with ExitStack() as stk:
            wraw = [p.sb(f"a_wraw{i}", [128, D], BF16, stk) for i in range(2)]
            atmp = [p.sb(f"a_atmp{i}", [128, TW], F32, stk) for i in range(2)]
            sqb = [p.sb(f"a_sqtb{i}", [128, TW], F32, stk) for i in range(2)]
            emit_mm_l0a(p, "mm", aps["w0"], bias_sb, com["h_sb"], aps["a_scr"],
                        bmP, cst, com["ps_mm"], com["ps_s1"], com["ps_s2"],
                        wraw, atmp, sqb)
    with ExitStack() as stk:
        phase_rows2_update(p, "u", com, aps["x0"], aps["a_scr"], aps["x1"],
                           stk, do_stats=True)

    with ExitStack() as stk_f:
        r_fmT = p.sb("r_fmT", [8, T], BF16, stk_f)
        ocs_sb = p.sb("ocs_sb", [8, D], BF16, stk_f)
        emit_rows(p, "str", com["ps_s1"], com["ps_s2"], com["ps_aux"],
                  cst, com["rowbufs"], com["RB"], com["NB"])
        with ExitStack() as stk:
            phase_h(p, "h", com, aps["x1"], stk)
        with ExitStack() as stk:
            gw_sb = p.sb("gw_sb", [128, DC * 8], BF16, stk)
            ics_sb = p.sb("ics_sb", [8, D], BF16, stk)
            lg_fm = p.sb("lg_fm", [8, T], F32, stk)
            lg_tm = p.sb("lg_tm", [128, 128], F32, stk)
            scr_tm = p.sb("scr_tm", [128, 32], F32, stk)
            r_tm = p.sb("r_tm", [128, 128], F32, stk)
            ldg = p.sem("ldg")
            with nc.Block() as block:
                @block.gpsimd
                def _(g):
                    g.dma_start(out=gw_sb[:, :], in_=aps["gw"]).then_inc(ldg, 16)
                    g.dma_start(out=ics_sb[:, :], in_=aps["ics"]).then_inc(ldg, 16)
                    g.dma_start(out=ocs_sb[:, :], in_=aps["ocs"]).then_inc(ldg, 16)
                    g.wait_ge(ldg, p.semv["ldg"] + 48)
            p.semv["ldg"] += 48
            emit_gates(p, "gt", gw_sb, com["h_sb"], com["ps_aux"], cst,
                       lg_fm, lg_tm, scr_tm, r_tm, r_fmT)
            if dbg is not None:
                dsem = p.sem("dbgg")
                with nc.Block() as block:
                    @block.sync
                    def _(sync):
                        sync.dma_start(out=dbg["r_fmT"], in_=r_fmT[:, :]).then_inc(dsem, 16)
                        sync.dma_start(out=dbg["lg_fm"], in_=lg_fm[:, :]).then_inc(dsem, 16)
                        sync.dma_start(out=dbg["lg_tm"], in_=lg_tm[:, :]).then_inc(dsem, 16)
                        sync.dma_start(out=dbg["r_tm"], in_=r_tm[:, :]).then_inc(dsem, 16)
                        sync.wait_ge(dsem, p.semv["dbgg"] + 64)
                p.semv["dbgg"] += 64
            emit_upass(p, "up", ics_sb, r_fmT, com["h_sb"], com["ps_aux"], cst)
        with ExitStack() as stk:
            wraw = [p.sb(f"f_wraw{i}", [128, D], BF16, stk) for i in range(2)]
            os_sb = [p.sb(f"f_os{i}", [128, TW], F32, stk) for i in range(2)]
            tev = [p.sb(f"f_tev{i}", [128, TW], F32, stk) for i in range(2)]
            atmp = [p.sb(f"f_atmp{i}", [128, TW], F32, stk) for i in range(2)]
            sqb = [p.sb(f"f_sqtb{i}", [128, TW], F32, stk) for i in range(2)]
            emit_mm_l0f(p, "mm", aps["wf"], biasf_sb, ocs_sb, r_fmT,
                        com["h_sb"], aps["a_scr"], cst, com["ps_mm"],
                        com["ps_aux"][0], com["ps_s1"], com["ps_s2"],
                        wraw, os_sb, tev, atmp, sqb)
    with ExitStack() as stk:
        phase_rows2_update(p, "u", com, aps["x1"], aps["a_scr"], aps["x2"],
                           stk, do_stats=False)


def build_l0():
    p = Prog()
    aps = {
        "x0": p.din("x0", [D, T], F32),
        "w0": p.din("w0", [D, D], F32),
        "maskT": p.din("maskT", [D, D], I32),
        "bias0": p.din("bias0", [128, DC], F32),
        "gw": p.din("gw", [128, DC * 8], F32),
        "ics": p.din("ics", [8, D], F32),
        "ocs": p.din("ocs", [8, D], F32),
        "wf": p.din("wf", [D, D], F32),
        "biasf": p.din("biasf", [128, DC], F32),
        "x2": p.dout("x2", [D, T], F32),
        "a_scr": p.dint("a_scr", [D, T], F32),
        "x1": p.dint("x1_scr", [D, T], F32),
    }
    com = alloc_common(p)
    alloc_ln(p, com, p.ctx)
    emit_block0(p, com, aps)
    p.ctx.close()
    return p


def _tile_w(wT):
    K, N = wT.shape
    return np.ascontiguousarray(
        wT.reshape(K // 128, 128, N // 128, 128).transpose(2, 1, 0, 3))


def _shard_x(x):
    out = []
    for core in range(NCORES):
        b, half = core // 2, core % 2
        sl = x[b, half * T:(half + 1) * T, :]
        out.append(np.ascontiguousarray(sl.T))
    return out


def build_qkv():
    p = Prog()
    nc = p.nc
    x2 = p.din("x2", [D, T], F32)
    qc = p.din("qc", [D, D], I32)
    qs = p.din("qs", [DC, D], F32)
    kc = p.din("kc", [1024, D], I32)
    ks = p.din("ks", [8, D], F32)
    vc = p.din("vc", [1024, D], I32)
    vs = p.din("vs", [8, D], F32)
    cosd = p.din("cosd", [128, T], BF16)
    ssind = p.din("ssind", [128, T], BF16)
    q_out = p.dout("q_rot", [D, T], BF16)
    k_out = p.dout("k_rot", [1024, T], BF16)
    v_out = p.dout("v_tm", [T, 1024], BF16)

    com = alloc_common(p)
    cst = com["cst"]
    with ExitStack() as stkln:
        alloc_ln(p, com, stkln)
        with ExitStack() as stk:
            phase_stats_rows(p, "st2", com, x2, stk)
        with ExitStack() as stk:
            phase_h(p, "h2", com, x2, stk)
    with ExitStack() as stk:
        bufs = {
            "wbf": [p.sb(f"dwbf{i}", [128, D], BF16, stk) for i in range(2)],
            "scrow": p.sb("scrow", [1, D], F32, stk),
            "et": [p.sb(f"et{i}", [128, TW], BF16, stk) for i in range(2)],
            "esw": p.sb("esw", [128, TW], BF16, stk),
            "e1": p.sb("e1", [128, TW], BF16, stk),
            "e2": p.sb("e2", [128, TW], BF16, stk),
            "er": [p.sb(f"er{i}", [128, TW], BF16, stk) for i in range(2)],
            "COS": p.sb("COS", [128, T], BF16, stk),
            "SSIN": p.sb("SSIN", [128, T], BF16, stk),
        }
        psbt = [p.psum(f"psbt{i}", [128, TW], BF16, stk) for i in range(2)]
        ldc = p.sem("ldc")
        with nc.Block() as block:
            @block.sync
            def _(sync):
                sync.dma_start(out=bufs["COS"][:, :], in_=cosd).then_inc(ldc, 16)
                sync.dma_start(out=bufs["SSIN"][:, :], in_=ssind).then_inc(ldc, 16)
                sync.wait_ge(ldc, p.semv["ldc"] + 32)
        p.semv["ldc"] += 32
        emit_deqproj(p, "dq", qc, qs, DC, "rope", com["h_sb"], cst, bufs,
                     q_out, com["ps_mm"], com["ps_aux"], psbt)
        emit_deqproj(p, "dq", kc, ks, 8, "rope", com["h_sb"], cst, bufs,
                     k_out, com["ps_mm"], com["ps_aux"], psbt)
        emit_deqproj(p, "dq", vc, vs, 8, "vtrans", com["h_sb"], cst, bufs,
                     v_out, com["ps_mm"], com["ps_aux"], psbt)
    p.ctx.close()
    return p


def _rope_tables(half):
    pos = (half * T + np.arange(T, dtype=np.float64))
    jf = np.arange(64, dtype=np.float64)
    inv = 1.0 / (10000.0 ** (2.0 * jf / 128.0))
    f = pos[None, :] * inv[:, None]
    cos = np.repeat(np.cos(f), 2, axis=0)
    sin = np.sin(f)
    ssin = np.empty((128, T), np.float64)
    ssin[0::2] = -sin
    ssin[1::2] = sin
    bf = ml_dtypes.bfloat16
    return (np.ascontiguousarray(cos.astype(bf)),
            np.ascontiguousarray(ssin.astype(bf)))


def _prep_qkv(inputs):
    def sc_dev(s, n_chunks):
        return np.ascontiguousarray(
            s.reshape(32, n_chunks, 128).transpose(1, 0, 2).reshape(n_chunks, D))
    return {
        "qc": _tile_w(np.ascontiguousarray(inputs["L1_attn_q_codes"])).reshape(D, D),
        "qs": sc_dev(inputs["L1_attn_q_scales"], DC),
        "kc": _tile_w(np.ascontiguousarray(inputs["L1_attn_k_codes"])).reshape(1024, D),
        "ks": sc_dev(inputs["L1_attn_k_scales"], 8),
        "vc": _tile_w(np.ascontiguousarray(inputs["L1_attn_v_codes"])).reshape(1024, D),
        "vs": sc_dev(inputs["L1_attn_v_scales"], 8),
    }


def _prep_l0(inputs):
    maps = {
        "w0": _tile_w(np.ascontiguousarray(inputs["L0_attn_weight"].T)).reshape(D, D),
        "maskT": np.ascontiguousarray(inputs["L0_attn_mask"].T),
        "bias0": np.ascontiguousarray(inputs["L0_attn_bias"].reshape(DC, 128).T),
        "gw": np.ascontiguousarray(
            inputs["L0_ffn_gate_w"].T.reshape(DC, 128, 8).transpose(1, 0, 2)
            .reshape(128, DC * 8)),
        "ics": np.ascontiguousarray(inputs["L0_ffn_in_channel_scale"]),
        "ocs": np.ascontiguousarray(inputs["L0_ffn_out_channel_scale"]),
        "wf": _tile_w(np.ascontiguousarray(inputs["L0_ffn_weight"].T)).reshape(D, D),
        "biasf": np.ascontiguousarray(inputs["L0_ffn_bias"].reshape(DC, 128).T),
    }
    return maps


def build_full():
    p = Prog()
    nc = p.nc
    aps = {
        "x0": p.din("x0", [D, T], F32),
        "w0": p.din("w0", [D, D], F32),
        "maskT": p.din("maskT", [D, D], I32),
        "bias0": p.din("bias0", [128, DC], F32),
        "gw": p.din("gw", [128, DC * 8], F32),
        "ics": p.din("ics", [8, D], F32),
        "ocs": p.din("ocs", [8, D], F32),
        "wf": p.din("wf", [D, D], F32),
        "biasf": p.din("biasf", [128, DC], F32),
        "x1": p.dint("x1_scr", [D, T], F32),
        "x2": p.dint("x2_scr", [D, T], F32),
        "a_scr": p.dint("a_scr", [D, T], F32),
    }
    qc = p.din("qc", [D, D], I32)
    qs = p.din("qs", [DC, D], F32)
    kc = p.din("kc", [1024, D], I32)
    ks = p.din("ks", [8, D], F32)
    vc = p.din("vc", [1024, D], I32)
    vs = p.din("vs", [8, D], F32)
    cosd = p.din("cosd", [128, T], BF16)
    ssind = p.din("ssind", [128, T], BF16)
    tri = p.din("tri", [128, 128], F32)
    oc = p.din("oc", [D, D], I32)
    osd = p.din("osd", [DC, D], F32)
    aps_f1 = {
        "x3": p.dint("x3_scr", [D, T], F32),
        "bW": p.din("bW", [D, D], F32),
        "rw": p.din("rw", [128, DC * 8], F32),
        "lA": p.din("lA", [D, 128], F32),
        "lB": p.din("lB", [128, D], F32),
        "P2": p.din("P2", [8, 128], F32),
        "out": p.dout("out", [D, T], F32),
        "a_scr": aps["a_scr"],
    }
    agqk_in = p.dint("agqk_in", [5120, T], BF16)
    agqk_out = p.dint("agqk_out", [2 * 5120, T], BF16)
    agv_in = p.dint("agv_in", [T, 1024], BF16)
    agv_out = p.dint("agv_out", [2 * T, 1024], BF16)
    agat_in = p.dint("agat_in", [2048, 2 * T], BF16)
    agat_out = p.dint("agat_out", [4096, 2 * T], BF16)
    qfull = p.dint("qfull", [2048, 2 * T], BF16)
    kfull = p.dint("kfull", [512, 2 * T], BF16)
    vfull = p.dint("vfull", [2 * T, 512], BF16)
    attfull = p.dint("attfull", [D, T], BF16)

    com = alloc_common(p)
    cst = com["cst"]

    with ExitStack() as stkln:
        alloc_ln(p, com, stkln, tag="b0")
        emit_block0(p, com, aps, stack=stkln)
        with ExitStack() as stk:
            phase_stats_rows(p, "st", com, aps["x2"], stk)
        with ExitStack() as stk:
            phase_h(p, "h", com, aps["x2"], stk)

    qk_h = agqk_in.tensor
    v_h = agv_in.tensor

    def q_dst(o, j):
        return bass.AP(qk_h, o * 128 * T + j * TW, [[T, 128], [1, TW]])

    def k_dst(o, j):
        return bass.AP(qk_h, (4096 + o * 128) * T + j * TW, [[T, 128], [1, TW]])

    def v_dst(o, j):
        return bass.AP(v_h, j * TW * 1024 + o * 128,
                       [[1024, 128], [128 * 1024, 4], [1, 128]])

    with ExitStack() as stk:
        bufs = {
            "wbf": [p.sb(f"dwbf{i}", [128, D], BF16, stk) for i in range(2)],
            "scrow": p.sb("scrow", [1, D], F32, stk),
            "et": [p.sb(f"et{i}", [128, TW], BF16, stk) for i in range(2)],
            "esw": p.sb("esw", [128, TW], BF16, stk),
            "e1": p.sb("e1", [128, TW], BF16, stk),
            "e2": p.sb("e2", [128, TW], BF16, stk),
            "er": [p.sb(f"er{i}", [128, TW], BF16, stk) for i in range(2)],
            "COS": p.sb("COS", [128, T], BF16, stk),
            "SSIN": p.sb("SSIN", [128, T], BF16, stk),
        }
        psbt = [p.psum(f"psbt{i}", [128, TW], BF16, stk) for i in range(2)]
        ldc = p.sem("ldc")
        with nc.Block() as block:
            @block.sync
            def _(sync):
                sync.dma_start(out=bufs["COS"][:, :], in_=cosd).then_inc(ldc, 16)
                sync.dma_start(out=bufs["SSIN"][:, :], in_=ssind).then_inc(ldc, 16)
                sync.wait_ge(ldc, p.semv["ldc"] + 32)
        p.semv["ldc"] += 32
        emit_deqproj(p, "dq", qc, qs, DC, "rope", com["h_sb"], cst, bufs,
                     None, com["ps_mm"], com["ps_aux"], psbt, dst_fn=q_dst)
        emit_deqproj(p, "dq", kc, ks, 8, "rope", com["h_sb"], cst, bufs,
                     None, com["ps_mm"], com["ps_aux"], psbt, dst_fn=k_dst)
        emit_deqproj(p, "dq", vc, vs, 8, "vtrans", com["h_sb"], cst, bufs,
                     None, com["ps_mm"], com["ps_aux"], psbt, dst_fn=v_dst)

    cc = p.sem("cc")
    with nc.Block() as block:
        @block.gpsimd
        def _(g):
            g.collective_compute(
                "AllGather", ALU.bypass,
                replica_groups=[[0, 1], [2, 3], [4, 5], [6, 7]],
                ins=[agqk_in.opt()], outs=[agqk_out.opt()],
            ).then_inc(cc)
            g.collective_compute(
                "AllGather", ALU.bypass,
                replica_groups=[[0, 1], [2, 3], [4, 5], [6, 7]],
                ins=[agv_in.opt()], outs=[agv_out.opt()],
            ).then_inc(cc)
            g.wait_ge(cc, p.semv["cc"] + 2)
    p.semv["cc"] += 2

    pid = nc.partition_id()
    roq = (pid % 2) * 2048
    rok = (pid % 2) * 512
    rpk = p.sem("rpk")
    with nc.Block() as block:
        @block.gpsimd
        def _(g):
            n = 0
            for s2 in range(2):
                g.dma_start(out=qfull[:, s2 * T:(s2 + 1) * T],
                            in_=agqk_out[bass.ds(roq + s2 * 5120, 2048), :]
                            ).then_inc(rpk, 16)
                g.dma_start(out=kfull[:, s2 * T:(s2 + 1) * T],
                            in_=agqk_out[bass.ds(rok + (s2 * 5120 + 4096), 512), :]
                            ).then_inc(rpk, 16)
                g.dma_start(out=vfull[s2 * T:(s2 + 1) * T, :],
                            in_=agv_out[s2 * T:(s2 + 1) * T, bass.ds(rok, 512)]
                            ).then_inc(rpk, 16)
                n += 3
            g.wait_ge(rpk, p.semv["rpk"] + 16 * n)
    p.semv["rpk"] += 16 * 6

    with ExitStack() as stk:
        abufs = {
            "kg": p.sb("kg", [128, 2 * T], BF16, stk),
            "vg": p.sb("vg", [128, 2 * T], BF16, stk),
            "qh": p.sb("qh", [128, 2 * T], BF16, stk),
            "Srow": [p.sb(f"Srow{i}", [128, 2 * T], F32, stk) for i in range(2)],
            "Prow": [p.sb(f"Prow{i}", [128, 2 * T], BF16, stk) for i in range(2)],
            "PT": [p.sb(f"PTb{i}", [128, 1024], BF16, stk) for i in range(2)],
            "TRI": p.sb("TRI", [128, 128], F32, stk),
            "cmax": p.sb("cmax", [128, 8], F32, stk),
            "negmax": [p.sb(f"negmax{i}", [128, 1], F32, stk) for i in range(2)],
            "rowsum": [p.sb(f"rowsum{i}", [128, 1], F32, stk) for i in range(2)],
            "dinv": [p.sb(f"dinv{i}", [128, 1], F32, stk) for i in range(2)],
        }
        psbt = [p.psum(f"psbta{i}", [128, 1024], BF16, stk) for i in range(2)]
        ldt = p.sem("ldt")
        with nc.Block() as block:
            @block.sync
            def _(sync):
                sync.dma_start(out=abufs["TRI"][:, :], in_=tri).then_inc(ldt, 16)
                sync.wait_ge(ldt, p.semv["ldt"] + 16)
        p.semv["ldt"] += 16
        emit_attention(p, "at", qfull, kfull, vfull, com["h_sb"], cst, abufs,
                       com["ps_mm"], com["ps_aux"], psbt)

    sta = p.sem("sta")
    with nc.Block() as block:
        @block.sync
        def _(sync):
            for hl in range(16):
                sync.dma_start(out=agat_in[hl * 128:(hl + 1) * 128, :],
                               in_=com["h_sb"][:, hl * 4096:(hl + 1) * 4096]
                               ).then_inc(sta, 16)
            sync.wait_ge(sta, p.semv["sta"] + 16 * 16)
    p.semv["sta"] += 16 * 16
    with nc.Block() as block:
        @block.gpsimd
        def _(g):
            g.collective_compute(
                "AllGather", ALU.bypass,
                replica_groups=[[0, 1], [2, 3], [4, 5], [6, 7]],
                ins=[agat_in.opt()], outs=[agat_out.opt()],
            ).then_inc(cc)
            g.wait_ge(cc, p.semv["cc"] + 1)
    p.semv["cc"] += 1

    lda = p.sem("lda")
    ldb = p.sem("ldb")
    with nc.Block() as block:
        @block.gpsimd
        def _(g):
            coff = (nc.partition_id() % 2) * T
            g.dma_start(out=attfull[:, :],
                        in_=agat_out[:, bass.ds(coff, T)]).then_inc(lda, 16)
            g.wait_ge(lda, p.semv["lda"] + 16)
    p.semv["lda"] += 16
    with nc.Block() as block:
        @block.sync
        def _(sync):
            for c in range(DC):
                sync.dma_start(out=com["h_sb"][:, c * T:(c + 1) * T],
                               in_=attfull[c * 128:(c + 1) * 128, :]
                               ).then_inc(ldb, 16)
            sync.wait_ge(ldb, p.semv["ldb"] + 16 * DC)
    p.semv["ldb"] += 16 * DC

    with ExitStack() as stk:
        bufs = {
            "wbf": [p.sb(f"owbf{i}", [128, D], BF16, stk) for i in range(2)],
            "scrow": p.sb("oscrow", [1, D], F32, stk),
            "atmp": [p.sb(f"oatmp{i}", [128, TW], F32, stk) for i in range(2)],
            "sqb": [p.sb(f"osqb{i}", [128, TW], F32, stk) for i in range(2)],
        }
        emit_oproj(p, "mm", oc, osd, com["h_sb"], aps["a_scr"], cst,
                   com["ps_mm"], com["ps_aux"], com["ps_s1"], com["ps_s2"], bufs)
    with ExitStack() as stkln:
        alloc_ln(p, com, stkln, tag="u2")
        with ExitStack() as stk:
            phase_rows2_update(p, "u", com, aps["x2"], aps["a_scr"],
                               aps_f1["x3"], stk, do_stats=False)

    emit_ffn1(p, com, aps_f1)
    p.ctx.close()
    return p


def _full_in_maps(inputs):
    x_sh = _shard_x(inputs["L0_attn_data"].astype(np.float32, copy=False))
    w_l0 = _prep_l0(inputs)
    w_qkv = _prep_qkv(inputs)
    w_f1 = _prep_ffn1(inputs)

    def sc_dev(s, n_chunks):
        return np.ascontiguousarray(
            s.reshape(32, n_chunks, 128).transpose(1, 0, 2).reshape(n_chunks, D))

    oc = _tile_w(np.ascontiguousarray(inputs["L1_attn_o_codes"])).reshape(D, D)
    osd = sc_dev(inputs["L1_attn_o_scales"], DC)
    tri = _tri_mask()
    in_maps = []
    for c in range(NCORES):
        cos, ssin = _rope_tables(c % 2)
        m = dict(w_l0)
        m.update(w_qkv)
        m.update(w_f1)
        m.update(x0=x_sh[c], cosd=cos, ssind=ssin, tri=tri, oc=oc, osd=osd)
        in_maps.append(m)
    return in_maps


def kernel(**inputs):
    in_maps = _full_in_maps(inputs)
    p = build_full()
    res = run_bass_kernel_spmd(p.nc, in_maps, core_ids=list(range(NCORES)))
    out = np.empty((4, 4096, 4096), np.float32)
    for c in range(NCORES):
        b, half = c // 2, c % 2
        out[b, half * T:(half + 1) * T, :] = res.results[c]["out"].T
    return out


# version 06
# speedup vs baseline: 1.1320x; 1.0231x over previous
import numpy as np
import ml_dtypes
import concourse.bass as bass
import concourse.mybir as mybir
from concourse.bass_utils import run_bass_kernel_spmd
from contextlib import ExitStack

F32 = mybir.dt.float32
BF16 = mybir.dt.bfloat16
I32 = mybir.dt.int32
AF = mybir.ActivationFunctionType
ALU = mybir.AluOpType
AX = mybir.AxisListType

D = 4096
DC = 32
T = 2048
NT = 4
TW = 512
NCORES = 8
EPS = 1e-5
NEG = -30000.0


class Prog:
    def __init__(self):
        self.nc = bass.Bass("TRN2", target_bir_lowering=False, debug=False)
        self.ctx = ExitStack()
        self.sems = {}
        self.semv = {}

    def sem(self, name):
        if name not in self.sems:
            self.sems[name] = self.ctx.enter_context(self.nc.semaphore(name))
            self.semv[name] = 0
        return self.sems[name]

    def sem2(self, name):
        return [self.sem(name + "0"), self.sem(name + "1")]

    def _uniq(self, name):
        if not hasattr(self, "_names"):
            self._names = {}
        k = self._names.get(name, 0)
        self._names[name] = k + 1
        return name if k == 0 else f"{name}__{k}"

    def sb(self, name, shape, dt=F32, stack=None):
        return (stack or self.ctx).enter_context(
            self.nc.sbuf_tensor(self._uniq(name), shape, dt))

    def psum(self, name, shape, dt=F32, stack=None):
        return (stack or self.ctx).enter_context(
            self.nc.psum_tensor(self._uniq(name), shape, dt))

    def din(self, name, shape, dt=F32):
        return self.nc.dram_tensor(name, shape, dt, kind="ExternalInput").ap()

    def dout(self, name, shape, dt=F32):
        return self.nc.dram_tensor(name, shape, dt, kind="ExternalOutput").ap()

    def dint(self, name, shape, dt=F32):
        return self.nc.dram_tensor(name, shape, dt, kind="Internal").ap()


def ap3(t, parts, dims):
    h = t[parts[0]:parts[0] + parts[1], 0:1]
    return bass.AP(h.tensor, h.offset, [h.ap[0]] + [list(d) for d in dims])



def emit_consts(p: Prog, cst):
    nc = p.nc
    with nc.Block() as block:
        @block.gpsimd
        def _(g):
            g.iota(cst["io_free"][:, :], [[1, 128]], channel_multiplier=0)
            g.iota(cst["io_part"][:, :], [[0, 128]], channel_multiplier=1)
            g.iota(cst["io_g32"][:, :], [[1, 4], [0, 32]], channel_multiplier=0)

        @block.vector
        def _(v):
            v.memset(cst["ones32"][:, :], 1.0)
            v.memset(cst["ones1"][:, :], 1.0)
            v.memset(cst["eps"][:, :], EPS)
            v.memset(cst["neg8"][:, :], -8.0)

    with nc.Block() as block:
        @block.vector
        def _(v):
            v.tensor_tensor(cst["ident_bf"][:, :], cst["io_free"][:, :],
                            cst["io_part"][:, :], ALU.is_equal)
            v.tensor_tensor(cst["ident_f"][:, :], cst["io_free"][:, :],
                            cst["io_part"][:, :], ALU.is_equal)


def emit_stream_stats(p: Prog, tag, src_ap, ps_s1, ps_s2, cst, xbufs, sqbuf):
    nc = p.nc
    ld = p.sem2(tag + "_ld"); b_ld = [p.semv[tag + "_ld0"], p.semv[tag + "_ld1"]]
    sq = p.sem(tag + "_sq"); b_sq = p.semv[tag + "_sq"]
    mm = p.sem(tag + "_mm"); b_mm = p.semv[tag + "_mm"]
    ldt = lambda c: b_ld[c % 2] + 16 * (c // 2 + 1)

    with nc.Block() as block:
        @block.sync
        def _(sync):
            for c in range(DC):
                if c >= 2:
                    sync.wait_ge(sq, b_sq + c - 1)
                    sync.wait_ge(mm, b_mm + c - 1)
                sync.dma_start(out=xbufs[c % 2][:, :],
                               in_=src_ap[c * 128:(c + 1) * 128, :]).then_inc(ld[c % 2], 16)

        @block.scalar
        def _(s):
            for c in range(DC):
                s.wait_ge(ld[c % 2], ldt(c))
                if c >= 2:
                    s.wait_ge(mm, b_mm + c - 1)
                s.activation(sqbuf[c % 2][:, :], xbufs[c % 2][:, :],
                             AF.Square).then_inc(sq, 1)

        @block.tensor
        def _(t):
            for c in range(DC):
                t.wait_ge(ld[c % 2], ldt(c))
                for j in range(NT):
                    t.matmul(ps_s1[32 * j:32 * j + 32, :], cst["ones32"][:, :],
                             xbufs[c % 2][:, j * TW:(j + 1) * TW],
                             start=(c == 0), stop=(c == DC - 1),
                             tile_position=(0, 32 * j), skip_group_check=True)
                t.wait_ge(sq, b_sq + c + 1)
                for j in range(NT):
                    mmi = t.matmul(ps_s2[32 * j:32 * j + 32, :], cst["ones32"][:, :],
                                   sqbuf[c % 2][:, j * TW:(j + 1) * TW],
                                   start=(c == 0), stop=(c == DC - 1),
                                   tile_position=(0, 32 * j), skip_group_check=True)
                mmi.then_inc(mm, 1)

    p.semv[tag + "_ld0"] = b_ld[0] + 16 * (DC // 2)
    p.semv[tag + "_ld1"] = b_ld[1] + 16 * (DC // 2)
    p.semv[tag + "_sq"] = b_sq + DC
    p.semv[tag + "_mm"] = b_mm + DC


def emit_rows(p: Prog, tag, ps_s1, ps_s2, ps_aux, cst, rowbufs, RB, NB):
    nc = p.nc
    s1g, s2g, varg, rstdg, negmg = (rowbufs[k] for k in
                                    ("s1", "s2", "var", "rstd", "negm"))
    bc = p.sem(tag + "_bc"); b_bc = p.semv[tag + "_bc"]
    ev = p.sem(tag + "_ev"); b_ev = p.semv[tag + "_ev"]

    with nc.Block() as block:
        @block.vector
        def _(v):
            v.tensor_copy(s1g[:, :], ps_s1[:, :])
            v.tensor_copy(s2g[:, :], ps_s2[:, :])
            v.tensor_scalar(s1g[:, :], s1g[:, :], 1.0 / D, None, ALU.mult)
            v.tensor_scalar(s2g[:, :], s2g[:, :], 1.0 / D, None, ALU.mult)
            v.tensor_tensor(varg[:, :], s1g[:, :], s1g[:, :], ALU.mult)
            v.tensor_tensor(varg[:, :], s2g[:, :], varg[:, :], ALU.subtract).then_inc(ev, 1)

        @block.scalar
        def _(s):
            s.wait_ge(ev, b_ev + 1)
            s.activation(rstdg[:, :], varg[:, :], AF.Ln, bias=cst["eps"][:, :], scale=1.0)
            s.activation(rstdg[:, :], rstdg[:, :], AF.Exp, scale=-0.5).then_inc(ev, 1)

        @block.vector
        def _(v):
            v.wait_ge(ev, b_ev + 2)
            v.scalar_tensor_tensor(negmg[:, :], s1g[:, :], -1.0, rstdg[:, :],
                                   ALU.mult, ALU.mult).then_inc(ev, 1)

        @block.tensor
        def _(t):
            t.wait_ge(ev, b_ev + 3)
            for j in range(NT):
                t.matmul(ps_aux[0][:, :], cst["sel"][j][:, :], rstdg[:, :],
                         start=True, stop=True).then_inc(bc, 1)
                t.wait_ge(ev, b_ev + 3 + 2 * j + 1)
                t.matmul(ps_aux[1][:, :], cst["sel"][j][:, :], negmg[:, :],
                         start=True, stop=True).then_inc(bc, 1)
                t.wait_ge(ev, b_ev + 3 + 2 * j + 2)

        @block.vector
        def _(v):
            for j in range(NT):
                v.wait_ge(bc, b_bc + 2 * j + 1)
                v.tensor_copy(RB[:, j * TW:(j + 1) * TW], ps_aux[0][:, :]).then_inc(ev, 1)
                v.wait_ge(bc, b_bc + 2 * j + 2)
                v.tensor_copy(NB[:, j * TW:(j + 1) * TW], ps_aux[1][:, :]).then_inc(ev, 1)

    p.semv[tag + "_bc"] = b_bc + 2 * NT
    p.semv[tag + "_ev"] = b_ev + 3 + 2 * NT


def emit_hpass(p: Prog, tag, src_ap, RB, NB, h_sb, xbufs, tscr):
    nc = p.nc
    ld = p.sem2(tag + "_ld"); b_ld = [p.semv[tag + "_ld0"], p.semv[tag + "_ld1"]]
    hd = p.sem(tag + "_hd"); b_hd = p.semv[tag + "_hd"]

    with nc.Block() as block:
        @block.sync
        def _(sync):
            for c in range(DC):
                if c >= 2:
                    sync.wait_ge(hd, b_hd + c - 1)
                sync.dma_start(out=xbufs[c % 2][:, :],
                               in_=src_ap[c * 128:(c + 1) * 128, :]).then_inc(ld[c % 2], 16)

        @block.vector
        def _(v):
            for c in range(DC):
                v.wait_ge(ld[c % 2], b_ld[c % 2] + 16 * (c // 2 + 1))
                v.tensor_tensor(tscr[:, :], xbufs[c % 2][:, :], RB[:, :], ALU.mult)
                v.tensor_tensor(h_sb[:, c * T:(c + 1) * T], tscr[:, :], NB[:, :],
                                ALU.add).then_inc(hd, 1)

    p.semv[tag + "_ld0"] = b_ld[0] + 16 * (DC // 2)
    p.semv[tag + "_ld1"] = b_ld[1] + 16 * (DC // 2)
    p.semv[tag + "_hd"] = b_hd + DC


def emit_update(p: Prog, tag, x_ap, a_ap, RB2, NB2, out_ap, ps_s1, ps_s2, cst,
                bufs, do_stats=True):
    nc = p.nc
    HW = T // 2
    NI = DC * 2
    lx = p.sem2(tag + "_lx"); b_lx = [p.semv[tag + "_lx0"], p.semv[tag + "_lx1"]]
    la = p.sem2(tag + "_la"); b_la = [p.semv[tag + "_la0"], p.semv[tag + "_la1"]]
    st = p.sem2(tag + "_st"); b_st = [p.semv[tag + "_st0"], p.semv[tag + "_st1"]]
    up = p.sem(tag + "_up"); b_up = p.semv[tag + "_up"]
    sq = p.sem(tag + "_sq"); b_sq = p.semv[tag + "_sq"]
    mm = p.sem(tag + "_mm"); b_mm = p.semv[tag + "_mm"]
    xb, ab, xn, t1, sqb = bufs["xb"], bufs["ab"], bufs["xn"], bufs["t1"], bufs["sq"]
    lxt = lambda i: b_lx[i % 2] + 16 * (i // 2 + 1)
    lat = lambda i: b_la[i % 2] + 16 * (i // 2 + 1)
    stt = lambda i: b_st[i % 2] + 16 * (i // 2 + 1)

    def dsl(i):
        c, u = i // 2, i % 2
        return (slice(c * 128, (c + 1) * 128), slice(u * HW, (u + 1) * HW))

    with nc.Block() as block:
        @block.sync
        def _(sync):
            for i in range(NI):
                if i >= 2:
                    sync.wait_ge(up, b_up + i - 1)
                r, tsl = dsl(i)
                sync.dma_start(out=xb[i % 2][:, :],
                               in_=x_ap[r, tsl]).then_inc(lx[i % 2], 16)

        @block.gpsimd
        def _(g):
            for i in range(NI):
                if i >= 2:
                    g.wait_ge(up, b_up + i - 1)
                r, tsl = dsl(i)
                g.dma_start(out=ab[i % 2][:, :],
                            in_=a_ap[r, tsl]).then_inc(la[i % 2], 16)
                if i >= 1:
                    g.wait_ge(up, b_up + i)
                    rp, tp = dsl(i - 1)
                    g.dma_start(out=out_ap[rp, tp],
                                in_=xn[(i - 1) % 2][:, :]).then_inc(st[(i - 1) % 2], 16)
            g.wait_ge(up, b_up + NI)
            rp, tp = dsl(NI - 1)
            g.dma_start(out=out_ap[rp, tp],
                        in_=xn[(NI - 1) % 2][:, :]).then_inc(st[(NI - 1) % 2], 16)
            g.wait_ge(st[0], b_st[0] + 16 * (NI // 2))
            g.wait_ge(st[1], b_st[1] + 16 * (NI // 2))

        @block.vector
        def _(v):
            for i in range(NI):
                u = i % 2
                v.wait_ge(la[i % 2], lat(i))
                v.tensor_tensor(t1[:, :], ab[i % 2][:, :],
                                RB2[:, u * HW:(u + 1) * HW], ALU.mult)
                v.tensor_tensor(t1[:, :], t1[:, :],
                                NB2[:, u * HW:(u + 1) * HW], ALU.add)
                v.wait_ge(lx[i % 2], lxt(i))
                if i >= 2:
                    v.wait_ge(st[i % 2], stt(i - 2))
                    if do_stats:
                        v.wait_ge(sq, b_sq + i - 1)
                        v.wait_ge(mm, b_mm + i - 1)
                v.tensor_tensor(xn[i % 2][:, :], t1[:, :], xb[i % 2][:, :],
                                ALU.add).then_inc(up, 1)

        if do_stats:
            @block.scalar
            def _(s):
                for i in range(NI):
                    s.wait_ge(up, b_up + i + 1)
                    if i >= 2:
                        s.wait_ge(mm, b_mm + i - 1)
                    s.activation(sqb[i % 2][:, :], xn[i % 2][:, :],
                                 AF.Square).then_inc(sq, 1)

            @block.tensor
            def _(t):
                for i in range(NI):
                    c, u = i // 2, i % 2
                    t.wait_ge(up, b_up + i + 1)
                    for jj in range(2):
                        j = 2 * u + jj
                        t.matmul(ps_s1[32 * j:32 * j + 32, :], cst["ones32"][:, :],
                                 xn[i % 2][:, jj * TW:(jj + 1) * TW],
                                 start=(c == 0), stop=(c == DC - 1),
                                 tile_position=(0, 32 * j), skip_group_check=True)
                    t.wait_ge(sq, b_sq + i + 1)
                    for jj in range(2):
                        j = 2 * u + jj
                        mmi = t.matmul(ps_s2[32 * j:32 * j + 32, :], cst["ones32"][:, :],
                                       sqb[i % 2][:, jj * TW:(jj + 1) * TW],
                                       start=(c == 0), stop=(c == DC - 1),
                                       tile_position=(0, 32 * j), skip_group_check=True)
                    mmi.then_inc(mm, 1)

    for k in range(2):
        p.semv[tag + f"_lx{k}"] = b_lx[k] + 16 * (NI // 2)
        p.semv[tag + f"_la{k}"] = b_la[k] + 16 * (NI // 2)
        p.semv[tag + f"_st{k}"] = b_st[k] + 16 * (NI // 2)
    p.semv[tag + "_up"] = b_up + NI
    if do_stats:
        p.semv[tag + "_sq"] = b_sq + NI
        p.semv[tag + "_mm"] = b_mm + NI


def emit_maskprep(p: Prog, tag, maskT_ap, cst, bmP, stack):
    nc = p.nc
    mbuf = [p.sb(f"{tag}_mb{i}", [128, 4096], I32, stack) for i in range(2)]
    s1t = p.sb(f"{tag}_s1", [128, 128], F32, stack)
    bmTnb = p.sb(f"{tag}_bmTnb", [128, 128], F32, stack)
    bm01 = p.sb(f"{tag}_bm01", [128, 128], F32, stack)
    selc_a = p.sb(f"{tag}_sca", [128, 128], I32, stack)
    ps_tr = p.psum(f"{tag}_pstr", [128, 128], F32, stack)
    ps_b = p.psum(f"{tag}_psb", [128, 128], F32, stack)
    ld = p.sem2(tag + "_ld"); b_ld = [p.semv[tag + "_ld0"], p.semv[tag + "_ld1"]]
    rd = p.sem(tag + "_rd"); b_rd = p.semv[tag + "_rd"]
    tr = p.sem(tag + "_tr"); b_tr = p.semv[tag + "_tr"]

    with nc.Block() as block:
        @block.sync
        def _(sync):
            for i in range(DC):
                if i >= 2:
                    sync.wait_ge(rd, b_rd + 2 * (i - 1))
                sync.dma_start(out=mbuf[i % 2][:, :],
                               in_=maskT_ap[i * 128:(i + 1) * 128, :]).then_inc(ld[i % 2], 16)

        @block.vector
        def _(v):
            for i in range(DC):
                v.wait_ge(ld[i % 2], b_ld[i % 2] + 16 * (i // 2 + 1))
                v.reduce_sum(s1t[:, :],
                             mbuf[i % 2][:, :].rearrange("p (n r) -> p n r", r=32),
                             axis=AX.X).then_inc(rd, 1)
                v.wait_ge(tr, b_tr + i + 1)
                v.reduce_sum(bmTnb[:, 4 * i:4 * i + 4],
                             ps_tr[:, :].rearrange("p (n r) -> p n r", r=32),
                             axis=AX.X).then_inc(rd, 1)

        @block.tensor
        def _(t):
            for i in range(DC):
                t.wait_ge(rd, b_rd + 2 * i + 1)
                t.transpose(ps_tr[:, :], s1t[:, :], cst["ident_f"][:, :]).then_inc(tr, 1)
                t.wait_ge(rd, b_rd + 2 * i + 2)
    p.semv[tag + "_ld0"] = b_ld[0] + 16 * (DC // 2)
    p.semv[tag + "_ld1"] = b_ld[1] + 16 * (DC // 2)
    p.semv[tag + "_rd"] = b_rd + 2 * DC
    p.semv[tag + "_tr"] = b_tr + DC

    with nc.Block() as block:
        @block.vector
        def _(v):
            v.tensor_scalar(bmTnb[:, :], bmTnb[:, :], 0.5, None, ALU.is_gt)

    with nc.Block() as block:
        @block.tensor
        def _(t):
            t.transpose(ps_tr[:, :], bmTnb[:, :], cst["ident_f"][:, :])

    with nc.Block() as block:
        @block.vector
        def _(v):
            v.tensor_copy(bm01[:, :], ps_tr[:, :])

    b_tr = p.semv[tag + "_tr"]; b_rd = p.semv[tag + "_rd"]
    with nc.Block() as block:
        @block.vector
        def _(v):
            for c in range(DC):
                if c >= 1:
                    v.wait_ge(tr, b_tr + c)
                v.tensor_scalar(selc_a[:, :], cst["io_g32"][:, :], 4 * c, None, ALU.add)
                v.tensor_tensor(s1t[:, :], cst["io_part"][:, :], selc_a[:, :],
                                ALU.is_equal).then_inc(rd, 1)
                v.wait_ge(tr, b_tr + c + 1)
                v.tensor_copy(bmP[:, c * 128:(c + 1) * 128], ps_b[:, :]).then_inc(rd, 1)

        @block.tensor
        def _(t):
            for c in range(DC):
                t.wait_ge(rd, b_rd + 2 * c + 1)
                t.matmul(ps_b[:, :], s1t[:, :], bm01[:, :],
                         start=True, stop=True).then_inc(tr, 1)
    p.semv[tag + "_rd"] = b_rd + 2 * DC
    p.semv[tag + "_tr"] = b_tr + DC


def emit_mm_l0a(p: Prog, tag, w_ap, bias_sb, h_sb, a_out_ap, bmP, cst,
                ps_mm, ps_s1, ps_s2, wraw, atmp, sqb):
    nc = p.nc
    wl = p.sem2(tag + "_wl"); b_wl = [p.semv[tag + "_wl0"], p.semv[tag + "_wl1"]]
    spd = p.sem2(tag + "_spd"); b_spd = [p.semv[tag + "_spd0"], p.semv[tag + "_spd1"]]
    wp = p.sem(tag + "_wp"); b_wp = p.semv[tag + "_wp"]
    mmd = p.sem(tag + "_mmd"); b_mmd = p.semv[tag + "_mmd"]
    evd = p.sem(tag + "_evd"); b_evd = p.semv[tag + "_evd"]
    sqd = p.sem(tag + "_sqd"); b_sqd = p.semv[tag + "_sqd"]
    std = p.sem(tag + "_std"); b_std = p.semv[tag + "_std"]
    wlt = lambda o: b_wl[o % 2] + 16 * (o // 2 + 1)
    spt = lambda ti: b_spd[ti % 2] + 16 * (ti // 2 + 1)

    NTILE = DC * NT

    with nc.Block() as block:
        @block.gpsimd
        def _(g):
            for o in range(DC):
                if o >= 2:
                    g.wait_ge(mmd, b_mmd + (o - 1) * NT)
                g.dma_start(out=wraw[o % 2][:, :],
                            in_=w_ap[o * 128:(o + 1) * 128, :]).then_inc(wl[o % 2], 16)

        @block.vector
        def _(v):
            for o in range(DC):
                v.wait_ge(wl[o % 2], wlt(o))
                bh = bmP[:, 4 * o:4 * o + 1]
                bview = bass.AP(bh.tensor, bh.offset,
                                [bh.ap[0], [128, DC], [1, 4], [0, 32]])
                v.tensor_tensor(
                    wraw[o % 2][:, :].rearrange("p (c f r) -> p c f r", c=DC, f=4),
                    wraw[o % 2][:, :].rearrange("p (c f r) -> p c f r", c=DC, f=4),
                    bview, ALU.mult).then_inc(wp, 1)

        @block.tensor
        def _(t):
            for o in range(DC):
                t.wait_ge(wp, b_wp + o + 1)
                for j in range(NT):
                    ti = o * NT + j
                    bank = ti % 2
                    if ti >= 2:
                        t.wait_ge(evd, b_evd + ti - 1)
                    for c in range(DC):
                        mi = t.matmul(ps_mm[bank][:, :],
                                      wraw[o % 2][:, c * 128:(c + 1) * 128],
                                      h_sb[:, c * T + j * TW:c * T + (j + 1) * TW],
                                      start=(c == 0), stop=(c == DC - 1))
                    mi.then_inc(mmd, 1)
                    if ti >= 1:
                        op, jp = (ti - 1) // NT, (ti - 1) % NT
                        t.wait_ge(sqd, b_sqd + ti)
                        t.matmul(ps_s1[32 * jp:32 * jp + 32, :], cst["ones32"][:, :],
                                 atmp[(ti - 1) % 2][:, :],
                                 start=(op == 0), stop=(op == DC - 1),
                                 tile_position=(0, 32 * jp), skip_group_check=True)
                        t.matmul(ps_s2[32 * jp:32 * jp + 32, :], cst["ones32"][:, :],
                                 sqb[(ti - 1) % 2][:, :],
                                 start=(op == 0), stop=(op == DC - 1),
                                 tile_position=(0, 32 * jp), skip_group_check=True).then_inc(std, 1)
            ti = NTILE
            op, jp = (ti - 1) // NT, (ti - 1) % NT
            t.wait_ge(sqd, b_sqd + ti)
            t.matmul(ps_s1[32 * jp:32 * jp + 32, :], cst["ones32"][:, :],
                     atmp[(ti - 1) % 2][:, :], start=(op == 0), stop=True,
                     tile_position=(0, 32 * jp), skip_group_check=True)
            t.matmul(ps_s2[32 * jp:32 * jp + 32, :], cst["ones32"][:, :],
                     sqb[(ti - 1) % 2][:, :], start=(op == 0), stop=True,
                     tile_position=(0, 32 * jp), skip_group_check=True).then_inc(std, 1)

        @block.scalar
        def _(s):
            for o in range(DC):
                for j in range(NT):
                    ti = o * NT + j
                    s.wait_ge(mmd, b_mmd + ti + 1)
                    if ti >= 2:
                        s.wait_ge(spd[ti % 2], spt(ti - 2))
                        s.wait_ge(std, b_std + ti - 1)
                    s.activation(atmp[ti % 2][:, :], ps_mm[ti % 2][:, :],
                                 AF.Identity, bias=bias_sb[:, o:o + 1],
                                 scale=1.0).then_inc(evd, 1)
                    s.activation(sqb[ti % 2][:, :], atmp[ti % 2][:, :],
                                 AF.Square).then_inc(sqd, 1)

        @block.sync
        def _(sync):
            for o in range(DC):
                for j in range(NT):
                    ti = o * NT + j
                    sync.wait_ge(evd, b_evd + ti + 1)
                    sync.dma_start(out=a_out_ap[o * 128:(o + 1) * 128,
                                                j * TW:(j + 1) * TW],
                                   in_=atmp[ti % 2][:, :]).then_inc(spd[ti % 2], 16)
            sync.wait_ge(spd[0], b_spd[0] + 16 * (NTILE // 2))
            sync.wait_ge(spd[1], b_spd[1] + 16 * (NTILE // 2))

    for k in range(2):
        p.semv[tag + f"_wl{k}"] = b_wl[k] + 16 * (DC // 2)
        p.semv[tag + f"_spd{k}"] = b_spd[k] + 16 * (NTILE // 2)
    p.semv[tag + "_wp"] = b_wp + DC
    p.semv[tag + "_mmd"] = b_mmd + NTILE
    p.semv[tag + "_evd"] = b_evd + NTILE
    p.semv[tag + "_sqd"] = b_sqd + NTILE
    p.semv[tag + "_std"] = b_std + NTILE


def emit_gates(p: Prog, tag, gw_sb, h_sb, ps_aux, cst, lg_fm, lg_tm, scr_tm,
               r_tm, r_fmT):
    nc = p.nc
    gd = p.sem(tag + "_gd"); b_gd = p.semv[tag + "_gd"]
    ev = p.sem(tag + "_ev"); b_ev = p.semv[tag + "_ev"]

    with nc.Block() as block:
        @block.tensor
        def _(t):
            for j in range(NT):
                if j >= 1:
                    t.wait_ge(ev, b_ev + j)
                for c in range(DC):
                    mi = t.matmul(ps_aux[0][0:8, :], gw_sb[:, c * 8:(c + 1) * 8],
                                  h_sb[:, c * T + j * TW:c * T + (j + 1) * TW],
                                  start=(c == 0), stop=(c == DC - 1))
                mi.then_inc(gd, 1)

        @block.vector
        def _(v):
            for j in range(NT):
                v.wait_ge(gd, b_gd + j + 1)
                v.tensor_copy(lg_fm[0:8, j * TW:(j + 1) * TW],
                              ps_aux[0][0:8, :]).then_inc(ev, 1)
    p.semv[tag + "_gd"] = b_gd + NT
    p.semv[tag + "_ev"] = b_ev + NT

    b_gd = p.semv[tag + "_gd"]; b_ev = p.semv[tag + "_ev"]
    with nc.Block() as block:
        @block.tensor
        def _(t):
            for k in range(16):
                mi = t.transpose(ps_aux[1][:, k * 8:(k + 1) * 8],
                                 lg_fm[0:8, k * 128:(k + 1) * 128],
                                 cst["ident_f"][0:8, 0:8])
            mi.then_inc(gd, 1)

        @block.vector
        def _(v):
            v.wait_ge(gd, b_gd + 1)
            v.tensor_copy(lg_tm[:, :], ps_aux[1][:, 0:128])
    p.semv[tag + "_gd"] = b_gd + 1

    sm = p.sem(tag + "_sm"); b_sm = p.semv[tag + "_sm"]
    with nc.Block() as block:
        @block.vector
        def _(v):
            v.reduce_max(scr_tm[:, 0:16],
                         lg_tm[:, :].rearrange("p (g e) -> p g e", e=8), axis=AX.X)
            mh = scr_tm[:, 0:1]
            mview = bass.AP(mh.tensor, mh.offset, [mh.ap[0], [1, 16], [0, 8]])
            v.tensor_tensor(lg_tm[:, :].rearrange("p (g e) -> p g e", e=8),
                            lg_tm[:, :].rearrange("p (g e) -> p g e", e=8),
                            mview, ALU.subtract).then_inc(sm, 1)

        @block.scalar
        def _(s):
            s.wait_ge(sm, b_sm + 1)
            s.activation(lg_tm[:, :], lg_tm[:, :], AF.Exp).then_inc(sm, 1)

        @block.vector
        def _(v):
            v.wait_ge(sm, b_sm + 2)
            v.reduce_sum(scr_tm[:, 0:16],
                         lg_tm[:, :].rearrange("p (g e) -> p g e", e=8), axis=AX.X)
            v.drain()
            v.reciprocal(scr_tm[:, 16:32], scr_tm[:, 0:16])
            v.drain()
            dh = scr_tm[:, 16:17]
            dview = bass.AP(dh.tensor, dh.offset, [dh.ap[0], [1, 16], [0, 8]])
            v.tensor_tensor(r_tm[:, :].rearrange("p (g e) -> p g e", e=8),
                            lg_tm[:, :].rearrange("p (g e) -> p g e", e=8),
                            dview, ALU.mult).then_inc(sm, 1)
    p.semv[tag + "_sm"] = b_sm + 3

    b_gd = p.semv[tag + "_gd"]; b_ev = p.semv[tag + "_ev"]
    with nc.Block() as block:
        @block.tensor
        def _(t):
            for g4 in range(4):
                if g4 >= 1:
                    t.wait_ge(ev, b_ev + g4)
                for k4 in range(4):
                    k = g4 * 4 + k4
                    mi = t.transpose(ps_aux[0][0:8, k4 * 128:(k4 + 1) * 128],
                                     r_tm[:, k * 8:(k + 1) * 8],
                                     cst["ident_f"][:, :])
                mi.then_inc(gd, 1)

        @block.vector
        def _(v):
            for g4 in range(4):
                v.wait_ge(gd, b_gd + g4 + 1)
                v.tensor_copy(r_fmT[0:8, g4 * TW:(g4 + 1) * TW],
                              ps_aux[0][0:8, 0:TW]).then_inc(ev, 1)
    p.semv[tag + "_gd"] = b_gd + 4
    p.semv[tag + "_ev"] = b_ev + 4


def emit_upass(p: Prog, tag, ics_sb, r_fmT, h_sb, ps_aux, cst):
    nc = p.nc
    ism = p.sem(tag + "_ism"); b_ism = p.semv[tag + "_ism"]
    ut = p.sem(tag + "_ut"); b_ut = p.semv[tag + "_ut"]

    with nc.Block() as block:
        @block.tensor
        def _(t):
            for c in range(DC):
                for j in range(NT):
                    idx = c * NT + j
                    if idx >= 2:
                        t.wait_ge(ut, b_ut + idx - 1)
                    t.matmul(ps_aux[idx % 2][:, :],
                             ics_sb[0:8, c * 128:(c + 1) * 128],
                             r_fmT[0:8, j * TW:(j + 1) * TW],
                             start=True, stop=True).then_inc(ism, 1)

        @block.vector
        def _(v):
            for c in range(DC):
                for j in range(NT):
                    idx = c * NT + j
                    v.wait_ge(ism, b_ism + idx + 1)
                    sl = slice(c * T + j * TW, c * T + (j + 1) * TW)
                    v.tensor_tensor(h_sb[:, sl], h_sb[:, sl],
                                    ps_aux[idx % 2][:, :], ALU.mult).then_inc(ut, 1)
    p.semv[tag + "_ism"] = b_ism + DC * NT
    p.semv[tag + "_ut"] = b_ut + DC * NT


def emit_mm_l0f(p: Prog, tag, w_ap, biasf_sb, ocs_sb, r_fmT, h_sb, a_out_ap,
                cst, ps_mm, ps_os, ps_s1, ps_s2, wraw, os_sb, tev, atmp, sqb):
    nc = p.nc
    wl = p.sem2(tag + "_wl"); b_wl = [p.semv[tag + "_wl0"], p.semv[tag + "_wl1"]]
    spd = p.sem2(tag + "_spd"); b_spd = [p.semv[tag + "_spd0"], p.semv[tag + "_spd1"]]
    wp = p.sem(tag + "_wp"); b_wp = p.semv[tag + "_wp"]
    mmd = p.sem(tag + "_mmd"); b_mmd = p.semv[tag + "_mmd"]
    osd = p.sem(tag + "_osd"); b_osd = p.semv[tag + "_osd"]
    osc = p.sem(tag + "_osc"); b_osc = p.semv[tag + "_osc"]
    tvd = p.sem(tag + "_tvd"); b_tvd = p.semv[tag + "_tvd"]
    acd = p.sem(tag + "_acd"); b_acd = p.semv[tag + "_acd"]
    sqd = p.sem(tag + "_sqd"); b_sqd = p.semv[tag + "_sqd"]
    std = p.sem(tag + "_std"); b_std = p.semv[tag + "_std"]
    wlt = lambda o: b_wl[o % 2] + 16 * (o // 2 + 1)
    spt = lambda ti: b_spd[ti % 2] + 16 * (ti // 2 + 1)
    NTILE = DC * NT

    with nc.Block() as block:
        @block.gpsimd
        def _(g):
            for o in range(DC):
                if o >= 2:
                    g.wait_ge(mmd, b_mmd + (o - 1) * NT)
                g.dma_start(out=wraw[o % 2][:, :],
                            in_=w_ap[o * 128:(o + 1) * 128, :]).then_inc(wl[o % 2], 16)

        @block.scalar
        def _(s):
            def evac(ti):
                o = ti // NT
                s.wait_ge(tvd, b_tvd + ti + 1)
                if ti >= 2:
                    s.wait_ge(spd[ti % 2], spt(ti - 2))
                    s.wait_ge(std, b_std + ti - 1)
                s.activation(atmp[ti % 2][:, :], tev[ti % 2][:, :],
                             AF.Identity, bias=biasf_sb[:, o:o + 1],
                             scale=1.0).then_inc(acd, 1)
                s.activation(sqb[ti % 2][:, :], atmp[ti % 2][:, :],
                             AF.Square).then_inc(sqd, 1)

            for o in range(DC):
                if o >= 2:
                    for j in range(NT):
                        evac((o - 2) * NT + j)
                s.wait_ge(wl[o % 2], wlt(o))
                s.activation(wraw[o % 2][:, :], wraw[o % 2][:, :],
                             AF.Sign).then_inc(wp, 1)
            for ti in range((DC - 2) * NT, DC * NT):
                evac(ti)

        @block.tensor
        def _(t):
            for o in range(DC):
                t.wait_ge(wp, b_wp + o + 1)
                for j in range(NT):
                    ti = o * NT + j
                    t.wait_ge(osc, b_osc + ti)
                    t.matmul(ps_os[:, :],
                             ocs_sb[0:8, o * 128:(o + 1) * 128],
                             r_fmT[0:8, j * TW:(j + 1) * TW],
                             start=True, stop=True).then_inc(osd, 1)
                    if ti >= 2:
                        t.wait_ge(tvd, b_tvd + ti - 1)
                    for c in range(DC):
                        mi = t.matmul(ps_mm[ti % 2][:, :],
                                      wraw[o % 2][:, c * 128:(c + 1) * 128],
                                      h_sb[:, c * T + j * TW:c * T + (j + 1) * TW],
                                      start=(c == 0), stop=(c == DC - 1))
                    mi.then_inc(mmd, 1)
                    if ti >= 1:
                        op, jp = (ti - 1) // NT, (ti - 1) % NT
                        t.wait_ge(sqd, b_sqd + ti)
                        t.matmul(ps_s1[32 * jp:32 * jp + 32, :], cst["ones32"][:, :],
                                 atmp[(ti - 1) % 2][:, :],
                                 start=(op == 0), stop=(op == DC - 1),
                                 tile_position=(0, 32 * jp), skip_group_check=True)
                        t.matmul(ps_s2[32 * jp:32 * jp + 32, :], cst["ones32"][:, :],
                                 sqb[(ti - 1) % 2][:, :],
                                 start=(op == 0), stop=(op == DC - 1),
                                 tile_position=(0, 32 * jp),
                                 skip_group_check=True).then_inc(std, 1)
            ti = NTILE
            op, jp = (ti - 1) // NT, (ti - 1) % NT
            t.wait_ge(sqd, b_sqd + ti)
            t.matmul(ps_s1[32 * jp:32 * jp + 32, :], cst["ones32"][:, :],
                     atmp[(ti - 1) % 2][:, :], start=(op == 0), stop=True,
                     tile_position=(0, 32 * jp), skip_group_check=True)
            t.matmul(ps_s2[32 * jp:32 * jp + 32, :], cst["ones32"][:, :],
                     sqb[(ti - 1) % 2][:, :], start=(op == 0), stop=True,
                     tile_position=(0, 32 * jp),
                     skip_group_check=True).then_inc(std, 1)

        @block.vector
        def _(v):
            for o in range(DC):
                for j in range(NT):
                    ti = o * NT + j
                    v.wait_ge(osd, b_osd + ti + 1)
                    if ti >= 2:
                        v.wait_ge(tvd, b_tvd + ti - 1)
                    v.tensor_copy(os_sb[ti % 2][:, :], ps_os[:, :]).then_inc(osc, 1)
                    v.wait_ge(mmd, b_mmd + ti + 1)
                    if ti >= 2:
                        v.wait_ge(acd, b_acd + ti - 1)
                    v.tensor_tensor(tev[ti % 2][:, :], ps_mm[ti % 2][:, :],
                                    os_sb[ti % 2][:, :], ALU.mult).then_inc(tvd, 1)

        @block.sync
        def _(sync):
            for o in range(DC):
                for j in range(NT):
                    ti = o * NT + j
                    sync.wait_ge(acd, b_acd + ti + 1)
                    sync.dma_start(out=a_out_ap[o * 128:(o + 1) * 128,
                                                j * TW:(j + 1) * TW],
                                   in_=atmp[ti % 2][:, :]).then_inc(spd[ti % 2], 16)
            sync.wait_ge(spd[0], b_spd[0] + 16 * (NTILE // 2))
            sync.wait_ge(spd[1], b_spd[1] + 16 * (NTILE // 2))

    for k in range(2):
        p.semv[tag + f"_wl{k}"] = b_wl[k] + 16 * (DC // 2)
        p.semv[tag + f"_spd{k}"] = b_spd[k] + 16 * (NTILE // 2)
    p.semv[tag + "_wp"] = b_wp + DC
    p.semv[tag + "_mmd"] = b_mmd + NTILE
    p.semv[tag + "_osd"] = b_osd + NTILE
    p.semv[tag + "_osc"] = b_osc + NTILE
    p.semv[tag + "_tvd"] = b_tvd + NTILE
    p.semv[tag + "_acd"] = b_acd + NTILE
    p.semv[tag + "_sqd"] = b_sqd + NTILE
    p.semv[tag + "_std"] = b_std + NTILE


def emit_deqproj(p: Prog, tag, codes_ap, sc_ap, NOUT, mode, h_sb, cst, bufs,
                 out_ap, ps_mm, ps_aux, psbt=None, dst_fn=None):
    nc = p.nc
    cl = p.sem2(tag + "_cl"); b_cl = [p.semv[tag + "_cl0"], p.semv[tag + "_cl1"]]
    sl = p.sem(tag + "_sl"); b_sl = p.semv[tag + "_sl"]
    spd = p.sem2(tag + "_spd"); b_spd = [p.semv[tag + "_spd0"], p.semv[tag + "_spd1"]]
    acp = p.sem(tag + "_acp"); b_acp = p.semv[tag + "_acp"]
    bcm = p.sem(tag + "_bcm"); b_bcm = p.semv[tag + "_bcm"]
    dqd = p.sem(tag + "_dqd"); b_dqd = p.semv[tag + "_dqd"]
    mmd = p.sem(tag + "_mmd"); b_mmd = p.semv[tag + "_mmd"]
    acd = p.sem(tag + "_acd"); b_acd = p.semv[tag + "_acd"]
    rpd = p.sem(tag + "_rpd"); b_rpd = p.semv[tag + "_rpd"]
    trd = p.sem(tag + "_trd"); b_trd = p.semv[tag + "_trd"]
    wbf, scrow = bufs["wbf"], bufs["scrow"]
    et, esw, e1, e2, er = bufs["et"], bufs["esw"], bufs["e1"], bufs["e2"], bufs["er"]
    COS, SSIN = bufs.get("COS"), bufs.get("SSIN")
    clt = lambda o: b_cl[o % 2] + 16 * (o // 2 + 1)
    spt = lambda ti: b_spd[ti % 2] + 16 * (ti // 2 + 1)
    NTILE = NOUT * NT

    with nc.Block() as block:
        @block.gpsimd
        def _(g):
            for o in range(NOUT):
                if o >= 2:
                    g.wait_ge(mmd, b_mmd + (o - 1) * NT)
                g.dma_start(out=wbf[o % 2][:, :],
                            in_=codes_ap[o * 128:(o + 1) * 128, :]).then_inc(cl[o % 2], 16)

        @block.sync
        def _(sync):
            def store(ti):
                o, j = ti // NT, ti % NT
                sync.wait_ge(rpd, b_rpd + ti + 1)
                if dst_fn is not None:
                    dst = dst_fn(o, j)
                    srcv = (er[ti % 2][:, :].rearrange("q (k f) -> q k f", k=4)
                            if mode == "vtrans" else er[ti % 2][:, :])
                    sync.dma_start(out=dst, in_=srcv).then_inc(spd[ti % 2], 16)
                elif mode == "rope":
                    sync.dma_start(out=out_ap[o * 128:(o + 1) * 128,
                                              j * TW:(j + 1) * TW],
                                   in_=er[ti % 2][:, :]).then_inc(spd[ti % 2], 16)
                else:
                    dst = out_ap[j * TW:(j + 1) * TW,
                                 o * 128:(o + 1) * 128].rearrange(
                                     "(k q) f -> q k f", q=128)
                    sync.dma_start(out=dst, in_=er[ti % 2][:, :].rearrange(
                        "q (k f) -> q k f", k=4)).then_inc(spd[ti % 2], 16)

            for o in range(NOUT):
                if o >= 2:
                    for j in range(NT):
                        store((o - 2) * NT + j)
                if o >= 1:
                    sync.wait_ge(bcm, b_bcm + 8 * o)
                sync.dma_start(out=scrow[:, :],
                               in_=sc_ap[o:o + 1, :]).then_inc(sl, 16)
            for ti in range((NOUT - 2) * NT, NOUT * NT):
                store(ti)
            sync.wait_ge(spd[0], b_spd[0] + 16 * ((NTILE + 1) // 2))
            sync.wait_ge(spd[1], b_spd[1] + 16 * (NTILE // 2))

        @block.scalar
        def _(s):
            def evac(ti):
                s.wait_ge(mmd, b_mmd + ti + 1)
                if ti >= 2:
                    s.wait_ge(rpd, b_rpd + ti - 1)
                s.activation(et[ti % 2][:, :], ps_mm[ti % 2][:, :],
                             AF.Identity).then_inc(acd, 1)

            for o in range(NOUT):
                if o >= 2:
                    for j in range(NT):
                        evac((o - 2) * NT + j)
                s.wait_ge(cl[o % 2], clt(o))
                s.activation(wbf[o % 2][:, :], wbf[o % 2][:, :],
                             AF.Identity, bias=cst["neg8"][:, :],
                             scale=1.0).then_inc(acp, 1)
            for ti in range((NOUT - 2) * NT, NOUT * NT):
                evac(ti)

        @block.tensor
        def _(t):
            for o in range(NOUT):
                t.wait_ge(sl, b_sl + 16 * (o + 1))
                for sidx in range(8):
                    if o * 8 + sidx >= 2:
                        t.wait_ge(dqd, b_dqd + o * 8 + sidx - 1)
                    t.matmul(ps_aux[(o * 8 + sidx) % 2][:, :], cst["ones1"][:, :],
                             scrow[0:1, sidx * TW:(sidx + 1) * TW],
                             start=True, stop=True).then_inc(bcm, 1)
                for j in range(NT):
                    ti = o * NT + j
                    t.wait_ge(dqd, b_dqd + 8 * (o + 1))
                    if ti >= 2:
                        t.wait_ge(acd, b_acd + ti - 1)
                    for c in range(DC):
                        mi = t.matmul(ps_mm[ti % 2][:, :],
                                      wbf[o % 2][:, c * 128:(c + 1) * 128],
                                      h_sb[:, c * T + j * TW:c * T + (j + 1) * TW],
                                      start=(c == 0), stop=(c == DC - 1))
                    mi.then_inc(mmd, 1)
                    if mode == "vtrans":
                        t.wait_ge(acd, b_acd + ti + 1)
                        t.wait_ge(rpd, b_rpd + max(ti - 1, 0))
                        for k4 in range(4):
                            mi2 = t.transpose(psbt[ti % 2][:, k4 * 128:(k4 + 1) * 128],
                                              et[ti % 2][:, k4 * 128:(k4 + 1) * 128],
                                              cst["ident_bf"][:, :])
                        mi2.then_inc(trd, 1)

        @block.vector
        def _(v):
            for o in range(NOUT):
                for sidx in range(8):
                    si = o * 8 + sidx
                    v.wait_ge(bcm, b_bcm + si + 1)
                    v.wait_ge(acp, b_acp + o + 1)
                    sslice = slice(sidx * TW, (sidx + 1) * TW)
                    v.tensor_tensor(wbf[o % 2][:, sslice], wbf[o % 2][:, sslice],
                                    ps_aux[si % 2][:, :], ALU.mult).then_inc(dqd, 1)
                for j in range(NT):
                    ti = o * NT + j
                    if mode == "rope":
                        v.wait_ge(acd, b_acd + ti + 1)
                        if ti >= 2:
                            v.wait_ge(spd[ti % 2], spt(ti - 2))
                        v.stream_shuffle(esw[:, :], et[ti % 2][:, :],
                                         [(k + 1 if k % 2 == 0 else k - 1)
                                          for k in range(32)])
                        v.tensor_tensor(e1[:, :], et[ti % 2][:, :],
                                        COS[:, j * TW:(j + 1) * TW], ALU.mult)
                        v.tensor_tensor(e2[:, :], esw[:, :],
                                        SSIN[:, j * TW:(j + 1) * TW], ALU.mult)
                        v.tensor_tensor(er[ti % 2][:, :], e1[:, :], e2[:, :],
                                        ALU.add).then_inc(rpd, 1)
                    else:
                        v.wait_ge(trd, b_trd + ti + 1)
                        if ti >= 2:
                            v.wait_ge(spd[ti % 2], spt(ti - 2))
                        v.tensor_copy(er[ti % 2][:, :],
                                      psbt[ti % 2][:, :]).then_inc(rpd, 1)

    for k in range(2):
        p.semv[tag + f"_cl{k}"] = b_cl[k] + 16 * ((NOUT + 1 - k) // 2)
        p.semv[tag + f"_spd{k}"] = b_spd[k] + 16 * ((NTILE + 1 - k) // 2)
    p.semv[tag + "_sl"] = b_sl + 16 * NOUT
    p.semv[tag + "_acp"] = b_acp + NOUT
    p.semv[tag + "_bcm"] = b_bcm + 8 * NOUT
    p.semv[tag + "_dqd"] = b_dqd + 8 * NOUT
    p.semv[tag + "_mmd"] = b_mmd + NTILE
    p.semv[tag + "_acd"] = b_acd + NTILE
    p.semv[tag + "_rpd"] = b_rpd + NTILE
    if mode == "vtrans":
        p.semv[tag + "_trd"] = b_trd + NTILE


SCALE = 128 ** -0.5


def attn_schedule():
    tiles = []
    ch_c = ev_c = gr_c = 0
    for gl in range(4):
        for h4 in range(4):
            for qt in range(32):
                L = qt * 128 + 128
                chunks = []
                k = 0
                while k < L:
                    size = min(512, L - k)
                    chunks.append((k, size))
                    k += size
                nsub = L // 128
                ngr = (nsub + 7) // 8
                tiles.append(dict(gl=gl, hloc=gl * 4 + h4, qt=qt, L=L,
                                  chunks=chunks, nsub=nsub, ngr=ngr,
                                  ch_base=ch_c, ev_base=ev_c, gr_base=gr_c))
                ch_c += len(chunks)
                ev_c += len(chunks) + 1
                gr_c += ngr
    return tiles, dict(chunks=ch_c, evacs=ev_c, groups=gr_c, tiles=len(tiles))


def emit_attention(p: Prog, tag, q_ap, kf_ap, vf_ap, att_sb, cst, bufs,
                   ps_s, ps_av, psbt):
    nc = p.nc
    sched, tot = attn_schedule()
    NTL = tot["tiles"]
    NSB = len(ps_s)
    kgl = p.sem(tag + "_kgl"); b_kgl = p.semv[tag + "_kgl"]
    vgl = p.sem(tag + "_vgl"); b_vgl = p.semv[tag + "_vgl"]
    qld = p.sem(tag + "_qld"); b_qld = p.semv[tag + "_qld"]
    smd = p.sem(tag + "_smd"); b_smd = p.semv[tag + "_smd"]
    sed = p.sem(tag + "_sed"); b_sed = p.semv[tag + "_sed"]
    nmx = p.sem(tag + "_nmx"); b_nmx = p.semv[tag + "_nmx"]
    exd = p.sem(tag + "_exd"); b_exd = p.semv[tag + "_exd"]
    pnd = p.sem(tag + "_pnd"); b_pnd = p.semv[tag + "_pnd"]
    trd = p.sem(tag + "_trd"); b_trd = p.semv[tag + "_trd"]
    ptd = p.sem(tag + "_ptd"); b_ptd = p.semv[tag + "_ptd"]
    avg = p.sem(tag + "_avg"); b_avg = p.semv[tag + "_avg"]
    aed = p.sem(tag + "_aed"); b_aed = p.semv[tag + "_aed"]
    kg, vg, qh = bufs["kg"], bufs["vg"], bufs["qh"]
    Srow, Prow, PT = bufs["Srow"], bufs["Prow"], bufs["PT"]
    TRI, cmax = bufs["TRI"], bufs["cmax"]
    negmax, rowsum, dinv = bufs["negmax"], bufs["rowsum"], bufs["dinv"]

    ch_evdone = []
    for t in sched:
        nch = len(t["chunks"])
        for ci in range(nch):
            if ci == nch - 1:
                ch_evdone.append(t["ev_base"] + nch + 1)
            else:
                ch_evdone.append(t["ev_base"] + ci + 1)

    def gr_done(ti):
        return sched[ti]["gr_base"] + sched[ti]["ngr"]

    with nc.Block() as block:
        @block.sync
        def _(sync):
            for ti, t in enumerate(sched):
                gl, hloc, qt = t["gl"], t["hloc"], t["qt"]
                if qt == 0:
                    if hloc >= 1:
                        sync.wait_ge(smd, b_smd + sched[32 * hloc]["ch_base"])
                    sync.dma_start(out=qh[:, :],
                                   in_=q_ap[hloc * 128:(hloc + 1) * 128, :]
                                   ).then_inc(qld, 16)
                if qt == 0 and hloc % 4 == 0:
                    sync.dma_start(out=kg[:, :],
                                   in_=kf_ap[gl * 128:(gl + 1) * 128, :]
                                   ).then_inc(kgl, 16)
                    if gl >= 1:
                        sync.wait_ge(avg, b_avg + gr_done(128 * gl - 1))
                    sync.dma_start(out=vg[:, :],
                                   in_=vf_ap[:, gl * 128:(gl + 1) * 128].rearrange(
                                       "(i q) f -> q i f", q=128)
                                   ).then_inc(vgl, 16)

        @block.tensor
        def _(te):
            def avgrp(tj, G):
                t = sched[tj]
                gg = t["gr_base"] + G
                te.wait_ge(ptd, b_ptd + gg + 1)
                subs = range(G * 8, min(G * 8 + 8, t["nsub"]))
                for si, sub in enumerate(subs):
                    mi = te.matmul(ps_av[tj % 2][:, 0:128],
                                   vg[:, sub * 128:(sub + 1) * 128],
                                   PT[gg % 2][:, si * 128:(si + 1) * 128],
                                   start=(sub == 0), stop=(sub == t["nsub"] - 1))
                mi.then_inc(avg, 1)

            def trav(tj):
                t = sched[tj]
                te.wait_ge(pnd, b_pnd + tj + 1)
                if tj >= 2:
                    te.wait_ge(aed, b_aed + tj - 1)
                if t["qt"] == 0 and t["hloc"] % 4 == 0:
                    te.wait_ge(vgl, b_vgl + 16 * (t["gl"] + 1))
                for G in range(t["ngr"]):
                    gg = t["gr_base"] + G
                    if gg >= 2:
                        te.wait_ge(ptd, b_ptd + gg - 1)
                    subs = range(G * 8, min(G * 8 + 8, t["nsub"]))
                    for si, sub in enumerate(subs):
                        mi = te.transpose(psbt[gg % 2][:, si * 128:(si + 1) * 128],
                                          Prow[tj % 2][:, sub * 128:(sub + 1) * 128],
                                          cst["ident_bf"][:, :])
                    mi.then_inc(trd, 1)
                    if G >= 1:
                        avgrp(tj, G - 1)
                avgrp(tj, t["ngr"] - 1)

            for ti, t in enumerate(sched):
                gl, hloc, qt = t["gl"], t["hloc"], t["qt"]
                if qt == 0 and hloc % 4 == 0:
                    te.wait_ge(kgl, b_kgl + 16 * (gl + 1))
                if qt == 0:
                    te.wait_ge(qld, b_qld + 16 * (hloc + 1))
                for ci, (k0, sz) in enumerate(t["chunks"]):
                    gc = t["ch_base"] + ci
                    if gc >= NSB:
                        te.wait_ge(sed, b_sed + ch_evdone[gc - NSB])
                    te.matmul(ps_s[gc % NSB][:, 0:sz],
                              qh[:, qt * 128:(qt + 1) * 128],
                              kg[:, k0:k0 + sz],
                              start=True, stop=True).then_inc(smd, 1)
                if ti >= 1:
                    trav(ti - 1)
            trav(NTL - 1)

        @block.vector
        def _(v):
            def norm(tj):
                v.wait_ge(exd, b_exd + tj + 1)
                v.tensor_scalar(Prow[tj % 2][:, 0:sched[tj]["L"]],
                                Prow[tj % 2][:, 0:sched[tj]["L"]],
                                dinv[tj % 2][:, :], None,
                                ALU.mult).then_inc(pnd, 1)

            for ti, t in enumerate(sched):
                nch = len(t["chunks"])
                if ti >= 1:
                    norm(ti - 1)
                for ci, (k0, sz) in enumerate(t["chunks"]):
                    gc = t["ch_base"] + ci
                    v.wait_ge(smd, b_smd + gc + 1)
                    if ti >= 2 and ci == 0:
                        v.wait_ge(exd, b_exd + ti - 1)
                    v.tensor_scalar(Srow[ti % 2][:, k0:k0 + sz],
                                    ps_s[gc % NSB][:, 0:sz], SCALE, None,
                                    ALU.mult, ALU.max,
                                    accum_out=cmax[:, ci:ci + 1]).then_inc(sed, 1)
                    if ci == nch - 1:
                        v.scalar_tensor_tensor(
                            Srow[ti % 2][:, t["L"] - 128:t["L"]],
                            ps_s[gc % NSB][:, sz - 128:sz], SCALE,
                            TRI[:, :], ALU.mult, ALU.add).then_inc(sed, 1)
                if ti >= 2:
                    v.wait_ge(exd, b_exd + ti - 1)
                v.reduce_max(negmax[ti % 2][:, :], cmax[:, 0:nch], axis=AX.X,
                             negate=True).then_inc(nmx, 1)
            norm(NTL - 1)

        @block.scalar
        def _(s):
            def pt_copies(tj):
                t = sched[tj]
                for G in range(t["ngr"]):
                    gg = t["gr_base"] + G
                    s.wait_ge(trd, b_trd + gg + 1)
                    if gg >= 2:
                        s.wait_ge(avg, b_avg + gg - 1)
                    nsubs = min(8, t["nsub"] - G * 8)
                    s.activation(PT[gg % 2][:, 0:nsubs * 128],
                                 psbt[gg % 2][:, 0:nsubs * 128],
                                 AF.Identity).then_inc(ptd, 1)

            def av_evac(tj):
                tp = sched[tj]
                s.wait_ge(avg, b_avg + gr_done(tj))
                s.activation(att_sb[:, tp["hloc"] * 4096 + tp["qt"] * 128:
                                    tp["hloc"] * 4096 + (tp["qt"] + 1) * 128],
                             ps_av[tj % 2][:, 0:128],
                             AF.Identity).then_inc(aed, 1)

            for ti, t in enumerate(sched):
                if ti >= 1:
                    pt_copies(ti - 1)
                    av_evac(ti - 1)
                s.wait_ge(nmx, b_nmx + ti + 1)
                s.wait_ge(sed, b_sed + t["ev_base"] + len(t["chunks"]) + 1)
                if ti >= 2:
                    s.wait_ge(trd, b_trd + gr_done(ti - 2))
                    s.wait_ge(pnd, b_pnd + ti - 1)
                s.activation(Prow[ti % 2][:, 0:t["L"]], Srow[ti % 2][:, 0:t["L"]],
                             AF.Exp, bias=negmax[ti % 2][:, :], scale=1.0,
                             accum_out=rowsum[ti % 2][:, :])
                s.activation(negmax[ti % 2][:, :], rowsum[ti % 2][:, :], AF.Ln)
                s.activation(dinv[ti % 2][:, :], negmax[ti % 2][:, :],
                             AF.Exp, scale=-1.0).then_inc(exd, 1)
            pt_copies(NTL - 1)
            av_evac(NTL - 1)

    p.semv[tag + "_kgl"] = b_kgl + 16 * 4
    p.semv[tag + "_vgl"] = b_vgl + 16 * 4
    p.semv[tag + "_qld"] = b_qld + 16 * 16
    p.semv[tag + "_smd"] = b_smd + tot["chunks"]
    p.semv[tag + "_sed"] = b_sed + tot["evacs"]
    p.semv[tag + "_nmx"] = b_nmx + NTL
    p.semv[tag + "_exd"] = b_exd + NTL
    p.semv[tag + "_pnd"] = b_pnd + NTL
    p.semv[tag + "_trd"] = b_trd + tot["groups"]
    p.semv[tag + "_ptd"] = b_ptd + tot["groups"]
    p.semv[tag + "_avg"] = b_avg + tot["groups"]
    p.semv[tag + "_aed"] = b_aed + NTL


def emit_oproj(p: Prog, tag, codes_ap, sc_ap, h_sb, a_out_ap, cst,
               ps_mm, ps_aux, ps_s1, ps_s2, bufs):
    nc = p.nc
    cl = p.sem2(tag + "_cl"); b_cl = [p.semv[tag + "_cl0"], p.semv[tag + "_cl1"]]
    sl = p.sem(tag + "_sl"); b_sl = p.semv[tag + "_sl"]
    spd = p.sem2(tag + "_spd"); b_spd = [p.semv[tag + "_spd0"], p.semv[tag + "_spd1"]]
    acp = p.sem(tag + "_acp"); b_acp = p.semv[tag + "_acp"]
    bcm = p.sem(tag + "_bcm"); b_bcm = p.semv[tag + "_bcm"]
    dqd = p.sem(tag + "_dqd"); b_dqd = p.semv[tag + "_dqd"]
    mmd = p.sem(tag + "_mmd"); b_mmd = p.semv[tag + "_mmd"]
    acd = p.sem(tag + "_acd"); b_acd = p.semv[tag + "_acd"]
    sqd = p.sem(tag + "_sqd"); b_sqd = p.semv[tag + "_sqd"]
    std = p.sem(tag + "_std"); b_std = p.semv[tag + "_std"]
    wbf, scrow = bufs["wbf"], bufs["scrow"]
    atmp, sqb = bufs["atmp"], bufs["sqb"]
    clt = lambda o: b_cl[o % 2] + 16 * (o // 2 + 1)
    spt = lambda ti: b_spd[ti % 2] + 16 * (ti // 2 + 1)
    NTILE = DC * NT

    with nc.Block() as block:
        @block.gpsimd
        def _(g):
            for o in range(DC):
                if o >= 2:
                    g.wait_ge(mmd, b_mmd + (o - 1) * NT)
                g.dma_start(out=wbf[o % 2][:, :],
                            in_=codes_ap[o * 128:(o + 1) * 128, :]).then_inc(cl[o % 2], 16)

        @block.sync
        def _(sync):
            def spill(ti):
                o, j = ti // NT, ti % NT
                sync.wait_ge(acd, b_acd + ti + 1)
                sync.dma_start(out=a_out_ap[o * 128:(o + 1) * 128,
                                            j * TW:(j + 1) * TW],
                               in_=atmp[ti % 2][:, :]).then_inc(spd[ti % 2], 16)

            for o in range(DC):
                if o >= 2:
                    for j in range(NT):
                        spill((o - 2) * NT + j)
                if o >= 1:
                    sync.wait_ge(bcm, b_bcm + 8 * o)
                sync.dma_start(out=scrow[:, :],
                               in_=sc_ap[o:o + 1, :]).then_inc(sl, 16)
            for ti in range((DC - 2) * NT, DC * NT):
                spill(ti)
            sync.wait_ge(spd[0], b_spd[0] + 16 * (NTILE // 2))
            sync.wait_ge(spd[1], b_spd[1] + 16 * (NTILE // 2))

        @block.scalar
        def _(s):
            def evac(ti):
                s.wait_ge(mmd, b_mmd + ti + 1)
                if ti >= 2:
                    s.wait_ge(spd[ti % 2], spt(ti - 2))
                    s.wait_ge(std, b_std + ti - 1)
                s.activation(atmp[ti % 2][:, :], ps_mm[ti % 2][:, :],
                             AF.Identity).then_inc(acd, 1)
                s.activation(sqb[ti % 2][:, :], atmp[ti % 2][:, :],
                             AF.Square).then_inc(sqd, 1)

            for o in range(DC):
                if o >= 2:
                    for j in range(NT):
                        evac((o - 2) * NT + j)
                s.wait_ge(cl[o % 2], clt(o))
                s.activation(wbf[o % 2][:, :], wbf[o % 2][:, :],
                             AF.Identity, bias=cst["neg8"][:, :],
                             scale=1.0).then_inc(acp, 1)
            for ti in range((DC - 2) * NT, DC * NT):
                evac(ti)

        @block.tensor
        def _(t):
            for o in range(DC):
                t.wait_ge(sl, b_sl + 16 * (o + 1))
                for sidx in range(8):
                    if o * 8 + sidx >= 2:
                        t.wait_ge(dqd, b_dqd + o * 8 + sidx - 1)
                    t.matmul(ps_aux[(o * 8 + sidx) % 2][:, :], cst["ones1"][:, :],
                             scrow[0:1, sidx * TW:(sidx + 1) * TW],
                             start=True, stop=True).then_inc(bcm, 1)
                for j in range(NT):
                    ti = o * NT + j
                    t.wait_ge(dqd, b_dqd + 8 * (o + 1))
                    if ti >= 2:
                        t.wait_ge(acd, b_acd + ti - 1)
                    for c in range(DC):
                        mi = t.matmul(ps_mm[ti % 2][:, :],
                                      wbf[o % 2][:, c * 128:(c + 1) * 128],
                                      h_sb[:, c * T + j * TW:c * T + (j + 1) * TW],
                                      start=(c == 0), stop=(c == DC - 1))
                    mi.then_inc(mmd, 1)
                    if ti >= 1:
                        op, jp = (ti - 1) // NT, (ti - 1) % NT
                        t.wait_ge(sqd, b_sqd + ti)
                        t.matmul(ps_s1[32 * jp:32 * jp + 32, :], cst["ones32"][:, :],
                                 atmp[(ti - 1) % 2][:, :],
                                 start=(op == 0), stop=(op == DC - 1),
                                 tile_position=(0, 32 * jp), skip_group_check=True)
                        t.matmul(ps_s2[32 * jp:32 * jp + 32, :], cst["ones32"][:, :],
                                 sqb[(ti - 1) % 2][:, :],
                                 start=(op == 0), stop=(op == DC - 1),
                                 tile_position=(0, 32 * jp),
                                 skip_group_check=True).then_inc(std, 1)
            ti = NTILE
            op, jp = (ti - 1) // NT, (ti - 1) % NT
            t.wait_ge(sqd, b_sqd + ti)
            t.matmul(ps_s1[32 * jp:32 * jp + 32, :], cst["ones32"][:, :],
                     atmp[(ti - 1) % 2][:, :], start=(op == 0), stop=True,
                     tile_position=(0, 32 * jp), skip_group_check=True)
            t.matmul(ps_s2[32 * jp:32 * jp + 32, :], cst["ones32"][:, :],
                     sqb[(ti - 1) % 2][:, :], start=(op == 0), stop=True,
                     tile_position=(0, 32 * jp),
                     skip_group_check=True).then_inc(std, 1)

        @block.vector
        def _(v):
            for o in range(DC):
                for sidx in range(8):
                    si = o * 8 + sidx
                    v.wait_ge(bcm, b_bcm + si + 1)
                    v.wait_ge(acp, b_acp + o + 1)
                    sslice = slice(sidx * TW, (sidx + 1) * TW)
                    v.tensor_tensor(wbf[o % 2][:, sslice], wbf[o % 2][:, sslice],
                                    ps_aux[si % 2][:, :], ALU.mult).then_inc(dqd, 1)

    for k in range(2):
        p.semv[tag + f"_cl{k}"] = b_cl[k] + 16 * (DC // 2)
        p.semv[tag + f"_spd{k}"] = b_spd[k] + 16 * (NTILE // 2)
    p.semv[tag + "_sl"] = b_sl + 16 * DC
    p.semv[tag + "_acp"] = b_acp + DC
    p.semv[tag + "_bcm"] = b_bcm + 8 * DC
    p.semv[tag + "_dqd"] = b_dqd + 8 * DC
    p.semv[tag + "_mmd"] = b_mmd + NTILE
    p.semv[tag + "_acd"] = b_acd + NTILE
    p.semv[tag + "_sqd"] = b_sqd + NTILE
    p.semv[tag + "_std"] = b_std + NTILE


def build_attn():
    p = Prog()
    q_att = p.din("q_att", [2048, 2 * T], BF16)
    k_att = p.din("k_att", [512, 2 * T], BF16)
    v_att = p.din("v_att", [2 * T, 512], BF16)
    tri = p.din("tri", [128, 128], F32)
    att_out = p.dout("att_out", [2048, 2 * T], BF16)

    com = alloc_common(p)
    cst = com["cst"]
    nc = p.nc
    with ExitStack() as stk:
        bufs = {
            "kg": p.sb("kg", [128, 2 * T], BF16, stk),
            "vg": p.sb("vg", [128, 2 * T], BF16, stk),
            "qh": p.sb("qh", [128, 2 * T], BF16, stk),
            "Srow": [p.sb(f"Srow{i}", [128, 2 * T], F32, stk) for i in range(2)],
            "Prow": [p.sb(f"Prow{i}", [128, 2 * T], BF16, stk) for i in range(2)],
            "PT": [p.sb(f"PTb{i}", [128, 1024], BF16, stk) for i in range(2)],
            "TRI": p.sb("TRI", [128, 128], F32, stk),
            "cmax": p.sb("cmax", [128, 8], F32, stk),
            "negmax": [p.sb(f"negmax{i}", [128, 1], F32, stk) for i in range(2)],
            "rowsum": [p.sb(f"rowsum{i}", [128, 1], F32, stk) for i in range(2)],
            "dinv": [p.sb(f"dinv{i}", [128, 1], F32, stk) for i in range(2)],
        }
        psbt = [p.psum(f"psbt{i}", [128, 1024], BF16, stk) for i in range(2)]
        ldt = p.sem("ldt")
        with nc.Block() as block:
            @block.sync
            def _(sync):
                sync.dma_start(out=bufs["TRI"][:, :], in_=tri).then_inc(ldt, 16)
                sync.wait_ge(ldt, p.semv["ldt"] + 16)
        p.semv["ldt"] += 16
        emit_attention(p, "at", q_att, k_att, v_att, com["h_sb"], cst, bufs,
                       com["ps_mm"] + [com["ps_s1"], com["ps_s2"]],
                       com["ps_aux"], psbt)
        std = p.sem("stda")
        with nc.Block() as block:
            @block.sync
            def _(sync):
                for hl in range(16):
                    sync.dma_start(out=att_out[hl * 128:(hl + 1) * 128, :],
                                   in_=com["h_sb"][:, hl * 4096:(hl + 1) * 4096]
                                   ).then_inc(std, 16)
                sync.wait_ge(std, p.semv["stda"] + 16 * 16)
        p.semv["stda"] += 16 * 16
    p.ctx.close()
    return p


def _tri_mask():
    r = np.arange(128)
    return np.where(r[:, None] >= r[None, :], 0.0, NEG).astype(np.float32)


def build_oproj():
    p = Prog()
    nc = p.nc
    att = p.din("att", [D, T], BF16)
    oc = p.din("oc", [D, D], I32)
    osd = p.din("osd", [DC, D], F32)
    x2 = p.din("x2", [D, T], F32)
    x3 = p.dout("x3", [D, T], F32)
    a_dram = p.dint("a_scr", [D, T], F32)

    com = alloc_common(p)
    cst = com["cst"]
    lda = p.sem("lda")
    with nc.Block() as block:
        @block.sync
        def _(sync):
            for c in range(DC):
                sync.dma_start(out=com["h_sb"][:, c * T:(c + 1) * T],
                               in_=att[c * 128:(c + 1) * 128, :]).then_inc(lda, 16)
            sync.wait_ge(lda, p.semv["lda"] + 16 * DC)
    p.semv["lda"] += 16 * DC

    with ExitStack() as stk:
        bufs = {
            "wbf": [p.sb(f"owbf{i}", [128, D], BF16, stk) for i in range(2)],
            "scrow": p.sb("oscrow", [1, D], F32, stk),
            "atmp": [p.sb(f"oatmp{i}", [128, TW], F32, stk) for i in range(2)],
            "sqb": [p.sb(f"osqb{i}", [128, TW], F32, stk) for i in range(2)],
        }
        emit_oproj(p, "mm", oc, osd, com["h_sb"], a_dram, cst,
                   com["ps_mm"], com["ps_aux"], com["ps_s1"], com["ps_s2"], bufs)
    with ExitStack() as stkln:
        alloc_ln(p, com, stkln)
        with ExitStack() as stk:
            phase_rows2_update(p, "u2", com, x2, a_dram, x3, stk, do_stats=False)
    p.ctx.close()
    return p


def emit_router(p: Prog, tag, rw_sb, h_sb, ps_aux, cst, lg_fm, lg_tm, scr_tm,
                r_tm, w_fmT):
    nc = p.nc
    gd = p.sem(tag + "_gd"); b_gd = p.semv[tag + "_gd"]
    ev = p.sem(tag + "_ev"); b_ev = p.semv[tag + "_ev"]

    with nc.Block() as block:
        @block.tensor
        def _(t):
            for j in range(NT):
                if j >= 1:
                    t.wait_ge(ev, b_ev + j)
                for c in range(DC):
                    mi = t.matmul(ps_aux[0][0:8, :], rw_sb[:, c * 8:(c + 1) * 8],
                                  h_sb[:, c * T + j * TW:c * T + (j + 1) * TW],
                                  start=(c == 0), stop=(c == DC - 1))
                mi.then_inc(gd, 1)

        @block.vector
        def _(v):
            for j in range(NT):
                v.wait_ge(gd, b_gd + j + 1)
                v.tensor_copy(lg_fm[0:8, j * TW:(j + 1) * TW],
                              ps_aux[0][0:8, :]).then_inc(ev, 1)
    p.semv[tag + "_gd"] = b_gd + NT
    p.semv[tag + "_ev"] = b_ev + NT

    b_gd = p.semv[tag + "_gd"]
    with nc.Block() as block:
        @block.tensor
        def _(t):
            for k in range(16):
                mi = t.transpose(ps_aux[1][:, k * 8:(k + 1) * 8],
                                 lg_fm[0:8, k * 128:(k + 1) * 128],
                                 cst["ident_f"][0:8, 0:8])
            mi.then_inc(gd, 1)

        @block.vector
        def _(v):
            v.wait_ge(gd, b_gd + 1)
            v.tensor_copy(lg_tm[:, :], ps_aux[1][:, 0:128])
    p.semv[tag + "_gd"] = b_gd + 1

    sm = p.sem(tag + "_sm"); b_sm = p.semv[tag + "_sm"]
    with nc.Block() as block:
        @block.vector
        def _(v):
            g3 = lambda x: x[:, :].rearrange("p (g e) -> p g e", e=8)
            mx1 = scr_tm[:, 0:16]
            mx2 = scr_tm[:, 16:32]
            mh = scr_tm[:, 0:1]
            m1view = bass.AP(mh.tensor, mh.offset, [mh.ap[0], [1, 16], [0, 8]])
            m2h = scr_tm[:, 16:17]
            m2view = bass.AP(m2h.tensor, m2h.offset, [m2h.ap[0], [1, 16], [0, 8]])
            v.reduce_max(mx1, g3(lg_tm), axis=AX.X)
            v.tensor_tensor(g3(r_tm), g3(lg_tm), m1view, ALU.is_ge)
            v.scalar_tensor_tensor(g3(r_tm), r_tm[:, :].rearrange(
                "p (g e) -> p g e", e=8), NEG, g3(lg_tm), ALU.mult, ALU.add)
            v.reduce_max(mx2, g3(r_tm), axis=AX.X)
            v.tensor_tensor(g3(lg_tm), g3(lg_tm), m1view,
                            ALU.subtract).then_inc(sm, 1)

        @block.scalar
        def _(s):
            s.wait_ge(sm, b_sm + 1)
            s.activation(lg_tm[:, :], lg_tm[:, :], AF.Exp).then_inc(sm, 1)

        @block.vector
        def _(v):
            v.wait_ge(sm, b_sm + 2)
            mx1 = scr_tm[:, 0:16]
            mx2 = scr_tm[:, 16:32]
            v.tensor_tensor(mx2, mx2, mx1, ALU.subtract).then_inc(sm, 1)

        @block.scalar
        def _(s):
            s.wait_ge(sm, b_sm + 3)
            s.activation(scr_tm[:, 16:32], scr_tm[:, 16:32],
                         AF.Exp).then_inc(sm, 1)

        @block.vector
        def _(v):
            v.wait_ge(sm, b_sm + 4)
            g3 = lambda x: x[:, :].rearrange("p (g e) -> p g e", e=8)
            e2h = scr_tm[:, 16:17]
            e2view = bass.AP(e2h.tensor, e2h.offset, [e2h.ap[0], [1, 16], [0, 8]])
            v.tensor_tensor(g3(r_tm), g3(lg_tm), e2view, ALU.is_ge)
            v.tensor_tensor(lg_tm[:, :], lg_tm[:, :], r_tm[:, :], ALU.mult)
            v.tensor_scalar(scr_tm[:, 32:48], scr_tm[:, 16:32], 1.0, None, ALU.add)
            v.drain()
            v.reciprocal(scr_tm[:, 48:64], scr_tm[:, 32:48])
            v.drain()
            dh = scr_tm[:, 48:49]
            dview = bass.AP(dh.tensor, dh.offset, [dh.ap[0], [1, 16], [0, 8]])
            v.tensor_tensor(g3(r_tm), g3(lg_tm), dview, ALU.mult).then_inc(sm, 1)
    p.semv[tag + "_sm"] = b_sm + 5

    b_gd = p.semv[tag + "_gd"]; b_ev = p.semv[tag + "_ev"]
    with nc.Block() as block:
        @block.tensor
        def _(t):
            for g4 in range(4):
                if g4 >= 1:
                    t.wait_ge(ev, b_ev + g4)
                for k4 in range(4):
                    k = g4 * 4 + k4
                    mi = t.transpose(ps_aux[0][0:8, k4 * 128:(k4 + 1) * 128],
                                     r_tm[:, k * 8:(k + 1) * 8],
                                     cst["ident_f"][:, :])
                mi.then_inc(gd, 1)

        @block.vector
        def _(v):
            for g4 in range(4):
                v.wait_ge(gd, b_gd + g4 + 1)
                v.tensor_copy(w_fmT[0:8, g4 * TW:(g4 + 1) * TW],
                              ps_aux[0][0:8, 0:TW]).then_inc(ev, 1)
    p.semv[tag + "_gd"] = b_gd + 4
    p.semv[tag + "_ev"] = b_ev + 4


def emit_mid(p: Prog, tag, lA_sb, P2_sb, w_fmT, h_sb, mid_sb, wx_sb, ps_mm,
             ps_aux, cst):
    nc = p.nc
    wxd = p.sem(tag + "_wxd"); b_wxd = p.semv[tag + "_wxd"]
    wxc = p.sem(tag + "_wxc"); b_wxc = p.semv[tag + "_wxc"]
    mmd = p.sem(tag + "_mmd"); b_mmd = p.semv[tag + "_mmd"]
    mtd = p.sem(tag + "_mtd"); b_mtd = p.semv[tag + "_mtd"]

    with nc.Block() as block:
        @block.tensor
        def _(t):
            for j in range(NT):
                if j >= 2:
                    t.wait_ge(wxc, b_wxc + j - 1)
                t.matmul(ps_aux[j % 2][:, :], P2_sb[0:8, :],
                         w_fmT[0:8, j * TW:(j + 1) * TW],
                         start=True, stop=True).then_inc(wxd, 1)
            for j in range(NT):
                if j >= 2:
                    t.wait_ge(mtd, b_mtd + j - 1)
                for c in range(DC):
                    mi = t.matmul(ps_mm[j % 2][:, :],
                                  lA_sb[:, c * 128:(c + 1) * 128],
                                  h_sb[:, c * T + j * TW:c * T + (j + 1) * TW],
                                  start=(c == 0), stop=(c == DC - 1))
                mi.then_inc(mmd, 1)

        @block.vector
        def _(v):
            for j in range(NT):
                v.wait_ge(wxd, b_wxd + j + 1)
                v.tensor_copy(wx_sb[:, j * TW:(j + 1) * TW],
                              ps_aux[j % 2][:, :]).then_inc(wxc, 1)
            for j in range(NT):
                v.wait_ge(mmd, b_mmd + j + 1)
                v.tensor_tensor(mid_sb[:, j * TW:(j + 1) * TW],
                                ps_mm[j % 2][:, :],
                                wx_sb[:, j * TW:(j + 1) * TW],
                                ALU.mult).then_inc(mtd, 1)
    p.semv[tag + "_wxd"] = b_wxd + NT
    p.semv[tag + "_wxc"] = b_wxc + NT
    p.semv[tag + "_mmd"] = b_mmd + NT
    p.semv[tag + "_mtd"] = b_mtd + NT


def emit_mm_l1f(p: Prog, tag, w_ap, lB_sb, mid_sb, h_sb, a_out_ap, cst,
                ps_mm, ps_s1, ps_s2, wraw, atmp, sqb):
    nc = p.nc
    wl = p.sem2(tag + "_wl"); b_wl = [p.semv[tag + "_wl0"], p.semv[tag + "_wl1"]]
    spd = p.sem2(tag + "_spd"); b_spd = [p.semv[tag + "_spd0"], p.semv[tag + "_spd1"]]
    mmd = p.sem(tag + "_mmd"); b_mmd = p.semv[tag + "_mmd"]
    evd = p.sem(tag + "_evd"); b_evd = p.semv[tag + "_evd"]
    sqd = p.sem(tag + "_sqd"); b_sqd = p.semv[tag + "_sqd"]
    std = p.sem(tag + "_std"); b_std = p.semv[tag + "_std"]
    wlt = lambda o: b_wl[o % 2] + 16 * (o // 2 + 1)
    spt = lambda ti: b_spd[ti % 2] + 16 * (ti // 2 + 1)
    NTILE = DC * NT

    with nc.Block() as block:
        @block.gpsimd
        def _(g):
            for o in range(DC):
                if o >= 2:
                    g.wait_ge(mmd, b_mmd + (o - 1) * NT)
                g.dma_start(out=wraw[o % 2][:, :],
                            in_=w_ap[o * 128:(o + 1) * 128, :]).then_inc(wl[o % 2], 16)

        @block.tensor
        def _(t):
            for o in range(DC):
                t.wait_ge(wl[o % 2], wlt(o))
                for j in range(NT):
                    ti = o * NT + j
                    if ti >= 2:
                        t.wait_ge(evd, b_evd + ti - 1)
                    for c in range(DC):
                        t.matmul(ps_mm[ti % 2][:, :],
                                 wraw[o % 2][:, c * 128:(c + 1) * 128],
                                 h_sb[:, c * T + j * TW:c * T + (j + 1) * TW],
                                 start=(c == 0), stop=False)
                    mi = t.matmul(ps_mm[ti % 2][:, :],
                                  lB_sb[:, o * 128:(o + 1) * 128],
                                  mid_sb[:, j * TW:(j + 1) * TW],
                                  start=False, stop=True)
                    mi.then_inc(mmd, 1)
                    if ti >= 1:
                        op, jp = (ti - 1) // NT, (ti - 1) % NT
                        t.wait_ge(sqd, b_sqd + ti)
                        t.matmul(ps_s1[32 * jp:32 * jp + 32, :], cst["ones32"][:, :],
                                 atmp[(ti - 1) % 2][:, :],
                                 start=(op == 0), stop=(op == DC - 1),
                                 tile_position=(0, 32 * jp), skip_group_check=True)
                        t.matmul(ps_s2[32 * jp:32 * jp + 32, :], cst["ones32"][:, :],
                                 sqb[(ti - 1) % 2][:, :],
                                 start=(op == 0), stop=(op == DC - 1),
                                 tile_position=(0, 32 * jp),
                                 skip_group_check=True).then_inc(std, 1)
            ti = NTILE
            op, jp = (ti - 1) // NT, (ti - 1) % NT
            t.wait_ge(sqd, b_sqd + ti)
            t.matmul(ps_s1[32 * jp:32 * jp + 32, :], cst["ones32"][:, :],
                     atmp[(ti - 1) % 2][:, :], start=(op == 0), stop=True,
                     tile_position=(0, 32 * jp), skip_group_check=True)
            t.matmul(ps_s2[32 * jp:32 * jp + 32, :], cst["ones32"][:, :],
                     sqb[(ti - 1) % 2][:, :], start=(op == 0), stop=True,
                     tile_position=(0, 32 * jp),
                     skip_group_check=True).then_inc(std, 1)

        @block.scalar
        def _(s):
            for o in range(DC):
                for j in range(NT):
                    ti = o * NT + j
                    s.wait_ge(mmd, b_mmd + ti + 1)
                    if ti >= 2:
                        s.wait_ge(spd[ti % 2], spt(ti - 2))
                        s.wait_ge(std, b_std + ti - 1)
                    s.activation(atmp[ti % 2][:, :], ps_mm[ti % 2][:, :],
                                 AF.Identity).then_inc(evd, 1)
                    s.activation(sqb[ti % 2][:, :], atmp[ti % 2][:, :],
                                 AF.Square).then_inc(sqd, 1)

        @block.sync
        def _(sync):
            for o in range(DC):
                for j in range(NT):
                    ti = o * NT + j
                    sync.wait_ge(evd, b_evd + ti + 1)
                    sync.dma_start(out=a_out_ap[o * 128:(o + 1) * 128,
                                                j * TW:(j + 1) * TW],
                                   in_=atmp[ti % 2][:, :]).then_inc(spd[ti % 2], 16)
            sync.wait_ge(spd[0], b_spd[0] + 16 * (NTILE // 2))
            sync.wait_ge(spd[1], b_spd[1] + 16 * (NTILE // 2))

    for k in range(2):
        p.semv[tag + f"_wl{k}"] = b_wl[k] + 16 * (DC // 2)
        p.semv[tag + f"_spd{k}"] = b_spd[k] + 16 * (NTILE // 2)
    p.semv[tag + "_mmd"] = b_mmd + NTILE
    p.semv[tag + "_evd"] = b_evd + NTILE
    p.semv[tag + "_sqd"] = b_sqd + NTILE
    p.semv[tag + "_std"] = b_std + NTILE


def emit_ffn1(p: Prog, com, aps):
    nc = p.nc
    cst = com["cst"]
    with ExitStack() as stkln:
        alloc_ln(p, com, stkln)
        with ExitStack() as stk:
            phase_stats_rows(p, "st", com, aps["x3"], stk)
        with ExitStack() as stk:
            phase_h(p, "h", com, aps["x3"], stk)

    with ExitStack() as stkf:
        mid_sb = p.sb("mid_sb", [128, T], BF16, stkf)
        lB_sb = p.sb("lB_sb", [128, D], BF16, stkf)
        with ExitStack() as stk:
            rw_sb = p.sb("rw_sb", [128, DC * 8], BF16, stk)
            lA_sb = p.sb("lA_sb", [128, DC * 128], BF16, stk)
            P2_sb = p.sb("P2_sb", [8, 128], F32, stk)
            lg_fm = p.sb("lg_fm", [8, T], F32, stk)
            lg_tm = p.sb("lg_tm", [128, 128], F32, stk)
            scr_tm = p.sb("scr_tm", [128, 64], F32, stk)
            r_tm = p.sb("r_tm", [128, 128], F32, stk)
            w_fmT = p.sb("w_fmT", [8, T], F32, stk)
            wx_sb = p.sb("wx_sb", [128, T], F32, stk)
            ldr = p.sem("ldr")
            ldr2 = p.sem("ldr2")
            with nc.Block() as block:
                @block.gpsimd
                def _(g):
                    g.dma_start(out=rw_sb[:, :], in_=aps["rw"]).then_inc(ldr, 16)
                    g.dma_start(out=lA_sb[:, :].rearrange("p (c e) -> p c e", c=DC),
                                in_=aps["lA"].rearrange("(c p) e -> p c e", p=128)
                                ).then_inc(ldr, 16)
                    g.dma_start(out=lB_sb[:, :], in_=aps["lB"]).then_inc(ldr, 16)
                    g.wait_ge(ldr, p.semv["ldr"] + 48)

                @block.sync
                def _(sync):
                    sync.dma_start(out=P2_sb[:, :], in_=aps["P2"]).then_inc(ldr2, 16)
                    sync.wait_ge(ldr2, p.semv["ldr2"] + 16)
            p.semv["ldr"] += 48
            p.semv["ldr2"] += 16
            emit_router(p, "gt", rw_sb, com["h_sb"], com["ps_aux"], cst,
                        lg_fm, lg_tm, scr_tm, r_tm, w_fmT)
            emit_mid(p, "md", lA_sb, P2_sb, w_fmT, com["h_sb"], mid_sb, wx_sb,
                     com["ps_mm"], com["ps_aux"], cst)
        with ExitStack() as stk:
            wraw = [p.sb(f"b_wraw{i}", [128, D], BF16, stk) for i in range(2)]
            atmp = [p.sb(f"b_atmp{i}", [128, TW], F32, stk) for i in range(2)]
            sqb = [p.sb(f"b_sqtb{i}", [128, TW], F32, stk) for i in range(2)]
            emit_mm_l1f(p, "mm", aps["bW"], lB_sb, mid_sb, com["h_sb"],
                        aps["a_scr"], cst, com["ps_mm"], com["ps_s1"],
                        com["ps_s2"], wraw, atmp, sqb)
    with ExitStack() as stkln:
        alloc_ln(p, com, stkln, tag="u3")
        with ExitStack() as stk:
            phase_rows2_update(p, "u", com, aps["x3"], aps["a_scr"],
                               aps["out"], stk, do_stats=False)


def build_ffn1():
    p = Prog()
    aps = {
        "x3": p.din("x3", [D, T], F32),
        "bW": p.din("bW", [D, D], F32),
        "rw": p.din("rw", [128, DC * 8], F32),
        "lA": p.din("lA", [D, 128], F32),
        "lB": p.din("lB", [128, D], F32),
        "P2": p.din("P2", [8, 128], F32),
        "out": p.dout("out", [D, T], F32),
        "a_scr": p.dint("a_scr", [D, T], F32),
    }
    com = alloc_common(p)
    emit_ffn1(p, com, aps)
    p.ctx.close()
    return p


def _prep_ffn1(inputs):
    P2 = np.zeros((8, 128), np.float32)
    for e in range(8):
        P2[e, e * 16:(e + 1) * 16] = 2.0
    return {
        "bW": _tile_w(np.ascontiguousarray(inputs["L1_ffn_base_W"].T)).reshape(D, D),
        "rw": np.ascontiguousarray(
            inputs["L1_ffn_router_W"].T.reshape(DC, 128, 8).transpose(1, 0, 2)
            .reshape(128, DC * 8)),
        "lA": np.ascontiguousarray(
            inputs["L1_ffn_lora_A"].transpose(1, 0, 2).reshape(D, 128)),
        "lB": np.ascontiguousarray(inputs["L1_ffn_lora_B"].reshape(128, D)),
        "P2": P2,
    }


def alloc_common(p: Prog):
    cst = {
        "io_free": p.sb("io_free", [128, 128], I32),
        "io_part": p.sb("io_part", [128, 128], I32),
        "io_g32": p.sb("io_g32", [128, 128], I32),
        "ones32": p.sb("ones32", [128, 32], F32),
        "ones1": p.sb("ones1", [1, 128], F32),
        "eps": p.sb("eps", [128, 1], F32),
        "neg8": p.sb("neg8", [128, 1], F32),
        "ident_bf": p.sb("ident_bf", [128, 128], BF16),
        "ident_f": p.sb("ident_f", [128, 128], F32),
    }
    com = {
        "cst": cst,
        "ps_s1": p.psum("ps_s1", [128, TW], F32),
        "ps_s2": p.psum("ps_s2", [128, TW], F32),
        "ps_aux": [p.psum(f"ps_aux{i}", [128, TW], F32) for i in range(2)],
        "ps_mm": [p.psum(f"ps_mm{i}", [128, TW], F32) for i in range(2)],
        "h_sb": p.sb("h_sb", [128, DC * T], BF16),
    }
    emit_consts(p, cst)
    return com


def alloc_ln(p: Prog, com, stack, tag=""):
    com["RB"] = p.sb(tag + "RB", [128, T], F32, stack)
    com["NB"] = p.sb(tag + "NB", [128, T], F32, stack)
    com["rowbufs"] = {k: p.sb(f"{tag}row_{k}", [128, TW], F32, stack)
                      for k in ("s1", "s2", "var", "rstd", "negm")}
    cst = com["cst"]
    cst["sel"] = [p.sb(f"{tag}sel{j}", [128, 128], F32, stack)
                  for j in range(NT)]
    with p.nc.Block() as block:
        @block.vector
        def _(v):
            for j in range(NT):
                v.tensor_scalar(cst["sel"][j][:, :], cst["io_part"][:, :],
                                32 * j, None, ALU.is_equal)


def phase_stats_rows(p: Prog, tag, com, src_ap, stack):
    xb = [p.sb(f"{tag}_xb{i}", [128, T], F32, stack) for i in range(2)]
    sq = [p.sb(f"{tag}_sq{i}", [128, T], F32, stack) for i in range(2)]
    emit_stream_stats(p, tag + "s", src_ap, com["ps_s1"], com["ps_s2"],
                      com["cst"], xb, sq)
    emit_rows(p, tag + "r", com["ps_s1"], com["ps_s2"], com["ps_aux"],
              com["cst"], com["rowbufs"], com["RB"], com["NB"])


def phase_h(p: Prog, tag, com, src_ap, stack):
    xb = [p.sb(f"{tag}_xb{i}", [128, T], F32, stack) for i in range(2)]
    ts = p.sb(f"{tag}_ts", [128, T], F32, stack)
    emit_hpass(p, tag, src_ap, com["RB"], com["NB"], com["h_sb"], xb, ts)


def phase_rows2_update(p: Prog, tag, com, x_ap, a_ap, out_ap, stack,
                       do_stats=True):
    emit_rows(p, tag + "r", com["ps_s1"], com["ps_s2"], com["ps_aux"],
              com["cst"], com["rowbufs"], com["RB"], com["NB"])
    HW = T // 2
    bufs = {
        "xb": [p.sb(f"{tag}_xb{i}", [128, HW], F32, stack) for i in range(2)],
        "ab": [p.sb(f"{tag}_ab{i}", [128, HW], F32, stack) for i in range(2)],
        "xn": [p.sb(f"{tag}_xn{i}", [128, HW], F32, stack) for i in range(2)],
        "t1": p.sb(f"{tag}_t1", [128, HW], F32, stack),
        "sq": [p.sb(f"{tag}_sqh{i}", [128, HW], F32, stack) for i in range(2)],
    }
    emit_update(p, tag + "u", x_ap, a_ap, com["RB"], com["NB"], out_ap,
                com["ps_s1"], com["ps_s2"], com["cst"], bufs, do_stats=do_stats)


def emit_block0(p: Prog, com, aps, dbg=None, stack=None):
    nc = p.nc
    cst = com["cst"]
    bias_sb = p.sb("bias_sb", [128, DC], F32, stack)
    biasf_sb = p.sb("biasf_sb", [128, DC], F32, stack)

    ld0 = p.sem("ld0")
    with nc.Block() as block:
        @block.sync
        def _(sync):
            sync.dma_start(out=bias_sb[:, :], in_=aps["bias0"]).then_inc(ld0, 16)
            sync.dma_start(out=biasf_sb[:, :], in_=aps["biasf"]).then_inc(ld0, 16)
            sync.wait_ge(ld0, p.semv["ld0"] + 32)
    p.semv["ld0"] += 32

    with ExitStack() as stk_w:
        bmP = p.sb("bmP", [128, D], BF16, stk_w)
        with ExitStack() as stk:
            emit_maskprep(p, "mk", aps["maskT"], cst, bmP, stk)
        with ExitStack() as stk:
            phase_stats_rows(p, "st", com, aps["x0"], stk)
        with ExitStack() as stk:
            phase_h(p, "h", com, aps["x0"], stk)
        with ExitStack() as stk:
            wraw = [p.sb(f"a_wraw{i}", [128, D], BF16, stk) for i in range(2)]
            atmp = [p.sb(f"a_atmp{i}", [128, TW], F32, stk) for i in range(2)]
            sqb = [p.sb(f"a_sqtb{i}", [128, TW], F32, stk) for i in range(2)]
            emit_mm_l0a(p, "mm", aps["w0"], bias_sb, com["h_sb"], aps["a_scr"],
                        bmP, cst, com["ps_mm"], com["ps_s1"], com["ps_s2"],
                        wraw, atmp, sqb)
    with ExitStack() as stk:
        phase_rows2_update(p, "u", com, aps["x0"], aps["a_scr"], aps["x1"],
                           stk, do_stats=True)

    with ExitStack() as stk_f:
        r_fmT = p.sb("r_fmT", [8, T], BF16, stk_f)
        ocs_sb = p.sb("ocs_sb", [8, D], BF16, stk_f)
        emit_rows(p, "str", com["ps_s1"], com["ps_s2"], com["ps_aux"],
                  cst, com["rowbufs"], com["RB"], com["NB"])
        with ExitStack() as stk:
            phase_h(p, "h", com, aps["x1"], stk)
        with ExitStack() as stk:
            gw_sb = p.sb("gw_sb", [128, DC * 8], BF16, stk)
            ics_sb = p.sb("ics_sb", [8, D], BF16, stk)
            lg_fm = p.sb("lg_fm", [8, T], F32, stk)
            lg_tm = p.sb("lg_tm", [128, 128], F32, stk)
            scr_tm = p.sb("scr_tm", [128, 32], F32, stk)
            r_tm = p.sb("r_tm", [128, 128], F32, stk)
            ldg = p.sem("ldg")
            with nc.Block() as block:
                @block.gpsimd
                def _(g):
                    g.dma_start(out=gw_sb[:, :], in_=aps["gw"]).then_inc(ldg, 16)
                    g.dma_start(out=ics_sb[:, :], in_=aps["ics"]).then_inc(ldg, 16)
                    g.dma_start(out=ocs_sb[:, :], in_=aps["ocs"]).then_inc(ldg, 16)
                    g.wait_ge(ldg, p.semv["ldg"] + 48)
            p.semv["ldg"] += 48
            emit_gates(p, "gt", gw_sb, com["h_sb"], com["ps_aux"], cst,
                       lg_fm, lg_tm, scr_tm, r_tm, r_fmT)
            if dbg is not None:
                dsem = p.sem("dbgg")
                with nc.Block() as block:
                    @block.sync
                    def _(sync):
                        sync.dma_start(out=dbg["r_fmT"], in_=r_fmT[:, :]).then_inc(dsem, 16)
                        sync.dma_start(out=dbg["lg_fm"], in_=lg_fm[:, :]).then_inc(dsem, 16)
                        sync.dma_start(out=dbg["lg_tm"], in_=lg_tm[:, :]).then_inc(dsem, 16)
                        sync.dma_start(out=dbg["r_tm"], in_=r_tm[:, :]).then_inc(dsem, 16)
                        sync.wait_ge(dsem, p.semv["dbgg"] + 64)
                p.semv["dbgg"] += 64
            emit_upass(p, "up", ics_sb, r_fmT, com["h_sb"], com["ps_aux"], cst)
        with ExitStack() as stk:
            wraw = [p.sb(f"f_wraw{i}", [128, D], BF16, stk) for i in range(2)]
            os_sb = [p.sb(f"f_os{i}", [128, TW], F32, stk) for i in range(2)]
            tev = [p.sb(f"f_tev{i}", [128, TW], F32, stk) for i in range(2)]
            atmp = [p.sb(f"f_atmp{i}", [128, TW], F32, stk) for i in range(2)]
            sqb = [p.sb(f"f_sqtb{i}", [128, TW], F32, stk) for i in range(2)]
            emit_mm_l0f(p, "mm", aps["wf"], biasf_sb, ocs_sb, r_fmT,
                        com["h_sb"], aps["a_scr"], cst, com["ps_mm"],
                        com["ps_aux"][0], com["ps_s1"], com["ps_s2"],
                        wraw, os_sb, tev, atmp, sqb)
    with ExitStack() as stk:
        phase_rows2_update(p, "u", com, aps["x1"], aps["a_scr"], aps["x2"],
                           stk, do_stats=False)


def build_l0():
    p = Prog()
    aps = {
        "x0": p.din("x0", [D, T], F32),
        "w0": p.din("w0", [D, D], F32),
        "maskT": p.din("maskT", [D, D], I32),
        "bias0": p.din("bias0", [128, DC], F32),
        "gw": p.din("gw", [128, DC * 8], F32),
        "ics": p.din("ics", [8, D], F32),
        "ocs": p.din("ocs", [8, D], F32),
        "wf": p.din("wf", [D, D], F32),
        "biasf": p.din("biasf", [128, DC], F32),
        "x2": p.dout("x2", [D, T], F32),
        "a_scr": p.dint("a_scr", [D, T], F32),
        "x1": p.dint("x1_scr", [D, T], F32),
    }
    com = alloc_common(p)
    alloc_ln(p, com, p.ctx)
    emit_block0(p, com, aps)
    p.ctx.close()
    return p


def _tile_w(wT):
    K, N = wT.shape
    return np.ascontiguousarray(
        wT.reshape(K // 128, 128, N // 128, 128).transpose(2, 1, 0, 3))


def _shard_x(x):
    out = []
    for core in range(NCORES):
        b, half = core // 2, core % 2
        sl = x[b, half * T:(half + 1) * T, :]
        out.append(np.ascontiguousarray(sl.T))
    return out


def build_qkv():
    p = Prog()
    nc = p.nc
    x2 = p.din("x2", [D, T], F32)
    qc = p.din("qc", [D, D], I32)
    qs = p.din("qs", [DC, D], F32)
    kc = p.din("kc", [1024, D], I32)
    ks = p.din("ks", [8, D], F32)
    vc = p.din("vc", [1024, D], I32)
    vs = p.din("vs", [8, D], F32)
    cosd = p.din("cosd", [128, T], BF16)
    ssind = p.din("ssind", [128, T], BF16)
    q_out = p.dout("q_rot", [D, T], BF16)
    k_out = p.dout("k_rot", [1024, T], BF16)
    v_out = p.dout("v_tm", [T, 1024], BF16)

    com = alloc_common(p)
    cst = com["cst"]
    with ExitStack() as stkln:
        alloc_ln(p, com, stkln)
        with ExitStack() as stk:
            phase_stats_rows(p, "st2", com, x2, stk)
        with ExitStack() as stk:
            phase_h(p, "h2", com, x2, stk)
    with ExitStack() as stk:
        bufs = {
            "wbf": [p.sb(f"dwbf{i}", [128, D], BF16, stk) for i in range(2)],
            "scrow": p.sb("scrow", [1, D], F32, stk),
            "et": [p.sb(f"et{i}", [128, TW], BF16, stk) for i in range(2)],
            "esw": p.sb("esw", [128, TW], BF16, stk),
            "e1": p.sb("e1", [128, TW], BF16, stk),
            "e2": p.sb("e2", [128, TW], BF16, stk),
            "er": [p.sb(f"er{i}", [128, TW], BF16, stk) for i in range(2)],
            "COS": p.sb("COS", [128, T], BF16, stk),
            "SSIN": p.sb("SSIN", [128, T], BF16, stk),
        }
        psbt = [p.psum(f"psbt{i}", [128, TW], BF16, stk) for i in range(2)]
        ldc = p.sem("ldc")
        with nc.Block() as block:
            @block.sync
            def _(sync):
                sync.dma_start(out=bufs["COS"][:, :], in_=cosd).then_inc(ldc, 16)
                sync.dma_start(out=bufs["SSIN"][:, :], in_=ssind).then_inc(ldc, 16)
                sync.wait_ge(ldc, p.semv["ldc"] + 32)
        p.semv["ldc"] += 32
        emit_deqproj(p, "dq", qc, qs, DC, "rope", com["h_sb"], cst, bufs,
                     q_out, com["ps_mm"], com["ps_aux"], psbt)
        emit_deqproj(p, "dq", kc, ks, 8, "rope", com["h_sb"], cst, bufs,
                     k_out, com["ps_mm"], com["ps_aux"], psbt)
        emit_deqproj(p, "dq", vc, vs, 8, "vtrans", com["h_sb"], cst, bufs,
                     v_out, com["ps_mm"], com["ps_aux"], psbt)
    p.ctx.close()
    return p


def _rope_tables(half):
    pos = (half * T + np.arange(T, dtype=np.float64))
    jf = np.arange(64, dtype=np.float64)
    inv = 1.0 / (10000.0 ** (2.0 * jf / 128.0))
    f = pos[None, :] * inv[:, None]
    cos = np.repeat(np.cos(f), 2, axis=0)
    sin = np.sin(f)
    ssin = np.empty((128, T), np.float64)
    ssin[0::2] = -sin
    ssin[1::2] = sin
    bf = ml_dtypes.bfloat16
    return (np.ascontiguousarray(cos.astype(bf)),
            np.ascontiguousarray(ssin.astype(bf)))


def _prep_qkv(inputs):
    def sc_dev(s, n_chunks):
        return np.ascontiguousarray(
            s.reshape(32, n_chunks, 128).transpose(1, 0, 2).reshape(n_chunks, D))
    return {
        "qc": _tile_w(np.ascontiguousarray(inputs["L1_attn_q_codes"])).reshape(D, D),
        "qs": sc_dev(inputs["L1_attn_q_scales"], DC),
        "kc": _tile_w(np.ascontiguousarray(inputs["L1_attn_k_codes"])).reshape(1024, D),
        "ks": sc_dev(inputs["L1_attn_k_scales"], 8),
        "vc": _tile_w(np.ascontiguousarray(inputs["L1_attn_v_codes"])).reshape(1024, D),
        "vs": sc_dev(inputs["L1_attn_v_scales"], 8),
    }


def _prep_l0(inputs):
    maps = {
        "w0": _tile_w(np.ascontiguousarray(inputs["L0_attn_weight"].T)).reshape(D, D),
        "maskT": np.ascontiguousarray(inputs["L0_attn_mask"].T),
        "bias0": np.ascontiguousarray(inputs["L0_attn_bias"].reshape(DC, 128).T),
        "gw": np.ascontiguousarray(
            inputs["L0_ffn_gate_w"].T.reshape(DC, 128, 8).transpose(1, 0, 2)
            .reshape(128, DC * 8)),
        "ics": np.ascontiguousarray(inputs["L0_ffn_in_channel_scale"]),
        "ocs": np.ascontiguousarray(inputs["L0_ffn_out_channel_scale"]),
        "wf": _tile_w(np.ascontiguousarray(inputs["L0_ffn_weight"].T)).reshape(D, D),
        "biasf": np.ascontiguousarray(inputs["L0_ffn_bias"].reshape(DC, 128).T),
    }
    return maps


def build_full():
    p = Prog()
    nc = p.nc
    aps = {
        "x0": p.din("x0", [D, T], F32),
        "w0": p.din("w0", [D, D], F32),
        "maskT": p.din("maskT", [D, D], I32),
        "bias0": p.din("bias0", [128, DC], F32),
        "gw": p.din("gw", [128, DC * 8], F32),
        "ics": p.din("ics", [8, D], F32),
        "ocs": p.din("ocs", [8, D], F32),
        "wf": p.din("wf", [D, D], F32),
        "biasf": p.din("biasf", [128, DC], F32),
        "x1": p.dint("x1_scr", [D, T], F32),
        "x2": p.dint("x2_scr", [D, T], F32),
        "a_scr": p.dint("a_scr", [D, T], F32),
    }
    qc = p.din("qc", [D, D], I32)
    qs = p.din("qs", [DC, D], F32)
    kc = p.din("kc", [1024, D], I32)
    ks = p.din("ks", [8, D], F32)
    vc = p.din("vc", [1024, D], I32)
    vs = p.din("vs", [8, D], F32)
    cosd = p.din("cosd", [128, T], BF16)
    ssind = p.din("ssind", [128, T], BF16)
    tri = p.din("tri", [128, 128], F32)
    oc = p.din("oc", [D, D], I32)
    osd = p.din("osd", [DC, D], F32)
    aps_f1 = {
        "x3": p.dint("x3_scr", [D, T], F32),
        "bW": p.din("bW", [D, D], F32),
        "rw": p.din("rw", [128, DC * 8], F32),
        "lA": p.din("lA", [D, 128], F32),
        "lB": p.din("lB", [128, D], F32),
        "P2": p.din("P2", [8, 128], F32),
        "out": p.dout("out", [D, T], F32),
        "a_scr": aps["a_scr"],
    }
    agqk_in = p.dint("agqk_in", [5120, T], BF16)
    agqk_out = p.dint("agqk_out", [2 * 5120, T], BF16)
    agv_in = p.dint("agv_in", [T, 1024], BF16)
    agv_out = p.dint("agv_out", [2 * T, 1024], BF16)
    agat_in = p.dint("agat_in", [2048, 2 * T], BF16)
    agat_out = p.dint("agat_out", [4096, 2 * T], BF16)
    qfull = p.dint("qfull", [2048, 2 * T], BF16)
    kfull = p.dint("kfull", [512, 2 * T], BF16)
    vfull = p.dint("vfull", [2 * T, 512], BF16)
    attfull = p.dint("attfull", [D, T], BF16)

    com = alloc_common(p)
    cst = com["cst"]

    with ExitStack() as stkln:
        alloc_ln(p, com, stkln, tag="b0")
        emit_block0(p, com, aps, stack=stkln)
        with ExitStack() as stk:
            phase_stats_rows(p, "st", com, aps["x2"], stk)
        with ExitStack() as stk:
            phase_h(p, "h", com, aps["x2"], stk)

    qk_h = agqk_in.tensor
    v_h = agv_in.tensor

    def q_dst(o, j):
        return bass.AP(qk_h, o * 128 * T + j * TW, [[T, 128], [1, TW]])

    def k_dst(o, j):
        return bass.AP(qk_h, (4096 + o * 128) * T + j * TW, [[T, 128], [1, TW]])

    def v_dst(o, j):
        return bass.AP(v_h, j * TW * 1024 + o * 128,
                       [[1024, 128], [128 * 1024, 4], [1, 128]])

    with ExitStack() as stk:
        bufs = {
            "wbf": [p.sb(f"dwbf{i}", [128, D], BF16, stk) for i in range(2)],
            "scrow": p.sb("scrow", [1, D], F32, stk),
            "et": [p.sb(f"et{i}", [128, TW], BF16, stk) for i in range(2)],
            "esw": p.sb("esw", [128, TW], BF16, stk),
            "e1": p.sb("e1", [128, TW], BF16, stk),
            "e2": p.sb("e2", [128, TW], BF16, stk),
            "er": [p.sb(f"er{i}", [128, TW], BF16, stk) for i in range(2)],
            "COS": p.sb("COS", [128, T], BF16, stk),
            "SSIN": p.sb("SSIN", [128, T], BF16, stk),
        }
        psbt = [p.psum(f"psbt{i}", [128, TW], BF16, stk) for i in range(2)]
        ldc = p.sem("ldc")
        with nc.Block() as block:
            @block.sync
            def _(sync):
                sync.dma_start(out=bufs["COS"][:, :], in_=cosd).then_inc(ldc, 16)
                sync.dma_start(out=bufs["SSIN"][:, :], in_=ssind).then_inc(ldc, 16)
                sync.wait_ge(ldc, p.semv["ldc"] + 32)
        p.semv["ldc"] += 32
        emit_deqproj(p, "dq", qc, qs, DC, "rope", com["h_sb"], cst, bufs,
                     None, com["ps_mm"], com["ps_aux"], psbt, dst_fn=q_dst)
        emit_deqproj(p, "dq", kc, ks, 8, "rope", com["h_sb"], cst, bufs,
                     None, com["ps_mm"], com["ps_aux"], psbt, dst_fn=k_dst)
        emit_deqproj(p, "dq", vc, vs, 8, "vtrans", com["h_sb"], cst, bufs,
                     None, com["ps_mm"], com["ps_aux"], psbt, dst_fn=v_dst)

    cc = p.sem("cc")
    with nc.Block() as block:
        @block.gpsimd
        def _(g):
            g.collective_compute(
                "AllGather", ALU.bypass,
                replica_groups=[[0, 1], [2, 3], [4, 5], [6, 7]],
                ins=[agqk_in.opt()], outs=[agqk_out.opt()],
            ).then_inc(cc)
            g.collective_compute(
                "AllGather", ALU.bypass,
                replica_groups=[[0, 1], [2, 3], [4, 5], [6, 7]],
                ins=[agv_in.opt()], outs=[agv_out.opt()],
            ).then_inc(cc)
            g.wait_ge(cc, p.semv["cc"] + 2)
    p.semv["cc"] += 2

    pid = nc.partition_id()
    roq = (pid % 2) * 2048
    rok = (pid % 2) * 512
    rpk = p.sem("rpk")
    with nc.Block() as block:
        @block.gpsimd
        def _(g):
            n = 0
            for s2 in range(2):
                g.dma_start(out=qfull[:, s2 * T:(s2 + 1) * T],
                            in_=agqk_out[bass.ds(roq + s2 * 5120, 2048), :]
                            ).then_inc(rpk, 16)
                g.dma_start(out=kfull[:, s2 * T:(s2 + 1) * T],
                            in_=agqk_out[bass.ds(rok + (s2 * 5120 + 4096), 512), :]
                            ).then_inc(rpk, 16)
                g.dma_start(out=vfull[s2 * T:(s2 + 1) * T, :],
                            in_=agv_out[s2 * T:(s2 + 1) * T, bass.ds(rok, 512)]
                            ).then_inc(rpk, 16)
                n += 3
            g.wait_ge(rpk, p.semv["rpk"] + 16 * n)
    p.semv["rpk"] += 16 * 6

    with ExitStack() as stk:
        abufs = {
            "kg": p.sb("kg", [128, 2 * T], BF16, stk),
            "vg": p.sb("vg", [128, 2 * T], BF16, stk),
            "qh": p.sb("qh", [128, 2 * T], BF16, stk),
            "Srow": [p.sb(f"Srow{i}", [128, 2 * T], F32, stk) for i in range(2)],
            "Prow": [p.sb(f"Prow{i}", [128, 2 * T], BF16, stk) for i in range(2)],
            "PT": [p.sb(f"PTb{i}", [128, 1024], BF16, stk) for i in range(2)],
            "TRI": p.sb("TRI", [128, 128], F32, stk),
            "cmax": p.sb("cmax", [128, 8], F32, stk),
            "negmax": [p.sb(f"negmax{i}", [128, 1], F32, stk) for i in range(2)],
            "rowsum": [p.sb(f"rowsum{i}", [128, 1], F32, stk) for i in range(2)],
            "dinv": [p.sb(f"dinv{i}", [128, 1], F32, stk) for i in range(2)],
        }
        psbt = [p.psum(f"psbta{i}", [128, 1024], BF16, stk) for i in range(2)]
        ldt = p.sem("ldt")
        with nc.Block() as block:
            @block.sync
            def _(sync):
                sync.dma_start(out=abufs["TRI"][:, :], in_=tri).then_inc(ldt, 16)
                sync.wait_ge(ldt, p.semv["ldt"] + 16)
        p.semv["ldt"] += 16
        emit_attention(p, "at", qfull, kfull, vfull, com["h_sb"], cst, abufs,
                       com["ps_mm"] + [com["ps_s1"], com["ps_s2"]],
                       com["ps_aux"], psbt)

    sta = p.sem("sta")
    with nc.Block() as block:
        @block.sync
        def _(sync):
            for hl in range(16):
                sync.dma_start(out=agat_in[hl * 128:(hl + 1) * 128, :],
                               in_=com["h_sb"][:, hl * 4096:(hl + 1) * 4096]
                               ).then_inc(sta, 16)
            sync.wait_ge(sta, p.semv["sta"] + 16 * 16)
    p.semv["sta"] += 16 * 16
    with nc.Block() as block:
        @block.gpsimd
        def _(g):
            g.collective_compute(
                "AllGather", ALU.bypass,
                replica_groups=[[0, 1], [2, 3], [4, 5], [6, 7]],
                ins=[agat_in.opt()], outs=[agat_out.opt()],
            ).then_inc(cc)
            g.wait_ge(cc, p.semv["cc"] + 1)
    p.semv["cc"] += 1

    lda = p.sem("lda")
    ldb = p.sem("ldb")
    with nc.Block() as block:
        @block.gpsimd
        def _(g):
            coff = (nc.partition_id() % 2) * T
            g.dma_start(out=attfull[:, :],
                        in_=agat_out[:, bass.ds(coff, T)]).then_inc(lda, 16)
            g.wait_ge(lda, p.semv["lda"] + 16)
    p.semv["lda"] += 16
    with nc.Block() as block:
        @block.sync
        def _(sync):
            for c in range(DC):
                sync.dma_start(out=com["h_sb"][:, c * T:(c + 1) * T],
                               in_=attfull[c * 128:(c + 1) * 128, :]
                               ).then_inc(ldb, 16)
            sync.wait_ge(ldb, p.semv["ldb"] + 16 * DC)
    p.semv["ldb"] += 16 * DC

    with ExitStack() as stk:
        bufs = {
            "wbf": [p.sb(f"owbf{i}", [128, D], BF16, stk) for i in range(2)],
            "scrow": p.sb("oscrow", [1, D], F32, stk),
            "atmp": [p.sb(f"oatmp{i}", [128, TW], F32, stk) for i in range(2)],
            "sqb": [p.sb(f"osqb{i}", [128, TW], F32, stk) for i in range(2)],
        }
        emit_oproj(p, "mm", oc, osd, com["h_sb"], aps["a_scr"], cst,
                   com["ps_mm"], com["ps_aux"], com["ps_s1"], com["ps_s2"], bufs)
    with ExitStack() as stkln:
        alloc_ln(p, com, stkln, tag="u2")
        with ExitStack() as stk:
            phase_rows2_update(p, "u", com, aps["x2"], aps["a_scr"],
                               aps_f1["x3"], stk, do_stats=False)

    emit_ffn1(p, com, aps_f1)
    p.ctx.close()
    return p


def _full_in_maps(inputs):
    x_sh = _shard_x(inputs["L0_attn_data"].astype(np.float32, copy=False))
    w_l0 = _prep_l0(inputs)
    w_qkv = _prep_qkv(inputs)
    w_f1 = _prep_ffn1(inputs)

    def sc_dev(s, n_chunks):
        return np.ascontiguousarray(
            s.reshape(32, n_chunks, 128).transpose(1, 0, 2).reshape(n_chunks, D))

    oc = _tile_w(np.ascontiguousarray(inputs["L1_attn_o_codes"])).reshape(D, D)
    osd = sc_dev(inputs["L1_attn_o_scales"], DC)
    tri = _tri_mask()
    in_maps = []
    for c in range(NCORES):
        cos, ssin = _rope_tables(c % 2)
        m = dict(w_l0)
        m.update(w_qkv)
        m.update(w_f1)
        m.update(x0=x_sh[c], cosd=cos, ssind=ssin, tri=tri, oc=oc, osd=osd)
        in_maps.append(m)
    return in_maps


def kernel(**inputs):
    in_maps = _full_in_maps(inputs)
    p = build_full()
    res = run_bass_kernel_spmd(p.nc, in_maps, core_ids=list(range(NCORES)))
    out = np.empty((4, 4096, 4096), np.float32)
    for c in range(NCORES):
        b, half = c // 2, c % 2
        out[b, half * T:(half + 1) * T, :] = res.results[c]["out"].T
    return out


# version 08 (final)
# speedup vs baseline: 1.1901x; 1.0513x over previous
import numpy as np
import ml_dtypes
import concourse.bass as bass
import concourse.mybir as mybir
from concourse.bass_utils import run_bass_kernel_spmd
from contextlib import ExitStack

F32 = mybir.dt.float32
BF16 = mybir.dt.bfloat16
I32 = mybir.dt.int32
AF = mybir.ActivationFunctionType
ALU = mybir.AluOpType
AX = mybir.AxisListType

D = 4096
DC = 32
T = 2048
NT = 4
TW = 512
NCORES = 8
EPS = 1e-5
NEG = -30000.0


class Prog:
    def __init__(self):
        self.nc = bass.Bass("TRN2", target_bir_lowering=False, debug=False)
        self.ctx = ExitStack()
        self.sems = {}
        self.semv = {}

    def sem(self, name):
        if name not in self.sems:
            self.sems[name] = self.ctx.enter_context(self.nc.semaphore(name))
            self.semv[name] = 0
        return self.sems[name]

    def sem2(self, name):
        return [self.sem(name + "0"), self.sem(name + "1")]

    def _uniq(self, name):
        if not hasattr(self, "_names"):
            self._names = {}
        k = self._names.get(name, 0)
        self._names[name] = k + 1
        return name if k == 0 else f"{name}__{k}"

    def sb(self, name, shape, dt=F32, stack=None):
        return (stack or self.ctx).enter_context(
            self.nc.sbuf_tensor(self._uniq(name), shape, dt))

    def psum(self, name, shape, dt=F32, stack=None):
        return (stack or self.ctx).enter_context(
            self.nc.psum_tensor(self._uniq(name), shape, dt))

    def din(self, name, shape, dt=F32):
        return self.nc.dram_tensor(name, shape, dt, kind="ExternalInput").ap()

    def dout(self, name, shape, dt=F32):
        return self.nc.dram_tensor(name, shape, dt, kind="ExternalOutput").ap()

    def dint(self, name, shape, dt=F32):
        return self.nc.dram_tensor(name, shape, dt, kind="Internal").ap()


def ap3(t, parts, dims):
    h = t[parts[0]:parts[0] + parts[1], 0:1]
    return bass.AP(h.tensor, h.offset, [h.ap[0]] + [list(d) for d in dims])



def emit_consts(p: Prog, cst):
    nc = p.nc
    with nc.Block() as block:
        @block.gpsimd
        def _(g):
            g.iota(cst["io_free"][:, :], [[1, 128]], channel_multiplier=0)
            g.iota(cst["io_part"][:, :], [[0, 128]], channel_multiplier=1)
            g.iota(cst["io_g32"][:, :], [[1, 4], [0, 32]], channel_multiplier=0)

        @block.vector
        def _(v):
            v.memset(cst["ones32"][:, :], 1.0)
            v.memset(cst["ones1"][:, :], 1.0)
            v.memset(cst["eps"][:, :], EPS)
            v.memset(cst["neg8"][:, :], -8.0)

    with nc.Block() as block:
        @block.vector
        def _(v):
            v.tensor_tensor(cst["ident_bf"][:, :], cst["io_free"][:, :],
                            cst["io_part"][:, :], ALU.is_equal)
            v.tensor_tensor(cst["ident_f"][:, :], cst["io_free"][:, :],
                            cst["io_part"][:, :], ALU.is_equal)


def emit_stream_stats(p: Prog, tag, src_ap, ps_s1, ps_s2, cst, xbufs, sqbuf):
    nc = p.nc
    ld = p.sem2(tag + "_ld"); b_ld = [p.semv[tag + "_ld0"], p.semv[tag + "_ld1"]]
    sq = p.sem(tag + "_sq"); b_sq = p.semv[tag + "_sq"]
    mm = p.sem(tag + "_mm"); b_mm = p.semv[tag + "_mm"]
    ldt = lambda c: b_ld[c % 2] + 16 * (c // 2 + 1)

    with nc.Block() as block:
        @block.sync
        def _(sync):
            for c in range(DC):
                if c >= 2:
                    sync.wait_ge(sq, b_sq + c - 1)
                    sync.wait_ge(mm, b_mm + c - 1)
                sync.dma_start(out=xbufs[c % 2][:, :],
                               in_=src_ap[c * 128:(c + 1) * 128, :]).then_inc(ld[c % 2], 16)

        @block.scalar
        def _(s):
            for c in range(DC):
                s.wait_ge(ld[c % 2], ldt(c))
                if c >= 2:
                    s.wait_ge(mm, b_mm + c - 1)
                s.activation(sqbuf[c % 2][:, :], xbufs[c % 2][:, :],
                             AF.Square).then_inc(sq, 1)

        @block.tensor
        def _(t):
            for c in range(DC):
                t.wait_ge(ld[c % 2], ldt(c))
                for j in range(NT):
                    t.matmul(ps_s1[32 * j:32 * j + 32, :], cst["ones32"][:, :],
                             xbufs[c % 2][:, j * TW:(j + 1) * TW],
                             start=(c == 0), stop=(c == DC - 1),
                             tile_position=(0, 32 * j), skip_group_check=True)
                t.wait_ge(sq, b_sq + c + 1)
                for j in range(NT):
                    mmi = t.matmul(ps_s2[32 * j:32 * j + 32, :], cst["ones32"][:, :],
                                   sqbuf[c % 2][:, j * TW:(j + 1) * TW],
                                   start=(c == 0), stop=(c == DC - 1),
                                   tile_position=(0, 32 * j), skip_group_check=True)
                mmi.then_inc(mm, 1)

    p.semv[tag + "_ld0"] = b_ld[0] + 16 * (DC // 2)
    p.semv[tag + "_ld1"] = b_ld[1] + 16 * (DC // 2)
    p.semv[tag + "_sq"] = b_sq + DC
    p.semv[tag + "_mm"] = b_mm + DC


def emit_rows(p: Prog, tag, ps_s1, ps_s2, ps_aux, cst, rowbufs, RB, NB):
    nc = p.nc
    s1g, s2g, varg, rstdg, negmg = (rowbufs[k] for k in
                                    ("s1", "s2", "var", "rstd", "negm"))
    bc = p.sem(tag + "_bc"); b_bc = p.semv[tag + "_bc"]
    ev = p.sem(tag + "_ev"); b_ev = p.semv[tag + "_ev"]

    with nc.Block() as block:
        @block.vector
        def _(v):
            v.tensor_copy(s1g[:, :], ps_s1[:, :])
            v.tensor_copy(s2g[:, :], ps_s2[:, :])
            v.tensor_scalar(s1g[:, :], s1g[:, :], 1.0 / D, None, ALU.mult)
            v.tensor_scalar(s2g[:, :], s2g[:, :], 1.0 / D, None, ALU.mult)
            v.tensor_tensor(varg[:, :], s1g[:, :], s1g[:, :], ALU.mult)
            v.tensor_tensor(varg[:, :], s2g[:, :], varg[:, :], ALU.subtract).then_inc(ev, 1)

        @block.scalar
        def _(s):
            s.wait_ge(ev, b_ev + 1)
            s.activation(rstdg[:, :], varg[:, :], AF.Ln, bias=cst["eps"][:, :], scale=1.0)
            s.activation(rstdg[:, :], rstdg[:, :], AF.Exp, scale=-0.5).then_inc(ev, 1)

        @block.vector
        def _(v):
            v.wait_ge(ev, b_ev + 2)
            v.scalar_tensor_tensor(negmg[:, :], s1g[:, :], -1.0, rstdg[:, :],
                                   ALU.mult, ALU.mult).then_inc(ev, 1)

        @block.tensor
        def _(t):
            t.wait_ge(ev, b_ev + 3)
            for j in range(NT):
                t.matmul(ps_aux[0][:, :], cst["sel"][j][:, :], rstdg[:, :],
                         start=True, stop=True).then_inc(bc, 1)
                t.wait_ge(ev, b_ev + 3 + 2 * j + 1)
                t.matmul(ps_aux[1][:, :], cst["sel"][j][:, :], negmg[:, :],
                         start=True, stop=True).then_inc(bc, 1)
                t.wait_ge(ev, b_ev + 3 + 2 * j + 2)

        @block.vector
        def _(v):
            for j in range(NT):
                v.wait_ge(bc, b_bc + 2 * j + 1)
                v.tensor_copy(RB[:, j * TW:(j + 1) * TW], ps_aux[0][:, :]).then_inc(ev, 1)
                v.wait_ge(bc, b_bc + 2 * j + 2)
                v.tensor_copy(NB[:, j * TW:(j + 1) * TW], ps_aux[1][:, :]).then_inc(ev, 1)

    p.semv[tag + "_bc"] = b_bc + 2 * NT
    p.semv[tag + "_ev"] = b_ev + 3 + 2 * NT


def emit_hpass(p: Prog, tag, src_ap, RB, NB, h_sb, xbufs, tscr):
    nc = p.nc
    ld = p.sem2(tag + "_ld"); b_ld = [p.semv[tag + "_ld0"], p.semv[tag + "_ld1"]]
    hd = p.sem(tag + "_hd"); b_hd = p.semv[tag + "_hd"]

    with nc.Block() as block:
        @block.sync
        def _(sync):
            for c in range(DC):
                if c >= 2:
                    sync.wait_ge(hd, b_hd + c - 1)
                sync.dma_start(out=xbufs[c % 2][:, :],
                               in_=src_ap[c * 128:(c + 1) * 128, :]).then_inc(ld[c % 2], 16)

        @block.vector
        def _(v):
            for c in range(DC):
                v.wait_ge(ld[c % 2], b_ld[c % 2] + 16 * (c // 2 + 1))
                v.tensor_tensor(tscr[:, :], xbufs[c % 2][:, :], RB[:, :], ALU.mult)
                v.tensor_tensor(h_sb[:, c * T:(c + 1) * T], tscr[:, :], NB[:, :],
                                ALU.add).then_inc(hd, 1)

    p.semv[tag + "_ld0"] = b_ld[0] + 16 * (DC // 2)
    p.semv[tag + "_ld1"] = b_ld[1] + 16 * (DC // 2)
    p.semv[tag + "_hd"] = b_hd + DC


def emit_update(p: Prog, tag, x_ap, a_ap, RB2, NB2, out_ap, ps_s1, ps_s2, cst,
                bufs, do_stats=True):
    nc = p.nc
    HW = T // 2
    NI = DC * 2
    lx = p.sem2(tag + "_lx"); b_lx = [p.semv[tag + "_lx0"], p.semv[tag + "_lx1"]]
    la = p.sem2(tag + "_la"); b_la = [p.semv[tag + "_la0"], p.semv[tag + "_la1"]]
    st = p.sem2(tag + "_st"); b_st = [p.semv[tag + "_st0"], p.semv[tag + "_st1"]]
    up = p.sem(tag + "_up"); b_up = p.semv[tag + "_up"]
    sq = p.sem(tag + "_sq"); b_sq = p.semv[tag + "_sq"]
    mm = p.sem(tag + "_mm"); b_mm = p.semv[tag + "_mm"]
    xb, ab, xn, t1, sqb = bufs["xb"], bufs["ab"], bufs["xn"], bufs["t1"], bufs["sq"]
    lxt = lambda i: b_lx[i % 2] + 16 * (i // 2 + 1)
    lat = lambda i: b_la[i % 2] + 16 * (i // 2 + 1)
    stt = lambda i: b_st[i % 2] + 16 * (i // 2 + 1)

    def dsl(i):
        c, u = i // 2, i % 2
        return (slice(c * 128, (c + 1) * 128), slice(u * HW, (u + 1) * HW))

    with nc.Block() as block:
        @block.sync
        def _(sync):
            for i in range(NI):
                if i >= 2:
                    sync.wait_ge(up, b_up + i - 1)
                r, tsl = dsl(i)
                sync.dma_start(out=xb[i % 2][:, :],
                               in_=x_ap[r, tsl]).then_inc(lx[i % 2], 16)

        @block.gpsimd
        def _(g):
            for i in range(NI):
                if i >= 2:
                    g.wait_ge(up, b_up + i - 1)
                r, tsl = dsl(i)
                g.dma_start(out=ab[i % 2][:, :],
                            in_=a_ap[r, tsl]).then_inc(la[i % 2], 16)
                if i >= 1:
                    g.wait_ge(up, b_up + i)
                    rp, tp = dsl(i - 1)
                    g.dma_start(out=out_ap[rp, tp],
                                in_=xn[(i - 1) % 2][:, :]).then_inc(st[(i - 1) % 2], 16)
            g.wait_ge(up, b_up + NI)
            rp, tp = dsl(NI - 1)
            g.dma_start(out=out_ap[rp, tp],
                        in_=xn[(NI - 1) % 2][:, :]).then_inc(st[(NI - 1) % 2], 16)
            g.wait_ge(st[0], b_st[0] + 16 * (NI // 2))
            g.wait_ge(st[1], b_st[1] + 16 * (NI // 2))

        @block.vector
        def _(v):
            for i in range(NI):
                u = i % 2
                v.wait_ge(la[i % 2], lat(i))
                v.tensor_tensor(t1[:, :], ab[i % 2][:, :],
                                RB2[:, u * HW:(u + 1) * HW], ALU.mult)
                v.tensor_tensor(t1[:, :], t1[:, :],
                                NB2[:, u * HW:(u + 1) * HW], ALU.add)
                v.wait_ge(lx[i % 2], lxt(i))
                if i >= 2:
                    v.wait_ge(st[i % 2], stt(i - 2))
                    if do_stats:
                        v.wait_ge(sq, b_sq + i - 1)
                        v.wait_ge(mm, b_mm + i - 1)
                v.tensor_tensor(xn[i % 2][:, :], t1[:, :], xb[i % 2][:, :],
                                ALU.add).then_inc(up, 1)

        if do_stats:
            @block.scalar
            def _(s):
                for i in range(NI):
                    s.wait_ge(up, b_up + i + 1)
                    if i >= 2:
                        s.wait_ge(mm, b_mm + i - 1)
                    s.activation(sqb[i % 2][:, :], xn[i % 2][:, :],
                                 AF.Square).then_inc(sq, 1)

            @block.tensor
            def _(t):
                for i in range(NI):
                    c, u = i // 2, i % 2
                    t.wait_ge(up, b_up + i + 1)
                    for jj in range(2):
                        j = 2 * u + jj
                        t.matmul(ps_s1[32 * j:32 * j + 32, :], cst["ones32"][:, :],
                                 xn[i % 2][:, jj * TW:(jj + 1) * TW],
                                 start=(c == 0), stop=(c == DC - 1),
                                 tile_position=(0, 32 * j), skip_group_check=True)
                    t.wait_ge(sq, b_sq + i + 1)
                    for jj in range(2):
                        j = 2 * u + jj
                        mmi = t.matmul(ps_s2[32 * j:32 * j + 32, :], cst["ones32"][:, :],
                                       sqb[i % 2][:, jj * TW:(jj + 1) * TW],
                                       start=(c == 0), stop=(c == DC - 1),
                                       tile_position=(0, 32 * j), skip_group_check=True)
                    mmi.then_inc(mm, 1)

    for k in range(2):
        p.semv[tag + f"_lx{k}"] = b_lx[k] + 16 * (NI // 2)
        p.semv[tag + f"_la{k}"] = b_la[k] + 16 * (NI // 2)
        p.semv[tag + f"_st{k}"] = b_st[k] + 16 * (NI // 2)
    p.semv[tag + "_up"] = b_up + NI
    if do_stats:
        p.semv[tag + "_sq"] = b_sq + NI
        p.semv[tag + "_mm"] = b_mm + NI


def emit_maskprep(p: Prog, tag, maskT_ap, cst, bmP, stack):
    nc = p.nc
    mbuf = [p.sb(f"{tag}_mb{i}", [128, 4096], I32, stack) for i in range(2)]
    s1t = p.sb(f"{tag}_s1", [128, 128], F32, stack)
    bmTnb = p.sb(f"{tag}_bmTnb", [128, 128], F32, stack)
    bm01 = p.sb(f"{tag}_bm01", [128, 128], F32, stack)
    selc_a = p.sb(f"{tag}_sca", [128, 128], I32, stack)
    ps_tr = p.psum(f"{tag}_pstr", [128, 128], F32, stack)
    ps_b = p.psum(f"{tag}_psb", [128, 128], F32, stack)
    ld = p.sem2(tag + "_ld"); b_ld = [p.semv[tag + "_ld0"], p.semv[tag + "_ld1"]]
    rd = p.sem(tag + "_rd"); b_rd = p.semv[tag + "_rd"]
    tr = p.sem(tag + "_tr"); b_tr = p.semv[tag + "_tr"]

    with nc.Block() as block:
        @block.sync
        def _(sync):
            for i in range(DC):
                if i >= 2:
                    sync.wait_ge(rd, b_rd + 2 * (i - 1))
                sync.dma_start(out=mbuf[i % 2][:, :],
                               in_=maskT_ap[i * 128:(i + 1) * 128, :]).then_inc(ld[i % 2], 16)

        @block.vector
        def _(v):
            for i in range(DC):
                v.wait_ge(ld[i % 2], b_ld[i % 2] + 16 * (i // 2 + 1))
                v.reduce_sum(s1t[:, :],
                             mbuf[i % 2][:, :].rearrange("p (n r) -> p n r", r=32),
                             axis=AX.X).then_inc(rd, 1)
                v.wait_ge(tr, b_tr + i + 1)
                v.reduce_sum(bmTnb[:, 4 * i:4 * i + 4],
                             ps_tr[:, :].rearrange("p (n r) -> p n r", r=32),
                             axis=AX.X).then_inc(rd, 1)

        @block.tensor
        def _(t):
            for i in range(DC):
                t.wait_ge(rd, b_rd + 2 * i + 1)
                t.transpose(ps_tr[:, :], s1t[:, :], cst["ident_f"][:, :]).then_inc(tr, 1)
                t.wait_ge(rd, b_rd + 2 * i + 2)
    p.semv[tag + "_ld0"] = b_ld[0] + 16 * (DC // 2)
    p.semv[tag + "_ld1"] = b_ld[1] + 16 * (DC // 2)
    p.semv[tag + "_rd"] = b_rd + 2 * DC
    p.semv[tag + "_tr"] = b_tr + DC

    with nc.Block() as block:
        @block.vector
        def _(v):
            v.tensor_scalar(bmTnb[:, :], bmTnb[:, :], 0.5, None, ALU.is_gt)

    with nc.Block() as block:
        @block.tensor
        def _(t):
            t.transpose(ps_tr[:, :], bmTnb[:, :], cst["ident_f"][:, :])

    with nc.Block() as block:
        @block.vector
        def _(v):
            v.tensor_copy(bm01[:, :], ps_tr[:, :])

    b_tr = p.semv[tag + "_tr"]; b_rd = p.semv[tag + "_rd"]
    with nc.Block() as block:
        @block.vector
        def _(v):
            for c in range(DC):
                if c >= 1:
                    v.wait_ge(tr, b_tr + c)
                v.tensor_scalar(selc_a[:, :], cst["io_g32"][:, :], 4 * c, None, ALU.add)
                v.tensor_tensor(s1t[:, :], cst["io_part"][:, :], selc_a[:, :],
                                ALU.is_equal).then_inc(rd, 1)
                v.wait_ge(tr, b_tr + c + 1)
                v.tensor_copy(bmP[:, c * 128:(c + 1) * 128], ps_b[:, :]).then_inc(rd, 1)

        @block.tensor
        def _(t):
            for c in range(DC):
                t.wait_ge(rd, b_rd + 2 * c + 1)
                t.matmul(ps_b[:, :], s1t[:, :], bm01[:, :],
                         start=True, stop=True).then_inc(tr, 1)
    p.semv[tag + "_rd"] = b_rd + 2 * DC
    p.semv[tag + "_tr"] = b_tr + DC


def emit_mm_l0a(p: Prog, tag, w_ap, bias_sb, h_sb, a_out_ap, bmP, cst,
                ps_mm, ps_s1, ps_s2, wraw, atmp, sqb):
    nc = p.nc
    wl = p.sem2(tag + "_wl"); b_wl = [p.semv[tag + "_wl0"], p.semv[tag + "_wl1"]]
    spd = p.sem2(tag + "_spd"); b_spd = [p.semv[tag + "_spd0"], p.semv[tag + "_spd1"]]
    wp = p.sem(tag + "_wp"); b_wp = p.semv[tag + "_wp"]
    mmd = p.sem(tag + "_mmd"); b_mmd = p.semv[tag + "_mmd"]
    evd = p.sem(tag + "_evd"); b_evd = p.semv[tag + "_evd"]
    sqd = p.sem(tag + "_sqd"); b_sqd = p.semv[tag + "_sqd"]
    std = p.sem(tag + "_std"); b_std = p.semv[tag + "_std"]
    wlt = lambda o: b_wl[o % 2] + 16 * (o // 2 + 1)
    spt = lambda ti: b_spd[ti % 2] + 16 * (ti // 2 + 1)

    NTILE = DC * NT

    with nc.Block() as block:
        @block.gpsimd
        def _(g):
            for o in range(DC):
                if o >= 2:
                    g.wait_ge(mmd, b_mmd + (o - 1) * NT)
                g.dma_start(out=wraw[o % 2][:, :],
                            in_=w_ap[o * 128:(o + 1) * 128, :]).then_inc(wl[o % 2], 16)

        @block.vector
        def _(v):
            for o in range(DC):
                v.wait_ge(wl[o % 2], wlt(o))
                bh = bmP[:, 4 * o:4 * o + 1]
                bview = bass.AP(bh.tensor, bh.offset,
                                [bh.ap[0], [128, DC], [1, 4], [0, 32]])
                v.tensor_tensor(
                    wraw[o % 2][:, :].rearrange("p (c f r) -> p c f r", c=DC, f=4),
                    wraw[o % 2][:, :].rearrange("p (c f r) -> p c f r", c=DC, f=4),
                    bview, ALU.mult).then_inc(wp, 1)

        @block.tensor
        def _(t):
            for o in range(DC):
                t.wait_ge(wp, b_wp + o + 1)
                for j in range(NT):
                    ti = o * NT + j
                    bank = ti % 2
                    if ti >= 2:
                        t.wait_ge(evd, b_evd + ti - 1)
                    for c in range(DC):
                        mi = t.matmul(ps_mm[bank][:, :],
                                      wraw[o % 2][:, c * 128:(c + 1) * 128],
                                      h_sb[:, c * T + j * TW:c * T + (j + 1) * TW],
                                      start=(c == 0), stop=(c == DC - 1))
                    mi.then_inc(mmd, 1)
                    if ti >= 1:
                        op, jp = (ti - 1) // NT, (ti - 1) % NT
                        t.wait_ge(sqd, b_sqd + ti)
                        t.matmul(ps_s1[32 * jp:32 * jp + 32, :], cst["ones32"][:, :],
                                 atmp[(ti - 1) % 2][:, :],
                                 start=(op == 0), stop=(op == DC - 1),
                                 tile_position=(0, 32 * jp), skip_group_check=True)
                        t.matmul(ps_s2[32 * jp:32 * jp + 32, :], cst["ones32"][:, :],
                                 sqb[(ti - 1) % 2][:, :],
                                 start=(op == 0), stop=(op == DC - 1),
                                 tile_position=(0, 32 * jp), skip_group_check=True).then_inc(std, 1)
            ti = NTILE
            op, jp = (ti - 1) // NT, (ti - 1) % NT
            t.wait_ge(sqd, b_sqd + ti)
            t.matmul(ps_s1[32 * jp:32 * jp + 32, :], cst["ones32"][:, :],
                     atmp[(ti - 1) % 2][:, :], start=(op == 0), stop=True,
                     tile_position=(0, 32 * jp), skip_group_check=True)
            t.matmul(ps_s2[32 * jp:32 * jp + 32, :], cst["ones32"][:, :],
                     sqb[(ti - 1) % 2][:, :], start=(op == 0), stop=True,
                     tile_position=(0, 32 * jp), skip_group_check=True).then_inc(std, 1)

        @block.scalar
        def _(s):
            for o in range(DC):
                for j in range(NT):
                    ti = o * NT + j
                    s.wait_ge(mmd, b_mmd + ti + 1)
                    if ti >= 2:
                        s.wait_ge(spd[ti % 2], spt(ti - 2))
                        s.wait_ge(std, b_std + ti - 1)
                    s.activation(atmp[ti % 2][:, :], ps_mm[ti % 2][:, :],
                                 AF.Identity, bias=bias_sb[:, o:o + 1],
                                 scale=1.0).then_inc(evd, 1)
                    s.activation(sqb[ti % 2][:, :], atmp[ti % 2][:, :],
                                 AF.Square).then_inc(sqd, 1)

        @block.sync
        def _(sync):
            for o in range(DC):
                for j in range(NT):
                    ti = o * NT + j
                    sync.wait_ge(evd, b_evd + ti + 1)
                    sync.dma_start(out=a_out_ap[o * 128:(o + 1) * 128,
                                                j * TW:(j + 1) * TW],
                                   in_=atmp[ti % 2][:, :]).then_inc(spd[ti % 2], 16)
            sync.wait_ge(spd[0], b_spd[0] + 16 * (NTILE // 2))
            sync.wait_ge(spd[1], b_spd[1] + 16 * (NTILE // 2))

    for k in range(2):
        p.semv[tag + f"_wl{k}"] = b_wl[k] + 16 * (DC // 2)
        p.semv[tag + f"_spd{k}"] = b_spd[k] + 16 * (NTILE // 2)
    p.semv[tag + "_wp"] = b_wp + DC
    p.semv[tag + "_mmd"] = b_mmd + NTILE
    p.semv[tag + "_evd"] = b_evd + NTILE
    p.semv[tag + "_sqd"] = b_sqd + NTILE
    p.semv[tag + "_std"] = b_std + NTILE


def emit_gates(p: Prog, tag, gw_sb, h_sb, ps_aux, cst, lg_fm, lg_tm, scr_tm,
               r_tm, r_fmT):
    nc = p.nc
    gd = p.sem(tag + "_gd"); b_gd = p.semv[tag + "_gd"]
    ev = p.sem(tag + "_ev"); b_ev = p.semv[tag + "_ev"]

    with nc.Block() as block:
        @block.tensor
        def _(t):
            for j in range(NT):
                if j >= 1:
                    t.wait_ge(ev, b_ev + j)
                for c in range(DC):
                    mi = t.matmul(ps_aux[0][0:8, :], gw_sb[:, c * 8:(c + 1) * 8],
                                  h_sb[:, c * T + j * TW:c * T + (j + 1) * TW],
                                  start=(c == 0), stop=(c == DC - 1))
                mi.then_inc(gd, 1)

        @block.vector
        def _(v):
            for j in range(NT):
                v.wait_ge(gd, b_gd + j + 1)
                v.tensor_copy(lg_fm[0:8, j * TW:(j + 1) * TW],
                              ps_aux[0][0:8, :]).then_inc(ev, 1)
    p.semv[tag + "_gd"] = b_gd + NT
    p.semv[tag + "_ev"] = b_ev + NT

    b_gd = p.semv[tag + "_gd"]; b_ev = p.semv[tag + "_ev"]
    with nc.Block() as block:
        @block.tensor
        def _(t):
            for k in range(16):
                mi = t.transpose(ps_aux[1][:, k * 8:(k + 1) * 8],
                                 lg_fm[0:8, k * 128:(k + 1) * 128],
                                 cst["ident_f"][0:8, 0:8])
            mi.then_inc(gd, 1)

        @block.vector
        def _(v):
            v.wait_ge(gd, b_gd + 1)
            v.tensor_copy(lg_tm[:, :], ps_aux[1][:, 0:128])
    p.semv[tag + "_gd"] = b_gd + 1

    sm = p.sem(tag + "_sm"); b_sm = p.semv[tag + "_sm"]
    with nc.Block() as block:
        @block.vector
        def _(v):
            v.reduce_max(scr_tm[:, 0:16],
                         lg_tm[:, :].rearrange("p (g e) -> p g e", e=8), axis=AX.X)
            mh = scr_tm[:, 0:1]
            mview = bass.AP(mh.tensor, mh.offset, [mh.ap[0], [1, 16], [0, 8]])
            v.tensor_tensor(lg_tm[:, :].rearrange("p (g e) -> p g e", e=8),
                            lg_tm[:, :].rearrange("p (g e) -> p g e", e=8),
                            mview, ALU.subtract).then_inc(sm, 1)

        @block.scalar
        def _(s):
            s.wait_ge(sm, b_sm + 1)
            s.activation(lg_tm[:, :], lg_tm[:, :], AF.Exp).then_inc(sm, 1)

        @block.vector
        def _(v):
            v.wait_ge(sm, b_sm + 2)
            v.reduce_sum(scr_tm[:, 0:16],
                         lg_tm[:, :].rearrange("p (g e) -> p g e", e=8), axis=AX.X)
            v.drain()
            v.reciprocal(scr_tm[:, 16:32], scr_tm[:, 0:16])
            v.drain()
            dh = scr_tm[:, 16:17]
            dview = bass.AP(dh.tensor, dh.offset, [dh.ap[0], [1, 16], [0, 8]])
            v.tensor_tensor(r_tm[:, :].rearrange("p (g e) -> p g e", e=8),
                            lg_tm[:, :].rearrange("p (g e) -> p g e", e=8),
                            dview, ALU.mult).then_inc(sm, 1)
    p.semv[tag + "_sm"] = b_sm + 3

    b_gd = p.semv[tag + "_gd"]; b_ev = p.semv[tag + "_ev"]
    with nc.Block() as block:
        @block.tensor
        def _(t):
            for g4 in range(4):
                if g4 >= 1:
                    t.wait_ge(ev, b_ev + g4)
                for k4 in range(4):
                    k = g4 * 4 + k4
                    mi = t.transpose(ps_aux[0][0:8, k4 * 128:(k4 + 1) * 128],
                                     r_tm[:, k * 8:(k + 1) * 8],
                                     cst["ident_f"][:, :])
                mi.then_inc(gd, 1)

        @block.vector
        def _(v):
            for g4 in range(4):
                v.wait_ge(gd, b_gd + g4 + 1)
                v.tensor_copy(r_fmT[0:8, g4 * TW:(g4 + 1) * TW],
                              ps_aux[0][0:8, 0:TW]).then_inc(ev, 1)
    p.semv[tag + "_gd"] = b_gd + 4
    p.semv[tag + "_ev"] = b_ev + 4


def emit_upass(p: Prog, tag, ics_sb, r_fmT, h_sb, ps_aux, cst):
    nc = p.nc
    ism = p.sem(tag + "_ism"); b_ism = p.semv[tag + "_ism"]
    ut = p.sem(tag + "_ut"); b_ut = p.semv[tag + "_ut"]

    with nc.Block() as block:
        @block.tensor
        def _(t):
            for c in range(DC):
                for j in range(NT):
                    idx = c * NT + j
                    if idx >= 2:
                        t.wait_ge(ut, b_ut + idx - 1)
                    t.matmul(ps_aux[idx % 2][:, :],
                             ics_sb[0:8, c * 128:(c + 1) * 128],
                             r_fmT[0:8, j * TW:(j + 1) * TW],
                             start=True, stop=True).then_inc(ism, 1)

        @block.vector
        def _(v):
            for c in range(DC):
                for j in range(NT):
                    idx = c * NT + j
                    v.wait_ge(ism, b_ism + idx + 1)
                    sl = slice(c * T + j * TW, c * T + (j + 1) * TW)
                    v.tensor_tensor(h_sb[:, sl], h_sb[:, sl],
                                    ps_aux[idx % 2][:, :], ALU.mult).then_inc(ut, 1)
    p.semv[tag + "_ism"] = b_ism + DC * NT
    p.semv[tag + "_ut"] = b_ut + DC * NT


def emit_mm_l0f(p: Prog, tag, w_ap, biasf_sb, ocs_sb, r_fmT, h_sb, a_out_ap,
                cst, ps_mm, ps_os, ps_s1, ps_s2, wraw, os_sb, tev, atmp, sqb):
    nc = p.nc
    wl = p.sem2(tag + "_wl"); b_wl = [p.semv[tag + "_wl0"], p.semv[tag + "_wl1"]]
    spd = p.sem2(tag + "_spd"); b_spd = [p.semv[tag + "_spd0"], p.semv[tag + "_spd1"]]
    wp = p.sem(tag + "_wp"); b_wp = p.semv[tag + "_wp"]
    mmd = p.sem(tag + "_mmd"); b_mmd = p.semv[tag + "_mmd"]
    osd = p.sem(tag + "_osd"); b_osd = p.semv[tag + "_osd"]
    osc = p.sem(tag + "_osc"); b_osc = p.semv[tag + "_osc"]
    tvd = p.sem(tag + "_tvd"); b_tvd = p.semv[tag + "_tvd"]
    acd = p.sem(tag + "_acd"); b_acd = p.semv[tag + "_acd"]
    sqd = p.sem(tag + "_sqd"); b_sqd = p.semv[tag + "_sqd"]
    std = p.sem(tag + "_std"); b_std = p.semv[tag + "_std"]
    wlt = lambda o: b_wl[o % 2] + 16 * (o // 2 + 1)
    spt = lambda ti: b_spd[ti % 2] + 16 * (ti // 2 + 1)
    NTILE = DC * NT

    with nc.Block() as block:
        @block.gpsimd
        def _(g):
            for o in range(DC):
                if o >= 2:
                    g.wait_ge(mmd, b_mmd + (o - 1) * NT)
                g.dma_start(out=wraw[o % 2][:, :],
                            in_=w_ap[o * 128:(o + 1) * 128, :]).then_inc(wl[o % 2], 16)

        @block.scalar
        def _(s):
            def evac(ti):
                o = ti // NT
                s.wait_ge(tvd, b_tvd + ti + 1)
                if ti >= 2:
                    s.wait_ge(spd[ti % 2], spt(ti - 2))
                    s.wait_ge(std, b_std + ti - 1)
                s.activation(atmp[ti % 2][:, :], tev[ti % 2][:, :],
                             AF.Identity, bias=biasf_sb[:, o:o + 1],
                             scale=1.0).then_inc(acd, 1)
                s.activation(sqb[ti % 2][:, :], atmp[ti % 2][:, :],
                             AF.Square).then_inc(sqd, 1)

            for o in range(DC):
                if o >= 2:
                    for j in range(NT):
                        evac((o - 2) * NT + j)
                s.wait_ge(wl[o % 2], wlt(o))
                s.activation(wraw[o % 2][:, :], wraw[o % 2][:, :],
                             AF.Sign).then_inc(wp, 1)
            for ti in range((DC - 2) * NT, DC * NT):
                evac(ti)

        @block.tensor
        def _(t):
            for o in range(DC):
                t.wait_ge(wp, b_wp + o + 1)
                for j in range(NT):
                    ti = o * NT + j
                    t.wait_ge(osc, b_osc + ti)
                    t.matmul(ps_os[:, :],
                             ocs_sb[0:8, o * 128:(o + 1) * 128],
                             r_fmT[0:8, j * TW:(j + 1) * TW],
                             start=True, stop=True).then_inc(osd, 1)
                    if ti >= 2:
                        t.wait_ge(tvd, b_tvd + ti - 1)
                    for c in range(DC):
                        mi = t.matmul(ps_mm[ti % 2][:, :],
                                      wraw[o % 2][:, c * 128:(c + 1) * 128],
                                      h_sb[:, c * T + j * TW:c * T + (j + 1) * TW],
                                      start=(c == 0), stop=(c == DC - 1))
                    mi.then_inc(mmd, 1)
                    if ti >= 1:
                        op, jp = (ti - 1) // NT, (ti - 1) % NT
                        t.wait_ge(sqd, b_sqd + ti)
                        t.matmul(ps_s1[32 * jp:32 * jp + 32, :], cst["ones32"][:, :],
                                 atmp[(ti - 1) % 2][:, :],
                                 start=(op == 0), stop=(op == DC - 1),
                                 tile_position=(0, 32 * jp), skip_group_check=True)
                        t.matmul(ps_s2[32 * jp:32 * jp + 32, :], cst["ones32"][:, :],
                                 sqb[(ti - 1) % 2][:, :],
                                 start=(op == 0), stop=(op == DC - 1),
                                 tile_position=(0, 32 * jp),
                                 skip_group_check=True).then_inc(std, 1)
            ti = NTILE
            op, jp = (ti - 1) // NT, (ti - 1) % NT
            t.wait_ge(sqd, b_sqd + ti)
            t.matmul(ps_s1[32 * jp:32 * jp + 32, :], cst["ones32"][:, :],
                     atmp[(ti - 1) % 2][:, :], start=(op == 0), stop=True,
                     tile_position=(0, 32 * jp), skip_group_check=True)
            t.matmul(ps_s2[32 * jp:32 * jp + 32, :], cst["ones32"][:, :],
                     sqb[(ti - 1) % 2][:, :], start=(op == 0), stop=True,
                     tile_position=(0, 32 * jp),
                     skip_group_check=True).then_inc(std, 1)

        @block.vector
        def _(v):
            for o in range(DC):
                for j in range(NT):
                    ti = o * NT + j
                    v.wait_ge(osd, b_osd + ti + 1)
                    if ti >= 2:
                        v.wait_ge(tvd, b_tvd + ti - 1)
                    v.tensor_copy(os_sb[ti % 2][:, :], ps_os[:, :]).then_inc(osc, 1)
                    v.wait_ge(mmd, b_mmd + ti + 1)
                    if ti >= 2:
                        v.wait_ge(acd, b_acd + ti - 1)
                    v.tensor_tensor(tev[ti % 2][:, :], ps_mm[ti % 2][:, :],
                                    os_sb[ti % 2][:, :], ALU.mult).then_inc(tvd, 1)

        @block.sync
        def _(sync):
            for o in range(DC):
                for j in range(NT):
                    ti = o * NT + j
                    sync.wait_ge(acd, b_acd + ti + 1)
                    sync.dma_start(out=a_out_ap[o * 128:(o + 1) * 128,
                                                j * TW:(j + 1) * TW],
                                   in_=atmp[ti % 2][:, :]).then_inc(spd[ti % 2], 16)
            sync.wait_ge(spd[0], b_spd[0] + 16 * (NTILE // 2))
            sync.wait_ge(spd[1], b_spd[1] + 16 * (NTILE // 2))

    for k in range(2):
        p.semv[tag + f"_wl{k}"] = b_wl[k] + 16 * (DC // 2)
        p.semv[tag + f"_spd{k}"] = b_spd[k] + 16 * (NTILE // 2)
    p.semv[tag + "_wp"] = b_wp + DC
    p.semv[tag + "_mmd"] = b_mmd + NTILE
    p.semv[tag + "_osd"] = b_osd + NTILE
    p.semv[tag + "_osc"] = b_osc + NTILE
    p.semv[tag + "_tvd"] = b_tvd + NTILE
    p.semv[tag + "_acd"] = b_acd + NTILE
    p.semv[tag + "_sqd"] = b_sqd + NTILE
    p.semv[tag + "_std"] = b_std + NTILE


def emit_deqproj(p: Prog, tag, codes_ap, sc_ap, NOUT, mode, h_sb, cst, bufs,
                 out_ap, ps_mm, ps_aux, psbt=None, dst_fn=None):
    nc = p.nc
    cl = p.sem2(tag + "_cl"); b_cl = [p.semv[tag + "_cl0"], p.semv[tag + "_cl1"]]
    sl = p.sem(tag + "_sl"); b_sl = p.semv[tag + "_sl"]
    spd = p.sem2(tag + "_spd"); b_spd = [p.semv[tag + "_spd0"], p.semv[tag + "_spd1"]]
    acp = p.sem(tag + "_acp"); b_acp = p.semv[tag + "_acp"]
    bcm = p.sem(tag + "_bcm"); b_bcm = p.semv[tag + "_bcm"]
    dqd = p.sem(tag + "_dqd"); b_dqd = p.semv[tag + "_dqd"]
    mmd = p.sem(tag + "_mmd"); b_mmd = p.semv[tag + "_mmd"]
    acd = p.sem(tag + "_acd"); b_acd = p.semv[tag + "_acd"]
    rpd = p.sem(tag + "_rpd"); b_rpd = p.semv[tag + "_rpd"]
    trd = p.sem(tag + "_trd"); b_trd = p.semv[tag + "_trd"]
    wbf, scrow = bufs["wbf"], bufs["scrow"]
    et, esw, e1, e2, er = bufs["et"], bufs["esw"], bufs["e1"], bufs["e2"], bufs["er"]
    COS, SSIN = bufs.get("COS"), bufs.get("SSIN")
    clt = lambda o: b_cl[o % 2] + 16 * (o // 2 + 1)
    spt = lambda ti: b_spd[ti % 2] + 16 * (ti // 2 + 1)
    NTILE = NOUT * NT

    with nc.Block() as block:
        @block.gpsimd
        def _(g):
            for o in range(NOUT):
                if o >= 2:
                    g.wait_ge(mmd, b_mmd + (o - 1) * NT)
                g.dma_start(out=wbf[o % 2][:, :],
                            in_=codes_ap[o * 128:(o + 1) * 128, :]).then_inc(cl[o % 2], 16)

        @block.sync
        def _(sync):
            def store(ti):
                o, j = ti // NT, ti % NT
                sync.wait_ge(rpd, b_rpd + ti + 1)
                if dst_fn is not None:
                    dst = dst_fn(o, j)
                    srcv = (er[ti % 2][:, :].rearrange("q (k f) -> q k f", k=4)
                            if mode == "vtrans" else er[ti % 2][:, :])
                    sync.dma_start(out=dst, in_=srcv).then_inc(spd[ti % 2], 16)
                elif mode == "rope":
                    sync.dma_start(out=out_ap[o * 128:(o + 1) * 128,
                                              j * TW:(j + 1) * TW],
                                   in_=er[ti % 2][:, :]).then_inc(spd[ti % 2], 16)
                else:
                    dst = out_ap[j * TW:(j + 1) * TW,
                                 o * 128:(o + 1) * 128].rearrange(
                                     "(k q) f -> q k f", q=128)
                    sync.dma_start(out=dst, in_=er[ti % 2][:, :].rearrange(
                        "q (k f) -> q k f", k=4)).then_inc(spd[ti % 2], 16)

            for o in range(NOUT):
                if o >= 2:
                    for j in range(NT):
                        store((o - 2) * NT + j)
                if o >= 1:
                    sync.wait_ge(bcm, b_bcm + 8 * o)
                sync.dma_start(out=scrow[:, :],
                               in_=sc_ap[o:o + 1, :]).then_inc(sl, 16)
            for ti in range((NOUT - 2) * NT, NOUT * NT):
                store(ti)
            sync.wait_ge(spd[0], b_spd[0] + 16 * ((NTILE + 1) // 2))
            sync.wait_ge(spd[1], b_spd[1] + 16 * (NTILE // 2))

        @block.scalar
        def _(s):
            def evac(ti):
                s.wait_ge(mmd, b_mmd + ti + 1)
                if ti >= 2:
                    s.wait_ge(rpd, b_rpd + ti - 1)
                s.activation(et[ti % 2][:, :], ps_mm[ti % 2][:, :],
                             AF.Identity).then_inc(acd, 1)

            for o in range(NOUT):
                if o >= 2:
                    for j in range(NT):
                        evac((o - 2) * NT + j)
                s.wait_ge(cl[o % 2], clt(o))
                s.activation(wbf[o % 2][:, :], wbf[o % 2][:, :],
                             AF.Identity, bias=cst["neg8"][:, :],
                             scale=1.0).then_inc(acp, 1)
            for ti in range((NOUT - 2) * NT, NOUT * NT):
                evac(ti)

        @block.tensor
        def _(t):
            for o in range(NOUT):
                t.wait_ge(sl, b_sl + 16 * (o + 1))
                for sidx in range(8):
                    if o * 8 + sidx >= 2:
                        t.wait_ge(dqd, b_dqd + o * 8 + sidx - 1)
                    t.matmul(ps_aux[(o * 8 + sidx) % 2][:, :], cst["ones1"][:, :],
                             scrow[0:1, sidx * TW:(sidx + 1) * TW],
                             start=True, stop=True).then_inc(bcm, 1)
                for j in range(NT):
                    ti = o * NT + j
                    t.wait_ge(dqd, b_dqd + 8 * (o + 1))
                    if ti >= 2:
                        t.wait_ge(acd, b_acd + ti - 1)
                    for c in range(DC):
                        mi = t.matmul(ps_mm[ti % 2][:, :],
                                      wbf[o % 2][:, c * 128:(c + 1) * 128],
                                      h_sb[:, c * T + j * TW:c * T + (j + 1) * TW],
                                      start=(c == 0), stop=(c == DC - 1))
                    mi.then_inc(mmd, 1)
                    if mode == "vtrans":
                        t.wait_ge(acd, b_acd + ti + 1)
                        t.wait_ge(rpd, b_rpd + max(ti - 1, 0))
                        for k4 in range(4):
                            mi2 = t.transpose(psbt[ti % 2][:, k4 * 128:(k4 + 1) * 128],
                                              et[ti % 2][:, k4 * 128:(k4 + 1) * 128],
                                              cst["ident_bf"][:, :])
                        mi2.then_inc(trd, 1)

        @block.vector
        def _(v):
            for o in range(NOUT):
                for sidx in range(8):
                    si = o * 8 + sidx
                    v.wait_ge(bcm, b_bcm + si + 1)
                    v.wait_ge(acp, b_acp + o + 1)
                    sslice = slice(sidx * TW, (sidx + 1) * TW)
                    v.tensor_tensor(wbf[o % 2][:, sslice], wbf[o % 2][:, sslice],
                                    ps_aux[si % 2][:, :], ALU.mult).then_inc(dqd, 1)
                for j in range(NT):
                    ti = o * NT + j
                    if mode == "rope":
                        v.wait_ge(acd, b_acd + ti + 1)
                        if ti >= 2:
                            v.wait_ge(spd[ti % 2], spt(ti - 2))
                        v.stream_shuffle(esw[:, :], et[ti % 2][:, :],
                                         [(k + 1 if k % 2 == 0 else k - 1)
                                          for k in range(32)])
                        v.tensor_tensor(e1[:, :], et[ti % 2][:, :],
                                        COS[:, j * TW:(j + 1) * TW], ALU.mult)
                        v.tensor_tensor(e2[:, :], esw[:, :],
                                        SSIN[:, j * TW:(j + 1) * TW], ALU.mult)
                        v.tensor_tensor(er[ti % 2][:, :], e1[:, :], e2[:, :],
                                        ALU.add).then_inc(rpd, 1)
                    else:
                        v.wait_ge(trd, b_trd + ti + 1)
                        if ti >= 2:
                            v.wait_ge(spd[ti % 2], spt(ti - 2))
                        v.tensor_copy(er[ti % 2][:, :],
                                      psbt[ti % 2][:, :]).then_inc(rpd, 1)

    for k in range(2):
        p.semv[tag + f"_cl{k}"] = b_cl[k] + 16 * ((NOUT + 1 - k) // 2)
        p.semv[tag + f"_spd{k}"] = b_spd[k] + 16 * ((NTILE + 1 - k) // 2)
    p.semv[tag + "_sl"] = b_sl + 16 * NOUT
    p.semv[tag + "_acp"] = b_acp + NOUT
    p.semv[tag + "_bcm"] = b_bcm + 8 * NOUT
    p.semv[tag + "_dqd"] = b_dqd + 8 * NOUT
    p.semv[tag + "_mmd"] = b_mmd + NTILE
    p.semv[tag + "_acd"] = b_acd + NTILE
    p.semv[tag + "_rpd"] = b_rpd + NTILE
    if mode == "vtrans":
        p.semv[tag + "_trd"] = b_trd + NTILE


SCALE = 128 ** -0.5


def attn_schedule():
    tiles = []
    ch_c = ev_c = gr_c = 0
    for gl in range(4):
        for h4 in range(4):
            for qt in range(32):
                L = qt * 128 + 128
                chunks = []
                k = 0
                while k < L:
                    size = min(512, L - k)
                    chunks.append((k, size))
                    k += size
                nsub = L // 128
                ngr = (nsub + 7) // 8
                tiles.append(dict(gl=gl, hloc=gl * 4 + h4, qt=qt, L=L,
                                  chunks=chunks, nsub=nsub, ngr=ngr,
                                  ch_base=ch_c, ev_base=ev_c, gr_base=gr_c))
                ch_c += len(chunks)
                ev_c += len(chunks) + 1
                gr_c += ngr
    return tiles, dict(chunks=ch_c, evacs=ev_c, groups=gr_c, tiles=len(tiles))


def emit_attention(p: Prog, tag, q_ap, kf_ap, vf_ap, att_sb, cst, bufs,
                   ps_s, ps_av, psbt, store_fn=None):
    nc = p.nc
    sched, tot = attn_schedule()
    NTL = tot["tiles"]
    NSB = len(ps_s)
    kgl = p.sem(tag + "_kgl"); b_kgl = p.semv[tag + "_kgl"]
    vgl = p.sem(tag + "_vgl"); b_vgl = p.semv[tag + "_vgl"]
    qld = p.sem(tag + "_qld"); b_qld = p.semv[tag + "_qld"]
    smd = p.sem(tag + "_smd"); b_smd = p.semv[tag + "_smd"]
    sed = p.sem(tag + "_sed"); b_sed = p.semv[tag + "_sed"]
    nmx = p.sem(tag + "_nmx"); b_nmx = p.semv[tag + "_nmx"]
    exd = p.sem(tag + "_exd"); b_exd = p.semv[tag + "_exd"]
    pnd = p.sem(tag + "_pnd"); b_pnd = p.semv[tag + "_pnd"]
    trd = p.sem(tag + "_trd"); b_trd = p.semv[tag + "_trd"]
    ptd = p.sem(tag + "_ptd"); b_ptd = p.semv[tag + "_ptd"]
    avg = p.sem(tag + "_avg"); b_avg = p.semv[tag + "_avg"]
    aed = p.sem(tag + "_aed"); b_aed = p.semv[tag + "_aed"]
    kg, vg, qh = bufs["kg"], bufs["vg"], bufs["qh"]
    Srow, Prow, PT = bufs["Srow"], bufs["Prow"], bufs["PT"]
    TRI, cmax = bufs["TRI"], bufs["cmax"]
    negmax, rowsum, dinv = bufs["negmax"], bufs["rowsum"], bufs["dinv"]

    ch_evdone = []
    for t in sched:
        nch = len(t["chunks"])
        for ci in range(nch):
            if ci == nch - 1:
                ch_evdone.append(t["ev_base"] + nch + 1)
            else:
                ch_evdone.append(t["ev_base"] + ci + 1)

    def gr_done(ti):
        return sched[ti]["gr_base"] + sched[ti]["ngr"]

    with nc.Block() as block:
        @block.sync
        def _(sync):
            for ti, t in enumerate(sched):
                gl, hloc, qt = t["gl"], t["hloc"], t["qt"]
                if qt == 0:
                    if hloc >= 1:
                        sync.wait_ge(smd, b_smd + sched[32 * hloc]["ch_base"])
                    sync.dma_start(out=qh[:, :],
                                   in_=q_ap[hloc * 128:(hloc + 1) * 128, :]
                                   ).then_inc(qld, 16)
                if qt == 0 and hloc % 4 == 0:
                    sync.dma_start(out=kg[:, :],
                                   in_=kf_ap[gl * 128:(gl + 1) * 128, :]
                                   ).then_inc(kgl, 16)
                    if gl >= 1:
                        sync.wait_ge(avg, b_avg + gr_done(128 * gl - 1))
                    sync.dma_start(out=vg[:, :],
                                   in_=vf_ap[:, gl * 128:(gl + 1) * 128].rearrange(
                                       "(i q) f -> q i f", q=128)
                                   ).then_inc(vgl, 16)
                if qt == 0 and store_fn is not None and hloc >= 1:
                    sync.wait_ge(aed, b_aed + 32 * hloc)
                    store_fn(sync, hloc - 1)
            if store_fn is not None:
                sync.wait_ge(aed, b_aed + NTL)
                store_fn(sync, 15)

        @block.tensor
        def _(te):
            def avgrp(tj, G):
                t = sched[tj]
                gg = t["gr_base"] + G
                te.wait_ge(ptd, b_ptd + gg + 1)
                subs = range(G * 8, min(G * 8 + 8, t["nsub"]))
                for si, sub in enumerate(subs):
                    mi = te.matmul(ps_av[tj % 2][:, 0:128],
                                   vg[:, sub * 128:(sub + 1) * 128],
                                   PT[gg % 2][:, si * 128:(si + 1) * 128],
                                   start=(sub == 0), stop=(sub == t["nsub"] - 1))
                mi.then_inc(avg, 1)

            def trav(tj):
                t = sched[tj]
                if tj >= 2:
                    te.wait_ge(aed, b_aed + tj - 1)
                if t["qt"] == 0 and t["hloc"] % 4 == 0:
                    te.wait_ge(vgl, b_vgl + 16 * (t["gl"] + 1))
                for G in range(t["ngr"]):
                    gg = t["gr_base"] + G
                    te.wait_ge(pnd, b_pnd + gg + 1)
                    if gg >= 2:
                        te.wait_ge(ptd, b_ptd + gg - 1)
                    subs = range(G * 8, min(G * 8 + 8, t["nsub"]))
                    for si, sub in enumerate(subs):
                        mi = te.transpose(psbt[gg % 2][:, si * 128:(si + 1) * 128],
                                          Prow[tj % 2][:, sub * 128:(sub + 1) * 128],
                                          cst["ident_bf"][:, :])
                    mi.then_inc(trd, 1)
                    if G >= 1:
                        avgrp(tj, G - 1)
                avgrp(tj, t["ngr"] - 1)

            for ti, t in enumerate(sched):
                gl, hloc, qt = t["gl"], t["hloc"], t["qt"]
                if qt == 0 and hloc % 4 == 0:
                    te.wait_ge(kgl, b_kgl + 16 * (gl + 1))
                if qt == 0:
                    te.wait_ge(qld, b_qld + 16 * (hloc + 1))
                for ci, (k0, sz) in enumerate(t["chunks"]):
                    gc = t["ch_base"] + ci
                    if gc >= NSB:
                        te.wait_ge(sed, b_sed + ch_evdone[gc - NSB])
                    te.matmul(ps_s[gc % NSB][:, 0:sz],
                              qh[:, qt * 128:(qt + 1) * 128],
                              kg[:, k0:k0 + sz],
                              start=True, stop=True).then_inc(smd, 1)
                if ti >= 1:
                    trav(ti - 1)
            trav(NTL - 1)

        @block.vector
        def _(v):
            def norm(tj):
                t = sched[tj]
                v.wait_ge(exd, b_exd + tj + 1)
                for G in range(t["ngr"]):
                    c0, c1 = G * 1024, min(t["L"], (G + 1) * 1024)
                    v.tensor_scalar(Prow[tj % 2][:, c0:c1],
                                    Prow[tj % 2][:, c0:c1],
                                    dinv[tj % 2][:, :], None,
                                    ALU.mult).then_inc(pnd, 1)

            for ti, t in enumerate(sched):
                nch = len(t["chunks"])
                if ti >= 1:
                    norm(ti - 1)
                for ci, (k0, sz) in enumerate(t["chunks"]):
                    gc = t["ch_base"] + ci
                    v.wait_ge(smd, b_smd + gc + 1)
                    if ti >= 2 and ci == 0:
                        v.wait_ge(exd, b_exd + ti - 1)
                    v.tensor_scalar(Srow[ti % 2][:, k0:k0 + sz],
                                    ps_s[gc % NSB][:, 0:sz], SCALE, None,
                                    ALU.mult, ALU.max,
                                    accum_out=cmax[:, ci:ci + 1]).then_inc(sed, 1)
                    if ci == nch - 1:
                        v.scalar_tensor_tensor(
                            Srow[ti % 2][:, t["L"] - 128:t["L"]],
                            ps_s[gc % NSB][:, sz - 128:sz], SCALE,
                            TRI[:, :], ALU.mult, ALU.add).then_inc(sed, 1)
                if ti >= 2:
                    v.wait_ge(exd, b_exd + ti - 1)
                v.reduce_max(negmax[ti % 2][:, :], cmax[:, 0:nch], axis=AX.X,
                             negate=True).then_inc(nmx, 1)
            norm(NTL - 1)

        @block.scalar
        def _(s):
            def pt_copies(tj):
                t = sched[tj]
                for G in range(t["ngr"]):
                    gg = t["gr_base"] + G
                    s.wait_ge(trd, b_trd + gg + 1)
                    if gg >= 2:
                        s.wait_ge(avg, b_avg + gg - 1)
                    nsubs = min(8, t["nsub"] - G * 8)
                    s.activation(PT[gg % 2][:, 0:nsubs * 128],
                                 psbt[gg % 2][:, 0:nsubs * 128],
                                 AF.Identity).then_inc(ptd, 1)

            def av_evac(tj):
                tp = sched[tj]
                s.wait_ge(avg, b_avg + gr_done(tj))
                s.activation(att_sb[:, tp["hloc"] * 4096 + tp["qt"] * 128:
                                    tp["hloc"] * 4096 + (tp["qt"] + 1) * 128],
                             ps_av[tj % 2][:, 0:128],
                             AF.Identity).then_inc(aed, 1)

            for ti, t in enumerate(sched):
                if ti >= 1:
                    pt_copies(ti - 1)
                    av_evac(ti - 1)
                s.wait_ge(nmx, b_nmx + ti + 1)
                s.wait_ge(sed, b_sed + t["ev_base"] + len(t["chunks"]) + 1)
                if ti >= 2:
                    s.wait_ge(trd, b_trd + gr_done(ti - 2))
                    s.wait_ge(pnd, b_pnd + gr_done(ti - 2))
                s.activation(Prow[ti % 2][:, 0:t["L"]], Srow[ti % 2][:, 0:t["L"]],
                             AF.Exp, bias=negmax[ti % 2][:, :], scale=1.0,
                             accum_out=rowsum[ti % 2][:, :])
                s.activation(negmax[ti % 2][:, :], rowsum[ti % 2][:, :], AF.Ln)
                s.activation(dinv[ti % 2][:, :], negmax[ti % 2][:, :],
                             AF.Exp, scale=-1.0).then_inc(exd, 1)
            pt_copies(NTL - 1)
            av_evac(NTL - 1)

    p.semv[tag + "_kgl"] = b_kgl + 16 * 4
    p.semv[tag + "_vgl"] = b_vgl + 16 * 4
    p.semv[tag + "_qld"] = b_qld + 16 * 16
    p.semv[tag + "_smd"] = b_smd + tot["chunks"]
    p.semv[tag + "_sed"] = b_sed + tot["evacs"]
    p.semv[tag + "_nmx"] = b_nmx + NTL
    p.semv[tag + "_exd"] = b_exd + NTL
    p.semv[tag + "_pnd"] = b_pnd + tot["groups"]
    p.semv[tag + "_trd"] = b_trd + tot["groups"]
    p.semv[tag + "_ptd"] = b_ptd + tot["groups"]
    p.semv[tag + "_avg"] = b_avg + tot["groups"]
    p.semv[tag + "_aed"] = b_aed + NTL


def emit_oproj(p: Prog, tag, codes_ap, sc_ap, h_sb, a_out_ap, cst,
               ps_mm, ps_aux, ps_s1, ps_s2, bufs):
    nc = p.nc
    cl = p.sem2(tag + "_cl"); b_cl = [p.semv[tag + "_cl0"], p.semv[tag + "_cl1"]]
    sl = p.sem(tag + "_sl"); b_sl = p.semv[tag + "_sl"]
    spd = p.sem2(tag + "_spd"); b_spd = [p.semv[tag + "_spd0"], p.semv[tag + "_spd1"]]
    acp = p.sem(tag + "_acp"); b_acp = p.semv[tag + "_acp"]
    bcm = p.sem(tag + "_bcm"); b_bcm = p.semv[tag + "_bcm"]
    dqd = p.sem(tag + "_dqd"); b_dqd = p.semv[tag + "_dqd"]
    mmd = p.sem(tag + "_mmd"); b_mmd = p.semv[tag + "_mmd"]
    acd = p.sem(tag + "_acd"); b_acd = p.semv[tag + "_acd"]
    sqd = p.sem(tag + "_sqd"); b_sqd = p.semv[tag + "_sqd"]
    std = p.sem(tag + "_std"); b_std = p.semv[tag + "_std"]
    wbf, scrow = bufs["wbf"], bufs["scrow"]
    atmp, sqb = bufs["atmp"], bufs["sqb"]
    clt = lambda o: b_cl[o % 2] + 16 * (o // 2 + 1)
    spt = lambda ti: b_spd[ti % 2] + 16 * (ti // 2 + 1)
    NTILE = DC * NT

    with nc.Block() as block:
        @block.gpsimd
        def _(g):
            for o in range(DC):
                if o >= 2:
                    g.wait_ge(mmd, b_mmd + (o - 1) * NT)
                g.dma_start(out=wbf[o % 2][:, :],
                            in_=codes_ap[o * 128:(o + 1) * 128, :]).then_inc(cl[o % 2], 16)

        @block.sync
        def _(sync):
            def spill(ti):
                o, j = ti // NT, ti % NT
                sync.wait_ge(acd, b_acd + ti + 1)
                sync.dma_start(out=a_out_ap[o * 128:(o + 1) * 128,
                                            j * TW:(j + 1) * TW],
                               in_=atmp[ti % 2][:, :]).then_inc(spd[ti % 2], 16)

            for o in range(DC):
                if o >= 2:
                    for j in range(NT):
                        spill((o - 2) * NT + j)
                if o >= 1:
                    sync.wait_ge(bcm, b_bcm + 8 * o)
                sync.dma_start(out=scrow[:, :],
                               in_=sc_ap[o:o + 1, :]).then_inc(sl, 16)
            for ti in range((DC - 2) * NT, DC * NT):
                spill(ti)
            sync.wait_ge(spd[0], b_spd[0] + 16 * (NTILE // 2))
            sync.wait_ge(spd[1], b_spd[1] + 16 * (NTILE // 2))

        @block.scalar
        def _(s):
            def evac(ti):
                s.wait_ge(mmd, b_mmd + ti + 1)
                if ti >= 2:
                    s.wait_ge(spd[ti % 2], spt(ti - 2))
                    s.wait_ge(std, b_std + ti - 1)
                s.activation(atmp[ti % 2][:, :], ps_mm[ti % 2][:, :],
                             AF.Identity).then_inc(acd, 1)
                s.activation(sqb[ti % 2][:, :], atmp[ti % 2][:, :],
                             AF.Square).then_inc(sqd, 1)

            for o in range(DC):
                if o >= 2:
                    for j in range(NT):
                        evac((o - 2) * NT + j)
                s.wait_ge(cl[o % 2], clt(o))
                s.activation(wbf[o % 2][:, :], wbf[o % 2][:, :],
                             AF.Identity, bias=cst["neg8"][:, :],
                             scale=1.0).then_inc(acp, 1)
            for ti in range((DC - 2) * NT, DC * NT):
                evac(ti)

        @block.tensor
        def _(t):
            for o in range(DC):
                t.wait_ge(sl, b_sl + 16 * (o + 1))
                for sidx in range(8):
                    if o * 8 + sidx >= 2:
                        t.wait_ge(dqd, b_dqd + o * 8 + sidx - 1)
                    t.matmul(ps_aux[(o * 8 + sidx) % 2][:, :], cst["ones1"][:, :],
                             scrow[0:1, sidx * TW:(sidx + 1) * TW],
                             start=True, stop=True).then_inc(bcm, 1)
                for j in range(NT):
                    ti = o * NT + j
                    t.wait_ge(dqd, b_dqd + 8 * (o + 1))
                    if ti >= 2:
                        t.wait_ge(acd, b_acd + ti - 1)
                    for c in range(DC):
                        mi = t.matmul(ps_mm[ti % 2][:, :],
                                      wbf[o % 2][:, c * 128:(c + 1) * 128],
                                      h_sb[:, c * T + j * TW:c * T + (j + 1) * TW],
                                      start=(c == 0), stop=(c == DC - 1))
                    mi.then_inc(mmd, 1)
                    if ti >= 1:
                        op, jp = (ti - 1) // NT, (ti - 1) % NT
                        t.wait_ge(sqd, b_sqd + ti)
                        t.matmul(ps_s1[32 * jp:32 * jp + 32, :], cst["ones32"][:, :],
                                 atmp[(ti - 1) % 2][:, :],
                                 start=(op == 0), stop=(op == DC - 1),
                                 tile_position=(0, 32 * jp), skip_group_check=True)
                        t.matmul(ps_s2[32 * jp:32 * jp + 32, :], cst["ones32"][:, :],
                                 sqb[(ti - 1) % 2][:, :],
                                 start=(op == 0), stop=(op == DC - 1),
                                 tile_position=(0, 32 * jp),
                                 skip_group_check=True).then_inc(std, 1)
            ti = NTILE
            op, jp = (ti - 1) // NT, (ti - 1) % NT
            t.wait_ge(sqd, b_sqd + ti)
            t.matmul(ps_s1[32 * jp:32 * jp + 32, :], cst["ones32"][:, :],
                     atmp[(ti - 1) % 2][:, :], start=(op == 0), stop=True,
                     tile_position=(0, 32 * jp), skip_group_check=True)
            t.matmul(ps_s2[32 * jp:32 * jp + 32, :], cst["ones32"][:, :],
                     sqb[(ti - 1) % 2][:, :], start=(op == 0), stop=True,
                     tile_position=(0, 32 * jp),
                     skip_group_check=True).then_inc(std, 1)

        @block.vector
        def _(v):
            for o in range(DC):
                for sidx in range(8):
                    si = o * 8 + sidx
                    v.wait_ge(bcm, b_bcm + si + 1)
                    v.wait_ge(acp, b_acp + o + 1)
                    sslice = slice(sidx * TW, (sidx + 1) * TW)
                    v.tensor_tensor(wbf[o % 2][:, sslice], wbf[o % 2][:, sslice],
                                    ps_aux[si % 2][:, :], ALU.mult).then_inc(dqd, 1)

    for k in range(2):
        p.semv[tag + f"_cl{k}"] = b_cl[k] + 16 * (DC // 2)
        p.semv[tag + f"_spd{k}"] = b_spd[k] + 16 * (NTILE // 2)
    p.semv[tag + "_sl"] = b_sl + 16 * DC
    p.semv[tag + "_acp"] = b_acp + DC
    p.semv[tag + "_bcm"] = b_bcm + 8 * DC
    p.semv[tag + "_dqd"] = b_dqd + 8 * DC
    p.semv[tag + "_mmd"] = b_mmd + NTILE
    p.semv[tag + "_acd"] = b_acd + NTILE
    p.semv[tag + "_sqd"] = b_sqd + NTILE
    p.semv[tag + "_std"] = b_std + NTILE


def build_attn():
    p = Prog()
    q_att = p.din("q_att", [2048, 2 * T], BF16)
    k_att = p.din("k_att", [512, 2 * T], BF16)
    v_att = p.din("v_att", [2 * T, 512], BF16)
    tri = p.din("tri", [128, 128], F32)
    att_out = p.dout("att_out", [2048, 2 * T], BF16)

    com = alloc_common(p)
    cst = com["cst"]
    nc = p.nc
    with ExitStack() as stk:
        bufs = {
            "kg": p.sb("kg", [128, 2 * T], BF16, stk),
            "vg": p.sb("vg", [128, 2 * T], BF16, stk),
            "qh": p.sb("qh", [128, 2 * T], BF16, stk),
            "Srow": [p.sb(f"Srow{i}", [128, 2 * T], F32, stk) for i in range(2)],
            "Prow": [p.sb(f"Prow{i}", [128, 2 * T], BF16, stk) for i in range(2)],
            "PT": [p.sb(f"PTb{i}", [128, 1024], BF16, stk) for i in range(2)],
            "TRI": p.sb("TRI", [128, 128], F32, stk),
            "cmax": p.sb("cmax", [128, 8], F32, stk),
            "negmax": [p.sb(f"negmax{i}", [128, 1], F32, stk) for i in range(2)],
            "rowsum": [p.sb(f"rowsum{i}", [128, 1], F32, stk) for i in range(2)],
            "dinv": [p.sb(f"dinv{i}", [128, 1], F32, stk) for i in range(2)],
        }
        psbt = [p.psum(f"psbt{i}", [128, 1024], BF16, stk) for i in range(2)]
        ldt = p.sem("ldt")
        with nc.Block() as block:
            @block.sync
            def _(sync):
                sync.dma_start(out=bufs["TRI"][:, :], in_=tri).then_inc(ldt, 16)
                sync.wait_ge(ldt, p.semv["ldt"] + 16)
        p.semv["ldt"] += 16
        emit_attention(p, "at", q_att, k_att, v_att, com["h_sb"], cst, bufs,
                       com["ps_mm"] + [com["ps_s1"], com["ps_s2"]],
                       com["ps_aux"], psbt)
        std = p.sem("stda")
        with nc.Block() as block:
            @block.sync
            def _(sync):
                for hl in range(16):
                    sync.dma_start(out=att_out[hl * 128:(hl + 1) * 128, :],
                                   in_=com["h_sb"][:, hl * 4096:(hl + 1) * 4096]
                                   ).then_inc(std, 16)
                sync.wait_ge(std, p.semv["stda"] + 16 * 16)
        p.semv["stda"] += 16 * 16
    p.ctx.close()
    return p


def _tri_mask():
    r = np.arange(128)
    return np.where(r[:, None] >= r[None, :], 0.0, NEG).astype(np.float32)


def build_oproj():
    p = Prog()
    nc = p.nc
    att = p.din("att", [D, T], BF16)
    oc = p.din("oc", [D, D], I32)
    osd = p.din("osd", [DC, D], F32)
    x2 = p.din("x2", [D, T], F32)
    x3 = p.dout("x3", [D, T], F32)
    a_dram = p.dint("a_scr", [D, T], F32)

    com = alloc_common(p)
    cst = com["cst"]
    lda = p.sem("lda")
    with nc.Block() as block:
        @block.sync
        def _(sync):
            for c in range(DC):
                sync.dma_start(out=com["h_sb"][:, c * T:(c + 1) * T],
                               in_=att[c * 128:(c + 1) * 128, :]).then_inc(lda, 16)
            sync.wait_ge(lda, p.semv["lda"] + 16 * DC)
    p.semv["lda"] += 16 * DC

    with ExitStack() as stk:
        bufs = {
            "wbf": [p.sb(f"owbf{i}", [128, D], BF16, stk) for i in range(2)],
            "scrow": p.sb("oscrow", [1, D], F32, stk),
            "atmp": [p.sb(f"oatmp{i}", [128, TW], F32, stk) for i in range(2)],
            "sqb": [p.sb(f"osqb{i}", [128, TW], F32, stk) for i in range(2)],
        }
        emit_oproj(p, "mm", oc, osd, com["h_sb"], a_dram, cst,
                   com["ps_mm"], com["ps_aux"], com["ps_s1"], com["ps_s2"], bufs)
    with ExitStack() as stkln:
        alloc_ln(p, com, stkln)
        with ExitStack() as stk:
            phase_rows2_update(p, "u2", com, x2, a_dram, x3, stk, do_stats=False)
    p.ctx.close()
    return p


def emit_router(p: Prog, tag, rw_sb, h_sb, ps_aux, cst, lg_fm, lg_tm, scr_tm,
                r_tm, w_fmT):
    nc = p.nc
    gd = p.sem(tag + "_gd"); b_gd = p.semv[tag + "_gd"]
    ev = p.sem(tag + "_ev"); b_ev = p.semv[tag + "_ev"]

    with nc.Block() as block:
        @block.tensor
        def _(t):
            for j in range(NT):
                if j >= 1:
                    t.wait_ge(ev, b_ev + j)
                for c in range(DC):
                    mi = t.matmul(ps_aux[0][0:8, :], rw_sb[:, c * 8:(c + 1) * 8],
                                  h_sb[:, c * T + j * TW:c * T + (j + 1) * TW],
                                  start=(c == 0), stop=(c == DC - 1))
                mi.then_inc(gd, 1)

        @block.vector
        def _(v):
            for j in range(NT):
                v.wait_ge(gd, b_gd + j + 1)
                v.tensor_copy(lg_fm[0:8, j * TW:(j + 1) * TW],
                              ps_aux[0][0:8, :]).then_inc(ev, 1)
    p.semv[tag + "_gd"] = b_gd + NT
    p.semv[tag + "_ev"] = b_ev + NT

    b_gd = p.semv[tag + "_gd"]
    with nc.Block() as block:
        @block.tensor
        def _(t):
            for k in range(16):
                mi = t.transpose(ps_aux[1][:, k * 8:(k + 1) * 8],
                                 lg_fm[0:8, k * 128:(k + 1) * 128],
                                 cst["ident_f"][0:8, 0:8])
            mi.then_inc(gd, 1)

        @block.vector
        def _(v):
            v.wait_ge(gd, b_gd + 1)
            v.tensor_copy(lg_tm[:, :], ps_aux[1][:, 0:128])
    p.semv[tag + "_gd"] = b_gd + 1

    sm = p.sem(tag + "_sm"); b_sm = p.semv[tag + "_sm"]
    with nc.Block() as block:
        @block.vector
        def _(v):
            g3 = lambda x: x[:, :].rearrange("p (g e) -> p g e", e=8)
            mx1 = scr_tm[:, 0:16]
            mx2 = scr_tm[:, 16:32]
            mh = scr_tm[:, 0:1]
            m1view = bass.AP(mh.tensor, mh.offset, [mh.ap[0], [1, 16], [0, 8]])
            m2h = scr_tm[:, 16:17]
            m2view = bass.AP(m2h.tensor, m2h.offset, [m2h.ap[0], [1, 16], [0, 8]])
            v.reduce_max(mx1, g3(lg_tm), axis=AX.X)
            v.tensor_tensor(g3(r_tm), g3(lg_tm), m1view, ALU.is_ge)
            v.scalar_tensor_tensor(g3(r_tm), r_tm[:, :].rearrange(
                "p (g e) -> p g e", e=8), NEG, g3(lg_tm), ALU.mult, ALU.add)
            v.reduce_max(mx2, g3(r_tm), axis=AX.X)
            v.tensor_tensor(g3(lg_tm), g3(lg_tm), m1view,
                            ALU.subtract).then_inc(sm, 1)

        @block.scalar
        def _(s):
            s.wait_ge(sm, b_sm + 1)
            s.activation(lg_tm[:, :], lg_tm[:, :], AF.Exp).then_inc(sm, 1)

        @block.vector
        def _(v):
            v.wait_ge(sm, b_sm + 2)
            mx1 = scr_tm[:, 0:16]
            mx2 = scr_tm[:, 16:32]
            v.tensor_tensor(mx2, mx2, mx1, ALU.subtract).then_inc(sm, 1)

        @block.scalar
        def _(s):
            s.wait_ge(sm, b_sm + 3)
            s.activation(scr_tm[:, 16:32], scr_tm[:, 16:32],
                         AF.Exp).then_inc(sm, 1)

        @block.vector
        def _(v):
            v.wait_ge(sm, b_sm + 4)
            g3 = lambda x: x[:, :].rearrange("p (g e) -> p g e", e=8)
            e2h = scr_tm[:, 16:17]
            e2view = bass.AP(e2h.tensor, e2h.offset, [e2h.ap[0], [1, 16], [0, 8]])
            v.tensor_tensor(g3(r_tm), g3(lg_tm), e2view, ALU.is_ge)
            v.tensor_tensor(lg_tm[:, :], lg_tm[:, :], r_tm[:, :], ALU.mult)
            v.tensor_scalar(scr_tm[:, 32:48], scr_tm[:, 16:32], 1.0, None, ALU.add)
            v.drain()
            v.reciprocal(scr_tm[:, 48:64], scr_tm[:, 32:48])
            v.drain()
            dh = scr_tm[:, 48:49]
            dview = bass.AP(dh.tensor, dh.offset, [dh.ap[0], [1, 16], [0, 8]])
            v.tensor_tensor(g3(r_tm), g3(lg_tm), dview, ALU.mult).then_inc(sm, 1)
    p.semv[tag + "_sm"] = b_sm + 5

    b_gd = p.semv[tag + "_gd"]; b_ev = p.semv[tag + "_ev"]
    with nc.Block() as block:
        @block.tensor
        def _(t):
            for g4 in range(4):
                if g4 >= 1:
                    t.wait_ge(ev, b_ev + g4)
                for k4 in range(4):
                    k = g4 * 4 + k4
                    mi = t.transpose(ps_aux[0][0:8, k4 * 128:(k4 + 1) * 128],
                                     r_tm[:, k * 8:(k + 1) * 8],
                                     cst["ident_f"][:, :])
                mi.then_inc(gd, 1)

        @block.vector
        def _(v):
            for g4 in range(4):
                v.wait_ge(gd, b_gd + g4 + 1)
                v.tensor_copy(w_fmT[0:8, g4 * TW:(g4 + 1) * TW],
                              ps_aux[0][0:8, 0:TW]).then_inc(ev, 1)
    p.semv[tag + "_gd"] = b_gd + 4
    p.semv[tag + "_ev"] = b_ev + 4


def emit_mid(p: Prog, tag, lA_sb, P2_sb, w_fmT, h_sb, mid_sb, wx_sb, ps_mm,
             ps_aux, cst):
    nc = p.nc
    wxd = p.sem(tag + "_wxd"); b_wxd = p.semv[tag + "_wxd"]
    wxc = p.sem(tag + "_wxc"); b_wxc = p.semv[tag + "_wxc"]
    mmd = p.sem(tag + "_mmd"); b_mmd = p.semv[tag + "_mmd"]
    mtd = p.sem(tag + "_mtd"); b_mtd = p.semv[tag + "_mtd"]

    with nc.Block() as block:
        @block.tensor
        def _(t):
            for j in range(NT):
                if j >= 2:
                    t.wait_ge(wxc, b_wxc + j - 1)
                t.matmul(ps_aux[j % 2][:, :], P2_sb[0:8, :],
                         w_fmT[0:8, j * TW:(j + 1) * TW],
                         start=True, stop=True).then_inc(wxd, 1)
            for j in range(NT):
                if j >= 2:
                    t.wait_ge(mtd, b_mtd + j - 1)
                for c in range(DC):
                    mi = t.matmul(ps_mm[j % 2][:, :],
                                  lA_sb[:, c * 128:(c + 1) * 128],
                                  h_sb[:, c * T + j * TW:c * T + (j + 1) * TW],
                                  start=(c == 0), stop=(c == DC - 1))
                mi.then_inc(mmd, 1)

        @block.vector
        def _(v):
            for j in range(NT):
                v.wait_ge(wxd, b_wxd + j + 1)
                v.tensor_copy(wx_sb[:, j * TW:(j + 1) * TW],
                              ps_aux[j % 2][:, :]).then_inc(wxc, 1)
            for j in range(NT):
                v.wait_ge(mmd, b_mmd + j + 1)
                v.tensor_tensor(mid_sb[:, j * TW:(j + 1) * TW],
                                ps_mm[j % 2][:, :],
                                wx_sb[:, j * TW:(j + 1) * TW],
                                ALU.mult).then_inc(mtd, 1)
    p.semv[tag + "_wxd"] = b_wxd + NT
    p.semv[tag + "_wxc"] = b_wxc + NT
    p.semv[tag + "_mmd"] = b_mmd + NT
    p.semv[tag + "_mtd"] = b_mtd + NT


def emit_mm_l1f(p: Prog, tag, w_ap, lB_sb, mid_sb, h_sb, a_out_ap, cst,
                ps_mm, ps_s1, ps_s2, wraw, atmp, sqb):
    nc = p.nc
    wl = p.sem2(tag + "_wl"); b_wl = [p.semv[tag + "_wl0"], p.semv[tag + "_wl1"]]
    spd = p.sem2(tag + "_spd"); b_spd = [p.semv[tag + "_spd0"], p.semv[tag + "_spd1"]]
    mmd = p.sem(tag + "_mmd"); b_mmd = p.semv[tag + "_mmd"]
    evd = p.sem(tag + "_evd"); b_evd = p.semv[tag + "_evd"]
    sqd = p.sem(tag + "_sqd"); b_sqd = p.semv[tag + "_sqd"]
    std = p.sem(tag + "_std"); b_std = p.semv[tag + "_std"]
    wlt = lambda o: b_wl[o % 2] + 16 * (o // 2 + 1)
    spt = lambda ti: b_spd[ti % 2] + 16 * (ti // 2 + 1)
    NTILE = DC * NT

    with nc.Block() as block:
        @block.gpsimd
        def _(g):
            for o in range(DC):
                if o >= 2:
                    g.wait_ge(mmd, b_mmd + (o - 1) * NT)
                g.dma_start(out=wraw[o % 2][:, :],
                            in_=w_ap[o * 128:(o + 1) * 128, :]).then_inc(wl[o % 2], 16)

        @block.tensor
        def _(t):
            for o in range(DC):
                t.wait_ge(wl[o % 2], wlt(o))
                for j in range(NT):
                    ti = o * NT + j
                    if ti >= 2:
                        t.wait_ge(evd, b_evd + ti - 1)
                    for c in range(DC):
                        t.matmul(ps_mm[ti % 2][:, :],
                                 wraw[o % 2][:, c * 128:(c + 1) * 128],
                                 h_sb[:, c * T + j * TW:c * T + (j + 1) * TW],
                                 start=(c == 0), stop=False)
                    mi = t.matmul(ps_mm[ti % 2][:, :],
                                  lB_sb[:, o * 128:(o + 1) * 128],
                                  mid_sb[:, j * TW:(j + 1) * TW],
                                  start=False, stop=True)
                    mi.then_inc(mmd, 1)
                    if ti >= 1:
                        op, jp = (ti - 1) // NT, (ti - 1) % NT
                        t.wait_ge(sqd, b_sqd + ti)
                        t.matmul(ps_s1[32 * jp:32 * jp + 32, :], cst["ones32"][:, :],
                                 atmp[(ti - 1) % 2][:, :],
                                 start=(op == 0), stop=(op == DC - 1),
                                 tile_position=(0, 32 * jp), skip_group_check=True)
                        t.matmul(ps_s2[32 * jp:32 * jp + 32, :], cst["ones32"][:, :],
                                 sqb[(ti - 1) % 2][:, :],
                                 start=(op == 0), stop=(op == DC - 1),
                                 tile_position=(0, 32 * jp),
                                 skip_group_check=True).then_inc(std, 1)
            ti = NTILE
            op, jp = (ti - 1) // NT, (ti - 1) % NT
            t.wait_ge(sqd, b_sqd + ti)
            t.matmul(ps_s1[32 * jp:32 * jp + 32, :], cst["ones32"][:, :],
                     atmp[(ti - 1) % 2][:, :], start=(op == 0), stop=True,
                     tile_position=(0, 32 * jp), skip_group_check=True)
            t.matmul(ps_s2[32 * jp:32 * jp + 32, :], cst["ones32"][:, :],
                     sqb[(ti - 1) % 2][:, :], start=(op == 0), stop=True,
                     tile_position=(0, 32 * jp),
                     skip_group_check=True).then_inc(std, 1)

        @block.scalar
        def _(s):
            for o in range(DC):
                for j in range(NT):
                    ti = o * NT + j
                    s.wait_ge(mmd, b_mmd + ti + 1)
                    if ti >= 2:
                        s.wait_ge(spd[ti % 2], spt(ti - 2))
                        s.wait_ge(std, b_std + ti - 1)
                    s.activation(atmp[ti % 2][:, :], ps_mm[ti % 2][:, :],
                                 AF.Identity).then_inc(evd, 1)
                    s.activation(sqb[ti % 2][:, :], atmp[ti % 2][:, :],
                                 AF.Square).then_inc(sqd, 1)

        @block.sync
        def _(sync):
            for o in range(DC):
                for j in range(NT):
                    ti = o * NT + j
                    sync.wait_ge(evd, b_evd + ti + 1)
                    sync.dma_start(out=a_out_ap[o * 128:(o + 1) * 128,
                                                j * TW:(j + 1) * TW],
                                   in_=atmp[ti % 2][:, :]).then_inc(spd[ti % 2], 16)
            sync.wait_ge(spd[0], b_spd[0] + 16 * (NTILE // 2))
            sync.wait_ge(spd[1], b_spd[1] + 16 * (NTILE // 2))

    for k in range(2):
        p.semv[tag + f"_wl{k}"] = b_wl[k] + 16 * (DC // 2)
        p.semv[tag + f"_spd{k}"] = b_spd[k] + 16 * (NTILE // 2)
    p.semv[tag + "_mmd"] = b_mmd + NTILE
    p.semv[tag + "_evd"] = b_evd + NTILE
    p.semv[tag + "_sqd"] = b_sqd + NTILE
    p.semv[tag + "_std"] = b_std + NTILE


def emit_ffn1(p: Prog, com, aps):
    nc = p.nc
    cst = com["cst"]
    with ExitStack() as stkln:
        alloc_ln(p, com, stkln)
        with ExitStack() as stk:
            phase_stats_rows(p, "st", com, aps["x3"], stk)
        with ExitStack() as stk:
            phase_h(p, "h", com, aps["x3"], stk)

    with ExitStack() as stkf:
        mid_sb = p.sb("mid_sb", [128, T], BF16, stkf)
        lB_sb = p.sb("lB_sb", [128, D], BF16, stkf)
        with ExitStack() as stk:
            rw_sb = p.sb("rw_sb", [128, DC * 8], BF16, stk)
            lA_sb = p.sb("lA_sb", [128, DC * 128], BF16, stk)
            P2_sb = p.sb("P2_sb", [8, 128], F32, stk)
            lg_fm = p.sb("lg_fm", [8, T], F32, stk)
            lg_tm = p.sb("lg_tm", [128, 128], F32, stk)
            scr_tm = p.sb("scr_tm", [128, 64], F32, stk)
            r_tm = p.sb("r_tm", [128, 128], F32, stk)
            w_fmT = p.sb("w_fmT", [8, T], F32, stk)
            wx_sb = p.sb("wx_sb", [128, T], F32, stk)
            ldr = p.sem("ldr")
            ldr2 = p.sem("ldr2")
            with nc.Block() as block:
                @block.gpsimd
                def _(g):
                    g.dma_start(out=rw_sb[:, :], in_=aps["rw"]).then_inc(ldr, 16)
                    g.dma_start(out=lA_sb[:, :].rearrange("p (c e) -> p c e", c=DC),
                                in_=aps["lA"].rearrange("(c p) e -> p c e", p=128)
                                ).then_inc(ldr, 16)
                    g.dma_start(out=lB_sb[:, :], in_=aps["lB"]).then_inc(ldr, 16)
                    g.wait_ge(ldr, p.semv["ldr"] + 48)

                @block.sync
                def _(sync):
                    sync.dma_start(out=P2_sb[:, :], in_=aps["P2"]).then_inc(ldr2, 16)
                    sync.wait_ge(ldr2, p.semv["ldr2"] + 16)
            p.semv["ldr"] += 48
            p.semv["ldr2"] += 16
            emit_router(p, "gt", rw_sb, com["h_sb"], com["ps_aux"], cst,
                        lg_fm, lg_tm, scr_tm, r_tm, w_fmT)
            emit_mid(p, "md", lA_sb, P2_sb, w_fmT, com["h_sb"], mid_sb, wx_sb,
                     com["ps_mm"], com["ps_aux"], cst)
        with ExitStack() as stk:
            wraw = [p.sb(f"b_wraw{i}", [128, D], BF16, stk) for i in range(2)]
            atmp = [p.sb(f"b_atmp{i}", [128, TW], F32, stk) for i in range(2)]
            sqb = [p.sb(f"b_sqtb{i}", [128, TW], F32, stk) for i in range(2)]
            emit_mm_l1f(p, "mm", aps["bW"], lB_sb, mid_sb, com["h_sb"],
                        aps["a_scr"], cst, com["ps_mm"], com["ps_s1"],
                        com["ps_s2"], wraw, atmp, sqb)
    with ExitStack() as stkln:
        alloc_ln(p, com, stkln, tag="u3")
        with ExitStack() as stk:
            phase_rows2_update(p, "u", com, aps["x3"], aps["a_scr"],
                               aps["out"], stk, do_stats=False)


def build_ffn1():
    p = Prog()
    aps = {
        "x3": p.din("x3", [D, T], F32),
        "bW": p.din("bW", [D, D], F32),
        "rw": p.din("rw", [128, DC * 8], F32),
        "lA": p.din("lA", [D, 128], F32),
        "lB": p.din("lB", [128, D], F32),
        "P2": p.din("P2", [8, 128], F32),
        "out": p.dout("out", [D, T], F32),
        "a_scr": p.dint("a_scr", [D, T], F32),
    }
    com = alloc_common(p)
    emit_ffn1(p, com, aps)
    p.ctx.close()
    return p


def _prep_ffn1(inputs):
    P2 = np.zeros((8, 128), np.float32)
    for e in range(8):
        P2[e, e * 16:(e + 1) * 16] = 2.0
    return {
        "bW": _tile_w(np.ascontiguousarray(inputs["L1_ffn_base_W"].T)).reshape(D, D),
        "rw": np.ascontiguousarray(
            inputs["L1_ffn_router_W"].T.reshape(DC, 128, 8).transpose(1, 0, 2)
            .reshape(128, DC * 8)),
        "lA": np.ascontiguousarray(
            inputs["L1_ffn_lora_A"].transpose(1, 0, 2).reshape(D, 128)),
        "lB": np.ascontiguousarray(inputs["L1_ffn_lora_B"].reshape(128, D)),
        "P2": P2,
    }


def alloc_common(p: Prog):
    cst = {
        "io_free": p.sb("io_free", [128, 128], I32),
        "io_part": p.sb("io_part", [128, 128], I32),
        "io_g32": p.sb("io_g32", [128, 128], I32),
        "ones32": p.sb("ones32", [128, 32], F32),
        "ones1": p.sb("ones1", [1, 128], F32),
        "eps": p.sb("eps", [128, 1], F32),
        "neg8": p.sb("neg8", [128, 1], F32),
        "ident_bf": p.sb("ident_bf", [128, 128], BF16),
        "ident_f": p.sb("ident_f", [128, 128], F32),
    }
    com = {
        "cst": cst,
        "ps_s1": p.psum("ps_s1", [128, TW], F32),
        "ps_s2": p.psum("ps_s2", [128, TW], F32),
        "ps_aux": [p.psum(f"ps_aux{i}", [128, TW], F32) for i in range(2)],
        "ps_mm": [p.psum(f"ps_mm{i}", [128, TW], F32) for i in range(2)],
        "h_sb": p.sb("h_sb", [128, DC * T], BF16),
    }
    emit_consts(p, cst)
    return com


def alloc_ln(p: Prog, com, stack, tag=""):
    com["RB"] = p.sb(tag + "RB", [128, T], F32, stack)
    com["NB"] = p.sb(tag + "NB", [128, T], F32, stack)
    com["rowbufs"] = {k: p.sb(f"{tag}row_{k}", [128, TW], F32, stack)
                      for k in ("s1", "s2", "var", "rstd", "negm")}
    cst = com["cst"]
    cst["sel"] = [p.sb(f"{tag}sel{j}", [128, 128], F32, stack)
                  for j in range(NT)]
    with p.nc.Block() as block:
        @block.vector
        def _(v):
            for j in range(NT):
                v.tensor_scalar(cst["sel"][j][:, :], cst["io_part"][:, :],
                                32 * j, None, ALU.is_equal)


def phase_stats_rows(p: Prog, tag, com, src_ap, stack):
    xb = [p.sb(f"{tag}_xb{i}", [128, T], F32, stack) for i in range(2)]
    sq = [p.sb(f"{tag}_sq{i}", [128, T], F32, stack) for i in range(2)]
    emit_stream_stats(p, tag + "s", src_ap, com["ps_s1"], com["ps_s2"],
                      com["cst"], xb, sq)
    emit_rows(p, tag + "r", com["ps_s1"], com["ps_s2"], com["ps_aux"],
              com["cst"], com["rowbufs"], com["RB"], com["NB"])


def phase_h(p: Prog, tag, com, src_ap, stack):
    xb = [p.sb(f"{tag}_xb{i}", [128, T], F32, stack) for i in range(2)]
    ts = p.sb(f"{tag}_ts", [128, T], F32, stack)
    emit_hpass(p, tag, src_ap, com["RB"], com["NB"], com["h_sb"], xb, ts)


def phase_rows2_update(p: Prog, tag, com, x_ap, a_ap, out_ap, stack,
                       do_stats=True):
    emit_rows(p, tag + "r", com["ps_s1"], com["ps_s2"], com["ps_aux"],
              com["cst"], com["rowbufs"], com["RB"], com["NB"])
    HW = T // 2
    bufs = {
        "xb": [p.sb(f"{tag}_xb{i}", [128, HW], F32, stack) for i in range(2)],
        "ab": [p.sb(f"{tag}_ab{i}", [128, HW], F32, stack) for i in range(2)],
        "xn": [p.sb(f"{tag}_xn{i}", [128, HW], F32, stack) for i in range(2)],
        "t1": p.sb(f"{tag}_t1", [128, HW], F32, stack),
        "sq": [p.sb(f"{tag}_sqh{i}", [128, HW], F32, stack) for i in range(2)],
    }
    emit_update(p, tag + "u", x_ap, a_ap, com["RB"], com["NB"], out_ap,
                com["ps_s1"], com["ps_s2"], com["cst"], bufs, do_stats=do_stats)


def emit_block0(p: Prog, com, aps, dbg=None, stack=None):
    nc = p.nc
    cst = com["cst"]
    bias_sb = p.sb("bias_sb", [128, DC], F32, stack)
    biasf_sb = p.sb("biasf_sb", [128, DC], F32, stack)

    ld0 = p.sem("ld0")
    with nc.Block() as block:
        @block.sync
        def _(sync):
            sync.dma_start(out=bias_sb[:, :], in_=aps["bias0"]).then_inc(ld0, 16)
            sync.dma_start(out=biasf_sb[:, :], in_=aps["biasf"]).then_inc(ld0, 16)
            sync.wait_ge(ld0, p.semv["ld0"] + 32)
    p.semv["ld0"] += 32

    with ExitStack() as stk_w:
        bmP = p.sb("bmP", [128, D], BF16, stk_w)
        with ExitStack() as stk:
            emit_maskprep(p, "mk", aps["maskT"], cst, bmP, stk)
        with ExitStack() as stk:
            phase_stats_rows(p, "st", com, aps["x0"], stk)
        with ExitStack() as stk:
            phase_h(p, "h", com, aps["x0"], stk)
        with ExitStack() as stk:
            wraw = [p.sb(f"a_wraw{i}", [128, D], BF16, stk) for i in range(2)]
            atmp = [p.sb(f"a_atmp{i}", [128, TW], F32, stk) for i in range(2)]
            sqb = [p.sb(f"a_sqtb{i}", [128, TW], F32, stk) for i in range(2)]
            emit_mm_l0a(p, "mm", aps["w0"], bias_sb, com["h_sb"], aps["a_scr"],
                        bmP, cst, com["ps_mm"], com["ps_s1"], com["ps_s2"],
                        wraw, atmp, sqb)
    with ExitStack() as stk:
        phase_rows2_update(p, "u", com, aps["x0"], aps["a_scr"], aps["x1"],
                           stk, do_stats=True)

    with ExitStack() as stk_f:
        r_fmT = p.sb("r_fmT", [8, T], BF16, stk_f)
        ocs_sb = p.sb("ocs_sb", [8, D], BF16, stk_f)
        emit_rows(p, "str", com["ps_s1"], com["ps_s2"], com["ps_aux"],
                  cst, com["rowbufs"], com["RB"], com["NB"])
        with ExitStack() as stk:
            phase_h(p, "h", com, aps["x1"], stk)
        with ExitStack() as stk:
            gw_sb = p.sb("gw_sb", [128, DC * 8], BF16, stk)
            ics_sb = p.sb("ics_sb", [8, D], BF16, stk)
            lg_fm = p.sb("lg_fm", [8, T], F32, stk)
            lg_tm = p.sb("lg_tm", [128, 128], F32, stk)
            scr_tm = p.sb("scr_tm", [128, 32], F32, stk)
            r_tm = p.sb("r_tm", [128, 128], F32, stk)
            ldg = p.sem("ldg")
            with nc.Block() as block:
                @block.gpsimd
                def _(g):
                    g.dma_start(out=gw_sb[:, :], in_=aps["gw"]).then_inc(ldg, 16)
                    g.dma_start(out=ics_sb[:, :], in_=aps["ics"]).then_inc(ldg, 16)
                    g.dma_start(out=ocs_sb[:, :], in_=aps["ocs"]).then_inc(ldg, 16)
                    g.wait_ge(ldg, p.semv["ldg"] + 48)
            p.semv["ldg"] += 48
            emit_gates(p, "gt", gw_sb, com["h_sb"], com["ps_aux"], cst,
                       lg_fm, lg_tm, scr_tm, r_tm, r_fmT)
            if dbg is not None:
                dsem = p.sem("dbgg")
                with nc.Block() as block:
                    @block.sync
                    def _(sync):
                        sync.dma_start(out=dbg["r_fmT"], in_=r_fmT[:, :]).then_inc(dsem, 16)
                        sync.dma_start(out=dbg["lg_fm"], in_=lg_fm[:, :]).then_inc(dsem, 16)
                        sync.dma_start(out=dbg["lg_tm"], in_=lg_tm[:, :]).then_inc(dsem, 16)
                        sync.dma_start(out=dbg["r_tm"], in_=r_tm[:, :]).then_inc(dsem, 16)
                        sync.wait_ge(dsem, p.semv["dbgg"] + 64)
                p.semv["dbgg"] += 64
            emit_upass(p, "up", ics_sb, r_fmT, com["h_sb"], com["ps_aux"], cst)
        with ExitStack() as stk:
            wraw = [p.sb(f"f_wraw{i}", [128, D], BF16, stk) for i in range(2)]
            os_sb = [p.sb(f"f_os{i}", [128, TW], F32, stk) for i in range(2)]
            tev = [p.sb(f"f_tev{i}", [128, TW], F32, stk) for i in range(2)]
            atmp = [p.sb(f"f_atmp{i}", [128, TW], F32, stk) for i in range(2)]
            sqb = [p.sb(f"f_sqtb{i}", [128, TW], F32, stk) for i in range(2)]
            emit_mm_l0f(p, "mm", aps["wf"], biasf_sb, ocs_sb, r_fmT,
                        com["h_sb"], aps["a_scr"], cst, com["ps_mm"],
                        com["ps_aux"][0], com["ps_s1"], com["ps_s2"],
                        wraw, os_sb, tev, atmp, sqb)
    with ExitStack() as stk:
        phase_rows2_update(p, "u", com, aps["x1"], aps["a_scr"], aps["x2"],
                           stk, do_stats=False)


def build_l0():
    p = Prog()
    aps = {
        "x0": p.din("x0", [D, T], F32),
        "w0": p.din("w0", [D, D], F32),
        "maskT": p.din("maskT", [D, D], I32),
        "bias0": p.din("bias0", [128, DC], F32),
        "gw": p.din("gw", [128, DC * 8], F32),
        "ics": p.din("ics", [8, D], F32),
        "ocs": p.din("ocs", [8, D], F32),
        "wf": p.din("wf", [D, D], F32),
        "biasf": p.din("biasf", [128, DC], F32),
        "x2": p.dout("x2", [D, T], F32),
        "a_scr": p.dint("a_scr", [D, T], F32),
        "x1": p.dint("x1_scr", [D, T], F32),
    }
    com = alloc_common(p)
    alloc_ln(p, com, p.ctx)
    emit_block0(p, com, aps)
    p.ctx.close()
    return p


def _tile_w(wT):
    K, N = wT.shape
    return np.ascontiguousarray(
        wT.reshape(K // 128, 128, N // 128, 128).transpose(2, 1, 0, 3))


def _shard_x(x):
    out = []
    for core in range(NCORES):
        b, half = core // 2, core % 2
        sl = x[b, half * T:(half + 1) * T, :]
        out.append(np.ascontiguousarray(sl.T))
    return out


def build_qkv():
    p = Prog()
    nc = p.nc
    x2 = p.din("x2", [D, T], F32)
    qc = p.din("qc", [D, D], I32)
    qs = p.din("qs", [DC, D], F32)
    kc = p.din("kc", [1024, D], I32)
    ks = p.din("ks", [8, D], F32)
    vc = p.din("vc", [1024, D], I32)
    vs = p.din("vs", [8, D], F32)
    cosd = p.din("cosd", [128, T], BF16)
    ssind = p.din("ssind", [128, T], BF16)
    q_out = p.dout("q_rot", [D, T], BF16)
    k_out = p.dout("k_rot", [1024, T], BF16)
    v_out = p.dout("v_tm", [T, 1024], BF16)

    com = alloc_common(p)
    cst = com["cst"]
    with ExitStack() as stkln:
        alloc_ln(p, com, stkln)
        with ExitStack() as stk:
            phase_stats_rows(p, "st2", com, x2, stk)
        with ExitStack() as stk:
            phase_h(p, "h2", com, x2, stk)
    with ExitStack() as stk:
        bufs = {
            "wbf": [p.sb(f"dwbf{i}", [128, D], BF16, stk) for i in range(2)],
            "scrow": p.sb("scrow", [1, D], F32, stk),
            "et": [p.sb(f"et{i}", [128, TW], BF16, stk) for i in range(2)],
            "esw": p.sb("esw", [128, TW], BF16, stk),
            "e1": p.sb("e1", [128, TW], BF16, stk),
            "e2": p.sb("e2", [128, TW], BF16, stk),
            "er": [p.sb(f"er{i}", [128, TW], BF16, stk) for i in range(2)],
            "COS": p.sb("COS", [128, T], BF16, stk),
            "SSIN": p.sb("SSIN", [128, T], BF16, stk),
        }
        psbt = [p.psum(f"psbt{i}", [128, TW], BF16, stk) for i in range(2)]
        ldc = p.sem("ldc")
        with nc.Block() as block:
            @block.sync
            def _(sync):
                sync.dma_start(out=bufs["COS"][:, :], in_=cosd).then_inc(ldc, 16)
                sync.dma_start(out=bufs["SSIN"][:, :], in_=ssind).then_inc(ldc, 16)
                sync.wait_ge(ldc, p.semv["ldc"] + 32)
        p.semv["ldc"] += 32
        emit_deqproj(p, "dq", qc, qs, DC, "rope", com["h_sb"], cst, bufs,
                     q_out, com["ps_mm"], com["ps_aux"], psbt)
        emit_deqproj(p, "dq", kc, ks, 8, "rope", com["h_sb"], cst, bufs,
                     k_out, com["ps_mm"], com["ps_aux"], psbt)
        emit_deqproj(p, "dq", vc, vs, 8, "vtrans", com["h_sb"], cst, bufs,
                     v_out, com["ps_mm"], com["ps_aux"], psbt)
    p.ctx.close()
    return p


def _rope_tables(half):
    pos = (half * T + np.arange(T, dtype=np.float64))
    jf = np.arange(64, dtype=np.float64)
    inv = 1.0 / (10000.0 ** (2.0 * jf / 128.0))
    f = pos[None, :] * inv[:, None]
    cos = np.repeat(np.cos(f), 2, axis=0)
    sin = np.sin(f)
    ssin = np.empty((128, T), np.float64)
    ssin[0::2] = -sin
    ssin[1::2] = sin
    bf = ml_dtypes.bfloat16
    return (np.ascontiguousarray(cos.astype(bf)),
            np.ascontiguousarray(ssin.astype(bf)))


def _prep_qkv(inputs):
    def sc_dev(s, n_chunks):
        return np.ascontiguousarray(
            s.reshape(32, n_chunks, 128).transpose(1, 0, 2).reshape(n_chunks, D))
    return {
        "qc": _tile_w(np.ascontiguousarray(inputs["L1_attn_q_codes"])).reshape(D, D),
        "qs": sc_dev(inputs["L1_attn_q_scales"], DC),
        "kc": _tile_w(np.ascontiguousarray(inputs["L1_attn_k_codes"])).reshape(1024, D),
        "ks": sc_dev(inputs["L1_attn_k_scales"], 8),
        "vc": _tile_w(np.ascontiguousarray(inputs["L1_attn_v_codes"])).reshape(1024, D),
        "vs": sc_dev(inputs["L1_attn_v_scales"], 8),
    }


def _prep_l0(inputs):
    maps = {
        "w0": _tile_w(np.ascontiguousarray(inputs["L0_attn_weight"].T)).reshape(D, D),
        "maskT": np.ascontiguousarray(inputs["L0_attn_mask"].T),
        "bias0": np.ascontiguousarray(inputs["L0_attn_bias"].reshape(DC, 128).T),
        "gw": np.ascontiguousarray(
            inputs["L0_ffn_gate_w"].T.reshape(DC, 128, 8).transpose(1, 0, 2)
            .reshape(128, DC * 8)),
        "ics": np.ascontiguousarray(inputs["L0_ffn_in_channel_scale"]),
        "ocs": np.ascontiguousarray(inputs["L0_ffn_out_channel_scale"]),
        "wf": _tile_w(np.ascontiguousarray(inputs["L0_ffn_weight"].T)).reshape(D, D),
        "biasf": np.ascontiguousarray(inputs["L0_ffn_bias"].reshape(DC, 128).T),
    }
    return maps


def build_full():
    p = Prog()
    nc = p.nc
    aps = {
        "x0": p.din("x0", [D, T], F32),
        "w0": p.din("w0", [D, D], F32),
        "maskT": p.din("maskT", [D, D], I32),
        "bias0": p.din("bias0", [128, DC], F32),
        "gw": p.din("gw", [128, DC * 8], F32),
        "ics": p.din("ics", [8, D], F32),
        "ocs": p.din("ocs", [8, D], F32),
        "wf": p.din("wf", [D, D], F32),
        "biasf": p.din("biasf", [128, DC], F32),
        "x1": p.dint("x1_scr", [D, T], F32),
        "x2": p.dint("x2_scr", [D, T], F32),
        "a_scr": p.dint("a_scr", [D, T], F32),
    }
    qc = p.din("qc", [D, D], I32)
    qs = p.din("qs", [DC, D], F32)
    kc = p.din("kc", [1024, D], I32)
    ks = p.din("ks", [8, D], F32)
    vc = p.din("vc", [1024, D], I32)
    vs = p.din("vs", [8, D], F32)
    cosd = p.din("cosd", [128, T], BF16)
    ssind = p.din("ssind", [128, T], BF16)
    tri = p.din("tri", [128, 128], F32)
    oc = p.din("oc", [D, D], I32)
    osd = p.din("osd", [DC, D], F32)
    aps_f1 = {
        "x3": p.dint("x3_scr", [D, T], F32),
        "bW": p.din("bW", [D, D], F32),
        "rw": p.din("rw", [128, DC * 8], F32),
        "lA": p.din("lA", [D, 128], F32),
        "lB": p.din("lB", [128, D], F32),
        "P2": p.din("P2", [8, 128], F32),
        "out": p.dout("out", [D, T], F32),
        "a_scr": aps["a_scr"],
    }
    agqk_in = p.dint("agqk_in", [5120, T], BF16)
    agqk_out = p.dint("agqk_out", [2 * 5120, T], BF16)
    agv_in = p.dint("agv_in", [T, 1024], BF16)
    agv_out = p.dint("agv_out", [2 * T, 1024], BF16)
    agat_in = p.dint("agat_in", [2048, 2 * T], BF16)
    agat_out = p.dint("agat_out", [4096, 2 * T], BF16)
    qfull = p.dint("qfull", [2048, 2 * T], BF16)
    kfull = p.dint("kfull", [512, 2 * T], BF16)
    vfull = p.dint("vfull", [2 * T, 512], BF16)
    attfull = p.dint("attfull", [D, T], BF16)

    com = alloc_common(p)
    cst = com["cst"]

    with ExitStack() as stkln:
        alloc_ln(p, com, stkln, tag="b0")
        emit_block0(p, com, aps, stack=stkln)
        with ExitStack() as stk:
            phase_stats_rows(p, "st", com, aps["x2"], stk)
        with ExitStack() as stk:
            phase_h(p, "h", com, aps["x2"], stk)

    qk_h = agqk_in.tensor
    v_h = agv_in.tensor

    def q_dst(o, j):
        return bass.AP(qk_h, o * 128 * T + j * TW, [[T, 128], [1, TW]])

    def k_dst(o, j):
        return bass.AP(qk_h, (4096 + o * 128) * T + j * TW, [[T, 128], [1, TW]])

    def v_dst(o, j):
        return bass.AP(v_h, j * TW * 1024 + o * 128,
                       [[1024, 128], [128 * 1024, 4], [1, 128]])

    with ExitStack() as stk:
        bufs = {
            "wbf": [p.sb(f"dwbf{i}", [128, D], BF16, stk) for i in range(2)],
            "scrow": p.sb("scrow", [1, D], F32, stk),
            "et": [p.sb(f"et{i}", [128, TW], BF16, stk) for i in range(2)],
            "esw": p.sb("esw", [128, TW], BF16, stk),
            "e1": p.sb("e1", [128, TW], BF16, stk),
            "e2": p.sb("e2", [128, TW], BF16, stk),
            "er": [p.sb(f"er{i}", [128, TW], BF16, stk) for i in range(2)],
            "COS": p.sb("COS", [128, T], BF16, stk),
            "SSIN": p.sb("SSIN", [128, T], BF16, stk),
        }
        psbt = [p.psum(f"psbt{i}", [128, TW], BF16, stk) for i in range(2)]
        ldc = p.sem("ldc")
        with nc.Block() as block:
            @block.sync
            def _(sync):
                sync.dma_start(out=bufs["COS"][:, :], in_=cosd).then_inc(ldc, 16)
                sync.dma_start(out=bufs["SSIN"][:, :], in_=ssind).then_inc(ldc, 16)
                sync.wait_ge(ldc, p.semv["ldc"] + 32)
        p.semv["ldc"] += 32
        emit_deqproj(p, "dq", qc, qs, DC, "rope", com["h_sb"], cst, bufs,
                     None, com["ps_mm"], com["ps_aux"], psbt, dst_fn=q_dst)
        emit_deqproj(p, "dq", kc, ks, 8, "rope", com["h_sb"], cst, bufs,
                     None, com["ps_mm"], com["ps_aux"], psbt, dst_fn=k_dst)

        cc = p.sem("cc")
        with nc.Block() as block:
            @block.gpsimd
            def _(g):
                g.collective_compute(
                    "AllGather", ALU.bypass,
                    replica_groups=[[0, 1], [2, 3], [4, 5], [6, 7]],
                    ins=[agqk_in.opt()], outs=[agqk_out.opt()],
                ).then_inc(cc)

        emit_deqproj(p, "dq", vc, vs, 8, "vtrans", com["h_sb"], cst, bufs,
                     None, com["ps_mm"], com["ps_aux"], psbt, dst_fn=v_dst)

    with nc.Block() as block:
        @block.gpsimd
        def _(g):
            g.collective_compute(
                "AllGather", ALU.bypass,
                replica_groups=[[0, 1], [2, 3], [4, 5], [6, 7]],
                ins=[agv_in.opt()], outs=[agv_out.opt()],
            ).then_inc(cc)
    p.semv["cc"] += 2

    pid = nc.partition_id()
    roq = (pid % 2) * 2048
    rok = (pid % 2) * 512
    rpk = p.sem("rpk")
    with nc.Block() as block:
        @block.gpsimd
        def _(g):
            g.wait_ge(cc, p.semv["cc"])
            n = 0
            for s2 in range(2):
                g.dma_start(out=qfull[:, s2 * T:(s2 + 1) * T],
                            in_=agqk_out[bass.ds(roq + s2 * 5120, 2048), :]
                            ).then_inc(rpk, 16)
                g.dma_start(out=kfull[:, s2 * T:(s2 + 1) * T],
                            in_=agqk_out[bass.ds(rok + (s2 * 5120 + 4096), 512), :]
                            ).then_inc(rpk, 16)
                g.dma_start(out=vfull[s2 * T:(s2 + 1) * T, :],
                            in_=agv_out[s2 * T:(s2 + 1) * T, bass.ds(rok, 512)]
                            ).then_inc(rpk, 16)
                n += 3
            g.wait_ge(rpk, p.semv["rpk"] + 16 * n)
    p.semv["rpk"] += 16 * 6

    with ExitStack() as stk:
        abufs = {
            "kg": p.sb("kg", [128, 2 * T], BF16, stk),
            "vg": p.sb("vg", [128, 2 * T], BF16, stk),
            "qh": p.sb("qh", [128, 2 * T], BF16, stk),
            "Srow": [p.sb(f"Srow{i}", [128, 2 * T], F32, stk) for i in range(2)],
            "Prow": [p.sb(f"Prow{i}", [128, 2 * T], BF16, stk) for i in range(2)],
            "PT": [p.sb(f"PTb{i}", [128, 1024], BF16, stk) for i in range(2)],
            "TRI": p.sb("TRI", [128, 128], F32, stk),
            "cmax": p.sb("cmax", [128, 8], F32, stk),
            "negmax": [p.sb(f"negmax{i}", [128, 1], F32, stk) for i in range(2)],
            "rowsum": [p.sb(f"rowsum{i}", [128, 1], F32, stk) for i in range(2)],
            "dinv": [p.sb(f"dinv{i}", [128, 1], F32, stk) for i in range(2)],
        }
        psbt = [p.psum(f"psbta{i}", [128, 1024], BF16, stk) for i in range(2)]
        ldt = p.sem("ldt")
        with nc.Block() as block:
            @block.sync
            def _(sync):
                sync.dma_start(out=abufs["TRI"][:, :], in_=tri).then_inc(ldt, 16)
                sync.wait_ge(ldt, p.semv["ldt"] + 16)
        p.semv["ldt"] += 16
        sta = p.sem("sta")

        def att_store(sync, hl):
            sync.dma_start(out=agat_in[hl * 128:(hl + 1) * 128, :],
                           in_=com["h_sb"][:, hl * 4096:(hl + 1) * 4096]
                           ).then_inc(sta, 16)

        emit_attention(p, "at", qfull, kfull, vfull, com["h_sb"], cst, abufs,
                       com["ps_mm"] + [com["ps_s1"], com["ps_s2"]],
                       com["ps_aux"], psbt, store_fn=att_store)

    with nc.Block() as block:
        @block.sync
        def _(sync):
            sync.wait_ge(sta, p.semv["sta"] + 16 * 16)
    p.semv["sta"] += 16 * 16
    with nc.Block() as block:
        @block.gpsimd
        def _(g):
            g.collective_compute(
                "AllGather", ALU.bypass,
                replica_groups=[[0, 1], [2, 3], [4, 5], [6, 7]],
                ins=[agat_in.opt()], outs=[agat_out.opt()],
            ).then_inc(cc)
            g.wait_ge(cc, p.semv["cc"] + 1)
    p.semv["cc"] += 1

    lda = p.sem("lda")
    ldb = p.sem("ldb")
    with nc.Block() as block:
        @block.gpsimd
        def _(g):
            coff = (nc.partition_id() % 2) * T
            g.dma_start(out=attfull[:, :],
                        in_=agat_out[:, bass.ds(coff, T)]).then_inc(lda, 16)
            g.wait_ge(lda, p.semv["lda"] + 16)
    p.semv["lda"] += 16
    with nc.Block() as block:
        @block.sync
        def _(sync):
            for c in range(DC):
                sync.dma_start(out=com["h_sb"][:, c * T:(c + 1) * T],
                               in_=attfull[c * 128:(c + 1) * 128, :]
                               ).then_inc(ldb, 16)
            sync.wait_ge(ldb, p.semv["ldb"] + 16 * DC)
    p.semv["ldb"] += 16 * DC

    with ExitStack() as stk:
        bufs = {
            "wbf": [p.sb(f"owbf{i}", [128, D], BF16, stk) for i in range(2)],
            "scrow": p.sb("oscrow", [1, D], F32, stk),
            "atmp": [p.sb(f"oatmp{i}", [128, TW], F32, stk) for i in range(2)],
            "sqb": [p.sb(f"osqb{i}", [128, TW], F32, stk) for i in range(2)],
        }
        emit_oproj(p, "mm", oc, osd, com["h_sb"], aps["a_scr"], cst,
                   com["ps_mm"], com["ps_aux"], com["ps_s1"], com["ps_s2"], bufs)
    with ExitStack() as stkln:
        alloc_ln(p, com, stkln, tag="u2")
        with ExitStack() as stk:
            phase_rows2_update(p, "u", com, aps["x2"], aps["a_scr"],
                               aps_f1["x3"], stk, do_stats=False)

    emit_ffn1(p, com, aps_f1)
    p.ctx.close()
    return p


def _full_in_maps(inputs):
    x_sh = _shard_x(inputs["L0_attn_data"].astype(np.float32, copy=False))
    w_l0 = _prep_l0(inputs)
    w_qkv = _prep_qkv(inputs)
    w_f1 = _prep_ffn1(inputs)

    def sc_dev(s, n_chunks):
        return np.ascontiguousarray(
            s.reshape(32, n_chunks, 128).transpose(1, 0, 2).reshape(n_chunks, D))

    oc = _tile_w(np.ascontiguousarray(inputs["L1_attn_o_codes"])).reshape(D, D)
    osd = sc_dev(inputs["L1_attn_o_scales"], DC)
    tri = _tri_mask()
    in_maps = []
    for c in range(NCORES):
        cos, ssin = _rope_tables(c % 2)
        m = dict(w_l0)
        m.update(w_qkv)
        m.update(w_f1)
        m.update(x0=x_sh[c], cosd=cos, ssind=ssin, tri=tri, oc=oc, osd=osd)
        in_maps.append(m)
    return in_maps


def kernel(**inputs):
    in_maps = _full_in_maps(inputs)
    p = build_full()
    res = run_bass_kernel_spmd(p.nc, in_maps, core_ids=list(range(NCORES)))
    out = np.empty((4, 4096, 4096), np.float32)
    for c in range(NCORES):
        b, half = c // 2, c % 2
        out[b, half * T:(half + 1) * T, :] = res.results[c]["out"].T
    return out
